# version 19
# speedup vs baseline: 1.1116x; 1.0071x over previous
import numpy as np
import ml_dtypes

import concourse.bass as bass
from concourse.bass import PSumTensorHandle
import concourse.mybir as mybir
from concourse.bass_utils import run_bass_kernel_spmd

F32 = mybir.dt.float32
BF16 = mybir.dt.bfloat16
I32 = mybir.dt.int32
AF = mybir.ActivationFunctionType
ALU = mybir.AluOpType
AX = mybir.AxisListType
BF = ml_dtypes.bfloat16

N_CORE = 8
B_PER_CORE = 2
S = 2048
D = 1024
NTOK = B_PER_CORE * S
NTT = NTOK // 128
H = 16
DK = 64
N_EXP = 8
D_FF = 4096
D_INNER = 2048
D_STATE = 16
DT_RANK = 64
CONV_K = 4
HID_I8 = 3072
EPS = 1e-5
NEG = -1.0e30

ENGINES = ("sync", "scalar", "vector", "tensor", "gpsimd")


class P:
    def __init__(self, nc, es):
        self.nc = nc
        self.es = es
        self.thunks = {e: [] for e in ENGINES}
        self.sem = {e: es.enter_context(nc.semaphore("sem_" + e)) for e in ENGINES}
        self.NP = 2
        self.streams = {}
        self.gq_hist = []
        self.GQ_DEPTH = 4
        self.cnt = {e: 0 for e in ENGINES}
        self.waited = {e: {} for e in ENGINES}
        self.tiles = {}
        self.consts = {}

    def sb(self, name, pdim, fdim, dtype):
        es = getattr(self, "cur_es", None)
        if es is None:
            return self.nc.alloc_sbuf_tensor(name, [pdim, fdim], dtype)
        return es.enter_context(self.nc.sbuf_tensor(name, [pdim, fdim], dtype))

    import contextlib as _ctx

    @_ctx.contextmanager
    def scope(self):
        import contextlib
        prev = getattr(self, "cur_es", None)
        self.cur_es = contextlib.ExitStack()
        try:
            yield
        finally:
            self.cur_es.close()
            self.cur_es = prev

    def barrier(self):
        tgt = []
        for e in ENGINES:
            if self.cnt[e]:
                tgt.append((("e", e), self.sem[e], self.cnt[e]))
        for skey, st in self.streams.items():
            for i in range(self.NP):
                if st["per"][i]:
                    tgt.append((("d", (skey, i)), st["sems"][i],
                                16 * st["per"][i]))
        for e in ENGINES:
            wm = self.waited[e]
            waits = []
            for key, s, v in tgt:
                if key == ("e", e):
                    continue
                if wm.get(key, -1) < v:
                    wm[key] = v
                    waits.append((s, v))
            if waits:
                def thunk(engine, waits=waits):
                    for s, v in waits:
                        engine.wait_ge(s, v)
                self.thunks[e].append(thunk)

    def pf(self, i):
        key = ("f", i)
        if key not in self.consts:
            self.consts[key] = psum(self.nc, f"pbf32_{i}", 128, 512, F32)
        return self.consts[key]

    def pbf(self, i):
        key = ("b", i)
        if key not in self.consts:
            self.consts[key] = psum(self.nc, f"pbbf_{i}", 128, 1024, BF16)
        return self.consts[key]

    def ccol(self, val):
        if val not in self.consts:
            t = sb(self.nc, f"cc_{len(self.consts)}", 128, 1, F32)
            self.emit("vector", lambda v, t=t, val=val: v.memset(t[:], val),
                      writes=[t])
            self.consts[val] = t
        return self.consts[val]

    def warmup(self, funcs=(AF.Sqrt,)):
        nc = self.nc
        wsc = sb(nc, "warm_sc", 128, 1024, F32)
        wsb = sb(nc, "warm_sb", 128, 512, BF16)
        wcol = sb(nc, "warm_col", 128, 1, F32)
        wps = self.pf(5)
        wpsb = self.pbf(1)
        self.emit("vector", lambda v: v.memset(wsc[:], 1.0), writes=[wsc])
        self.emit("vector", lambda v: v.memset(wcol[:], 1.0), writes=[wcol])
        self.emit("vector", lambda v: v.tensor_copy(out=wsb[:], in_=wsc[:, 0:512]),
                  reads=[wsc], writes=[wsb])
        self.emit("tensor", lambda t: t.matmul(out=wps[:, 0:128], lhsT=wsb[:, 0:128],
                                               rhs=wsb[:, 0:128], start=True,
                                               stop=True), reads=[wsb], writes=[wps])
        self.emit("tensor", lambda t: t.matmul(out=wps[:, 0:128], lhsT=wsc[:, 0:128],
                                               rhs=wsc[:, 0:128], start=True,
                                               stop=True), reads=[wsc], writes=[wps])
        self.emit("gpsimd", lambda g: g.iota(out=wsc[:, 0:128], pattern=[[-1, 128]],
                                             base=0, channel_multiplier=1,
                                             allow_small_or_imprecise_dtypes=True),
                  writes=[wsc])
        self.emit("vector", lambda v: v.tensor_scalar(
            out=wsb[:, 0:128], in0=wsc[:, 0:128], scalar1=0.0, scalar2=None,
            op0=ALU.is_equal), reads=[wsc], writes=[wsb])
        self.emit("tensor", lambda t: t.transpose(out=wpsb[:, 0:128], in_=wsb[:, 0:128],
                                                  identity=wsb[:, 0:128]),
                  reads=[wsb], writes=[wpsb])
        self.emit("scalar", lambda s: s.copy(out=wsc[:, 0:128], in_=wps[:, 0:128]),
                  reads=[wps], writes=[wsc])
        self.emit("vector", lambda v: v.tensor_copy(out=wsb[:, 0:128],
                                                    in_=wpsb[:, 0:128]),
                  reads=[wpsb], writes=[wsb])
        self.emit("vector", lambda v: v.tensor_scalar(
            out=wsc[:], in0=wsc[:], scalar1=wcol[:, 0:1], scalar2=wcol[:, 0:1],
            op0=ALU.subtract, op1=ALU.mult), reads=[wsc, wcol], writes=[wsc])
        self.emit("vector", lambda v: v.scalar_tensor_tensor(
            out=wsc[:], in0=wsc[:], scalar=1.0, in1=wsc[:], op0=ALU.bypass,
            op1=ALU.mult, accum_out=wcol[:, 0:1]), reads=[wsc], writes=[wsc, wcol])
        self.emit("vector", lambda v: v.reduce_max(out=wcol[:], in_=wsc[:], axis=AX.X,
                                                   negate=True),
                  reads=[wsc], writes=[wcol])
        self.emit("vector", lambda v: v.reciprocal(out=wcol[:], in_=wcol[:]),
                  reads=[wcol], writes=[wcol])
        self.emit("vector", lambda v: v.tensor_tensor_scan(
            out=wsb[:], data0=wsb[:], data1=wsb[:], initial=0.0, op0=ALU.mult,
            op1=ALU.add), reads=[wsb], writes=[wsb])
        self.emit("gpsimd", lambda g: g.iota(out=wsb[:], pattern=[[1, 512]], base=0,
                                             channel_multiplier=1,
                                             allow_small_or_imprecise_dtypes=True),
                  writes=[wsb])
        self.emit("vector", lambda v: v.memset(wsc[:], 1.0), writes=[wsc])
        for f in funcs:
            self.emit("scalar", lambda s, f=f: s.activation(out=wsc[:], in_=wsc[:],
                                                            func=f),
                      reads=[wsc], writes=[wsc])
        for k in range(12):
            self.emit("scalar", lambda s: s.activation(out=wsc[:], in_=wsc[:],
                                                       func=AF.Copy),
                      reads=[wsc], writes=[wsc])
        self.emit("vector", lambda v: v.tensor_copy(out=wsb[:], in_=wsc[:, 0:512]),
                  reads=[wsc], writes=[wsb])
        self.emit("gpsimd", lambda g: g.tensor_copy(out=wsc[:, 0:512], in_=wsc[:, 512:1024]),
                  reads=[wsc], writes=[wsc])
        self.emit("tensor", lambda t: t.matmul(out=wps[:, 0:128], lhsT=wsb[:, 0:128],
                                               rhs=wsb[:, 0:128], start=True,
                                               stop=True), reads=[wsb], writes=[wps])
        self.emit("scalar", lambda s: s.copy(out=wsc[:, 0:128], in_=wps[:, 0:128]),
                  reads=[wps], writes=[wsc])

    def _sem_val(self, tok):
        kind, key, c = tok
        if kind == "e":
            return (self.sem[key], c, ("e", key))
        skey, idx = key
        return (self.streams[skey]["sems"][idx], 16 * c, ("d", key))

    def _entry(self, h):
        k = id(h)
        if k not in self.tiles:
            self.tiles[k] = {"w": None, "r": []}
        return self.tiles[k]

    def _resolve(self, eng, reads, writes):
        need = []
        for h in reads:
            e = self._entry(h)
            if e["w"] is not None:
                need.append(e["w"])
        for h in writes:
            e = self._entry(h)
            if e["w"] is not None:
                need.append(e["w"])
            need.extend(e["r"])
        waits = []
        wm = self.waited[eng]
        best = {}
        for tok in need:
            if tok[0] == "e" and tok[1] == eng and tok[2] == self.cnt[eng]:
                pass
            sem, val, key = self._sem_val(tok)
            if val > best.get(key, (None, -1))[1]:
                best[key] = (sem, val)
        for key, (sem, val) in best.items():
            if wm.get(key, -1) < val:
                wm[key] = val
                waits.append((sem, val))
        return waits

    def _update(self, tok, reads, writes):
        for h in writes:
            self.tiles[id(h)] = {"w": tok, "r": []}
        for h in reads:
            self._entry(h)["r"].append(tok)

    def emit(self, eng, build, reads=(), writes=()):
        assert not getattr(self, "_building", False), (
            "emit() during build replay (ccol called inside a lambda?)")
        rd, wr = [], list(writes)
        for h in reads:
            if isinstance(h, PSumTensorHandle):
                wr.append(h)
            else:
                rd.append(h)
        reads, writes = rd, wr
        waits = self._resolve(eng, reads, writes)
        sem = self.sem[eng]
        self.cnt[eng] += 1
        tok = ("e", eng, self.cnt[eng])

        def thunk(engine, waits=waits, build=build, sem=sem):
            for s, v in waits:
                engine.wait_ge(s, v)
            inst = build(engine)
            inst.then_inc(sem, 1)

        self.thunks[eng].append(thunk)
        self._update(tok, reads, writes)
        return tok

    def dma(self, eng, out, in_, accum=False, allow_slow=False, transpose=False):
        self._dma_transpose = transpose
        reads = [in_.tensor]
        writes = [out.tensor]
        if accum:
            reads.append(out.tensor)
        waits = self._resolve(eng, reads, writes)
        nbytes = 1
        for d in out.shape:
            nbytes *= int(d)
        nbytes *= np.dtype(mybir.dt.np(out.dtype)).itemsize
        skey = (eng, nbytes)
        if skey not in self.streams:
            self.streams[skey] = {
                "sems": [self.es.enter_context(
                    self.nc.semaphore(f"dma_{eng}_{len(self.streams)}_{i}"))
                    for i in range(self.NP)],
                "cnt": 0, "per": [0] * self.NP}
        st = self.streams[skey]
        idx = st["cnt"] % self.NP
        st["cnt"] += 1
        st["per"][idx] += 1
        sem = st["sems"][idx]
        tok = ("d", (skey, idx), st["per"][idx])
        kw = {}
        if accum:
            kw["accum_op"] = ALU.add
        if allow_slow:
            kw["allow_slow_non_contiguous"] = True
        if getattr(self, "_dma_transpose", False):
            kw["transpose"] = True

        gq = None
        if eng == "gpsimd":
            if len(self.gq_hist) >= self.GQ_DEPTH:
                gq = self.gq_hist[-self.GQ_DEPTH]
            self.gq_hist.append((sem, 16 * st["per"][idx]))

        def thunk(engine, waits=waits, out=out, in_=in_, sem=sem, kw=kw, gq=gq):
            for s, v in waits:
                engine.wait_ge(s, v)
            if gq is not None:
                engine.wait_ge(gq[0], gq[1])
            engine.dma_start(out=out, in_=in_, **kw).then_inc(sem, 16)

        self.thunks[eng].append(thunk)
        self._update(tok, reads, writes)
        return tok

    def idma(self, tag, build, reads=(), writes=()):
        eng = "gpsimd"
        waits = self._resolve(eng, list(reads), list(writes))
        skey = (eng, tag)
        if skey not in self.streams:
            self.streams[skey] = {
                "sems": [self.es.enter_context(
                    self.nc.semaphore(f"idma_{tag}_{i}"))
                    for i in range(self.NP)],
                "cnt": 0, "per": [0] * self.NP}
        st = self.streams[skey]
        idx = st["cnt"] % self.NP
        st["cnt"] += 1
        st["per"][idx] += 1
        sem = st["sems"][idx]
        tok = ("d", (skey, idx), st["per"][idx])

        if len(self.gq_hist) >= self.GQ_DEPTH:
            gq = self.gq_hist[-self.GQ_DEPTH]
        else:
            gq = None
        self.gq_hist.append((sem, 16 * st["per"][idx]))

        def thunk(engine, waits=waits, build=build, sem=sem, gq=gq):
            for s, v in waits:
                engine.wait_ge(s, v)
            if gq is not None:
                engine.wait_ge(gq[0], gq[1])
            build(engine).then_inc(sem, 16)

        self.thunks[eng].append(thunk)
        self._update(tok, list(reads), list(writes))
        return tok

    def finish(self):
        fins = []
        for skey, st in self.streams.items():
            for i in range(self.NP):
                if st["per"][i]:
                    fins.append((st["sems"][i], 16 * st["per"][i]))
        for e in ENGINES:
            if self.cnt[e] and e != "gpsimd":
                fins.append((self.sem[e], self.cnt[e]))

        def thunk(engine, fins=fins):
            for s, v in fins:
                engine.wait_ge(s, v)

        self.thunks["gpsimd"].append(thunk)

    def build(self):
        self.finish()
        self._building = True
        with self.nc.Block() as block:
            for eng in ENGINES:
                lst = self.thunks[eng]
                if not lst:
                    continue
                dec = getattr(block, eng)

                def body(engine, lst=lst):
                    for t in lst:
                        t(engine)

                dec(body)


def sb(nc, name, pdim, fdim, dtype):
    return nc.alloc_sbuf_tensor(name, [pdim, fdim], dtype)


def psum(nc, name, pdim=128, fdim=512, dtype=F32):
    return nc.alloc_psum_tensor(name, [pdim, fdim], dtype)


def build_identity(p, dtype, name):
    nc = p.nc
    idn = p.sb(name, 128, 128, dtype)
    iot = p.sb(name + "_i", 128, 128, F32)
    p.emit("gpsimd", lambda g: g.iota(out=iot[:], pattern=[[-1, 128]], base=0,
                                      channel_multiplier=1,
                                      allow_small_or_imprecise_dtypes=True),
           writes=[iot])
    p.emit("vector", lambda v: v.tensor_scalar(out=idn[:], in0=iot[:], scalar1=0.0,
                                               scalar2=None, op0=ALU.is_equal),
           reads=[iot], writes=[idn])
    return idn


def build_ones_row(p, name="ones_row", n=128):
    ones = p.sb(name, 1, n, F32)
    p.emit("vector", lambda v: v.memset(ones[:], 1.0), writes=[ones])
    return ones


def build_onehot0(p, name="onehot0"):
    oh = p.sb(name, 8, 128, F32)
    p.emit("vector", lambda v: v.memset(oh[:], 0.0), writes=[oh])
    p.emit("vector", lambda v: v.memset(oh[0:1, :], 1.0), reads=[oh], writes=[oh])
    return oh


def phase_ln(p, tag, x_hbm, xnT_hbm, ident_f32, gate=None, rstd2_hbm=None,
             xn_tm_hbm=None):
    nc = p.nc
    xb = [p.sb(tag + f"xb{i}", 128, D, F32) for i in range(2)]
    xn = [p.sb(tag + f"xn{i}", 128, D, F32) for i in range(2)]
    sq = [p.sb(tag + f"sq{i}", 128, D, F32) for i in range(2)]
    ssum = [p.sb(tag + f"ssum{i}", 128, 1, F32) for i in range(2)]
    sqs = [p.sb(tag + f"sqs{i}", 128, 1, F32) for i in range(2)]
    mean = [p.sb(tag + f"mean{i}", 128, 1, F32) for i in range(2)]
    sqd = [p.sb(tag + f"sqd{i}", 128, 1, F32) for i in range(2)]
    nvar = [p.sb(tag + f"nvar{i}", 128, 1, F32) for i in range(2)]
    sd = [p.sb(tag + f"sd{i}", 128, 1, F32) for i in range(2)]
    rstd = [p.sb(tag + f"rstd{i}", 128, 1, F32) for i in range(2)]
    pstr = [p.pf(0), p.pf(1)]
    epscol = p.ccol(EPS)
    if xn_tm_hbm is not None:
        xnb = [p.sb(tag + f"xnb{i}", 128, D, BF16) for i in range(2)]
    stage = [p.sb(tag + f"stg{j}", 128, NTOK, BF16) for j in range(8)]
    if gate is not None:
        gate_w_hbm, gate_b_hbm, glog_hbm = gate
        gw = p.sb(tag + "gw", 128, 8 * 8, F32)
        gb = p.sb(tag + "gb", 8, 8, F32)
        oh0 = build_onehot0(p, tag + "oh0")
        p.emit("vector", lambda v: v.memset(gb[:], 0.0), writes=[gb])
        xnTf = [p.sb(tag + f"xnTf{i}", 128, D, F32) for i in range(2)]
        glog_ps = [p.pf(2), p.pf(3)]
        glog_sb = [p.sb(tag + f"glsb{i}", 128, 8, F32) for i in range(2)]
        for k in range(8):
            p.dma("sync", gw[:, 8 * k:8 * (k + 1)], gate_w_hbm[128 * k:128 * (k + 1), :])
        p.dma("sync", gb[0:1, :], gate_b_hbm[:])
    if rstd2_hbm is not None:
        sq2s = [p.sb(tag + f"sq2s{i}", 128, 1, F32) for i in range(2)]
        sd2 = [p.sb(tag + f"sd2{i}", 128, 1, F32) for i in range(2)]
        rstd2 = [p.sb(tag + f"rstd2{i}", 128, 1, F32) for i in range(2)]
        r2ps = p.pf(4)
        r2row = p.sb(tag + "r2row", 1, NTOK, F32)

    for tt in range(NTT):
        i = tt % 2
        xt, xnt = xb[i], xn[i]
        p.dma("sync", xt[:], x_hbm[128 * tt:128 * (tt + 1), :])
        p.emit("vector", lambda v, xt=xt, i=i: v.reduce_sum(
            out=ssum[i][:], in_=xt[:], axis=AX.X), reads=[xt], writes=[ssum[i]])
        p.emit("vector", lambda v, xt=xt, i=i: v.scalar_tensor_tensor(
            out=sq[i][:], in0=xt[:], scalar=1.0, in1=xt[:], op0=ALU.bypass,
            op1=ALU.mult, accum_out=sqs[i][:, 0:1]), reads=[xt],
            writes=[sq[i], sqs[i]])
        p.emit("vector", lambda v, i=i: v.tensor_scalar(
            out=mean[i][:], in0=ssum[i][:], scalar1=1.0 / D, scalar2=None,
            op0=ALU.mult), reads=[ssum[i]], writes=[mean[i]])
        p.emit("vector", lambda v, i=i: v.tensor_scalar(
            out=sqd[i][:], in0=sqs[i][:], scalar1=1.0 / D, scalar2=None,
            op0=ALU.mult), reads=[sqs[i]], writes=[sqd[i]])
        p.emit("vector", lambda v, i=i: v.scalar_tensor_tensor(
            out=nvar[i][:], in0=mean[i][:], scalar=mean[i][:, 0:1], in1=sqd[i][:],
            op0=ALU.mult, op1=ALU.subtract), reads=[mean[i], sqd[i]],
            writes=[nvar[i]])
        p.emit("scalar", lambda s, i=i: s.activation(
            out=sd[i][:], in_=nvar[i][:], func=AF.Sqrt, scale=-1.0,
            bias=epscol[:, 0:1]), reads=[nvar[i], epscol], writes=[sd[i]])
        p.emit("vector", lambda v, i=i: v.reciprocal(out=rstd[i][:], in_=sd[i][:]),
               reads=[sd[i]], writes=[rstd[i]])
        p.emit("vector", lambda v, xt=xt, xnt=xnt, i=i: v.tensor_scalar(
            out=xnt[:], in0=xt[:], scalar1=mean[i][:, 0:1], scalar2=rstd[i][:, 0:1],
            op0=ALU.subtract, op1=ALU.mult), reads=[xt, mean[i], rstd[i]],
            writes=[xnt])
        if xn_tm_hbm is not None:
            p.emit("vector", lambda v, xnt=xnt, i=i: v.tensor_copy(
                out=xnb[i][:], in_=xnt[:]), reads=[xnt], writes=[xnb[i]])
            p.dma("gpsimd", xn_tm_hbm[128 * tt:128 * (tt + 1), :], xnb[i][:])
        if rstd2_hbm is not None:
            p.emit("vector", lambda v, xnt=xnt, i=i: v.scalar_tensor_tensor(
                out=sq[i][:], in0=xnt[:], scalar=1.0, in1=xnt[:], op0=ALU.bypass,
                op1=ALU.mult, accum_out=sq2s[i][:, 0:1]), reads=[xnt],
                writes=[sq[i], sq2s[i]])
            p.emit("scalar", lambda s, i=i: s.activation(
                out=sd2[i][:], in_=sq2s[i][:], func=AF.Sqrt, scale=1.0 / D,
                bias=epscol[:, 0:1]), reads=[sq2s[i], epscol], writes=[sd2[i]])
            p.emit("vector", lambda v, i=i: v.reciprocal(
                out=rstd2[i][:], in_=sd2[i][:]), reads=[sd2[i]], writes=[rstd2[i]])
            p.emit("tensor", lambda t, i=i: t.transpose(
                out=r2ps[0:1, 0:128], in_=rstd2[i][:], identity=ident_f32[:]),
                reads=[rstd2[i], ident_f32], writes=[r2ps])
            p.emit("scalar", lambda s, tt=tt: s.copy(
                out=r2row[0:1, 128 * tt:128 * (tt + 1)], in_=r2ps[0:1, 0:128]),
                reads=[r2ps], writes=[r2row])
        for j in range(8):
            pt = pstr[j % 2]
            p.emit("tensor", lambda t, xnt=xnt, pt=pt, j=j: t.transpose(
                out=pt[:, 0:128], in_=xnt[:, 128 * j:128 * (j + 1)],
                identity=ident_f32[:]), reads=[xnt, ident_f32], writes=[pt])
            p.emit("vector", lambda v, pt=pt, j=j, tt=tt: v.tensor_copy(
                out=stage[j][:, 128 * tt:128 * (tt + 1)], in_=pt[:, 0:128]),
                reads=[pt], writes=[stage[j]])
            if gate is not None:
                p.emit("scalar", lambda s, pt=pt, j=j, i=i: s.copy(
                    out=xnTf[i][:, 128 * j:128 * (j + 1)], in_=pt[:, 0:128]),
                    reads=[pt], writes=[xnTf[i]])
        if gate is not None:
            gps = glog_ps[i]

            def gate_mms(t, i=i, gps=gps):
                for j in range(8):
                    t.matmul(out=gps[:, 0:8], lhsT=xnTf[i][:, 128 * j:128 * (j + 1)],
                             rhs=gw[:, 8 * j:8 * (j + 1)], start=(j == 0), stop=False)
                return t.matmul(out=gps[:, 0:8], lhsT=oh0[:], rhs=gb[:],
                                start=False, stop=True)
            p.emit("tensor", gate_mms, reads=[xnTf[i], gw, gb, oh0], writes=[gps])
            p.emit("scalar", lambda s, i=i, gps=gps: s.copy(
                out=glog_sb[i][:], in_=gps[:, 0:8]), reads=[gps],
                writes=[glog_sb[i]])
            p.dma("gpsimd", glog_hbm[128 * tt:128 * (tt + 1), :], glog_sb[i][:])
    for j in range(8):
        p.dma("gpsimd", xnT_hbm[128 * j:128 * (j + 1), :], stage[j][:])
    if rstd2_hbm is not None:
        p.dma("gpsimd", rstd2_hbm[:], r2row[:])



def load_weight_T(p, w_hbm, rows, cols, ident_bf, tag, raw=None, wT=None):
    nc = p.nc
    rch = rows // 128
    cch = cols // 128
    if raw is None:
        raw = p.sb(tag + "_raw", 128, rch * cols, BF16)
    if wT is None:
        wT = p.sb(tag + "_T", 128, cch * rows, BF16)
    ps = [p.pbf(0), p.pbf(1)]
    for i in range(rch):
        p.dma("gpsimd", raw[:, i * cols:(i + 1) * cols],
              w_hbm[i * 128:(i + 1) * 128, :])
    for i in range(rch):
        for j in range(cch):
            pt = ps[(i * cch + j) % 2]
            p.emit("tensor", lambda t, i=i, j=j, pt=pt: t.transpose(
                out=pt[:, 0:128], in_=raw[:, i * cols + j * 128:i * cols + (j + 1) * 128],
                identity=ident_bf[:]), reads=[raw, ident_bf], writes=[pt])
            p.emit("scalar", lambda s, i=i, j=j, pt=pt: s.copy(
                out=wT[:, j * rows + i * 128:j * rows + (i + 1) * 128],
                in_=pt[:, 0:128]), reads=[pt], writes=[wT])
    return wT, raw


def load_weight_direct(p, w_hbm, rows, cols, tag):
    nc = p.nc
    rch = rows // 128
    w = p.sb(tag + "_W", 128, rch * cols, BF16)
    for j in range(rch):
        p.dma("gpsimd", w[:, j * cols:(j + 1) * cols],
              w_hbm[j * 128:(j + 1) * 128, :])
    return w


def load_xnT(p, xnT_hbm, tag):
    xs = [p.sb(tag + f"_x{j}", 128, NTOK, BF16) for j in range(8)]
    for j in range(8):
        p.dma("sync", xs[j][:], xnT_hbm[128 * j:128 * (j + 1), :])
    return xs


def phase_qkv(p, xnT_hbm, wq_hbm, wk_hbm, wv_hbm, qT_hbm, kT_hbm, v_hbm, ident_bf):
    nc = p.nc
    raw = None
    wqT, raw = load_weight_T(p, wq_hbm, D, D, ident_bf, "wq", raw)
    wkT, raw = load_weight_T(p, wk_hbm, D, D, ident_bf, "wk", raw)
    wvT, raw = load_weight_T(p, wv_hbm, D, D, ident_bf, "wv", raw)
    xn = load_xnT(p, xnT_hbm, "p2")
    pps = [p.pf(0), p.pf(1)]
    stg = [p.sb(f"p2stg{i}", 128, NTOK, BF16) for i in range(2)]
    vstg = [p.sb(f"p2vs{i}", 128, D, BF16) for i in range(2)]
    for wi, (wT, oT) in enumerate(((wqT, qT_hbm), (wkT, kT_hbm))):
        for m in range(8):
            st = stg[m % 2]
            for tc in range(8):
                pt = pps[tc % 2]

                def mms(t, wT=wT, m=m, tc=tc, pt=pt):
                    for k in range(8):
                        inst = t.matmul(
                            out=pt[:], lhsT=wT[:, k * D + 128 * m:k * D + 128 * (m + 1)],
                            rhs=xn[k][:, 512 * tc:512 * (tc + 1)],
                            start=(k == 0), stop=(k == 7))
                    return inst
                p.emit("tensor", mms, reads=[wT] + xn, writes=[pt])
                p.emit("vector", lambda v, st=st, tc=tc, pt=pt: v.tensor_copy(
                    out=st[:, 512 * tc:512 * (tc + 1)], in_=pt[:]),
                    reads=[pt], writes=[st])
            p.dma("gpsimd", oT[128 * m:128 * (m + 1), :], st[:])
    for tt in range(NTT):
        vst = vstg[tt % 2]
        for n in range(2):
            pt = pps[n % 2]

            def mms(t, tt=tt, n=n, pt=pt):
                for k in range(8):
                    inst = t.matmul(
                        out=pt[:], lhsT=xn[k][:, 128 * tt:128 * (tt + 1)],
                        rhs=wvT[:, k * D + 512 * n:k * D + 512 * (n + 1)],
                        start=(k == 0), stop=(k == 7))
                return inst
            p.emit("tensor", mms, reads=[wvT] + xn, writes=[pt])
            p.emit("vector", lambda v, vst=vst, n=n, pt=pt: v.tensor_copy(
                out=vst[:, 512 * n:512 * (n + 1)], in_=pt[:]),
                reads=[pt], writes=[vst])
        p.dma("gpsimd", v_hbm[128 * tt:128 * (tt + 1), :], vst[:])


def phase_att(p, qT_hbm, kT_hbm, v_hbm, slopes_hbm, aoT_hbm, ident_bf):
    nc = p.nc
    KW = 384
    rel = p.sb("rel", 128, KW, F32)
    tmpm = p.sb("tmpm", 128, KW, F32)
    maskadd = p.sb("maskadd", 128, KW, F32)
    bias = p.sb("bias", 128, H * KW, F32)
    oh0 = build_onehot0(p, "att_oh0")
    sl8 = p.sb("sl8", 8, 16, F32)
    slbc = p.sb("slbc", 128, 16, F32)
    slps = p.pf(2)
    p.emit("gpsimd", lambda g: g.iota(out=rel[:], pattern=[[-1, KW]], base=128,
                                      channel_multiplier=1,
                                      allow_small_or_imprecise_dtypes=True),
           writes=[rel])
    p.emit("vector", lambda v: v.memset(sl8[:], 0.0), writes=[sl8])
    p.dma("sync", sl8[0:1, :], slopes_hbm[:])
    p.emit("tensor", lambda t: t.matmul(out=slps[:, 0:16], lhsT=oh0[:], rhs=sl8[:],
                                        start=True, stop=True),
           reads=[oh0, sl8], writes=[slps])
    p.emit("scalar", lambda s: s.copy(out=slbc[:], in_=slps[:, 0:16]),
           reads=[slps], writes=[slbc])
    p.emit("vector", lambda v: v.tensor_scalar(out=tmpm[:], in0=rel[:], scalar1=128.0,
                                               scalar2=None, op0=ALU.is_gt),
           reads=[rel], writes=[tmpm])
    p.emit("vector", lambda v: v.tensor_scalar(out=maskadd[:], in0=rel[:],
                                               scalar1=-128.0, scalar2=None,
                                               op0=ALU.is_lt),
           reads=[rel], writes=[maskadd])
    p.emit("vector", lambda v: v.tensor_add(out=maskadd[:], in0=maskadd[:],
                                            in1=tmpm[:]),
           reads=[tmpm, maskadd], writes=[maskadd])
    p.emit("vector", lambda v: v.tensor_scalar(out=maskadd[:], in0=maskadd[:],
                                               scalar1=NEG, scalar2=None,
                                               op0=ALU.mult),
           reads=[maskadd], writes=[maskadd])
    for h in range(H):
        p.emit("scalar", lambda s, h=h: s.mul(out=bias[:, KW * h:KW * (h + 1)],
                                              in_=rel[:], mul=slbc[:, h:h + 1]),
               reads=[rel, slbc], writes=[bias])
        p.emit("vector", lambda v, h=h: v.tensor_add(
            out=bias[:, KW * h:KW * (h + 1)], in0=bias[:, KW * h:KW * (h + 1)],
            in1=maskadd[:]), reads=[maskadd, bias], writes=[bias])
    qb = [p.sb(f"att_q{j}", 128, S, BF16) for j in range(8)]
    kb = [p.sb(f"att_k{j}", 128, S, BF16) for j in range(8)]
    vb = [p.sb(f"att_v{j}", 128, D, BF16) for j in range(16)]
    ao = [p.sb(f"att_o{j}", 128, S, BF16) for j in range(8)]
    ND = 4
    scs = [p.sb(f"att_sc{i}", 128, KW, F32) for i in range(ND)]
    esc = [p.sb(f"att_esc{i}", 128, KW, F32) for i in range(ND)]
    attb = [p.sb(f"att_ab{i}", 128, KW, BF16) for i in range(ND)]
    attT = [p.sb(f"att_aT{i}", 128, KW, BF16) for i in range(ND)]
    nmx = [p.sb(f"att_nm{i}", 128, 1, F32) for i in range(ND)]
    ssm = [p.sb(f"att_ss{i}", 128, 1, F32) for i in range(ND)]
    rec = [p.sb(f"att_rc{i}", 128, 1, F32) for i in range(ND)]
    psc = [p.pf(0), p.pf(1), p.pf(3), p.pf(5)]
    ptr = [p.pbf(0), p.pbf(1)]
    pav = [p.pf(2), p.pf(4)]
    NQT = S // 128
    for b in range(B_PER_CORE):
        c0 = b * S
        for j in range(8):
            p.dma("sync", qb[j][:], qT_hbm[128 * j:128 * (j + 1), c0:c0 + S])
            p.dma("sync", kb[j][:], kT_hbm[128 * j:128 * (j + 1), c0:c0 + S])
        for j in range(16):
            p.dma("sync", vb[j][:], v_hbm[c0 + 128 * j:c0 + 128 * (j + 1), :])
        iters = []
        for h in range(H):
            for qt in range(NQT):
                qs = 128 * qt
                k0 = qs - 128
                cb0 = 0
                if qt == 0:
                    k0, kw, cb0 = 0, 256, 128
                elif qt == NQT - 1:
                    kw = 256
                else:
                    kw = KW
                iters.append((h // 2, 64 * (h % 2), h, qs, k0, kw, cb0))
        NIT = len(iters)
        NST = 9

        def stage(s, it):
            ch, r0, h, qs, k0, kw, cb0 = iters[it]
            i = it % ND
            i2 = it % 2
            nch = kw // 128
            if s == 0:
                p.emit("tensor", lambda t: t.matmul(
                    out=psc[i][:, 0:kw], lhsT=qb[ch][r0:r0 + 64, qs:qs + 128],
                    rhs=kb[ch][r0:r0 + 64, k0:k0 + kw], start=True, stop=True),
                    reads=[qb[ch], kb[ch]], writes=[psc[i]])
            elif s == 1:
                p.emit("vector", lambda v: v.scalar_tensor_tensor(
                    out=scs[i][:, 0:kw], in0=psc[i][:, 0:kw], scalar=0.125,
                    in1=bias[:, KW * h + cb0:KW * h + cb0 + kw],
                    op0=ALU.mult, op1=ALU.add),
                    reads=[psc[i], bias], writes=[scs[i]])
                p.emit("vector", lambda v: v.reduce_max(
                    out=nmx[i][:], in_=scs[i][:, 0:kw], axis=AX.X, negate=True),
                    reads=[scs[i]], writes=[nmx[i]])
            elif s == 2:
                p.emit("scalar", lambda s_: s_.activation(
                    out=esc[i][:, 0:kw], in_=scs[i][:, 0:kw], func=AF.Exp,
                    bias=nmx[i][:, 0:1], accum_out=ssm[i][:, 0:1]),
                    reads=[scs[i], nmx[i]], writes=[esc[i], ssm[i]])
            elif s == 3:
                p.emit("vector", lambda v: v.reciprocal(
                    out=rec[i][:], in_=ssm[i][:]), reads=[ssm[i]], writes=[rec[i]])
            elif s == 4:
                p.emit("scalar", lambda s_: s_.activation(
                    out=attb[i][:, 0:kw], in_=esc[i][:, 0:kw], func=AF.Copy,
                    scale=rec[i][:, 0:1]), reads=[esc[i], rec[i]],
                    writes=[attb[i]])
            elif s == 5:
                for c in range(nch):
                    p.emit("tensor", lambda t, c=c: t.transpose(
                        out=ptr[i2][:, 128 * c:128 * (c + 1)],
                        in_=attb[i][:, 128 * c:128 * (c + 1)],
                        identity=ident_bf[:]),
                        reads=[attb[i], ident_bf], writes=[ptr[i2]])
            elif s == 6:
                p.emit("scalar", lambda s_: s_.copy(
                    out=attT[i][:, 0:kw], in_=ptr[i2][:, 0:kw]),
                    reads=[ptr[i2]], writes=[attT[i]])
            elif s == 7:
                def avmms(t):
                    for c in range(nch):
                        kt = (k0 + 128 * c) // 128
                        inst = t.matmul(
                            out=pav[i2][0:64, 0:128],
                            lhsT=vb[kt][:, 64 * h:64 * (h + 1)],
                            rhs=attT[i][:, 128 * c:128 * (c + 1)],
                            start=(c == 0), stop=(c == nch - 1))
                    return inst
                p.emit("tensor", avmms, reads=[attT[i]] + vb, writes=[pav[i2]])
            elif s == 8:
                p.emit("vector", lambda v: v.tensor_copy(
                    out=ao[ch][r0:r0 + 64, qs:qs + 128],
                    in_=pav[i2][0:64, 0:128]),
                    reads=[pav[i2]], writes=[ao[ch]])

        for slot in range(NIT + NST - 1):
            for s in range(NST - 1, -1, -1):
                it = slot - s
                if 0 <= it < NIT:
                    stage(s, it)
        for j in range(8):
            p.dma("gpsimd", aoT_hbm[128 * j:128 * (j + 1), c0:c0 + S], ao[j][:])

def phase_wo(p, aoT_hbm, wo_hbm, s_hbm, ident_bf, ident_f32):
    nc = p.nc
    woT, _ = load_weight_T(p, wo_hbm, D, D, ident_bf, "wo")
    ao = load_xnT(p, aoT_hbm, "p4")
    pps = [p.pf(0), p.pf(1)]
    ptr = [p.pf(2), p.pf(3)]
    sT = [p.sb(f"p4sT{i}", 128, 512, F32) for i in range(2)]
    stg = [p.sb(f"p4stg{i}", 128, D, F32) for i in range(4)]
    for tc in range(8):
        for m in range(8):
            i = (tc * 8 + m) % 2

            def mms(t, m=m, tc=tc, i=i):
                for k in range(8):
                    inst = t.matmul(
                        out=pps[i][:], lhsT=woT[:, k * D + 128 * m:k * D + 128 * (m + 1)],
                        rhs=ao[k][:, 512 * tc:512 * (tc + 1)],
                        start=(k == 0), stop=(k == 7))
                return inst
            p.emit("tensor", mms, reads=[woT] + ao, writes=[pps[i]])
            p.emit("vector", lambda v, i=i: v.tensor_copy(out=sT[i][:], in_=pps[i][:]),
                   reads=[pps[i]], writes=[sT[i]])
            for u in range(4):
                p.emit("tensor", lambda t, i=i, u=u: t.transpose(
                    out=ptr[i][:, 0:128], in_=sT[i][:, 128 * u:128 * (u + 1)],
                    identity=ident_f32[:]), reads=[sT[i], ident_f32], writes=[ptr[i]])
                p.emit("scalar", lambda s, i=i, u=u, m=m: s.copy(
                    out=stg[u][:, 128 * m:128 * (m + 1)], in_=ptr[i][:, 0:128]),
                    reads=[ptr[i]], writes=[stg[u]])
        for u in range(4):
            tt = tc * 4 + u
            p.dma("gpsimd", s_hbm[128 * tt:128 * (tt + 1), :], stg[u][:])


def _run(build_fn, in_map_fn, out_names, n_cores=N_CORE):
    import contextlib
    es = contextlib.ExitStack()
    nc = bass.Bass("TRN2", target_bir_lowering=False)
    with es:
        p = P(nc, es)
        build_fn(p)
        p.build()
        in_maps = [in_map_fn(c) for c in range(n_cores)]
        res = run_bass_kernel_spmd(nc, in_maps, core_ids=list(range(n_cores)))
    return [[res.results[c][n] for c in range(n_cores)] for n in out_names]


def phase_prep_ffnw(p, wg_hbm, wu_hbm, wd_hbm, wgT_hbm, wuT_hbm, wdT_hbm,
                    ident_bf):
    raw = p.sb("pf_raw", 128, 24 * 1024, BF16)
    wT = p.sb("pf_wT", 128, 8 * HID_I8, BF16)
    for w_hbm, wT_hbm, rows, cols in ((wg_hbm, wgT_hbm, HID_I8, D),
                                      (wu_hbm, wuT_hbm, HID_I8, D),
                                      (wd_hbm, wdT_hbm, D, HID_I8)):
        load_weight_T(p, w_hbm, rows, cols, ident_bf, "pf", raw=raw, wT=wT)
        cch = cols // 128
        for j in range(cch):
            p.dma("gpsimd", wT_hbm[128 * j:128 * (j + 1), :],
                  wT[:, j * rows:(j + 1) * rows])


def phase_moe(p, xnT_hbm, glog_hbm, w1_hbm, b1_hbm, w2_hbm, b2_hbm, gT_hbm,
              ident_f32):
    nc = p.nc
    TCW = 512
    NTC = NTOK // TCW
    gl = [p.sb(f"m_gl{i}", 128, N_EXP, F32) for i in range(2)]
    nmx = [p.sb(f"m_nm{i}", 128, 1, F32) for i in range(2)]
    ssm = [p.sb(f"m_ss{i}", 128, 1, F32) for i in range(2)]
    rec = [p.sb(f"m_rc{i}", 128, 1, F32) for i in range(2)]
    pb = [p.sb(f"m_pb{i}", 128, N_EXP, F32) for i in range(2)]
    mx2 = [p.sb(f"m_mx{i}", 128, 1, F32) for i in range(2)]
    msk = [p.sb(f"m_msk{i}", 128, N_EXP, F32) for i in range(2)]
    wgt = [p.sb(f"m_wgt{i}", 128, N_EXP, F32) for i in range(2)]
    wgtT = p.sb("m_wgtT", 8, NTOK, F32)
    ptr = p.pf(4)
    for tt in range(NTT):
        i = tt % 2
        p.dma("sync", gl[i][:], glog_hbm[128 * tt:128 * (tt + 1), :])
        p.emit("vector", lambda v, i=i: v.reduce_max(
            out=nmx[i][:], in_=gl[i][:], axis=AX.X, negate=True),
            reads=[gl[i]], writes=[nmx[i]])
        p.emit("scalar", lambda s, i=i: s.activation(
            out=pb[i][:], in_=gl[i][:], func=AF.Exp, bias=nmx[i][:, 0:1],
            accum_out=ssm[i][:, 0:1]), reads=[gl[i], nmx[i]],
            writes=[pb[i], ssm[i]])
        p.emit("vector", lambda v, i=i: v.reciprocal(out=rec[i][:], in_=ssm[i][:]),
               reads=[ssm[i]], writes=[rec[i]])
        p.emit("vector", lambda v, i=i: v.tensor_scalar(
            out=pb[i][:], in0=pb[i][:], scalar1=rec[i][:, 0:1], scalar2=None,
            op0=ALU.mult), reads=[pb[i], rec[i]], writes=[pb[i]])
        p.emit("vector", lambda v, i=i: v.reduce_max(
            out=mx2[i][:], in_=pb[i][:], axis=AX.X),
            reads=[pb[i]], writes=[mx2[i]])
        p.emit("vector", lambda v, i=i: v.tensor_scalar(
            out=msk[i][:], in0=pb[i][:], scalar1=mx2[i][:, 0:1], scalar2=None,
            op0=ALU.is_equal), reads=[pb[i], mx2[i]], writes=[msk[i]])
        p.emit("vector", lambda v, i=i: v.tensor_mul(
            out=wgt[i][:], in0=pb[i][:], in1=msk[i][:]),
            reads=[pb[i], msk[i]], writes=[wgt[i]])
        p.emit("tensor", lambda t, i=i: t.transpose(
            out=ptr[0:8, 0:128], in_=wgt[i][:], identity=ident_f32[:]),
            reads=[wgt[i], ident_f32], writes=[ptr])
        p.emit("scalar", lambda s, tt=tt: s.copy(
            out=wgtT[:, 128 * tt:128 * (tt + 1)], in_=ptr[0:8, 0:128]),
            reads=[ptr], writes=[wgtT])
    ohs = p.sb("m_ohs", 8, 8 * 128, F32)
    ohi = p.sb("m_ohi", 8, 8 * 128, F32)
    p.emit("gpsimd", lambda g: g.iota(out=ohi[:], pattern=[[-1, 8], [0, 128]],
                                      base=0, channel_multiplier=1,
                                      allow_small_or_imprecise_dtypes=True),
           writes=[ohi])
    p.emit("vector", lambda v: v.tensor_scalar(out=ohs[:], in0=ohi[:], scalar1=0.0,
                                               scalar2=None, op0=ALU.is_equal),
           reads=[ohi], writes=[ohs])
    NW = 3
    w1c = [p.sb(f"m_w1c{i}", 128, 8 * 128, BF16) for i in range(NW)]
    w1b = [p.sb(f"m_w1b{i}", 128, 8 * 128, BF16) for i in range(NW)]
    w2c = [p.sb(f"m_w2c{i}", 128, 32 * 128, BF16) for i in range(NW)]
    xsl = [p.sb(f"m_xsl{i}", 128, 8 * TCW, BF16) for i in range(2)]
    h_sb = [p.sb(f"m_h{i}", 128, 32 * TCW, BF16) for i in range(2)]
    wbc = [p.sb(f"m_wbc{i}", 128, TCW, F32) for i in range(2)]
    b1r = p.sb("m_b1r", 32, 128, F32)
    b2r = p.sb("m_b2r", 8, 128, F32)
    b1a = p.sb("m_b1a", 128, 8 * 32, F32)
    b2a = p.sb("m_b2a", 128, 8 * 8, F32)
    gacc = p.sb("m_gacc", 128, 8 * TCW, F32)
    t2 = [p.sb(f"m_t2{i}", 128, TCW, F32) for i in range(2)]
    hps = [p.pf(0), p.pf(1)]
    yps = [p.pf(2), p.pf(3)]
    bps = p.pf(4)
    wps = p.pf(5)
    for e in range(N_EXP):
        p.dma("sync", b1r[:], b1_hbm[32 * e:32 * (e + 1), :])
        p.dma("sync", b2r[:], b2_hbm[8 * e:8 * (e + 1), :])
        p.emit("tensor", lambda t: t.transpose(out=bps[0:128, 0:32], in_=b1r[:],
                                               identity=ident_f32[0:32, 0:32]),
               reads=[b1r, ident_f32], writes=[bps])
        p.emit("scalar", lambda s, e=e: s.copy(out=b1a[:, 32 * e:32 * (e + 1)],
                                               in_=bps[0:128, 0:32]),
               reads=[bps], writes=[b1a])
        p.emit("tensor", lambda t: t.transpose(out=bps[0:128, 32:40], in_=b2r[:],
                                               identity=ident_f32[0:8, 0:8]),
               reads=[b2r, ident_f32], writes=[bps])
        p.emit("scalar", lambda s, e=e: s.copy(out=b2a[:, 8 * e:8 * (e + 1)],
                                               in_=bps[0:128, 32:40]),
               reads=[bps], writes=[b2a])
    for tc in range(NTC):
        it = tc % 2
        for k in range(8):
            p.dma("sync", xsl[it][:, TCW * k:TCW * (k + 1)],
                  xnT_hbm[128 * k:128 * (k + 1), TCW * tc:TCW * (tc + 1)])
        p.emit("vector", lambda v: v.memset(gacc[:], 0.0),
               reads=[gacc], writes=[gacc])
        for e in range(N_EXP):
            ie = e % 2
            p.emit("tensor", lambda t, e=e, tc=tc: t.matmul(
                out=wps[:], lhsT=ohs[:, 128 * e:128 * (e + 1)],
                rhs=wgtT[:, TCW * tc:TCW * (tc + 1)], start=True, stop=True),
                reads=[ohs, wgtT], writes=[wps])
            p.emit("scalar", lambda s, ie=ie: s.copy(out=wbc[ie][:], in_=wps[:]),
                   reads=[wps], writes=[wbc[ie]])
            for hc2 in range(16):
                iw = hc2 % NW
                p.dma("gpsimd",
                      w1c[iw][:].rearrange("p (k m) -> p k m", k=8),
                      w1_hbm[D * e:D * (e + 1),
                             256 * hc2:256 * hc2 + 128].rearrange(
                                 "(k p) m -> p k m", p=128))
                p.dma("gpsimd",
                      w1b[iw][:].rearrange("p (k m) -> p k m", k=8),
                      w1_hbm[D * e:D * (e + 1),
                             256 * hc2 + 128:256 * (hc2 + 1)].rearrange(
                                 "(k p) m -> p k m", p=128))

                def mm_h(t, it=it, iw=iw):
                    for k in range(8):
                        t.matmul(
                            out=hps[0][:], lhsT=w1c[iw][:, 128 * k:128 * (k + 1)],
                            rhs=xsl[it][:, TCW * k:TCW * (k + 1)],
                            start=(k == 0), stop=(k == 7))
                    for k in range(8):
                        inst = t.matmul(
                            out=hps[1][:], lhsT=w1b[iw][:, 128 * k:128 * (k + 1)],
                            rhs=xsl[it][:, TCW * k:TCW * (k + 1)],
                            start=(k == 0), stop=(k == 7))
                    return inst
                p.emit("tensor", mm_h, reads=[w1c[iw], w1b[iw], xsl[it]],
                       writes=[hps[0], hps[1]])
                p.emit("scalar", lambda s, hc2=hc2, ie=ie, e=e: s.activation(
                    out=h_sb[ie][:, TCW * 2 * hc2:TCW * (2 * hc2 + 1)],
                    in_=hps[0][:], func=AF.Gelu,
                    bias=b1a[:, 32 * e + 2 * hc2:32 * e + 2 * hc2 + 1]),
                    reads=[hps[0], b1a], writes=[h_sb[ie]])
                p.emit("scalar", lambda s, hc2=hc2, ie=ie, e=e: s.activation(
                    out=h_sb[ie][:, TCW * (2 * hc2 + 1):TCW * (2 * hc2 + 2)],
                    in_=hps[1][:], func=AF.Gelu,
                    bias=b1a[:, 32 * e + 2 * hc2 + 1:32 * e + 2 * hc2 + 2]),
                    reads=[hps[1], b1a], writes=[h_sb[ie]])
            for dc in range(8):
                iw = dc % NW
                i2 = dc % 2
                yp = yps[i2]
                p.dma("gpsimd",
                      w2c[iw][:].rearrange("p (k m) -> p k m", k=32),
                      w2_hbm[D_FF * e:D_FF * (e + 1),
                             128 * dc:128 * (dc + 1)].rearrange(
                                 "(k p) m -> p k m", p=128))

                def mm_y(t, ie=ie, yp=yp, iw=iw):
                    for k in range(32):
                        inst = t.matmul(
                            out=yp[:], lhsT=w2c[iw][:, 128 * k:128 * (k + 1)],
                            rhs=h_sb[ie][:, TCW * k:TCW * (k + 1)],
                            start=(k == 0), stop=(k == 31))
                    return inst
                p.emit("tensor", mm_y, reads=[w2c[iw], h_sb[ie]], writes=[yp])
                p.emit("vector", lambda v, i2=i2, ie=ie, e=e, dc=dc, yp=yp:
                       v.scalar_tensor_tensor(
                           out=t2[i2][:], in0=yp[:],
                           scalar=b2a[:, 8 * e + dc:8 * e + dc + 1],
                           in1=wbc[ie][:], op0=ALU.add, op1=ALU.mult),
                       reads=[yp, b2a, wbc[ie]], writes=[t2[i2]])
                p.emit("vector", lambda v, i2=i2, dc=dc: v.tensor_add(
                    out=gacc[:, TCW * dc:TCW * (dc + 1)],
                    in0=gacc[:, TCW * dc:TCW * (dc + 1)], in1=t2[i2][:]),
                    reads=[t2[i2], gacc], writes=[gacc])
        for dc in range(8):
            p.dma("gpsimd",
                  gT_hbm[128 * dc:128 * (dc + 1), TCW * tc:TCW * (tc + 1)],
                  gacc[:, TCW * dc:TCW * (dc + 1)])


NSLOT = 15
NPAD = NSLOT * 512


def phase_moe_routed(p, xn_tm_hbm, glog_hbm, w1_hbm, b1_hbm, w2_hbm, b2r8_hbm,
                     g_hbm, gpos_hbm, spos_hbm, wgtg_hbm, xpk_hbm, ident_f32,
                     ident_bf):
    nc = p.nc
    from concourse.bass_core_rust import IndirectOffsetOnAxis as IOA
    gl = [p.sb(f"r_gl{i}", 128, N_EXP, F32) for i in range(2)]
    nmx = [p.sb(f"r_nm{i}", 128, 1, F32) for i in range(2)]
    ssm = [p.sb(f"r_ss{i}", 128, 1, F32) for i in range(2)]
    rec = [p.sb(f"r_rc{i}", 128, 1, F32) for i in range(2)]
    pb = [p.sb(f"r_pb{i}", 128, N_EXP, F32) for i in range(2)]
    mx2 = [p.sb(f"r_mx{i}", 128, 1, F32) for i in range(2)]
    msk = [p.sb(f"r_msk{i}", 128, N_EXP, F32) for i in range(2)]
    wgt = [p.sb(f"r_wgt{i}", 128, N_EXP, F32) for i in range(2)]
    wcol_all = p.sb("r_wca", 128, NTT, F32)
    msk8 = p.sb("r_msk8", 8, NTOK, F32)
    ptA = p.pf(4)
    for tt in range(NTT):
        i = tt % 2
        p.dma("sync", gl[i][:], glog_hbm[128 * tt:128 * (tt + 1), :])
        p.emit("vector", lambda v, i=i: v.reduce_max(
            out=nmx[i][:], in_=gl[i][:], axis=AX.X, negate=True),
            reads=[gl[i]], writes=[nmx[i]])
        p.emit("scalar", lambda s, i=i: s.activation(
            out=pb[i][:], in_=gl[i][:], func=AF.Exp, bias=nmx[i][:, 0:1],
            accum_out=ssm[i][:, 0:1]), reads=[gl[i], nmx[i]],
            writes=[pb[i], ssm[i]])
        p.emit("vector", lambda v, i=i: v.reciprocal(out=rec[i][:], in_=ssm[i][:]),
               reads=[ssm[i]], writes=[rec[i]])
        p.emit("vector", lambda v, i=i: v.tensor_scalar(
            out=pb[i][:], in0=pb[i][:], scalar1=rec[i][:, 0:1], scalar2=None,
            op0=ALU.mult), reads=[pb[i], rec[i]], writes=[pb[i]])
        p.emit("vector", lambda v, i=i: v.reduce_max(
            out=mx2[i][:], in_=pb[i][:], axis=AX.X),
            reads=[pb[i]], writes=[mx2[i]])
        p.emit("vector", lambda v, i=i: v.tensor_scalar(
            out=msk[i][:], in0=pb[i][:], scalar1=mx2[i][:, 0:1], scalar2=None,
            op0=ALU.is_equal), reads=[pb[i], mx2[i]], writes=[msk[i]])
        p.emit("vector", lambda v, i=i: v.tensor_mul(
            out=wgt[i][:], in0=pb[i][:], in1=msk[i][:]),
            reads=[pb[i], msk[i]], writes=[wgt[i]])
        p.emit("vector", lambda v, i=i, tt=tt: v.reduce_sum(
            out=wcol_all[:, tt:tt + 1], in_=wgt[i][:], axis=AX.X),
            reads=[wgt[i]], writes=[wcol_all])
        p.emit("tensor", lambda t, i=i: t.transpose(
            out=ptA[0:8, 0:128], in_=msk[i][:], identity=ident_f32[:]),
            reads=[msk[i], ident_f32], writes=[ptA])
        p.emit("scalar", lambda s, tt=tt: s.copy(
            out=msk8[:, 128 * tt:128 * (tt + 1)], in_=ptA[0:8, 0:128]),
            reads=[ptA], writes=[msk8])
    R = p.sb("r_R", 8, NTOK, F32)
    counts = p.sb("r_cnt", 8, 1, F32)
    t1c = p.sb("r_t1", 8, 1, F32)
    frc = p.sb("r_fr", 8, 1, F32)
    ceilc = p.sb("r_ceil", 8, 1, F32)
    excl = p.sb("r_excl", 8, 1, F32)
    rowst = p.sb("r_rwst", 8, 1, F32)
    Lstr = p.sb("r_Lstr", 8, 8, F32)
    Lio = p.sb("r_Lio", 8, 8, F32)
    ones8 = p.sb("r_on8", 8, 1, F32)
    iotaS = p.sb("r_ioS", 8, NSLOT, F32)
    cmp = p.sb("r_cmp", 8, NSLOT, F32)
    T2E = p.sb("r_T2E", NSLOT, 1, F32)
    tpad = p.sb("r_tpad", 8, NSLOT, F32)
    T2Ebc = p.sb("r_T2Eb", 128, NSLOT, F32)
    e1024 = p.sb("r_e1k", 128, NSLOT, F32)
    e4096 = p.sb("r_e4k", 128, NSLOT, F32)
    e32 = p.sb("r_e32", 128, NSLOT, F32)
    oh0 = build_onehot0(p, "r_oh0")
    oh0b = p.sb("r_oh0b", 8, 128, BF16)
    p.emit("vector", lambda v: v.tensor_copy(out=oh0b[:], in_=oh0[:]),
           reads=[oh0], writes=[oh0b])
    p.emit("vector", lambda v: v.tensor_tensor_scan(
        out=R[:], data0=msk8[:], data1=msk8[:], initial=0.0, op0=ALU.add,
        op1=ALU.bypass), reads=[msk8], writes=[R])
    p.emit("vector", lambda v: v.reduce_sum(out=counts[:], in_=msk8[:], axis=AX.X),
           reads=[msk8], writes=[counts])
    thr = p.sb("r_thr", 8, 8, F32)
    p.emit("gpsimd", lambda g: g.iota(out=thr[:], pattern=[[512, 8]], base=0,
                                      channel_multiplier=0,
                                      allow_small_or_imprecise_dtypes=True),
           writes=[thr])
    p.emit("vector", lambda v: v.tensor_scalar(
        out=t1c[:, 0:8] if False else thr[:], in0=thr[:],
        scalar1=counts[:, 0:1], scalar2=None, op0=ALU.is_lt),
        reads=[thr, counts], writes=[thr])
    p.emit("vector", lambda v: v.reduce_sum(out=ceilc[:], in_=thr[:], axis=AX.X),
           reads=[thr], writes=[ceilc])
    p.emit("gpsimd", lambda g: g.iota(out=Lio[:], pattern=[[1, 8]], base=-1,
                                      channel_multiplier=-1,
                                      allow_small_or_imprecise_dtypes=True),
           writes=[Lio])
    p.emit("vector", lambda v: v.tensor_scalar(out=Lstr[:], in0=Lio[:],
                                               scalar1=0.0, scalar2=None,
                                               op0=ALU.is_ge),
           reads=[Lio], writes=[Lstr])
    p.emit("vector", lambda v: v.memset(ones8[:], 1.0), writes=[ones8])
    p.emit("tensor", lambda t: t.matmul(out=ptA[0:8, 0:1], lhsT=Lstr[:],
                                        rhs=ceilc[:], start=True, stop=True),
           reads=[Lstr, ceilc], writes=[ptA])
    p.emit("scalar", lambda s: s.copy(out=excl[:], in_=ptA[0:8, 0:1]),
           reads=[ptA], writes=[excl])
    p.emit("vector", lambda v: v.tensor_scalar(
        out=rowst[:], in0=excl[:], scalar1=512.0, scalar2=None, op0=ALU.mult),
        reads=[excl], writes=[rowst])
    p.emit("gpsimd", lambda g: g.iota(out=iotaS[:], pattern=[[1, NSLOT]], base=0,
                                      channel_multiplier=0,
                                      allow_small_or_imprecise_dtypes=True),
           writes=[iotaS])
    p.emit("vector", lambda v: v.tensor_scalar(
        out=cmp[:], in0=iotaS[:], scalar1=excl[:, 0:1], scalar2=None,
        op0=ALU.is_ge), reads=[iotaS, excl], writes=[cmp])
    p.emit("tensor", lambda t: t.matmul(out=ptA[0:1, 0:NSLOT], lhsT=ones8[:],
                                        rhs=cmp[:], start=True, stop=True),
           reads=[cmp, ones8], writes=[ptA])
    p.emit("vector", lambda v: v.memset(tpad[:], 0.0), writes=[tpad])
    p.emit("scalar", lambda s: s.copy(out=tpad[0:1, :], in_=ptA[0:1, 0:NSLOT]),
           reads=[ptA], writes=[tpad])
    p.emit("tensor", lambda t: t.matmul(out=ptA[0:128, 0:NSLOT], lhsT=oh0[:],
                                        rhs=tpad[:], start=True, stop=True),
           reads=[oh0, tpad], writes=[ptA])
    p.emit("scalar", lambda s: s.copy(out=T2Ebc[:], in_=ptA[0:128, 0:NSLOT]),
           reads=[ptA], writes=[T2Ebc])
    p.emit("vector", lambda v: v.tensor_scalar(
        out=T2Ebc[:], in0=T2Ebc[:], scalar1=1.0, scalar2=None,
        op0=ALU.subtract), reads=[T2Ebc], writes=[T2Ebc])
    for dst, f in ((e1024, 1024.0), (e4096, 4096.0), (e32, 128.0)):
        p.emit("vector", lambda v, dst=dst, f=f: v.tensor_scalar(
            out=dst[:], in0=T2Ebc[:], scalar1=f, scalar2=None, op0=ALU.mult),
            reads=[T2Ebc], writes=[dst])
    p.emit("vector", lambda v: v.tensor_scalar(
        out=R[:], in0=R[:], scalar1=rowst[:, 0:1], scalar2=1.0, op0=ALU.add,
        op1=ALU.subtract), reads=[R, rowst], writes=[R])
    p.emit("vector", lambda v: v.tensor_mul(out=R[:], in0=R[:], in1=msk8[:]),
           reads=[msk8, R], writes=[R])
    zrow = p.sb("r_zrow", 128, 2 * NPAD // 128, I32)
    p.emit("vector", lambda v: v.memset(zrow[:], 0), writes=[zrow])
    p.dma("gpsimd", gpos_hbm.rearrange("(p a) c -> p (a c)", p=128), zrow[:])
    p.emit("gpsimd", lambda g: g.iota(out=zrow[:], pattern=[[0, 2 * NPAD // 128]],
                                      base=NTOK, channel_multiplier=1),
           reads=[zrow], writes=[zrow])
    p.dma("gpsimd", spos_hbm.rearrange("(p a) c -> p (a c)", p=128), zrow[:])
    zf = p.sb("r_zf", 128, 2 * NPAD // 128, F32)
    p.emit("vector", lambda v: v.memset(zf[:], 0.0), writes=[zf])
    p.dma("gpsimd", wgtg_hbm.rearrange("(p a) c -> p (a c)", p=128), zf[:])
    Rt = [p.sb(f"r_Rt{i}", 128, 8, F32) for i in range(2)]
    destf = [p.sb(f"r_df{i}", 128, 1, F32) for i in range(2)]
    desti = [p.sb(f"r_di{i}", 128, 1, I32) for i in range(2)]
    iotok = [p.sb(f"r_it{i}", 128, 2, I32) for i in range(2)]
    wcol2 = [p.sb(f"r_w2c{i}", 128, 2, F32) for i in range(2)]
    for tt in range(NTT):
        i = tt % 2
        p.emit("tensor", lambda t, tt=tt: t.transpose(
            out=ptA[0:128, 0:8], in_=R[:, 128 * tt:128 * (tt + 1)],
            identity=ident_f32[0:8, 0:8]), reads=[R, ident_f32], writes=[ptA])
        p.emit("scalar", lambda s, i=i: s.copy(out=Rt[i][:], in_=ptA[0:128, 0:8]),
               reads=[ptA], writes=[Rt[i]])
        p.emit("vector", lambda v, i=i: v.reduce_sum(
            out=destf[i][:], in_=Rt[i][:], axis=AX.X),
            reads=[Rt[i]], writes=[destf[i]])
        p.emit("vector", lambda v, i=i: v.tensor_copy(
            out=desti[i][:], in_=destf[i][:]), reads=[destf[i]], writes=[desti[i]])
        p.emit("gpsimd", lambda g, i=i, tt=tt: g.iota(
            out=iotok[i][:], pattern=[[0, 2]], base=128 * tt,
            channel_multiplier=1), writes=[iotok[i]])
        p.emit("vector", lambda v, i=i, tt=tt: v.tensor_copy(
            out=wcol2[i][:, 0:1], in_=wcol_all[:, tt:tt + 1]),
            reads=[wcol_all], writes=[wcol2[i]])
        p.emit("vector", lambda v, i=i, tt=tt: v.tensor_copy(
            out=wcol2[i][:, 1:2], in_=wcol_all[:, tt:tt + 1]),
            reads=[wcol_all], writes=[wcol2[i]])
        p.idma("scat", lambda g, i=i: g.indirect_dma_start(
            out=gpos_hbm, out_offset=IOA(desti[i][:], 0), in_=iotok[i][:],
            in_offset=None), reads=[desti[i], iotok[i]],
            writes=[gpos_hbm.tensor])
        p.idma("scat", lambda g, i=i: g.indirect_dma_start(
            out=spos_hbm, out_offset=IOA(desti[i][:], 0), in_=iotok[i][:],
            in_offset=None), reads=[desti[i], iotok[i]],
            writes=[spos_hbm.tensor])
        p.idma("scat", lambda g, i=i: g.indirect_dma_start(
            out=wgtg_hbm, out_offset=IOA(desti[i][:], 0),
            in_=wcol2[i][:], in_offset=None),
            reads=[desti[i], wcol2[i]],
            writes=[wgtg_hbm.tensor])
    io8 = p.sb("r_io8", 128, 8, F32)
    io32 = p.sb("r_io32", 128, 32, F32)
    io1 = p.sb("r_io1", 128, 1, F32)
    p.emit("gpsimd", lambda g: g.iota(out=io8[:], pattern=[[128, 8]], base=0,
                                      channel_multiplier=1,
                                      allow_small_or_imprecise_dtypes=True),
           writes=[io8])
    p.emit("gpsimd", lambda g: g.iota(out=io32[:], pattern=[[128, 32]], base=0,
                                      channel_multiplier=1,
                                      allow_small_or_imprecise_dtypes=True),
           writes=[io32])
    p.emit("gpsimd", lambda g: g.iota(out=io1[:], pattern=[[1, 1]], base=0,
                                      channel_multiplier=1,
                                      allow_small_or_imprecise_dtypes=True),
           writes=[io1])
    xpk_stage = [p.sb(f"r_xps{i}", 128, D, BF16) for i in range(2)]
    wif = p.sb("r_wif", 128, 42, F32)
    wii = p.sb("r_wii", 128, 42, I32)
    soidx = [p.sb(f"r_so{u}", 128, 1, I32) for u in range(4)]
    wcu = [p.sb(f"r_wc{u}", 128, 1, F32) for u in range(4)]
    xgT = p.sb("r_xgT", 128, 8 * 512, BF16)
    w1g = [p.sb(f"r_w1g{j}", 128, D_FF, BF16) for j in range(8)]
    w1cp = None
    w2g = [p.sb(f"r_w2g{i}", 128, 512, BF16) for i in range(32)]

    b1c = p.sb("r_b1c", 128, 32, F32)
    b2g8 = p.sb("r_b2g8", 8, D, BF16)
    h_sb = p.sb("r_h", 128, 32 * 512, BF16)
    ytm = [p.sb(f"r_ytm{i}", 128, 512, F32) for i in range(2)]
    hps = [p.pf(0), p.pf(1)]
    yps = [p.pf(2), p.pf(3), p.pf(4), p.pf(5)]
    ptb = [p.pbf(0), p.pbf(1)]
    p.emit("vector", lambda v: v.memset(b2g8[:], 0.0), writes=[b2g8])
    pk_idx = [p.sb(f"r_pki{i}", 128, 1, I32) for i in range(2)]
    for c in range(NSLOT * 4):
        ic = c % 2
        p.dma("sync", pk_idx[ic][:], gpos_hbm[128 * c:128 * (c + 1), 0:1],
              allow_slow=True)
        p.idma("xg", lambda g, ic=ic: g.indirect_dma_start(
            out=xpk_stage[ic][:], out_offset=None, in_=xn_tm_hbm,
            in_offset=IOA(pk_idx[ic][:], 0)),
            reads=[pk_idx[ic], xn_tm_hbm.tensor], writes=[xpk_stage[ic]])
        p.dma("gpsimd", xpk_hbm[128 * c:128 * (c + 1), :], xpk_stage[ic][:])
    import os as _os
    _nslot = int(_os.environ.get("K_SLOTS", str(NSLOT)))
    _part = int(_os.environ.get("K_PART", "5"))
    for s in range(_nslot):
        p.emit("vector", lambda v, s=s: v.tensor_scalar(
            out=wif[:, 0:8], in0=io8[:], scalar1=e1024[:, s:s + 1],
            scalar2=None, op0=ALU.add), reads=[io8, e1024], writes=[wif])
        p.emit("vector", lambda v, s=s: v.tensor_scalar(
            out=wif[:, 8:40], in0=io32[:], scalar1=e4096[:, s:s + 1],
            scalar2=None, op0=ALU.add), reads=[io32, e4096], writes=[wif])
        p.emit("vector", lambda v, s=s: v.tensor_scalar(
            out=wif[:, 40:41], in0=io1[:], scalar1=e32[:, s:s + 1],
            scalar2=None, op0=ALU.add), reads=[io1, e32], writes=[wif])
        p.emit("vector", lambda v, s=s: v.tensor_copy(
            out=wif[:, 41:42], in_=T2Ebc[:, s:s + 1]), reads=[T2Ebc],
            writes=[wif])
        p.emit("vector", lambda v: v.tensor_copy(out=wii[:], in_=wif[:]),
               reads=[wif], writes=[wii])
        for u in range(4):
            r0 = 512 * s + 128 * u
            p.dma("sync", soidx[u][:], spos_hbm[r0:r0 + 128, 0:1], allow_slow=True)
            p.dma("sync", wcu[u][:], wgtg_hbm[r0:r0 + 128, 0:1], allow_slow=True)
        for j in range(8):
            p.dma("sync", xgT[:, 512 * j:512 * (j + 1)],
                  xpk_hbm[512 * s:512 * (s + 1), 128 * j:128 * (j + 1)],
                  transpose=True)
        if _os.environ.get("K_W1PLAIN", "0") == "1":
            for j in range(8):
                p.dma("gpsimd", w1g[j][:], w1_hbm[128 * j:128 * (j + 1), :])
        else:
            for j in range(8):
                p.idma("w1", lambda g, j=j: g.indirect_dma_start(
                    out=w1g[j][:], out_offset=None, in_=w1_hbm,
                    in_offset=IOA(wii[:, j:j + 1], 0)),
                    reads=[wii, w1_hbm.tensor], writes=[w1g[j]])
        p.idma("b1", lambda g: g.indirect_dma_start(
            out=b1c[:], out_offset=None, in_=b1_hbm,
            in_offset=IOA(wii[:, 40:41], 0)),
            reads=[wii, b1_hbm.tensor], writes=[b1c])
        p.idma("b2", lambda g: g.indirect_dma_start(
            out=b2g8[:], out_offset=None, in_=b2r8_hbm,
            in_offset=IOA(wii[0:8, 41:42], 0)),
            reads=[wii, b2r8_hbm.tensor], writes=[b2g8])
        if _part < 2:
            continue
        if _part < 3:
            continue
        if _os.environ.get("K_W1COPY", "0") == "1":
            for j in range(8):
                p.emit("vector", lambda v, j=j: v.tensor_copy(
                    out=w1cp[j][:], in_=w1g[j][:]), reads=[w1g[j]],
                    writes=[w1cp[j]])
            w1u = w1cp
        else:
            w1u = w1g
        _nhc = int(_os.environ.get("K_HC", "16"))
        _unpair = _os.environ.get("K_UNPAIR", "0") == "1"
        for hc2 in range(_nhc):
            if _unpair:
                for half in range(2):
                    hp = hps[half]

                    def mm_u(t, hc2=hc2, half=half, hp=hp):
                        for k in range(8):
                            inst = t.matmul(
                                out=hp[:],
                                lhsT=w1u[k][:, 256 * hc2 + 128 * half:
                                            256 * hc2 + 128 * (half + 1)],
                                rhs=xgT[:, 512 * k:512 * (k + 1)],
                                start=(k == 0), stop=(k == 7))
                        return inst
                    p.emit("tensor", mm_u, reads=list(w1u) + [xgT], writes=[hp])
                    p.emit("scalar", lambda s_, hc2=hc2, half=half, hp=hp:
                           s_.activation(
                               out=h_sb[:, 512 * (2 * hc2 + half):
                                        512 * (2 * hc2 + half + 1)],
                               in_=hp[:], func=AF.Gelu,
                               bias=b1c[:, 2 * hc2 + half:2 * hc2 + half + 1]),
                           reads=[hp, b1c], writes=[h_sb])
                continue

            def mm_h(t, hc2=hc2):
                _rhs = lambda k: xgT[:, 512 * k:512 * (k + 1)]
                for k in range(8):
                    t.matmul(out=hps[0][:],
                             lhsT=w1u[k][:, 256 * hc2:256 * hc2 + 128],
                             rhs=_rhs(k),
                             start=(k == 0), stop=(k == 7))
                for k in range(8):
                    inst = t.matmul(out=hps[1][:],
                                    lhsT=w1u[k][:, 256 * hc2 + 128:256 * (hc2 + 1)],
                                    rhs=_rhs(k),
                                    start=(k == 0), stop=(k == 7))
                return inst
            p.emit("tensor", mm_h, reads=list(w1u) + [xgT],
                   writes=[hps[0], hps[1]])
            if _os.environ.get("K_NOB1BIAS", "0") == "1":
                p.emit("scalar", lambda s_, hc2=hc2: s_.activation(
                    out=h_sb[:, 512 * 2 * hc2:512 * (2 * hc2 + 1)],
                    in_=hps[0][:], func=AF.Gelu),
                    reads=[hps[0]], writes=[h_sb])
                p.emit("scalar", lambda s_, hc2=hc2: s_.activation(
                    out=h_sb[:, 512 * (2 * hc2 + 1):512 * (2 * hc2 + 2)],
                    in_=hps[1][:], func=AF.Gelu),
                    reads=[hps[1]], writes=[h_sb])
            else:
                p.emit("scalar", lambda s_, hc2=hc2: s_.activation(
                    out=h_sb[:, 512 * 2 * hc2:512 * (2 * hc2 + 1)], in_=hps[0][:],
                    func=AF.Gelu, bias=b1c[:, 2 * hc2:2 * hc2 + 1]),
                    reads=[hps[0], b1c], writes=[h_sb])
                p.emit("scalar", lambda s_, hc2=hc2: s_.activation(
                    out=h_sb[:, 512 * (2 * hc2 + 1):512 * (2 * hc2 + 2)],
                    in_=hps[1][:], func=AF.Gelu,
                    bias=b1c[:, 2 * hc2 + 1:2 * hc2 + 2]),
                    reads=[hps[1], b1c], writes=[h_sb])
        if _part < 4:
            continue
        for dh in range(2):
            for k in range(32):
                i2 = (32 * dh + k) % 32
                p.idma("w2", lambda g, k=k, dh=dh, i2=i2: g.indirect_dma_start(
                    out=w2g[i2][:], out_offset=None, in_=w2_hbm,
                    in_offset=IOA(wii[:, 8 + k:9 + k], 0),
                    element_offset=512 * dh),
                    reads=[wii, w2_hbm.tensor], writes=[w2g[i2]])

                def mm_y(t, k=k, i2=i2):
                    for u in range(4):
                        inst = t.matmul(
                            out=yps[u][:], lhsT=h_sb[:, 512 * k + 128 * u:
                                                     512 * k + 128 * (u + 1)],
                            rhs=w2g[i2][:], start=(k == 0), stop=False,
                            skip_group_check=True)
                    return inst
                p.emit("tensor", mm_y, reads=[h_sb, w2g[i2]],
                       writes=[yps[0], yps[1], yps[2], yps[3]])

            def mm_b2(t, dh=dh):
                for u in range(4):
                    inst = t.matmul(out=yps[u][:], lhsT=oh0b[:],
                                    rhs=b2g8[:, 512 * dh:512 * (dh + 1)],
                                    start=False, stop=True,
                                    skip_group_check=True)
                return inst
            p.emit("tensor", mm_b2, reads=[oh0b, b2g8],
                   writes=[yps[0], yps[1], yps[2], yps[3]])
            for u in range(4):
                if _part < 5:
                    continue
                iy = u % 2
                p.emit("vector", lambda v, u=u, iy=iy: v.tensor_scalar(
                    out=ytm[iy][:], in0=yps[u][:], scalar1=wcu[u][:, 0:1],
                    scalar2=None, op0=ALU.mult),
                    reads=[yps[u], wcu[u]], writes=[ytm[iy]])
                p.idma("yscat", lambda g, u=u, iy=iy, dh=dh:
                       g.indirect_dma_start(
                           out=g_hbm, out_offset=IOA(soidx[u][:], 0),
                           in_=ytm[iy][:], in_offset=None,
                           element_offset=512 * dh),
                       reads=[soidx[u], ytm[iy]],
                       writes=[g_hbm.tensor])


def phase_combine_ln(p, x_hbm, s_hbm, gT_hbm, x1_hbm, xnT_hbm, rstd2_hbm,
                     ident_f32, g_token_major=False):
    nc = p.nc
    tag = "c6"
    xb = [p.sb(tag + f"x{i}", 128, D, F32) for i in range(2)]
    sbuf_s = [p.sb(tag + f"s{i}", 128, D, F32) for i in range(2)]
    gtile = [p.sb(tag + f"g{i}", 128, D, F32) for i in range(2)]
    gblk = [p.sb(tag + f"gb{i}", 128, 128, F32) for i in range(2)]
    sg = [p.sb(tag + f"sg{i}", 128, D, F32) for i in range(2)]
    x1 = [p.sb(tag + f"x1{i}", 128, D, F32) for i in range(2)]
    xn = [p.sb(tag + f"xn{i}", 128, D, F32) for i in range(2)]
    sq = [p.sb(tag + f"sq{i}", 128, D, F32) for i in range(2)]
    ssum = [p.sb(tag + f"ssum{i}", 128, 1, F32) for i in range(2)]
    sqs = [p.sb(tag + f"sqs{i}", 128, 1, F32) for i in range(2)]
    mean = [p.sb(tag + f"mean{i}", 128, 1, F32) for i in range(2)]
    sqd = [p.sb(tag + f"sqd{i}", 128, 1, F32) for i in range(2)]
    nvar = [p.sb(tag + f"nvar{i}", 128, 1, F32) for i in range(2)]
    sd = [p.sb(tag + f"sd{i}", 128, 1, F32) for i in range(2)]
    rstd = [p.sb(tag + f"rstd{i}", 128, 1, F32) for i in range(2)]
    sq2s = [p.sb(tag + f"sq2s{i}", 128, 1, F32) for i in range(2)]
    sd2 = [p.sb(tag + f"sd2{i}", 128, 1, F32) for i in range(2)]
    rstd2 = [p.sb(tag + f"rstd2{i}", 128, 1, F32) for i in range(2)]
    r2row = p.sb(tag + "r2row", 1, NTOK, F32)
    stage = [p.sb(tag + f"stg{j}", 128, NTOK, BF16) for j in range(8)]
    pstr = [p.pf(0), p.pf(1)]
    ptr2 = [p.pf(2), p.pf(3)]
    r2ps = p.pf(4)
    epscol = p.ccol(EPS)
    for tt in range(NTT):
        i = tt % 2
        xt, st, gt, x1t, xnt = xb[i], sbuf_s[i], gtile[i], x1[i], xn[i]
        p.dma("sync", xt[:], x_hbm[128 * tt:128 * (tt + 1), :])
        p.dma("sync", st[:], s_hbm[128 * tt:128 * (tt + 1), :])
        if g_token_major:
            p.dma("sync", gt[:], gT_hbm[128 * tt:128 * (tt + 1), :])
        else:
            for j in range(8):
                gb = gblk[j % 2]
                p.dma("sync", gb[:], gT_hbm[128 * j:128 * (j + 1),
                                            128 * tt:128 * (tt + 1)])
                pt = ptr2[j % 2]
                p.emit("tensor", lambda t, gb=gb, pt=pt: t.transpose(
                    out=pt[:, 0:128], in_=gb[:], identity=ident_f32[:]),
                    reads=[gb, ident_f32], writes=[pt])
                p.emit("scalar", lambda s, pt=pt, j=j, gt=gt: s.copy(
                    out=gt[:, 128 * j:128 * (j + 1)], in_=pt[:, 0:128]),
                    reads=[pt], writes=[gt])
        p.emit("scalar", lambda s, gt=gt, i=i: s.activation(
            out=sg[i][:], in_=gt[:], func=AF.Silu), reads=[gt], writes=[sg[i]])
        p.emit("vector", lambda v, st=st, i=i: v.tensor_mul(
            out=sg[i][:], in0=sg[i][:], in1=st[:]),
            reads=[st, sg[i]], writes=[sg[i]])
        p.emit("vector", lambda v, xt=xt, i=i, x1t=x1t: v.tensor_add(
            out=x1t[:], in0=xt[:], in1=sg[i][:]),
            reads=[xt, sg[i]], writes=[x1t])
        p.dma("gpsimd", x1_hbm[128 * tt:128 * (tt + 1), :], x1t[:])
        p.emit("vector", lambda v, x1t=x1t, i=i: v.reduce_sum(
            out=ssum[i][:], in_=x1t[:], axis=AX.X), reads=[x1t], writes=[ssum[i]])
        p.emit("vector", lambda v, x1t=x1t, i=i: v.scalar_tensor_tensor(
            out=sq[i][:], in0=x1t[:], scalar=1.0, in1=x1t[:], op0=ALU.bypass,
            op1=ALU.mult, accum_out=sqs[i][:, 0:1]), reads=[x1t],
            writes=[sq[i], sqs[i]])
        p.emit("vector", lambda v, i=i: v.tensor_scalar(
            out=mean[i][:], in0=ssum[i][:], scalar1=1.0 / D, scalar2=None,
            op0=ALU.mult), reads=[ssum[i]], writes=[mean[i]])
        p.emit("vector", lambda v, i=i: v.tensor_scalar(
            out=sqd[i][:], in0=sqs[i][:], scalar1=1.0 / D, scalar2=None,
            op0=ALU.mult), reads=[sqs[i]], writes=[sqd[i]])
        p.emit("vector", lambda v, i=i: v.scalar_tensor_tensor(
            out=nvar[i][:], in0=mean[i][:], scalar=mean[i][:, 0:1], in1=sqd[i][:],
            op0=ALU.mult, op1=ALU.subtract), reads=[mean[i], sqd[i]],
            writes=[nvar[i]])
        p.emit("scalar", lambda s, i=i: s.activation(
            out=sd[i][:], in_=nvar[i][:], func=AF.Sqrt, scale=-1.0,
            bias=epscol[:, 0:1]), reads=[nvar[i], epscol], writes=[sd[i]])
        p.emit("vector", lambda v, i=i: v.reciprocal(out=rstd[i][:], in_=sd[i][:]),
               reads=[sd[i]], writes=[rstd[i]])
        p.emit("vector", lambda v, x1t=x1t, xnt=xnt, i=i: v.tensor_scalar(
            out=xnt[:], in0=x1t[:], scalar1=mean[i][:, 0:1],
            scalar2=rstd[i][:, 0:1], op0=ALU.subtract, op1=ALU.mult),
            reads=[x1t, mean[i], rstd[i]], writes=[xnt])
        p.emit("vector", lambda v, xnt=xnt, i=i: v.scalar_tensor_tensor(
            out=sq[i][:], in0=xnt[:], scalar=1.0, in1=xnt[:], op0=ALU.bypass,
            op1=ALU.mult, accum_out=sq2s[i][:, 0:1]), reads=[xnt],
            writes=[sq[i], sq2s[i]])
        p.emit("scalar", lambda s, i=i: s.activation(
            out=sd2[i][:], in_=sq2s[i][:], func=AF.Sqrt, scale=1.0 / D,
            bias=epscol[:, 0:1]), reads=[sq2s[i], epscol], writes=[sd2[i]])
        p.emit("vector", lambda v, i=i: v.reciprocal(
            out=rstd2[i][:], in_=sd2[i][:]), reads=[sd2[i]], writes=[rstd2[i]])
        p.emit("tensor", lambda t, i=i: t.transpose(
            out=r2ps[0:1, 0:128], in_=rstd2[i][:], identity=ident_f32[:]),
            reads=[rstd2[i], ident_f32], writes=[r2ps])
        p.emit("scalar", lambda s, tt=tt: s.copy(
            out=r2row[0:1, 128 * tt:128 * (tt + 1)], in_=r2ps[0:1, 0:128]),
            reads=[r2ps], writes=[r2row])
        for j in range(8):
            pt = pstr[j % 2]
            p.emit("tensor", lambda t, xnt=xnt, pt=pt, j=j: t.transpose(
                out=pt[:, 0:128], in_=xnt[:, 128 * j:128 * (j + 1)],
                identity=ident_f32[:]), reads=[xnt, ident_f32], writes=[pt])
            p.emit("vector", lambda v, pt=pt, j=j, tt=tt: v.tensor_copy(
                out=stage[j][:, 128 * tt:128 * (tt + 1)], in_=pt[:, 0:128]),
                reads=[pt], writes=[stage[j]])
    for j in range(8):
        p.dma("gpsimd", xnT_hbm[128 * j:128 * (j + 1), :], stage[j][:])
    p.dma("gpsimd", rstd2_hbm[:], r2row[:])


def phase_mamba_in(p, xnT_hbm, rstd2_hbm, normw_hbm, win_hbm, convw_hbm,
                   convb_hbm, xinT_hbm, resT_hbm, xsiluT_hbm, ident_f32):
    nc = p.nc
    tag = "c7"
    oh0 = build_onehot0(p, tag + "oh0")
    winm = [p.sb(tag + f"winm{i}", 128, 8 * 128, BF16) for i in range(2)]
    r2pad = p.sb(tag + "r2p", 8, NTOK, F32)
    r2bc = p.sb(tag + "r2bc", 128, NTOK, BF16)
    rps = p.pf(4)
    p.emit("vector", lambda v: v.memset(r2pad[:], 0.0), writes=[r2pad])
    p.dma("sync", r2pad[0:1, :], rstd2_hbm[:])
    for c in range(NTOK // 512):
        p.emit("tensor", lambda t, c=c: t.matmul(
            out=rps[:], lhsT=oh0[:], rhs=r2pad[:, 512 * c:512 * (c + 1)],
            start=True, stop=True), reads=[oh0, r2pad], writes=[rps])
        p.emit("scalar", lambda s, c=c: s.copy(
            out=r2bc[:, 512 * c:512 * (c + 1)], in_=rps[:]),
            reads=[rps], writes=[r2bc])
    nwr = p.sb(tag + "nwr", 8, 128, F32)
    nwc = p.sb(tag + "nwc", 128, 8, F32)
    p.dma("sync", nwr[:], normw_hbm[:])
    p.emit("tensor", lambda t: t.transpose(out=rps[0:128, 0:8], in_=nwr[:],
                                           identity=ident_f32[0:8, 0:8]),
           reads=[nwr, ident_f32], writes=[rps])
    p.emit("scalar", lambda s: s.copy(out=nwc[:], in_=rps[0:128, 0:8]),
           reads=[rps], writes=[nwc])
    hT = p.sb(tag + "hT", 128, 8 * NTOK, BF16)
    xc = [p.sb(tag + f"xc{i}", 128, NTOK, BF16) for i in range(2)]
    t1 = [p.sb(tag + f"t1{i}", 128, NTOK, BF16) for i in range(2)]
    for k in range(8):
        i = k % 2
        p.dma("sync", xc[i][:], xnT_hbm[128 * k:128 * (k + 1), :])
        p.emit("vector", lambda v, i=i, k=k: v.tensor_scalar(
            out=t1[i][:], in0=xc[i][:], scalar1=nwc[:, k:k + 1], scalar2=None,
            op0=ALU.mult), reads=[xc[i], nwc], writes=[t1[i]])
        p.emit("vector", lambda v, i=i, k=k: v.tensor_mul(
            out=hT[:, NTOK * k:NTOK * (k + 1)], in0=t1[i][:], in1=r2bc[:]),
            reads=[t1[i], r2bc], writes=[hT])
    pps = [p.pf(0), p.pf(1)]
    stg = [p.sb(tag + f"stg{i}", 128, NTOK, BF16) for i in range(2)]
    for m in range(32):
        st = stg[m % 2]
        wm = winm[m % 2]
        p.dma("gpsimd", wm[:].rearrange("p (k f) -> p k f", k=8),
              win_hbm[:, 128 * m:128 * (m + 1)].rearrange("(k p) f -> p k f",
                                                          p=128))
        for tc in range(8):
            pt = pps[tc % 2]

            def mms(t, m=m, tc=tc, pt=pt, wm=wm):
                for k in range(8):
                    inst = t.matmul(
                        out=pt[:], lhsT=wm[:, 128 * k:128 * (k + 1)],
                        rhs=hT[:, NTOK * k + 512 * tc:NTOK * k + 512 * (tc + 1)],
                        start=(k == 0), stop=(k == 7))
                return inst
            p.emit("tensor", mms, reads=[wm, hT], writes=[pt])
            p.emit("vector", lambda v, st=st, tc=tc, pt=pt: v.tensor_copy(
                out=st[:, 512 * tc:512 * (tc + 1)], in_=pt[:]),
                reads=[pt], writes=[st])
        if m < 16:
            p.dma("gpsimd", xinT_hbm[128 * m:128 * (m + 1), :], st[:])
        else:
            p.dma("gpsimd", resT_hbm[128 * (m - 16):128 * (m - 15), :], st[:])
    cw = p.sb(tag + "cw", 128, 4 * 16, F32)
    cbr = p.sb(tag + "cbr", 16, 128, F32)
    cbc = p.sb(tag + "cbc", 128, 16, F32)
    for dc in range(16):
        p.dma("sync", cw[:, 4 * dc:4 * (dc + 1)],
              convw_hbm[128 * dc:128 * (dc + 1), :])
    p.dma("sync", cbr[:], convb_hbm[:])
    p.emit("tensor", lambda t: t.transpose(out=rps[0:128, 8:24], in_=cbr[:],
                                           identity=ident_f32[0:16, 0:16]),
           reads=[cbr, ident_f32], writes=[rps])
    p.emit("scalar", lambda s: s.copy(out=cbc[:], in_=rps[0:128, 8:24]),
           reads=[rps], writes=[cbc])
    xpad = [p.sb(tag + f"xp{i}", 128, 3 + S, BF16) for i in range(2)]
    acc = [p.sb(tag + f"ac{i}", 128, S, F32) for i in range(2)]
    xsl = [p.sb(tag + f"xs{i}", 128, S, BF16) for i in range(2)]
    for b in range(B_PER_CORE):
        for dc in range(16):
            i = (b * 16 + dc) % 2
            xp, at, xo = xpad[i], acc[i], xsl[i]
            p.emit("vector", lambda v, xp=xp: v.memset(xp[:, 0:3], 0.0),
                   reads=[xp], writes=[xp])
            p.dma("sync", xp[:, 3:3 + S],
                  xinT_hbm[128 * dc:128 * (dc + 1), S * b:S * (b + 1)])
            p.emit("vector", lambda v, xp=xp, at=at, dc=dc: v.tensor_scalar(
                out=at[:], in0=xp[:, 0:S], scalar1=cw[:, 4 * dc:4 * dc + 1],
                scalar2=None, op0=ALU.mult), reads=[xp, cw], writes=[at])
            for j in range(1, 4):
                p.emit("vector", lambda v, xp=xp, at=at, dc=dc, j=j:
                       v.scalar_tensor_tensor(
                           out=at[:], in0=xp[:, j:j + S],
                           scalar=cw[:, 4 * dc + j:4 * dc + j + 1], in1=at[:],
                           op0=ALU.mult, op1=ALU.add),
                       reads=[xp, cw, at], writes=[at])
            p.emit("scalar", lambda s, at=at, xo=xo, dc=dc: s.activation(
                out=xo[:], in_=at[:], func=AF.Silu, bias=cbc[:, dc:dc + 1]),
                reads=[at, cbc], writes=[xo])
            p.dma("gpsimd", xsiluT_hbm[128 * dc:128 * (dc + 1), S * b:S * (b + 1)],
                  xo[:])


def phase_ssm_proj(p, xsiluT_hbm, wx_hbm, wdt_hbm, bdt_hbm, deltaT_hbm, duT_hbm,
                   Bm_hbm, Cm_hbm, ident_f32):
    nc = p.nc
    tag = "c8"
    wx = p.sb(tag + "wx", 128, 16 * 96, BF16)
    for k in range(16):
        p.dma("gpsimd", wx[:, 96 * k:96 * (k + 1)],
              wx_hbm[128 * k:128 * (k + 1), :])
    xs = [p.sb(tag + f"xs{i}", 128, 512, BF16) for i in range(16)]
    pdl = [p.pf(0), p.pf(1)]
    pB = p.pf(2)
    pC = p.pf(3)
    dltb = p.sb(tag + "dltb", 64, NTOK, BF16)
    Bms = p.sb(tag + "Bms", 16, NTOK, F32)
    Cms = p.sb(tag + "Cms", 16, NTOK, F32)
    xst = [p.sb(tag + f"xst{i}", 128, 512, BF16) for i in range(2)]
    for tc in range(8):
        for k in range(16):
            p.dma("sync", xs[k][:],
                  xsiluT_hbm[128 * k:128 * (k + 1), 512 * tc:512 * (tc + 1)])
        pd = pdl[tc % 2]

        def mm_d(t, tc=tc, pd=pd):
            for k in range(16):
                inst = t.matmul(out=pd[0:64, :], lhsT=wx[:, 96 * k:96 * k + 64],
                                rhs=xs[k][:], start=(k == 0), stop=(k == 15))
            return inst
        p.emit("tensor", mm_d, reads=[wx] + xs, writes=[pd])
        p.emit("vector", lambda v, tc=tc, pd=pd: v.tensor_copy(
            out=dltb[:, 512 * tc:512 * (tc + 1)], in_=pd[0:64, :]),
            reads=[pd], writes=[dltb])

        def mm_B(t, tc=tc):
            for k in range(16):
                inst = t.matmul(out=pB[0:16, :], lhsT=wx[:, 96 * k + 64:96 * k + 80],
                                rhs=xs[k][:], start=(k == 0), stop=(k == 15))
            return inst
        p.emit("tensor", mm_B, reads=[wx] + xs, writes=[pB])
        p.emit("scalar", lambda s, tc=tc: s.copy(
            out=Bms[:, 512 * tc:512 * (tc + 1)], in_=pB[0:16, :]),
            reads=[pB], writes=[Bms])

        def mm_C(t, tc=tc):
            for k in range(16):
                inst = t.matmul(out=pC[0:16, :], lhsT=wx[:, 96 * k + 80:96 * k + 96],
                                rhs=xs[k][:], start=(k == 0), stop=(k == 15))
            return inst
        p.emit("tensor", mm_C, reads=[wx] + xs, writes=[pC])
        p.emit("scalar", lambda s, tc=tc: s.copy(
            out=Cms[:, 512 * tc:512 * (tc + 1)], in_=pC[0:16, :]),
            reads=[pC], writes=[Cms])
    p.dma("gpsimd", Bm_hbm[:], Bms[:])
    p.dma("gpsimd", Cm_hbm[:], Cms[:])
    wdt = p.sb(tag + "wdt", 64, D_INNER, BF16)
    p.dma("gpsimd", wdt[:], wdt_hbm[:])
    bdr = p.sb(tag + "bdr", 16, 128, F32)
    bdc = p.sb(tag + "bdc", 128, 16, F32)
    rps = p.pf(4)
    p.dma("sync", bdr[:], bdt_hbm[:])
    p.emit("tensor", lambda t: t.transpose(out=rps[0:128, 0:16], in_=bdr[:],
                                           identity=ident_f32[0:16, 0:16]),
           reads=[bdr, ident_f32], writes=[rps])
    p.emit("scalar", lambda s: s.copy(out=bdc[:], in_=rps[0:128, 0:16]),
           reads=[rps], writes=[bdc])
    onec = p.ccol(1.0)
    pz = [p.pf(0), p.pf(1)]
    negz = [p.sb(tag + f"ng{i}", 128, 512, F32) for i in range(2)]
    nab = [p.sb(tag + f"na{i}", 128, 512, F32) for i in range(2)]
    e1 = [p.sb(tag + f"e1{i}", 128, 512, F32) for i in range(2)]
    l1 = [p.sb(tag + f"l1{i}", 128, 512, F32) for i in range(2)]
    rl = [p.sb(tag + f"rl{i}", 128, 512, F32) for i in range(2)]
    dl = [p.sb(tag + f"dl{i}", 128, 512, F32) for i in range(2)]
    dlb = [p.sb(tag + f"dlb{i}", 128, 512, BF16) for i in range(2)]
    dub = [p.sb(tag + f"du{i}", 128, 512, BF16) for i in range(2)]
    for m in range(16):
        for tc in range(8):
            i = (m * 8 + tc) % 2
            zp = pz[i]
            p.emit("tensor", lambda t, m=m, tc=tc, zp=zp: t.matmul(
                out=zp[:], lhsT=wdt[:, 128 * m:128 * (m + 1)],
                rhs=dltb[:, 512 * tc:512 * (tc + 1)], start=True, stop=True),
                reads=[wdt, dltb], writes=[zp])
            p.emit("vector", lambda v, i=i, m=m, zp=zp: v.tensor_scalar(
                out=rl[i][:], in0=zp[:], scalar1=bdc[:, m:m + 1], scalar2=None,
                op0=ALU.add), reads=[zp, bdc], writes=[rl[i]])
            p.emit("vector", lambda v, i=i: v.tensor_scalar(
                out=negz[i][:], in0=rl[i][:], scalar1=-1.0, scalar2=None,
                op0=ALU.mult), reads=[rl[i]], writes=[negz[i]])
            p.emit("vector", lambda v, i=i: v.tensor_tensor(
                out=nab[i][:], in0=negz[i][:], in1=rl[i][:], op=ALU.min),
                reads=[negz[i], rl[i]], writes=[nab[i]])
            p.emit("scalar", lambda s, i=i: s.activation(
                out=e1[i][:], in_=nab[i][:], func=AF.Exp),
                reads=[nab[i]], writes=[e1[i]])
            p.emit("scalar", lambda s, i=i: s.activation(
                out=l1[i][:], in_=e1[i][:], func=AF.Ln, bias=onec[:, 0:1]),
                reads=[e1[i], onec], writes=[l1[i]])
            p.emit("vector", lambda v, i=i: v.tensor_scalar_max(
                out=rl[i][:], in0=rl[i][:], scalar1=0.0),
                reads=[rl[i]], writes=[rl[i]])
            p.emit("vector", lambda v, i=i: v.tensor_add(
                out=dl[i][:], in0=rl[i][:], in1=l1[i][:]),
                reads=[rl[i], l1[i]], writes=[dl[i]])
            p.emit("vector", lambda v, i=i: v.tensor_copy(
                out=dlb[i][:], in_=dl[i][:]), reads=[dl[i]], writes=[dlb[i]])
            p.dma("gpsimd", deltaT_hbm[128 * m:128 * (m + 1),
                                       512 * tc:512 * (tc + 1)], dlb[i][:])
            p.dma("sync", xst[i][:],
                  xsiluT_hbm[128 * m:128 * (m + 1), 512 * tc:512 * (tc + 1)])
            p.emit("vector", lambda v, i=i: v.tensor_mul(
                out=dub[i][:], in0=dl[i][:], in1=xst[i][:]),
                reads=[dl[i], xst[i]], writes=[dub[i]])
            p.dma("gpsimd", duT_hbm[128 * m:128 * (m + 1),
                                    512 * tc:512 * (tc + 1)], dub[i][:])


def phase_scan_setup(p, deltaT_hbm, duT_hbm, xsiluT_hbm, resT_hbm, Bm_hbm,
                     Cm_hbm, A_hbm, Dv_hbm, yT_hbm, ident_f32):
    nc = p.nc
    tag = "c9"
    TH = 1024
    A_sb = p.sb(tag + "A", 128, 16 * D_STATE, F32)
    for dc in range(16):
        p.dma("sync", A_sb[:, 16 * dc:16 * (dc + 1)],
              A_hbm[128 * dc:128 * (dc + 1), :])
    Dr = p.sb(tag + "Dr", 16, 128, F32)
    Dc = p.sb(tag + "Dc", 128, 16, F32)
    rps = p.pf(4)
    p.dma("sync", Dr[:], Dv_hbm[:])
    p.emit("tensor", lambda t: t.transpose(out=rps[0:128, 0:16], in_=Dr[:],
                                           identity=ident_f32[0:16, 0:16]),
           reads=[Dr, ident_f32], writes=[rps])
    p.emit("scalar", lambda s: s.copy(out=Dc[:], in_=rps[0:128, 0:16]),
           reads=[rps], writes=[Dc])
    oh16 = p.sb(tag + "oh16", 16, 16 * 128, F32)
    ohi = p.sb(tag + "ohi", 16, 16 * 128, F32)
    p.emit("gpsimd", lambda g: g.iota(out=ohi[:], pattern=[[-1, 16], [0, 128]],
                                      base=0, channel_multiplier=1,
                                      allow_small_or_imprecise_dtypes=True),
           writes=[ohi])
    p.emit("vector", lambda v: v.tensor_scalar(out=oh16[:], in0=ohi[:],
                                               scalar1=0.0, scalar2=None,
                                               op0=ALU.is_equal),
           reads=[ohi], writes=[oh16])
    Brow = p.sb(tag + "Brow", 16, TH, F32)
    Crow = p.sb(tag + "Crow", 16, TH, F32)
    Bbc = p.sb(tag + "Bbc", 128, 16 * TH, BF16)
    Cbc = p.sb(tag + "Cbc", 128, 16 * TH, BF16)
    pbc = [p.pf(0), p.pf(1)]
    dts = [p.sb(tag + f"dt{i}", 128, TH, BF16) for i in range(3)]
    dus = [p.sb(tag + f"du{i}", 128, TH, BF16) for i in range(3)]
    xss = [p.sb(tag + f"xs{i}", 128, TH, BF16) for i in range(3)]
    rss = [p.sb(tag + f"rs{i}", 128, TH, BF16) for i in range(3)]
    dA = [p.sb(tag + f"dA{i}", 128, TH, BF16) for i in range(2)]
    db = [p.sb(tag + f"db{i}", 128, TH, BF16) for i in range(2)]
    cyall = p.sb(tag + "cy", 128, 16 * TH, BF16)
    treeA = p.sb(tag + "trA", 128, 8 * TH, BF16)
    yf = [p.sb(tag + "yf0", 128, TH, F32)] * 2
    y2 = [p.sb(tag + "y20", 128, TH, F32)] * 2
    sres = [p.sb(tag + "sr0", 128, TH, F32)] * 2
    yo = [p.sb(tag + f"yo{i}", 128, TH, BF16) for i in range(2)]
    carry_sv = p.sb(tag + "csv", 128, 16 * D_STATE, F32)

    def run_block(b, th):
            c0 = S * b + TH * th
            p.dma("sync", Brow[:], Bm_hbm[:, c0:c0 + TH])
            p.dma("sync", Crow[:], Cm_hbm[:, c0:c0 + TH])
            for n in range(16):
                for cc in range(TH // 512):
                    pt = pbc[cc % 2]
                    p.emit("tensor", lambda t, n=n, cc=cc, pt=pt: t.matmul(
                        out=pt[:], lhsT=oh16[:, 128 * n:128 * (n + 1)],
                        rhs=Brow[:, 512 * cc:512 * (cc + 1)],
                        start=True, stop=True), reads=[oh16, Brow], writes=[pt])
                    p.emit("scalar", lambda s, n=n, cc=cc, pt=pt: s.copy(
                        out=Bbc[:, TH * n + 512 * cc:TH * n + 512 * (cc + 1)],
                        in_=pt[:]), reads=[pt], writes=[Bbc])
                    p.emit("tensor", lambda t, n=n, cc=cc, pt=pt: t.matmul(
                        out=pt[:], lhsT=oh16[:, 128 * n:128 * (n + 1)],
                        rhs=Crow[:, 512 * cc:512 * (cc + 1)],
                        start=True, stop=True), reads=[oh16, Crow], writes=[pt])
                    p.emit("scalar", lambda s, n=n, cc=cc, pt=pt: s.copy(
                        out=Cbc[:, TH * n + 512 * cc:TH * n + 512 * (cc + 1)],
                        in_=pt[:]), reads=[pt], writes=[Cbc])
            for dc in range(16):
                i = dc % 2
                il = dc % 3
                p.dma("sync", dts[il][:],
                      deltaT_hbm[128 * dc:128 * (dc + 1), c0:c0 + TH])
                p.dma("sync", dus[il][:],
                      duT_hbm[128 * dc:128 * (dc + 1), c0:c0 + TH])
                p.dma("sync", xss[il][:],
                      xsiluT_hbm[128 * dc:128 * (dc + 1), c0:c0 + TH])
                p.dma("sync", rss[il][:],
                      resT_hbm[128 * dc:128 * (dc + 1), c0:c0 + TH])
                for n in range(16):
                    jn = n % 2
                    p.emit("scalar", lambda s, il=il, dc=dc, n=n, jn=jn: s.activation(
                        out=dA[jn][:], in_=dts[il][:], func=AF.Exp,
                        scale=A_sb[:, 16 * dc + n:16 * dc + n + 1]),
                        reads=[dts[il], A_sb], writes=[dA[jn]])
                    p.emit("vector", lambda v, il=il, n=n, jn=jn: v.tensor_mul(
                        out=db[jn][:], in0=dus[il][:],
                        in1=Bbc[:, TH * n:TH * (n + 1)]),
                        reads=[dus[il], Bbc], writes=[db[jn]])
                    if th == 0:
                        init = 0.0
                        rd_init = []
                    else:
                        init = carry_sv[:, 16 * dc + n:16 * dc + n + 1]
                        rd_init = [carry_sv]

                    def scan_op(v, init=init, n=n, jn=jn):
                        return v.tensor_tensor_scan(
                            out=cyall[:, TH * n:TH * (n + 1)], data0=dA[jn][:],
                            data1=db[jn][:], initial=init, op0=ALU.mult,
                            op1=ALU.add)
                    p.emit("vector", scan_op, reads=[dA[jn], db[jn]] + rd_init,
                           writes=[cyall])
                    if th < S // TH - 1:
                        p.emit("vector", lambda v, dc=dc, n=n: v.tensor_copy(
                            out=carry_sv[:, 16 * dc + n:16 * dc + n + 1],
                            in_=cyall[:, TH * (n + 1) - 1:TH * (n + 1)]),
                            reads=[cyall], writes=[carry_sv])
                    p.emit("vector", lambda v, n=n: v.tensor_mul(
                        out=cyall[:, TH * n:TH * (n + 1)],
                        in0=cyall[:, TH * n:TH * (n + 1)],
                        in1=Cbc[:, TH * n:TH * (n + 1)]),
                        reads=[cyall, Cbc], writes=[cyall])
                p.emit("vector", lambda v: v.tensor_add(
                    out=treeA[:], in0=cyall[:, 0:8 * TH],
                    in1=cyall[:, 8 * TH:16 * TH]), reads=[cyall], writes=[treeA])
                p.emit("vector", lambda v: v.tensor_add(
                    out=treeA[:, 0:4 * TH], in0=treeA[:, 0:4 * TH],
                    in1=treeA[:, 4 * TH:8 * TH]), reads=[treeA], writes=[treeA])
                p.emit("vector", lambda v: v.tensor_add(
                    out=treeA[:, 0:2 * TH], in0=treeA[:, 0:2 * TH],
                    in1=treeA[:, 2 * TH:4 * TH]), reads=[treeA], writes=[treeA])
                p.emit("vector", lambda v, i=i: v.tensor_add(
                    out=yf[i][:], in0=treeA[:, 0:TH], in1=treeA[:, TH:2 * TH]),
                    reads=[treeA], writes=[yf[i]])
                p.emit("vector", lambda v, i=i, il=il, dc=dc: v.scalar_tensor_tensor(
                    out=y2[i][:], in0=xss[il][:], scalar=Dc[:, dc:dc + 1],
                    in1=yf[i][:], op0=ALU.mult, op1=ALU.add),
                    reads=[xss[il], Dc, yf[i]], writes=[y2[i]])
                p.emit("scalar", lambda s, i=i, il=il: s.activation(
                    out=sres[i][:], in_=rss[il][:], func=AF.Silu),
                    reads=[rss[il]], writes=[sres[i]])
                p.emit("vector", lambda v, i=i: v.tensor_mul(
                    out=yo[i][:], in0=y2[i][:], in1=sres[i][:]),
                    reads=[y2[i], sres[i]], writes=[yo[i]])
                p.dma("gpsimd", yT_hbm[128 * dc:128 * (dc + 1), c0:c0 + TH],
                      yo[i][:])
    return run_block, S // TH


def phase_l1out(p, yT_hbm, wout_hbm, xnT_hbm, s1_hbm, ident_f32):
    nc = p.nc
    tag = "cA"
    wout = p.sb(tag + "wo", 128, 16 * D, BF16)
    for k in range(16):
        p.dma("gpsimd", wout[:, D * k:D * (k + 1)],
              wout_hbm[128 * k:128 * (k + 1), :])
    yc = [p.sb(tag + f"y{i}", 128, 16 * 512, BF16) for i in range(2)]
    xn1 = [p.sb(tag + f"xn{i}", 128, 512, BF16) for i in range(2)]
    s1T = [p.sb(tag + f"sT{i}", 128, 512, F32) for i in range(2)]
    stg = [p.sb(tag + f"st{i}", 128, D, F32) for i in range(4)]
    pps = [p.pf(0), p.pf(1)]
    ptr = [p.pf(2), p.pf(3)]
    for tc in range(8):
        i = tc % 2
        for k in range(16):
            p.dma("sync", yc[i][:, 512 * k:512 * (k + 1)],
                  yT_hbm[128 * k:128 * (k + 1), 512 * tc:512 * (tc + 1)])
        for m in range(8):
            im = (tc * 8 + m) % 2
            p.dma("sync", xn1[im][:],
                  xnT_hbm[128 * m:128 * (m + 1), 512 * tc:512 * (tc + 1)])

            def mms(t, m=m, i=i, im=im):
                for k in range(16):
                    inst = t.matmul(
                        out=pps[im][:], lhsT=wout[:, D * k + 128 * m:D * k + 128 * (m + 1)],
                        rhs=yc[i][:, 512 * k:512 * (k + 1)],
                        start=(k == 0), stop=(k == 15))
                return inst
            p.emit("tensor", mms, reads=[wout, yc[i]], writes=[pps[im]])
            p.emit("vector", lambda v, im=im: v.scalar_tensor_tensor(
                out=s1T[im][:], in0=pps[im][:], scalar=1.0, in1=xn1[im][:],
                op0=ALU.bypass, op1=ALU.add),
                reads=[pps[im], xn1[im]], writes=[s1T[im]])
            for u in range(4):
                pt = ptr[u % 2]
                p.emit("tensor", lambda t, im=im, u=u, pt=pt: t.transpose(
                    out=pt[:, 0:128], in_=s1T[im][:, 128 * u:128 * (u + 1)],
                    identity=ident_f32[:]), reads=[s1T[im], ident_f32],
                    writes=[pt])
                p.emit("scalar", lambda s, u=u, m=m, pt=pt: s.copy(
                    out=stg[u][:, 128 * m:128 * (m + 1)], in_=pt[:, 0:128]),
                    reads=[pt], writes=[stg[u]])
        for u in range(4):
            tt = tc * 4 + u
            p.dma("gpsimd", s1_hbm[128 * tt:128 * (tt + 1), :], stg[u][:])


def phase_ffn8(p, xnT_hbm, wg_hbm, wu_hbm, wd_hbm, gsilT_hbm, hT_hbm, g1_hbm,
               ident_bf, ident_f32):
    nc = p.nc
    tag = "cB"
    SC = 1.0 / 127.0
    HCH = HID_I8 // 128
    raw = None
    xsl = [p.sb(tag + f"x{i}", 128, 8 * 512, BF16) for i in range(2)]
    pps = [p.pf(0), p.pf(1)]
    out1 = [p.sb(tag + f"o{i}", 128, 512, BF16) for i in range(2)]
    wT_sh = p.sb(tag + "wTsh", 128, 8 * HID_I8, BF16)
    wgT, raw = load_weight_T(p, wg_hbm, HID_I8, D, ident_bf, tag + "wg", raw,
                             wT=wT_sh)
    for tc in range(8):
        i = tc % 2
        for k in range(8):
            p.dma("sync", xsl[i][:, 512 * k:512 * (k + 1)],
                  xnT_hbm[128 * k:128 * (k + 1), 512 * tc:512 * (tc + 1)])
        for hc in range(HCH):
            ih = (tc * HCH + hc) % 2

            def mms(t, hc=hc, i=i, ih=ih, wT=wgT):
                for k in range(8):
                    inst = t.matmul(
                        out=pps[ih][:],
                        lhsT=wT[:, HID_I8 * k + 128 * hc:HID_I8 * k + 128 * (hc + 1)],
                        rhs=xsl[i][:, 512 * k:512 * (k + 1)],
                        start=(k == 0), stop=(k == 7))
                return inst
            p.emit("tensor", mms, reads=[wgT, xsl[i]], writes=[pps[ih]])
            p.emit("scalar", lambda s, ih=ih: s.activation(
                out=out1[ih][:], in_=pps[ih][:], func=AF.Silu, scale=SC),
                reads=[pps[ih]], writes=[out1[ih]])
            p.dma("gpsimd", gsilT_hbm[128 * hc:128 * (hc + 1),
                                      512 * tc:512 * (tc + 1)], out1[ih][:])
    wuT, raw = load_weight_T(p, wu_hbm, HID_I8, D, ident_bf, tag + "wu", raw,
                             wT=wT_sh)
    gsl = [p.sb(tag + f"g{i}", 128, 512, BF16) for i in range(2)]
    for tc in range(8):
        i = tc % 2
        for k in range(8):
            p.dma("sync", xsl[i][:, 512 * k:512 * (k + 1)],
                  xnT_hbm[128 * k:128 * (k + 1), 512 * tc:512 * (tc + 1)])
        for hc in range(HCH):
            ih = (tc * HCH + hc) % 2
            p.dma("sync", gsl[ih][:],
                  gsilT_hbm[128 * hc:128 * (hc + 1), 512 * tc:512 * (tc + 1)])

            def mms(t, hc=hc, i=i, ih=ih, wT=wuT):
                for k in range(8):
                    inst = t.matmul(
                        out=pps[ih][:],
                        lhsT=wT[:, HID_I8 * k + 128 * hc:HID_I8 * k + 128 * (hc + 1)],
                        rhs=xsl[i][:, 512 * k:512 * (k + 1)],
                        start=(k == 0), stop=(k == 7))
                return inst
            p.emit("tensor", mms, reads=[wuT, xsl[i]], writes=[pps[ih]])
            p.emit("vector", lambda v, ih=ih: v.scalar_tensor_tensor(
                out=out1[ih][:], in0=pps[ih][:], scalar=SC, in1=gsl[ih][:],
                op0=ALU.mult, op1=ALU.mult),
                reads=[pps[ih], gsl[ih]], writes=[out1[ih]])
            p.dma("gpsimd", hT_hbm[128 * hc:128 * (hc + 1),
                                   512 * tc:512 * (tc + 1)], out1[ih][:])
    wdT, raw = load_weight_T(p, wd_hbm, D, HID_I8, ident_bf, tag + "wd", raw,
                             wT=wT_sh)
    hsl = [p.sb(tag + "h0", 128, HCH * 512, BF16)] * 2
    g1T = [p.sb(tag + f"gT{i}", 128, 512, F32) for i in range(2)]
    stg = [p.sb(tag + f"st{i}", 128, D, F32) for i in range(4)]
    ptr = [p.pf(2), p.pf(3)]
    for tc in range(8):
        i = tc % 2
        for k in range(HCH):
            p.dma("sync", hsl[i][:, 512 * k:512 * (k + 1)],
                  hT_hbm[128 * k:128 * (k + 1), 512 * tc:512 * (tc + 1)])
        for m in range(8):
            im = (tc * 8 + m) % 2

            def mms(t, m=m, i=i, im=im):
                for k in range(HCH):
                    inst = t.matmul(
                        out=pps[im][:], lhsT=wdT[:, D * k + 128 * m:D * k + 128 * (m + 1)],
                        rhs=hsl[i][:, 512 * k:512 * (k + 1)],
                        start=(k == 0), stop=(k == HCH - 1))
                return inst
            p.emit("tensor", mms, reads=[wdT, hsl[i]], writes=[pps[im]])
            p.emit("scalar", lambda s, im=im: s.activation(
                out=g1T[im][:], in_=pps[im][:], func=AF.Copy, scale=SC),
                reads=[pps[im]], writes=[g1T[im]])
            for u in range(4):
                pt = ptr[u % 2]
                p.emit("tensor", lambda t, im=im, u=u, pt=pt: t.transpose(
                    out=pt[:, 0:128], in_=g1T[im][:, 128 * u:128 * (u + 1)],
                    identity=ident_f32[:]), reads=[g1T[im], ident_f32],
                    writes=[pt])
                p.emit("scalar", lambda s, u=u, m=m, pt=pt: s.copy(
                    out=stg[u][:, 128 * m:128 * (m + 1)], in_=pt[:, 0:128]),
                    reads=[pt], writes=[stg[u]])
        for u in range(4):
            tt = tc * 4 + u
            p.dma("gpsimd", g1_hbm[128 * tt:128 * (tt + 1), :], stg[u][:])


def phase_final(p, x1_hbm, s1_hbm, g1_hbm, out_hbm):
    nc = p.nc
    tag = "cF"
    x1 = [p.sb(tag + f"x{i}", 128, D, F32) for i in range(2)]
    s1 = [p.sb(tag + f"s{i}", 128, D, F32) for i in range(2)]
    g1 = [p.sb(tag + f"g{i}", 128, D, F32) for i in range(2)]
    sg = [p.sb(tag + f"sg{i}", 128, D, F32) for i in range(2)]
    ot = [p.sb(tag + f"o{i}", 128, D, F32) for i in range(2)]
    for tt in range(NTT):
        i = tt % 2
        p.dma("sync", x1[i][:], x1_hbm[128 * tt:128 * (tt + 1), :])
        p.dma("sync", s1[i][:], s1_hbm[128 * tt:128 * (tt + 1), :])
        p.dma("sync", g1[i][:], g1_hbm[128 * tt:128 * (tt + 1), :])
        p.emit("scalar", lambda s, i=i: s.activation(
            out=sg[i][:], in_=g1[i][:], func=AF.Silu), reads=[g1[i]],
            writes=[sg[i]])
        p.emit("vector", lambda v, i=i: v.tensor_mul(
            out=sg[i][:], in0=sg[i][:], in1=s1[i][:]),
            reads=[s1[i], sg[i]], writes=[sg[i]])
        p.emit("vector", lambda v, i=i: v.tensor_add(
            out=ot[i][:], in0=x1[i][:], in1=sg[i][:]),
            reads=[x1[i], sg[i]], writes=[ot[i]])
        p.dma("gpsimd", out_hbm[128 * tt:128 * (tt + 1), :], ot[i][:])


import contextlib


class Launch:
    def __init__(self, name):
        self.es = contextlib.ExitStack()
        self.nc = bass.Bass("TRN2", target_bir_lowering=False, name=name)
        self.p = P(self.nc, self.es)
        self.aps = {}

    def inp(self, name, shape, dtype):
        ap = self.nc.dram_tensor(name, list(shape), dtype, kind="ExternalInput").ap()
        self.aps[name] = ap
        return ap

    def out(self, name, shape, dtype):
        ap = self.nc.dram_tensor(name, list(shape), dtype, kind="ExternalOutput").ap()
        self.aps[name] = ap
        return ap

    def run(self, in_maps, out_names):
        with self.es:
            self.p.build()
            res = run_bass_kernel_spmd(self.nc, in_maps,
                                       core_ids=list(range(len(in_maps))))
        return {n: [res.results[c][n] for c in range(len(in_maps))]
                for n in out_names}


def _rep(arr, n=N_CORE):
    return [arr for _ in range(n)]


def kernel_unfused(**inputs):
    f32 = np.float32
    x_full = np.ascontiguousarray(inputs["L0_attn_x"], f32)
    xs = [np.ascontiguousarray(x_full[2 * c:2 * c + 2].reshape(NTOK, D))
          for c in range(N_CORE)]
    wq = np.ascontiguousarray(inputs["L0_attn_Wq"], f32)
    wk = np.ascontiguousarray(inputs["L0_attn_Wk"], f32)
    wv = np.ascontiguousarray(inputs["L0_attn_Wv"], f32)
    wo = np.ascontiguousarray(inputs["L0_attn_Wo"], f32)
    slopes = np.ascontiguousarray(inputs["L0_attn_slopes"], f32).reshape(1, H)
    gate_w = np.ascontiguousarray(inputs["L0_ffn_gate_w"], f32)
    gate_b = np.ascontiguousarray(inputs["L0_ffn_gate_b"], f32).reshape(1, N_EXP)

    L1 = Launch("ln0")
    a = L1.aps
    L1.inp("x", (NTOK, D), F32)
    L1.inp("gw", (D, N_EXP), F32)
    L1.inp("gb", (1, N_EXP), F32)
    L1.out("xnT", (D, NTOK), BF16)
    L1.out("glog", (NTOK, N_EXP), F32)
    L1.p.warmup((AF.Sqrt,))
    idf = build_identity(L1.p, F32, "idf")
    phase_ln(L1.p, "p1", a["x"], a["xnT"], idf, gate=(a["gw"], a["gb"], a["glog"]))
    r1 = L1.run([{"x": xs[c], "gw": gate_w, "gb": gate_b} for c in range(N_CORE)],
                ["xnT", "glog"])

    L2 = Launch("qkv")
    a = L2.aps
    L2.inp("xnT", (D, NTOK), BF16)
    L2.inp("wq", (D, D), F32)
    L2.inp("wk", (D, D), F32)
    L2.inp("wv", (D, D), F32)
    L2.out("qT", (D, NTOK), BF16)
    L2.out("kT", (D, NTOK), BF16)
    L2.out("v", (NTOK, D), BF16)
    L2.p.warmup(())
    idb = build_identity(L2.p, BF16, "idb")
    phase_qkv(L2.p, a["xnT"], a["wq"], a["wk"], a["wv"], a["qT"], a["kT"], a["v"], idb)
    r2 = L2.run([{"xnT": r1["xnT"][c], "wq": wq, "wk": wk, "wv": wv}
                 for c in range(N_CORE)], ["qT", "kT", "v"])

    L3 = Launch("att")
    a = L3.aps
    L3.inp("qT", (D, NTOK), BF16)
    L3.inp("kT", (D, NTOK), BF16)
    L3.inp("v", (NTOK, D), BF16)
    L3.inp("slopes", (1, H), F32)
    L3.out("aoT", (D, NTOK), BF16)
    L3.p.warmup((AF.Exp,))
    idb = build_identity(L3.p, BF16, "idb")
    phase_att(L3.p, a["qT"], a["kT"], a["v"], a["slopes"], a["aoT"], idb)
    r3 = L3.run([{"qT": r2["qT"][c], "kT": r2["kT"][c], "v": r2["v"][c],
                  "slopes": slopes} for c in range(N_CORE)], ["aoT"])

    L4 = Launch("wo")
    a = L4.aps
    L4.inp("aoT", (D, NTOK), BF16)
    L4.inp("wo", (D, D), F32)
    L4.out("s", (NTOK, D), F32)
    L4.p.warmup(())
    idb = build_identity(L4.p, BF16, "idb")
    idf = build_identity(L4.p, F32, "idf")
    phase_wo(L4.p, a["aoT"], a["wo"], a["s"], idb, idf)
    r4 = L4.run([{"aoT": r3["aoT"][c], "wo": wo} for c in range(N_CORE)], ["s"])

    if inputs.get("_partial") == "s":
        return r4["s"], r1["glog"], r1["xnT"]

    w1 = np.ascontiguousarray(inputs["L0_ffn_w1"], f32).reshape(N_EXP * D, D_FF)
    b1 = np.ascontiguousarray(np.ascontiguousarray(inputs["L0_ffn_b1"], f32)
                              .reshape(N_EXP, 32, 128).transpose(0, 2, 1)
                              .reshape(N_EXP * 128, 32))
    w2 = np.ascontiguousarray(inputs["L0_ffn_w2"], f32).reshape(N_EXP * D_FF, D)
    b2 = np.ascontiguousarray(inputs["L0_ffn_b2"], f32).reshape(N_EXP * 8, 128)

    L5 = Launch("moe")
    a = L5.aps
    L5.inp("xnT", (D, NTOK), BF16)
    L5.inp("glog", (NTOK, N_EXP), F32)
    L5.inp("w1", (N_EXP * D, D_FF), F32)
    L5.inp("b1", (N_EXP * 32, 128), F32)
    L5.inp("w2", (N_EXP * D_FF, D), F32)
    L5.inp("b2", (N_EXP * 8, 128), F32)
    L5.out("gT", (D, NTOK), F32)
    L5.p.warmup((AF.Exp, AF.Gelu))
    idf = build_identity(L5.p, F32, "idf")
    phase_moe(L5.p, a["xnT"], a["glog"], a["w1"], a["b1"], a["w2"], a["b2"],
              a["gT"], idf)
    r5 = L5.run([{"xnT": r1["xnT"][c], "glog": r1["glog"][c], "w1": w1, "b1": b1,
                  "w2": w2, "b2": b2} for c in range(N_CORE)], ["gT"])

    L6 = Launch("comb")
    a = L6.aps
    L6.inp("x", (NTOK, D), F32)
    L6.inp("s", (NTOK, D), F32)
    L6.inp("gT", (D, NTOK), F32)
    L6.out("x1", (NTOK, D), F32)
    L6.out("xnT1", (D, NTOK), BF16)
    L6.out("rstd2", (1, NTOK), F32)
    L6.p.warmup((AF.Silu, AF.Sqrt))
    idf = build_identity(L6.p, F32, "idf")
    phase_combine_ln(L6.p, a["x"], a["s"], a["gT"], a["x1"], a["xnT1"],
                     a["rstd2"], idf)
    r6 = L6.run([{"x": xs[c], "s": r4["s"][c], "gT": r5["gT"][c]}
                 for c in range(N_CORE)], ["x1", "xnT1", "rstd2"])

    if inputs.get("_partial") == "x1":
        return r6["x1"], r6["xnT1"], r6["rstd2"]

    norm_w = np.ascontiguousarray(inputs["L1_attn_norm_w"], f32).reshape(8, 128)
    w_in = np.ascontiguousarray(inputs["L1_attn_W_in"], f32)
    conv_w = np.ascontiguousarray(inputs["L1_attn_conv_w"], f32)
    conv_b = np.ascontiguousarray(inputs["L1_attn_conv_b"], f32).reshape(16, 128)
    w_x = np.ascontiguousarray(inputs["L1_attn_W_x"], f32)
    w_dt = np.ascontiguousarray(inputs["L1_attn_W_dt"], f32)
    b_dt = np.ascontiguousarray(inputs["L1_attn_b_dt"], f32).reshape(16, 128)
    A_w = np.ascontiguousarray(inputs["L1_attn_A"], f32)
    D_w = np.ascontiguousarray(inputs["L1_attn_D"], f32).reshape(16, 128)
    w_out = np.ascontiguousarray(inputs["L1_attn_W_out"], f32)
    wg8 = np.ascontiguousarray(inputs["L1_ffn_w_gate"], np.int32)
    wu8 = np.ascontiguousarray(inputs["L1_ffn_w_up"], np.int32)
    wd8 = np.ascontiguousarray(inputs["L1_ffn_w_down"], np.int32)

    L7 = Launch("min")
    a = L7.aps
    L7.inp("xnT1", (D, NTOK), BF16)
    L7.inp("rstd2", (1, NTOK), F32)
    L7.inp("norm_w", (8, 128), F32)
    L7.inp("w_in", (D, 2 * D_INNER), F32)
    L7.inp("conv_w", (D_INNER, CONV_K), F32)
    L7.inp("conv_b", (16, 128), F32)
    L7.out("xinT", (D_INNER, NTOK), BF16)
    L7.out("resT", (D_INNER, NTOK), BF16)
    L7.out("xsiluT", (D_INNER, NTOK), BF16)
    L7.p.warmup((AF.Silu,))
    idf = build_identity(L7.p, F32, "idf")
    phase_mamba_in(L7.p, a["xnT1"], a["rstd2"], a["norm_w"], a["w_in"],
                   a["conv_w"], a["conv_b"], a["xinT"], a["resT"], a["xsiluT"],
                   idf)
    r7 = L7.run([{"xnT1": r6["xnT1"][c], "rstd2": r6["rstd2"][c],
                  "norm_w": norm_w, "w_in": w_in, "conv_w": conv_w,
                  "conv_b": conv_b} for c in range(N_CORE)],
                ["xinT", "resT", "xsiluT"])

    L8 = Launch("sproj")
    a = L8.aps
    L8.inp("xsiluT", (D_INNER, NTOK), BF16)
    L8.inp("w_x", (D_INNER, 96), F32)
    L8.inp("w_dt", (DT_RANK, D_INNER), F32)
    L8.inp("b_dt", (16, 128), F32)
    L8.out("deltaT", (D_INNER, NTOK), BF16)
    L8.out("duT", (D_INNER, NTOK), BF16)
    L8.out("Bm", (D_STATE, NTOK), F32)
    L8.out("Cm", (D_STATE, NTOK), F32)
    L8.p.warmup((AF.Exp, AF.Ln))
    idf = build_identity(L8.p, F32, "idf")
    phase_ssm_proj(L8.p, a["xsiluT"], a["w_x"], a["w_dt"], a["b_dt"],
                   a["deltaT"], a["duT"], a["Bm"], a["Cm"], idf)
    r8 = L8.run([{"xsiluT": r7["xsiluT"][c], "w_x": w_x, "w_dt": w_dt,
                  "b_dt": b_dt} for c in range(N_CORE)],
                ["deltaT", "duT", "Bm", "Cm"])

    L9 = Launch("scan")
    a = L9.aps
    L9.inp("deltaT", (D_INNER, NTOK), BF16)
    L9.inp("duT", (D_INNER, NTOK), BF16)
    L9.inp("xsiluT", (D_INNER, NTOK), BF16)
    L9.inp("resT", (D_INNER, NTOK), BF16)
    L9.inp("Bm", (D_STATE, NTOK), F32)
    L9.inp("Cm", (D_STATE, NTOK), F32)
    L9.inp("A", (D_INNER, D_STATE), F32)
    L9.inp("D", (16, 128), F32)
    L9.out("yT", (D_INNER, NTOK), BF16)
    L9.p.warmup((AF.Exp, AF.Silu))
    idf = build_identity(L9.p, F32, "idf")
    phase_scan(L9.p, a["deltaT"], a["duT"], a["xsiluT"], a["resT"], a["Bm"],
               a["Cm"], a["A"], a["D"], a["yT"], idf)
    r9 = L9.run([{"deltaT": r8["deltaT"][c], "duT": r8["duT"][c],
                  "xsiluT": r7["xsiluT"][c], "resT": r7["resT"][c],
                  "Bm": r8["Bm"][c], "Cm": r8["Cm"][c], "A": A_w, "D": D_w}
                 for c in range(N_CORE)], ["yT"])

    L10 = Launch("l1out")
    a = L10.aps
    L10.inp("yT", (D_INNER, NTOK), BF16)
    L10.inp("w_out", (D_INNER, D), F32)
    L10.inp("xnT1", (D, NTOK), BF16)
    L10.out("s1", (NTOK, D), F32)
    L10.p.warmup(())
    idf = build_identity(L10.p, F32, "idf")
    phase_l1out(L10.p, a["yT"], a["w_out"], a["xnT1"], a["s1"], idf)
    r10 = L10.run([{"yT": r9["yT"][c], "w_out": w_out, "xnT1": r6["xnT1"][c]}
                   for c in range(N_CORE)], ["s1"])

    L11 = Launch("ffn8")
    a = L11.aps
    L11.inp("xnT1", (D, NTOK), BF16)
    L11.inp("wg", (HID_I8, D), I32)
    L11.inp("wu", (HID_I8, D), I32)
    L11.inp("wd", (D, HID_I8), I32)
    gsilT = L11.nc.dram_tensor("gsilT", [HID_I8, NTOK], BF16, kind="Internal").ap()
    hTd = L11.nc.dram_tensor("hTd", [HID_I8, NTOK], BF16, kind="Internal").ap()
    L11.out("g1", (NTOK, D), F32)
    L11.p.warmup((AF.Silu,))
    idb = build_identity(L11.p, BF16, "idb")
    idf = build_identity(L11.p, F32, "idf")
    phase_ffn8(L11.p, a["xnT1"], a["wg"], a["wu"], a["wd"], gsilT, hTd,
               a["g1"], idb, idf)
    r11 = L11.run([{"xnT1": r6["xnT1"][c], "wg": wg8, "wu": wu8, "wd": wd8}
                   for c in range(N_CORE)], ["g1"])

    L12 = Launch("fin")
    a = L12.aps
    L12.inp("x1", (NTOK, D), F32)
    L12.inp("s1", (NTOK, D), F32)
    L12.inp("g1", (NTOK, D), F32)
    L12.out("out", (NTOK, D), F32)
    L12.p.warmup((AF.Silu,))
    phase_final(L12.p, a["x1"], a["s1"], a["g1"], a["out"])
    r12 = L12.run([{"x1": r6["x1"][c], "s1": r10["s1"][c], "g1": r11["g1"][c]}
                   for c in range(N_CORE)], ["out"])

    out = np.stack([np.asarray(r12["out"][c], f32).reshape(2, S, D)
                    for c in range(N_CORE)])
    return out.reshape(16, S, D)


def kernel(**inputs):
    f32 = np.float32
    x_full = np.ascontiguousarray(inputs["L0_attn_x"], f32)
    xs = [np.ascontiguousarray(x_full[2 * c:2 * c + 2].reshape(NTOK, D))
          for c in range(N_CORE)]
    wq = np.ascontiguousarray(inputs["L0_attn_Wq"], f32)
    wk = np.ascontiguousarray(inputs["L0_attn_Wk"], f32)
    wv = np.ascontiguousarray(inputs["L0_attn_Wv"], f32)
    wo = np.ascontiguousarray(inputs["L0_attn_Wo"], f32)
    slopes = np.ascontiguousarray(inputs["L0_attn_slopes"], f32).reshape(1, H)
    gate_w = np.ascontiguousarray(inputs["L0_ffn_gate_w"], f32)
    gate_b = np.ascontiguousarray(inputs["L0_ffn_gate_b"], f32).reshape(1, N_EXP)
    w1 = np.ascontiguousarray(inputs["L0_ffn_w1"], f32).reshape(N_EXP * D, D_FF)
    b1 = np.ascontiguousarray(np.ascontiguousarray(inputs["L0_ffn_b1"], f32)
                              .reshape(N_EXP, 32, 128).transpose(0, 2, 1)
                              .reshape(N_EXP * 128, 32))
    w2 = np.ascontiguousarray(inputs["L0_ffn_w2"], f32).reshape(N_EXP * D_FF, D)
    b2 = np.ascontiguousarray(inputs["L0_ffn_b2"], f32).reshape(N_EXP * 8, 128)
    b2r8 = np.ascontiguousarray(inputs["L0_ffn_b2"], f32)
    norm_w = np.ascontiguousarray(inputs["L1_attn_norm_w"], f32).reshape(8, 128)
    w_in = np.ascontiguousarray(inputs["L1_attn_W_in"], f32)
    conv_w = np.ascontiguousarray(inputs["L1_attn_conv_w"], f32)
    conv_b = np.ascontiguousarray(inputs["L1_attn_conv_b"], f32).reshape(16, 128)
    w_x = np.ascontiguousarray(inputs["L1_attn_W_x"], f32)
    w_dt = np.ascontiguousarray(inputs["L1_attn_W_dt"], f32)
    b_dt = np.ascontiguousarray(inputs["L1_attn_b_dt"], f32).reshape(16, 128)
    A_w = np.ascontiguousarray(inputs["L1_attn_A"], f32)
    D_w = np.ascontiguousarray(inputs["L1_attn_D"], f32).reshape(16, 128)
    w_out = np.ascontiguousarray(inputs["L1_attn_W_out"], f32)
    wg8 = np.ascontiguousarray(inputs["L1_ffn_w_gate"], np.int32)
    wu8 = np.ascontiguousarray(inputs["L1_ffn_w_up"], np.int32)
    wd8 = np.ascontiguousarray(inputs["L1_ffn_w_down"], np.int32)

    L = Launch("mega")
    nc, p = L.nc, L.p
    a = L.aps
    L.inp("x", (NTOK, D), F32)
    L.inp("wq", (D, D), F32)
    L.inp("wk", (D, D), F32)
    L.inp("wv", (D, D), F32)
    L.inp("wo", (D, D), F32)
    L.inp("slopes", (1, H), F32)
    L.inp("gw", (D, N_EXP), F32)
    L.inp("gb", (1, N_EXP), F32)
    L.inp("w1", (N_EXP * D, D_FF), F32)
    L.inp("b1", (N_EXP * 128, 32), F32)
    L.inp("w2", (N_EXP * D_FF, D), F32)
    L.inp("b2", (N_EXP * 8, 128), F32)
    L.inp("b2r8", (N_EXP, D), F32)
    L.inp("norm_w", (8, 128), F32)
    L.inp("w_in", (D, 2 * D_INNER), F32)
    L.inp("conv_w", (D_INNER, CONV_K), F32)
    L.inp("conv_b", (16, 128), F32)
    L.inp("w_x", (D_INNER, 96), F32)
    L.inp("w_dt", (DT_RANK, D_INNER), F32)
    L.inp("b_dt", (16, 128), F32)
    L.inp("A", (D_INNER, D_STATE), F32)
    L.inp("D", (16, 128), F32)
    L.inp("w_out", (D_INNER, D), F32)
    L.inp("wg", (HID_I8, D), I32)
    L.inp("wu", (HID_I8, D), I32)
    L.inp("wd", (D, HID_I8), I32)
    L.out("out", (NTOK, D), F32)

    def scr(name, shape, dtype):
        return nc.dram_tensor(name, list(shape), dtype, kind="Internal").ap()

    xnT0 = scr("xnT0", (D, NTOK), BF16)
    xn_tm = scr("xn_tm", (NTOK, D), BF16)
    gpos = scr("gpos", (NPAD, 2), I32)
    spos = scr("spos", (NPAD, 2), I32)
    wgtg = scr("wgtg", (NPAD, 2), F32)
    xpk = scr("xpk", (NPAD, D), BF16)
    glog = scr("glog", (NTOK, N_EXP), F32)
    qT = scr("qT", (D, NTOK), BF16)
    kT = scr("kT", (D, NTOK), BF16)
    vtm = scr("vtm", (NTOK, D), BF16)
    aoT = scr("aoT", (D, NTOK), BF16)
    s0 = scr("s0", (NTOK, D), F32)
    gT = scr("gT", (NTOK + 128, D), F32)
    x1 = scr("x1", (NTOK, D), F32)
    xnT1 = scr("xnT1", (D, NTOK), BF16)
    rstd2 = scr("rstd2", (1, NTOK), F32)
    xinT = scr("xinT", (D_INNER, NTOK), BF16)
    resT = scr("resT", (D_INNER, NTOK), BF16)
    xsiluT = scr("xsiluT", (D_INNER, NTOK), BF16)
    deltaT = scr("deltaT", (D_INNER, NTOK), BF16)
    duT = scr("duT", (D_INNER, NTOK), BF16)
    Bm = scr("Bm", (D_STATE, NTOK), F32)
    Cm = scr("Cm", (D_STATE, NTOK), F32)
    yT = scr("yT", (D_INNER, NTOK), BF16)
    s1 = scr("s1", (NTOK, D), F32)
    gsilT = scr("gsilT", (HID_I8, NTOK), BF16)
    hTd = scr("hTd", (HID_I8, NTOK), BF16)
    g1 = scr("g1", (NTOK, D), F32)
    wgT_s = scr("wgT_s", (D, HID_I8), BF16)
    wuT_s = scr("wuT_s", (D, HID_I8), BF16)
    wdT_s = scr("wdT_s", (HID_I8, D), BF16)

    p.warmup((AF.Sqrt, AF.Exp, AF.Gelu, AF.Silu, AF.Ln))
    idf = build_identity(p, F32, "idf")
    idb = build_identity(p, BF16, "idb")
    p.ccol(EPS)
    p.ccol(1.0)
    import os as _os
    _np_max = int(_os.environ.get("K_PHASES", "99"))
    _ph = [0]

    def _go():
        _ph[0] += 1
        return _ph[0] <= _np_max

    if _go():
        with p.scope():
            phase_ln(p, "p1", a["x"], xnT0, idf, gate=(a["gw"], a["gb"], glog),
                     xn_tm_hbm=xn_tm)
        p.barrier()
    if _go():
        with p.scope():
            phase_qkv(p, xnT0, a["wq"], a["wk"], a["wv"], qT, kT, vtm, idb)
        p.barrier()
    if _go():
        with p.scope():
            phase_att(p, qT, kT, vtm, a["slopes"], aoT, idb)
        p.barrier()
    if _go():
        with p.scope():
            phase_wo(p, aoT, a["wo"], s0, idb, idf)
        p.barrier()
    if _go():
        with p.scope():
            phase_moe_routed(p, xn_tm, glog, a["w1"], a["b1"], a["w2"],
                             a["b2r8"], gT, gpos, spos, wgtg, xpk, idf, idb)
        p.barrier()
    if _go():
        with p.scope():
            phase_combine_ln(p, a["x"], s0, gT, x1, xnT1, rstd2, idf,
                             g_token_major=True)
        p.barrier()
    if _go():
        with p.scope():
            phase_mamba_in(p, xnT1, rstd2, a["norm_w"], a["w_in"], a["conv_w"],
                           a["conv_b"], xinT, resT, xsiluT, idf)
        p.barrier()
    if _go():
        with p.scope():
            phase_ssm_proj(p, xsiluT, a["w_x"], a["w_dt"], a["b_dt"], deltaT, duT,
                           Bm, Cm, idf)
        p.barrier()
    if _go():
        with p.scope():
            run_scan, nth = phase_scan_setup(p, deltaT, duT, xsiluT, resT, Bm, Cm,
                                             a["A"], a["D"], yT, idf)
            for b in range(B_PER_CORE):
                for th in range(nth):
                    run_scan(b, th)
        p.barrier()
    if _go():
        with p.scope():
            phase_ffn8(p, xnT1, a["wg"], a["wu"], a["wd"], gsilT, hTd, g1, idb, idf)
        p.barrier()
    if _go():
        with p.scope():
            phase_l1out(p, yT, a["w_out"], xnT1, s1, idf)
        p.barrier()
    if _go():
        with p.scope():
            phase_final(p, x1, s1, g1, a["out"])

    in_maps = [{"x": xs[c], "wq": wq, "wk": wk, "wv": wv, "wo": wo,
                "slopes": slopes, "gw": gate_w, "gb": gate_b, "w1": w1,
                "b1": b1, "w2": w2, "b2": b2, "b2r8": b2r8, "norm_w": norm_w,
                "w_in": w_in,
                "conv_w": conv_w, "conv_b": conv_b, "w_x": w_x, "w_dt": w_dt,
                "b_dt": b_dt, "A": A_w, "D": D_w, "w_out": w_out, "wg": wg8,
                "wu": wu8, "wd": wd8} for c in range(N_CORE)]
    res = L.run(in_maps, ["out"])
    out = np.stack([np.asarray(res["out"][c], f32).reshape(2, S, D)
                    for c in range(N_CORE)])
    return out.reshape(16, S, D)


# version 20
# speedup vs baseline: 1.1667x; 1.0496x over previous
import numpy as np
import ml_dtypes

import concourse.bass as bass
from concourse.bass import PSumTensorHandle
import concourse.mybir as mybir
from concourse.bass_utils import run_bass_kernel_spmd

F32 = mybir.dt.float32
BF16 = mybir.dt.bfloat16
I32 = mybir.dt.int32
AF = mybir.ActivationFunctionType
ALU = mybir.AluOpType
AX = mybir.AxisListType
BF = ml_dtypes.bfloat16

N_CORE = 8
B_PER_CORE = 2
S = 2048
D = 1024
NTOK = B_PER_CORE * S
NTT = NTOK // 128
H = 16
DK = 64
N_EXP = 8
D_FF = 4096
D_INNER = 2048
D_STATE = 16
DT_RANK = 64
CONV_K = 4
HID_I8 = 3072
EPS = 1e-5
NEG = -1.0e30

ENGINES = ("sync", "scalar", "vector", "tensor", "gpsimd")


class P:
    def __init__(self, nc, es):
        self.nc = nc
        self.es = es
        self.thunks = {e: [] for e in ENGINES}
        self.sem = {e: es.enter_context(nc.semaphore("sem_" + e)) for e in ENGINES}
        self.NP = 2
        self.streams = {}
        self.gq_hist = []
        self.GQ_DEPTH = 4
        self.cnt = {e: 0 for e in ENGINES}
        self.waited = {e: {} for e in ENGINES}
        self.tiles = {}
        self.consts = {}

    def sb(self, name, pdim, fdim, dtype):
        es = getattr(self, "cur_es", None)
        if es is None:
            return self.nc.alloc_sbuf_tensor(name, [pdim, fdim], dtype)
        return es.enter_context(self.nc.sbuf_tensor(name, [pdim, fdim], dtype))

    import contextlib as _ctx

    @_ctx.contextmanager
    def scope(self):
        import contextlib
        prev = getattr(self, "cur_es", None)
        self.cur_es = contextlib.ExitStack()
        try:
            yield
        finally:
            self.cur_es.close()
            self.cur_es = prev

    def barrier(self):
        tgt = []
        for e in ENGINES:
            if self.cnt[e]:
                tgt.append((("e", e), self.sem[e], self.cnt[e]))
        for skey, st in self.streams.items():
            for i in range(self.NP):
                if st["per"][i]:
                    tgt.append((("d", (skey, i)), st["sems"][i],
                                16 * st["per"][i]))
        for e in ENGINES:
            wm = self.waited[e]
            waits = []
            for key, s, v in tgt:
                if key == ("e", e):
                    continue
                if wm.get(key, -1) < v:
                    wm[key] = v
                    waits.append((s, v))
            if waits:
                def thunk(engine, waits=waits):
                    for s, v in waits:
                        engine.wait_ge(s, v)
                self.thunks[e].append(thunk)

    def pf(self, i):
        key = ("f", i)
        if key not in self.consts:
            self.consts[key] = psum(self.nc, f"pbf32_{i}", 128, 512, F32)
        return self.consts[key]

    def pbf(self, i):
        key = ("b", i)
        if key not in self.consts:
            self.consts[key] = psum(self.nc, f"pbbf_{i}", 128, 1024, BF16)
        return self.consts[key]

    def ccol(self, val):
        if val not in self.consts:
            t = sb(self.nc, f"cc_{len(self.consts)}", 128, 1, F32)
            self.emit("vector", lambda v, t=t, val=val: v.memset(t[:], val),
                      writes=[t])
            self.consts[val] = t
        return self.consts[val]

    def warmup(self, funcs=(AF.Sqrt,)):
        nc = self.nc
        wsc = sb(nc, "warm_sc", 128, 1024, F32)
        wsb = sb(nc, "warm_sb", 128, 512, BF16)
        wcol = sb(nc, "warm_col", 128, 1, F32)
        wps = self.pf(5)
        wpsb = self.pbf(1)
        self.emit("vector", lambda v: v.memset(wsc[:], 1.0), writes=[wsc])
        self.emit("vector", lambda v: v.memset(wcol[:], 1.0), writes=[wcol])
        self.emit("vector", lambda v: v.tensor_copy(out=wsb[:], in_=wsc[:, 0:512]),
                  reads=[wsc], writes=[wsb])
        self.emit("tensor", lambda t: t.matmul(out=wps[:, 0:128], lhsT=wsb[:, 0:128],
                                               rhs=wsb[:, 0:128], start=True,
                                               stop=True), reads=[wsb], writes=[wps])
        self.emit("tensor", lambda t: t.matmul(out=wps[:, 0:128], lhsT=wsc[:, 0:128],
                                               rhs=wsc[:, 0:128], start=True,
                                               stop=True), reads=[wsc], writes=[wps])
        self.emit("gpsimd", lambda g: g.iota(out=wsc[:, 0:128], pattern=[[-1, 128]],
                                             base=0, channel_multiplier=1,
                                             allow_small_or_imprecise_dtypes=True),
                  writes=[wsc])
        self.emit("vector", lambda v: v.tensor_scalar(
            out=wsb[:, 0:128], in0=wsc[:, 0:128], scalar1=0.0, scalar2=None,
            op0=ALU.is_equal), reads=[wsc], writes=[wsb])
        self.emit("tensor", lambda t: t.transpose(out=wpsb[:, 0:128], in_=wsb[:, 0:128],
                                                  identity=wsb[:, 0:128]),
                  reads=[wsb], writes=[wpsb])
        self.emit("scalar", lambda s: s.copy(out=wsc[:, 0:128], in_=wps[:, 0:128]),
                  reads=[wps], writes=[wsc])
        self.emit("vector", lambda v: v.tensor_copy(out=wsb[:, 0:128],
                                                    in_=wpsb[:, 0:128]),
                  reads=[wpsb], writes=[wsb])
        self.emit("vector", lambda v: v.tensor_scalar(
            out=wsc[:], in0=wsc[:], scalar1=wcol[:, 0:1], scalar2=wcol[:, 0:1],
            op0=ALU.subtract, op1=ALU.mult), reads=[wsc, wcol], writes=[wsc])
        self.emit("vector", lambda v: v.scalar_tensor_tensor(
            out=wsc[:], in0=wsc[:], scalar=1.0, in1=wsc[:], op0=ALU.bypass,
            op1=ALU.mult, accum_out=wcol[:, 0:1]), reads=[wsc], writes=[wsc, wcol])
        self.emit("vector", lambda v: v.reduce_max(out=wcol[:], in_=wsc[:], axis=AX.X,
                                                   negate=True),
                  reads=[wsc], writes=[wcol])
        self.emit("vector", lambda v: v.reciprocal(out=wcol[:], in_=wcol[:]),
                  reads=[wcol], writes=[wcol])
        self.emit("vector", lambda v: v.tensor_tensor_scan(
            out=wsb[:], data0=wsb[:], data1=wsb[:], initial=0.0, op0=ALU.mult,
            op1=ALU.add), reads=[wsb], writes=[wsb])
        self.emit("gpsimd", lambda g: g.iota(out=wsb[:], pattern=[[1, 512]], base=0,
                                             channel_multiplier=1,
                                             allow_small_or_imprecise_dtypes=True),
                  writes=[wsb])
        self.emit("vector", lambda v: v.memset(wsc[:], 1.0), writes=[wsc])
        for f in funcs:
            self.emit("scalar", lambda s, f=f: s.activation(out=wsc[:], in_=wsc[:],
                                                            func=f),
                      reads=[wsc], writes=[wsc])
        for k in range(12):
            self.emit("scalar", lambda s: s.activation(out=wsc[:], in_=wsc[:],
                                                       func=AF.Copy),
                      reads=[wsc], writes=[wsc])
        self.emit("vector", lambda v: v.tensor_copy(out=wsb[:], in_=wsc[:, 0:512]),
                  reads=[wsc], writes=[wsb])
        self.emit("gpsimd", lambda g: g.tensor_copy(out=wsc[:, 0:512], in_=wsc[:, 512:1024]),
                  reads=[wsc], writes=[wsc])
        self.emit("tensor", lambda t: t.matmul(out=wps[:, 0:128], lhsT=wsb[:, 0:128],
                                               rhs=wsb[:, 0:128], start=True,
                                               stop=True), reads=[wsb], writes=[wps])
        self.emit("scalar", lambda s: s.copy(out=wsc[:, 0:128], in_=wps[:, 0:128]),
                  reads=[wps], writes=[wsc])

    def _sem_val(self, tok):
        kind, key, c = tok
        if kind == "e":
            return (self.sem[key], c, ("e", key))
        skey, idx = key
        return (self.streams[skey]["sems"][idx], 16 * c, ("d", key))

    def _entry(self, h):
        k = id(h)
        if k not in self.tiles:
            self.tiles[k] = {"w": None, "r": []}
        return self.tiles[k]

    def _resolve(self, eng, reads, writes):
        need = []
        for h in reads:
            e = self._entry(h)
            if e["w"] is not None:
                need.append(e["w"])
        for h in writes:
            e = self._entry(h)
            if e["w"] is not None:
                need.append(e["w"])
            need.extend(e["r"])
        waits = []
        wm = self.waited[eng]
        best = {}
        for tok in need:
            if tok[0] == "e" and tok[1] == eng and tok[2] == self.cnt[eng]:
                pass
            sem, val, key = self._sem_val(tok)
            if val > best.get(key, (None, -1))[1]:
                best[key] = (sem, val)
        for key, (sem, val) in best.items():
            if wm.get(key, -1) < val:
                wm[key] = val
                waits.append((sem, val))
        return waits

    def _update(self, tok, reads, writes):
        for h in writes:
            self.tiles[id(h)] = {"w": tok, "r": []}
        for h in reads:
            self._entry(h)["r"].append(tok)

    def emit(self, eng, build, reads=(), writes=()):
        assert not getattr(self, "_building", False), (
            "emit() during build replay (ccol called inside a lambda?)")
        rd, wr = [], list(writes)
        for h in reads:
            if isinstance(h, PSumTensorHandle):
                wr.append(h)
            else:
                rd.append(h)
        reads, writes = rd, wr
        waits = self._resolve(eng, reads, writes)
        sem = self.sem[eng]
        self.cnt[eng] += 1
        tok = ("e", eng, self.cnt[eng])

        def thunk(engine, waits=waits, build=build, sem=sem):
            for s, v in waits:
                engine.wait_ge(s, v)
            inst = build(engine)
            inst.then_inc(sem, 1)

        self.thunks[eng].append(thunk)
        self._update(tok, reads, writes)
        return tok

    def dma(self, eng, out, in_, accum=False, allow_slow=False, transpose=False):
        self._dma_transpose = transpose
        reads = [in_.tensor]
        writes = [out.tensor]
        if accum:
            reads.append(out.tensor)
        waits = self._resolve(eng, reads, writes)
        nbytes = 1
        for d in out.shape:
            nbytes *= int(d)
        nbytes *= np.dtype(mybir.dt.np(out.dtype)).itemsize
        skey = (eng, nbytes)
        if skey not in self.streams:
            self.streams[skey] = {
                "sems": [self.es.enter_context(
                    self.nc.semaphore(f"dma_{eng}_{len(self.streams)}_{i}"))
                    for i in range(self.NP)],
                "cnt": 0, "per": [0] * self.NP}
        st = self.streams[skey]
        idx = st["cnt"] % self.NP
        st["cnt"] += 1
        st["per"][idx] += 1
        sem = st["sems"][idx]
        tok = ("d", (skey, idx), st["per"][idx])
        kw = {}
        if accum:
            kw["accum_op"] = ALU.add
        if allow_slow:
            kw["allow_slow_non_contiguous"] = True
        if getattr(self, "_dma_transpose", False):
            kw["transpose"] = True

        gq = None
        if eng == "gpsimd":
            if len(self.gq_hist) >= self.GQ_DEPTH:
                gq = self.gq_hist[-self.GQ_DEPTH]
            self.gq_hist.append((sem, 16 * st["per"][idx]))

        def thunk(engine, waits=waits, out=out, in_=in_, sem=sem, kw=kw, gq=gq):
            for s, v in waits:
                engine.wait_ge(s, v)
            if gq is not None:
                engine.wait_ge(gq[0], gq[1])
            engine.dma_start(out=out, in_=in_, **kw).then_inc(sem, 16)

        self.thunks[eng].append(thunk)
        self._update(tok, reads, writes)
        return tok

    def idma(self, tag, build, reads=(), writes=()):
        eng = "gpsimd"
        waits = self._resolve(eng, list(reads), list(writes))
        skey = (eng, tag)
        if skey not in self.streams:
            self.streams[skey] = {
                "sems": [self.es.enter_context(
                    self.nc.semaphore(f"idma_{tag}_{i}"))
                    for i in range(self.NP)],
                "cnt": 0, "per": [0] * self.NP}
        st = self.streams[skey]
        idx = st["cnt"] % self.NP
        st["cnt"] += 1
        st["per"][idx] += 1
        sem = st["sems"][idx]
        tok = ("d", (skey, idx), st["per"][idx])

        if len(self.gq_hist) >= self.GQ_DEPTH:
            gq = self.gq_hist[-self.GQ_DEPTH]
        else:
            gq = None
        self.gq_hist.append((sem, 16 * st["per"][idx]))

        def thunk(engine, waits=waits, build=build, sem=sem, gq=gq):
            for s, v in waits:
                engine.wait_ge(s, v)
            if gq is not None:
                engine.wait_ge(gq[0], gq[1])
            build(engine).then_inc(sem, 16)

        self.thunks[eng].append(thunk)
        self._update(tok, list(reads), list(writes))
        return tok

    def finish(self):
        fins = []
        for skey, st in self.streams.items():
            for i in range(self.NP):
                if st["per"][i]:
                    fins.append((st["sems"][i], 16 * st["per"][i]))
        for e in ENGINES:
            if self.cnt[e] and e != "gpsimd":
                fins.append((self.sem[e], self.cnt[e]))

        def thunk(engine, fins=fins):
            for s, v in fins:
                engine.wait_ge(s, v)

        self.thunks["gpsimd"].append(thunk)

    def build(self):
        self.finish()
        self._building = True
        with self.nc.Block() as block:
            for eng in ENGINES:
                lst = self.thunks[eng]
                if not lst:
                    continue
                dec = getattr(block, eng)

                def body(engine, lst=lst):
                    for t in lst:
                        t(engine)

                dec(body)


def sb(nc, name, pdim, fdim, dtype):
    return nc.alloc_sbuf_tensor(name, [pdim, fdim], dtype)


def psum(nc, name, pdim=128, fdim=512, dtype=F32):
    return nc.alloc_psum_tensor(name, [pdim, fdim], dtype)


def build_identity(p, dtype, name):
    nc = p.nc
    idn = p.sb(name, 128, 128, dtype)
    iot = p.sb(name + "_i", 128, 128, F32)
    p.emit("gpsimd", lambda g: g.iota(out=iot[:], pattern=[[-1, 128]], base=0,
                                      channel_multiplier=1,
                                      allow_small_or_imprecise_dtypes=True),
           writes=[iot])
    p.emit("vector", lambda v: v.tensor_scalar(out=idn[:], in0=iot[:], scalar1=0.0,
                                               scalar2=None, op0=ALU.is_equal),
           reads=[iot], writes=[idn])
    return idn


def build_ones_row(p, name="ones_row", n=128):
    ones = p.sb(name, 1, n, F32)
    p.emit("vector", lambda v: v.memset(ones[:], 1.0), writes=[ones])
    return ones


def build_onehot0(p, name="onehot0"):
    oh = p.sb(name, 8, 128, F32)
    p.emit("vector", lambda v: v.memset(oh[:], 0.0), writes=[oh])
    p.emit("vector", lambda v: v.memset(oh[0:1, :], 1.0), reads=[oh], writes=[oh])
    return oh


def phase_ln(p, tag, x_hbm, xnT_hbm, ident_f32, gate=None, rstd2_hbm=None,
             xn_tm_hbm=None):
    nc = p.nc
    xb = [p.sb(tag + f"xb{i}", 128, D, F32) for i in range(2)]
    xn = [p.sb(tag + f"xn{i}", 128, D, F32) for i in range(2)]
    sq = [p.sb(tag + f"sq{i}", 128, D, F32) for i in range(2)]
    ssum = [p.sb(tag + f"ssum{i}", 128, 1, F32) for i in range(2)]
    sqs = [p.sb(tag + f"sqs{i}", 128, 1, F32) for i in range(2)]
    mean = [p.sb(tag + f"mean{i}", 128, 1, F32) for i in range(2)]
    sqd = [p.sb(tag + f"sqd{i}", 128, 1, F32) for i in range(2)]
    nvar = [p.sb(tag + f"nvar{i}", 128, 1, F32) for i in range(2)]
    sd = [p.sb(tag + f"sd{i}", 128, 1, F32) for i in range(2)]
    rstd = [p.sb(tag + f"rstd{i}", 128, 1, F32) for i in range(2)]
    pstr = [p.pf(0), p.pf(1)]
    epscol = p.ccol(EPS)
    if xn_tm_hbm is not None:
        xnb = [p.sb(tag + f"xnb{i}", 128, D, BF16) for i in range(2)]
    stage = [p.sb(tag + f"stg{j}", 128, NTOK, BF16) for j in range(8)]
    if gate is not None:
        gate_w_hbm, gate_b_hbm, glog_hbm = gate
        gw = p.sb(tag + "gw", 128, 8 * 8, F32)
        gb = p.sb(tag + "gb", 8, 8, F32)
        oh0 = build_onehot0(p, tag + "oh0")
        p.emit("vector", lambda v: v.memset(gb[:], 0.0), writes=[gb])
        xnTf = [p.sb(tag + f"xnTf{i}", 128, D, F32) for i in range(2)]
        glog_ps = [p.pf(2), p.pf(3)]
        glog_sb = [p.sb(tag + f"glsb{i}", 128, 8, F32) for i in range(2)]
        for k in range(8):
            p.dma("sync", gw[:, 8 * k:8 * (k + 1)], gate_w_hbm[128 * k:128 * (k + 1), :])
        p.dma("sync", gb[0:1, :], gate_b_hbm[:])
    if rstd2_hbm is not None:
        sq2s = [p.sb(tag + f"sq2s{i}", 128, 1, F32) for i in range(2)]
        sd2 = [p.sb(tag + f"sd2{i}", 128, 1, F32) for i in range(2)]
        rstd2 = [p.sb(tag + f"rstd2{i}", 128, 1, F32) for i in range(2)]
        r2ps = p.pf(4)
        r2row = p.sb(tag + "r2row", 1, NTOK, F32)

    for tt in range(NTT):
        i = tt % 2
        xt, xnt = xb[i], xn[i]
        p.dma("sync", xt[:], x_hbm[128 * tt:128 * (tt + 1), :])
        p.emit("vector", lambda v, xt=xt, i=i: v.reduce_sum(
            out=ssum[i][:], in_=xt[:], axis=AX.X), reads=[xt], writes=[ssum[i]])
        p.emit("vector", lambda v, xt=xt, i=i: v.scalar_tensor_tensor(
            out=sq[i][:], in0=xt[:], scalar=1.0, in1=xt[:], op0=ALU.bypass,
            op1=ALU.mult, accum_out=sqs[i][:, 0:1]), reads=[xt],
            writes=[sq[i], sqs[i]])
        p.emit("vector", lambda v, i=i: v.tensor_scalar(
            out=mean[i][:], in0=ssum[i][:], scalar1=1.0 / D, scalar2=None,
            op0=ALU.mult), reads=[ssum[i]], writes=[mean[i]])
        p.emit("vector", lambda v, i=i: v.tensor_scalar(
            out=sqd[i][:], in0=sqs[i][:], scalar1=1.0 / D, scalar2=None,
            op0=ALU.mult), reads=[sqs[i]], writes=[sqd[i]])
        p.emit("vector", lambda v, i=i: v.scalar_tensor_tensor(
            out=nvar[i][:], in0=mean[i][:], scalar=mean[i][:, 0:1], in1=sqd[i][:],
            op0=ALU.mult, op1=ALU.subtract), reads=[mean[i], sqd[i]],
            writes=[nvar[i]])
        p.emit("scalar", lambda s, i=i: s.activation(
            out=sd[i][:], in_=nvar[i][:], func=AF.Sqrt, scale=-1.0,
            bias=epscol[:, 0:1]), reads=[nvar[i], epscol], writes=[sd[i]])
        p.emit("vector", lambda v, i=i: v.reciprocal(out=rstd[i][:], in_=sd[i][:]),
               reads=[sd[i]], writes=[rstd[i]])
        p.emit("vector", lambda v, xt=xt, xnt=xnt, i=i: v.tensor_scalar(
            out=xnt[:], in0=xt[:], scalar1=mean[i][:, 0:1], scalar2=rstd[i][:, 0:1],
            op0=ALU.subtract, op1=ALU.mult), reads=[xt, mean[i], rstd[i]],
            writes=[xnt])
        if xn_tm_hbm is not None:
            p.emit("vector", lambda v, xnt=xnt, i=i: v.tensor_copy(
                out=xnb[i][:], in_=xnt[:]), reads=[xnt], writes=[xnb[i]])
            p.dma("gpsimd", xn_tm_hbm[128 * tt:128 * (tt + 1), :], xnb[i][:])
        if rstd2_hbm is not None:
            p.emit("vector", lambda v, xnt=xnt, i=i: v.scalar_tensor_tensor(
                out=sq[i][:], in0=xnt[:], scalar=1.0, in1=xnt[:], op0=ALU.bypass,
                op1=ALU.mult, accum_out=sq2s[i][:, 0:1]), reads=[xnt],
                writes=[sq[i], sq2s[i]])
            p.emit("scalar", lambda s, i=i: s.activation(
                out=sd2[i][:], in_=sq2s[i][:], func=AF.Sqrt, scale=1.0 / D,
                bias=epscol[:, 0:1]), reads=[sq2s[i], epscol], writes=[sd2[i]])
            p.emit("vector", lambda v, i=i: v.reciprocal(
                out=rstd2[i][:], in_=sd2[i][:]), reads=[sd2[i]], writes=[rstd2[i]])
            p.emit("tensor", lambda t, i=i: t.transpose(
                out=r2ps[0:1, 0:128], in_=rstd2[i][:], identity=ident_f32[:]),
                reads=[rstd2[i], ident_f32], writes=[r2ps])
            p.emit("scalar", lambda s, tt=tt: s.copy(
                out=r2row[0:1, 128 * tt:128 * (tt + 1)], in_=r2ps[0:1, 0:128]),
                reads=[r2ps], writes=[r2row])
        for j in range(8):
            pt = pstr[j % 2]
            p.emit("tensor", lambda t, xnt=xnt, pt=pt, j=j: t.transpose(
                out=pt[:, 0:128], in_=xnt[:, 128 * j:128 * (j + 1)],
                identity=ident_f32[:]), reads=[xnt, ident_f32], writes=[pt])
            p.emit("vector", lambda v, pt=pt, j=j, tt=tt: v.tensor_copy(
                out=stage[j][:, 128 * tt:128 * (tt + 1)], in_=pt[:, 0:128]),
                reads=[pt], writes=[stage[j]])
            if gate is not None:
                p.emit("scalar", lambda s, pt=pt, j=j, i=i: s.copy(
                    out=xnTf[i][:, 128 * j:128 * (j + 1)], in_=pt[:, 0:128]),
                    reads=[pt], writes=[xnTf[i]])
        if gate is not None:
            gps = glog_ps[i]

            def gate_mms(t, i=i, gps=gps):
                for j in range(8):
                    t.matmul(out=gps[:, 0:8], lhsT=xnTf[i][:, 128 * j:128 * (j + 1)],
                             rhs=gw[:, 8 * j:8 * (j + 1)], start=(j == 0), stop=False)
                return t.matmul(out=gps[:, 0:8], lhsT=oh0[:], rhs=gb[:],
                                start=False, stop=True)
            p.emit("tensor", gate_mms, reads=[xnTf[i], gw, gb, oh0], writes=[gps])
            p.emit("scalar", lambda s, i=i, gps=gps: s.copy(
                out=glog_sb[i][:], in_=gps[:, 0:8]), reads=[gps],
                writes=[glog_sb[i]])
            p.dma("gpsimd", glog_hbm[128 * tt:128 * (tt + 1), :], glog_sb[i][:])
    for j in range(8):
        p.dma("gpsimd", xnT_hbm[128 * j:128 * (j + 1), :], stage[j][:])
    if rstd2_hbm is not None:
        p.dma("gpsimd", rstd2_hbm[:], r2row[:])



def load_weight_T(p, w_hbm, rows, cols, ident_bf, tag, raw=None, wT=None):
    nc = p.nc
    rch = rows // 128
    cch = cols // 128
    if raw is None:
        raw = p.sb(tag + "_raw", 128, rch * cols, BF16)
    if wT is None:
        wT = p.sb(tag + "_T", 128, cch * rows, BF16)
    ps = [p.pbf(0), p.pbf(1)]
    for i in range(rch):
        p.dma("gpsimd", raw[:, i * cols:(i + 1) * cols],
              w_hbm[i * 128:(i + 1) * 128, :])
    for i in range(rch):
        for j in range(cch):
            pt = ps[(i * cch + j) % 2]
            p.emit("tensor", lambda t, i=i, j=j, pt=pt: t.transpose(
                out=pt[:, 0:128], in_=raw[:, i * cols + j * 128:i * cols + (j + 1) * 128],
                identity=ident_bf[:]), reads=[raw, ident_bf], writes=[pt])
            p.emit("scalar", lambda s, i=i, j=j, pt=pt: s.copy(
                out=wT[:, j * rows + i * 128:j * rows + (i + 1) * 128],
                in_=pt[:, 0:128]), reads=[pt], writes=[wT])
    return wT, raw


def load_weight_direct(p, w_hbm, rows, cols, tag):
    nc = p.nc
    rch = rows // 128
    w = p.sb(tag + "_W", 128, rch * cols, BF16)
    for j in range(rch):
        p.dma("gpsimd", w[:, j * cols:(j + 1) * cols],
              w_hbm[j * 128:(j + 1) * 128, :])
    return w


def load_xnT(p, xnT_hbm, tag):
    xs = [p.sb(tag + f"_x{j}", 128, NTOK, BF16) for j in range(8)]
    for j in range(8):
        p.dma("sync", xs[j][:], xnT_hbm[128 * j:128 * (j + 1), :])
    return xs


def phase_qkv(p, xnT_hbm, wq_hbm, wk_hbm, wv_hbm, qT_hbm, kT_hbm, v_hbm, ident_bf):
    nc = p.nc
    raw = None
    wqT, raw = load_weight_T(p, wq_hbm, D, D, ident_bf, "wq", raw)
    wkT, raw = load_weight_T(p, wk_hbm, D, D, ident_bf, "wk", raw)
    wvT, raw = load_weight_T(p, wv_hbm, D, D, ident_bf, "wv", raw)
    xn = load_xnT(p, xnT_hbm, "p2")
    pps = [p.pf(0), p.pf(1)]
    stg = [p.sb(f"p2stg{i}", 128, NTOK, BF16) for i in range(2)]
    vstg = [p.sb(f"p2vs{i}", 128, D, BF16) for i in range(2)]
    for wi, (wT, oT) in enumerate(((wqT, qT_hbm), (wkT, kT_hbm))):
        for m in range(8):
            st = stg[m % 2]
            for tc in range(8):
                pt = pps[tc % 2]

                def mms(t, wT=wT, m=m, tc=tc, pt=pt):
                    for k in range(8):
                        inst = t.matmul(
                            out=pt[:], lhsT=wT[:, k * D + 128 * m:k * D + 128 * (m + 1)],
                            rhs=xn[k][:, 512 * tc:512 * (tc + 1)],
                            start=(k == 0), stop=(k == 7))
                    return inst
                p.emit("tensor", mms, reads=[wT] + xn, writes=[pt])
                p.emit("vector", lambda v, st=st, tc=tc, pt=pt: v.tensor_copy(
                    out=st[:, 512 * tc:512 * (tc + 1)], in_=pt[:]),
                    reads=[pt], writes=[st])
            p.dma("gpsimd", oT[128 * m:128 * (m + 1), :], st[:])
    for tt in range(NTT):
        vst = vstg[tt % 2]
        for n in range(2):
            pt = pps[n % 2]

            def mms(t, tt=tt, n=n, pt=pt):
                for k in range(8):
                    inst = t.matmul(
                        out=pt[:], lhsT=xn[k][:, 128 * tt:128 * (tt + 1)],
                        rhs=wvT[:, k * D + 512 * n:k * D + 512 * (n + 1)],
                        start=(k == 0), stop=(k == 7))
                return inst
            p.emit("tensor", mms, reads=[wvT] + xn, writes=[pt])
            p.emit("vector", lambda v, vst=vst, n=n, pt=pt: v.tensor_copy(
                out=vst[:, 512 * n:512 * (n + 1)], in_=pt[:]),
                reads=[pt], writes=[vst])
        p.dma("gpsimd", v_hbm[128 * tt:128 * (tt + 1), :], vst[:])


def phase_att(p, qT_hbm, kT_hbm, v_hbm, slopes_hbm, aoT_hbm, ident_bf):
    nc = p.nc
    KW = 384
    rel = p.sb("rel", 128, KW, F32)
    tmpm = p.sb("tmpm", 128, KW, F32)
    maskadd = p.sb("maskadd", 128, KW, F32)
    bias = p.sb("bias", 128, H * KW, F32)
    oh0 = build_onehot0(p, "att_oh0")
    sl8 = p.sb("sl8", 8, 16, F32)
    slbc = p.sb("slbc", 128, 16, F32)
    slps = p.pf(2)
    p.emit("gpsimd", lambda g: g.iota(out=rel[:], pattern=[[-1, KW]], base=128,
                                      channel_multiplier=1,
                                      allow_small_or_imprecise_dtypes=True),
           writes=[rel])
    p.emit("vector", lambda v: v.memset(sl8[:], 0.0), writes=[sl8])
    p.dma("sync", sl8[0:1, :], slopes_hbm[:])
    p.emit("tensor", lambda t: t.matmul(out=slps[:, 0:16], lhsT=oh0[:], rhs=sl8[:],
                                        start=True, stop=True),
           reads=[oh0, sl8], writes=[slps])
    p.emit("scalar", lambda s: s.copy(out=slbc[:], in_=slps[:, 0:16]),
           reads=[slps], writes=[slbc])
    p.emit("vector", lambda v: v.tensor_scalar(out=tmpm[:], in0=rel[:], scalar1=128.0,
                                               scalar2=None, op0=ALU.is_gt),
           reads=[rel], writes=[tmpm])
    p.emit("vector", lambda v: v.tensor_scalar(out=maskadd[:], in0=rel[:],
                                               scalar1=-128.0, scalar2=None,
                                               op0=ALU.is_lt),
           reads=[rel], writes=[maskadd])
    p.emit("vector", lambda v: v.tensor_add(out=maskadd[:], in0=maskadd[:],
                                            in1=tmpm[:]),
           reads=[tmpm, maskadd], writes=[maskadd])
    p.emit("vector", lambda v: v.tensor_scalar(out=maskadd[:], in0=maskadd[:],
                                               scalar1=NEG, scalar2=None,
                                               op0=ALU.mult),
           reads=[maskadd], writes=[maskadd])
    for h in range(H):
        p.emit("scalar", lambda s, h=h: s.mul(out=bias[:, KW * h:KW * (h + 1)],
                                              in_=rel[:], mul=slbc[:, h:h + 1]),
               reads=[rel, slbc], writes=[bias])
        p.emit("vector", lambda v, h=h: v.tensor_add(
            out=bias[:, KW * h:KW * (h + 1)], in0=bias[:, KW * h:KW * (h + 1)],
            in1=maskadd[:]), reads=[maskadd, bias], writes=[bias])
    qb = [p.sb(f"att_q{j}", 128, S, BF16) for j in range(8)]
    kb = [p.sb(f"att_k{j}", 128, S, BF16) for j in range(8)]
    vb = [p.sb(f"att_v{j}", 128, D, BF16) for j in range(16)]
    ao = [p.sb(f"att_o{j}", 128, S, BF16) for j in range(8)]
    ND = 4
    scs = [p.sb(f"att_sc{i}", 128, KW, F32) for i in range(ND)]
    esc = [p.sb(f"att_esc{i}", 128, KW, F32) for i in range(ND)]
    attb = [p.sb(f"att_ab{i}", 128, KW, BF16) for i in range(ND)]
    attT = [p.sb(f"att_aT{i}", 128, KW, BF16) for i in range(ND)]
    nmx = [p.sb(f"att_nm{i}", 128, 1, F32) for i in range(ND)]
    ssm = [p.sb(f"att_ss{i}", 128, 1, F32) for i in range(ND)]
    rec = [p.sb(f"att_rc{i}", 128, 1, F32) for i in range(ND)]
    psc = [p.pf(0), p.pf(1), p.pf(3), p.pf(5)]
    ptr = [p.pbf(0), p.pbf(1)]
    pav = [p.pf(2), p.pf(4)]
    NQT = S // 128
    for b in range(B_PER_CORE):
        c0 = b * S
        for j in range(8):
            p.dma("sync", qb[j][:], qT_hbm[128 * j:128 * (j + 1), c0:c0 + S])
            p.dma("sync", kb[j][:], kT_hbm[128 * j:128 * (j + 1), c0:c0 + S])
        for j in range(16):
            p.dma("sync", vb[j][:], v_hbm[c0 + 128 * j:c0 + 128 * (j + 1), :])
        iters = []
        for h in range(H):
            for qt in range(NQT):
                qs = 128 * qt
                k0 = qs - 128
                cb0 = 0
                if qt == 0:
                    k0, kw, cb0 = 0, 256, 128
                elif qt == NQT - 1:
                    kw = 256
                else:
                    kw = KW
                iters.append((h // 2, 64 * (h % 2), h, qs, k0, kw, cb0))
        NIT = len(iters)
        NST = 9

        def stage(s, it):
            ch, r0, h, qs, k0, kw, cb0 = iters[it]
            i = it % ND
            i2 = it % 2
            nch = kw // 128
            if s == 0:
                p.emit("tensor", lambda t: t.matmul(
                    out=psc[i][:, 0:kw], lhsT=qb[ch][r0:r0 + 64, qs:qs + 128],
                    rhs=kb[ch][r0:r0 + 64, k0:k0 + kw], start=True, stop=True),
                    reads=[qb[ch], kb[ch]], writes=[psc[i]])
            elif s == 1:
                p.emit("vector", lambda v: v.scalar_tensor_tensor(
                    out=scs[i][:, 0:kw], in0=psc[i][:, 0:kw], scalar=0.125,
                    in1=bias[:, KW * h + cb0:KW * h + cb0 + kw],
                    op0=ALU.mult, op1=ALU.add),
                    reads=[psc[i], bias], writes=[scs[i]])
                p.emit("vector", lambda v: v.reduce_max(
                    out=nmx[i][:], in_=scs[i][:, 0:kw], axis=AX.X, negate=True),
                    reads=[scs[i]], writes=[nmx[i]])
            elif s == 2:
                p.emit("scalar", lambda s_: s_.activation(
                    out=esc[i][:, 0:kw], in_=scs[i][:, 0:kw], func=AF.Exp,
                    bias=nmx[i][:, 0:1], accum_out=ssm[i][:, 0:1]),
                    reads=[scs[i], nmx[i]], writes=[esc[i], ssm[i]])
            elif s == 3:
                p.emit("vector", lambda v: v.reciprocal(
                    out=rec[i][:], in_=ssm[i][:]), reads=[ssm[i]], writes=[rec[i]])
            elif s == 4:
                p.emit("scalar", lambda s_: s_.activation(
                    out=attb[i][:, 0:kw], in_=esc[i][:, 0:kw], func=AF.Copy,
                    scale=rec[i][:, 0:1]), reads=[esc[i], rec[i]],
                    writes=[attb[i]])
            elif s == 5:
                for c in range(nch):
                    p.emit("tensor", lambda t, c=c: t.transpose(
                        out=ptr[i2][:, 128 * c:128 * (c + 1)],
                        in_=attb[i][:, 128 * c:128 * (c + 1)],
                        identity=ident_bf[:]),
                        reads=[attb[i], ident_bf], writes=[ptr[i2]])
            elif s == 6:
                p.emit("scalar", lambda s_: s_.copy(
                    out=attT[i][:, 0:kw], in_=ptr[i2][:, 0:kw]),
                    reads=[ptr[i2]], writes=[attT[i]])
            elif s == 7:
                def avmms(t):
                    for c in range(nch):
                        kt = (k0 + 128 * c) // 128
                        inst = t.matmul(
                            out=pav[i2][0:64, 0:128],
                            lhsT=vb[kt][:, 64 * h:64 * (h + 1)],
                            rhs=attT[i][:, 128 * c:128 * (c + 1)],
                            start=(c == 0), stop=(c == nch - 1))
                    return inst
                p.emit("tensor", avmms, reads=[attT[i]] + vb, writes=[pav[i2]])
            elif s == 8:
                p.emit("vector", lambda v: v.tensor_copy(
                    out=ao[ch][r0:r0 + 64, qs:qs + 128],
                    in_=pav[i2][0:64, 0:128]),
                    reads=[pav[i2]], writes=[ao[ch]])

        for slot in range(NIT + NST - 1):
            for s in range(NST - 1, -1, -1):
                it = slot - s
                if 0 <= it < NIT:
                    stage(s, it)
        for j in range(8):
            p.dma("gpsimd", aoT_hbm[128 * j:128 * (j + 1), c0:c0 + S], ao[j][:])

def phase_wo(p, aoT_hbm, wo_hbm, s_hbm, ident_bf, ident_f32):
    nc = p.nc
    woT, _ = load_weight_T(p, wo_hbm, D, D, ident_bf, "wo")
    ao = load_xnT(p, aoT_hbm, "p4")
    pps = [p.pf(0), p.pf(1)]
    ptr = [p.pf(2), p.pf(3)]
    sT = [p.sb(f"p4sT{i}", 128, 512, F32) for i in range(2)]
    stg = [p.sb(f"p4stg{i}", 128, D, F32) for i in range(4)]
    for tc in range(8):
        for m in range(8):
            i = (tc * 8 + m) % 2

            def mms(t, m=m, tc=tc, i=i):
                for k in range(8):
                    inst = t.matmul(
                        out=pps[i][:], lhsT=woT[:, k * D + 128 * m:k * D + 128 * (m + 1)],
                        rhs=ao[k][:, 512 * tc:512 * (tc + 1)],
                        start=(k == 0), stop=(k == 7))
                return inst
            p.emit("tensor", mms, reads=[woT] + ao, writes=[pps[i]])
            p.emit("vector", lambda v, i=i: v.tensor_copy(out=sT[i][:], in_=pps[i][:]),
                   reads=[pps[i]], writes=[sT[i]])
            for u in range(4):
                p.emit("tensor", lambda t, i=i, u=u: t.transpose(
                    out=ptr[i][:, 0:128], in_=sT[i][:, 128 * u:128 * (u + 1)],
                    identity=ident_f32[:]), reads=[sT[i], ident_f32], writes=[ptr[i]])
                p.emit("scalar", lambda s, i=i, u=u, m=m: s.copy(
                    out=stg[u][:, 128 * m:128 * (m + 1)], in_=ptr[i][:, 0:128]),
                    reads=[ptr[i]], writes=[stg[u]])
        for u in range(4):
            tt = tc * 4 + u
            p.dma("gpsimd", s_hbm[128 * tt:128 * (tt + 1), :], stg[u][:])


def _run(build_fn, in_map_fn, out_names, n_cores=N_CORE):
    import contextlib
    es = contextlib.ExitStack()
    nc = bass.Bass("TRN2", target_bir_lowering=False)
    with es:
        p = P(nc, es)
        build_fn(p)
        p.build()
        in_maps = [in_map_fn(c) for c in range(n_cores)]
        res = run_bass_kernel_spmd(nc, in_maps, core_ids=list(range(n_cores)))
    return [[res.results[c][n] for c in range(n_cores)] for n in out_names]


def phase_prep_ffnw(p, wg_hbm, wu_hbm, wd_hbm, wgT_hbm, wuT_hbm, wdT_hbm,
                    ident_bf):
    raw = p.sb("pf_raw", 128, 24 * 1024, BF16)
    wT = p.sb("pf_wT", 128, 8 * HID_I8, BF16)
    for w_hbm, wT_hbm, rows, cols in ((wg_hbm, wgT_hbm, HID_I8, D),
                                      (wu_hbm, wuT_hbm, HID_I8, D),
                                      (wd_hbm, wdT_hbm, D, HID_I8)):
        load_weight_T(p, w_hbm, rows, cols, ident_bf, "pf", raw=raw, wT=wT)
        cch = cols // 128
        for j in range(cch):
            p.dma("gpsimd", wT_hbm[128 * j:128 * (j + 1), :],
                  wT[:, j * rows:(j + 1) * rows])


def phase_moe(p, xnT_hbm, glog_hbm, w1_hbm, b1_hbm, w2_hbm, b2_hbm, gT_hbm,
              ident_f32):
    nc = p.nc
    TCW = 512
    NTC = NTOK // TCW
    gl = [p.sb(f"m_gl{i}", 128, N_EXP, F32) for i in range(2)]
    nmx = [p.sb(f"m_nm{i}", 128, 1, F32) for i in range(2)]
    ssm = [p.sb(f"m_ss{i}", 128, 1, F32) for i in range(2)]
    rec = [p.sb(f"m_rc{i}", 128, 1, F32) for i in range(2)]
    pb = [p.sb(f"m_pb{i}", 128, N_EXP, F32) for i in range(2)]
    mx2 = [p.sb(f"m_mx{i}", 128, 1, F32) for i in range(2)]
    msk = [p.sb(f"m_msk{i}", 128, N_EXP, F32) for i in range(2)]
    wgt = [p.sb(f"m_wgt{i}", 128, N_EXP, F32) for i in range(2)]
    wgtT = p.sb("m_wgtT", 8, NTOK, F32)
    ptr = p.pf(4)
    for tt in range(NTT):
        i = tt % 2
        p.dma("sync", gl[i][:], glog_hbm[128 * tt:128 * (tt + 1), :])
        p.emit("vector", lambda v, i=i: v.reduce_max(
            out=nmx[i][:], in_=gl[i][:], axis=AX.X, negate=True),
            reads=[gl[i]], writes=[nmx[i]])
        p.emit("scalar", lambda s, i=i: s.activation(
            out=pb[i][:], in_=gl[i][:], func=AF.Exp, bias=nmx[i][:, 0:1],
            accum_out=ssm[i][:, 0:1]), reads=[gl[i], nmx[i]],
            writes=[pb[i], ssm[i]])
        p.emit("vector", lambda v, i=i: v.reciprocal(out=rec[i][:], in_=ssm[i][:]),
               reads=[ssm[i]], writes=[rec[i]])
        p.emit("vector", lambda v, i=i: v.tensor_scalar(
            out=pb[i][:], in0=pb[i][:], scalar1=rec[i][:, 0:1], scalar2=None,
            op0=ALU.mult), reads=[pb[i], rec[i]], writes=[pb[i]])
        p.emit("vector", lambda v, i=i: v.reduce_max(
            out=mx2[i][:], in_=pb[i][:], axis=AX.X),
            reads=[pb[i]], writes=[mx2[i]])
        p.emit("vector", lambda v, i=i: v.tensor_scalar(
            out=msk[i][:], in0=pb[i][:], scalar1=mx2[i][:, 0:1], scalar2=None,
            op0=ALU.is_equal), reads=[pb[i], mx2[i]], writes=[msk[i]])
        p.emit("vector", lambda v, i=i: v.tensor_mul(
            out=wgt[i][:], in0=pb[i][:], in1=msk[i][:]),
            reads=[pb[i], msk[i]], writes=[wgt[i]])
        p.emit("tensor", lambda t, i=i: t.transpose(
            out=ptr[0:8, 0:128], in_=wgt[i][:], identity=ident_f32[:]),
            reads=[wgt[i], ident_f32], writes=[ptr])
        p.emit("scalar", lambda s, tt=tt: s.copy(
            out=wgtT[:, 128 * tt:128 * (tt + 1)], in_=ptr[0:8, 0:128]),
            reads=[ptr], writes=[wgtT])
    ohs = p.sb("m_ohs", 8, 8 * 128, F32)
    ohi = p.sb("m_ohi", 8, 8 * 128, F32)
    p.emit("gpsimd", lambda g: g.iota(out=ohi[:], pattern=[[-1, 8], [0, 128]],
                                      base=0, channel_multiplier=1,
                                      allow_small_or_imprecise_dtypes=True),
           writes=[ohi])
    p.emit("vector", lambda v: v.tensor_scalar(out=ohs[:], in0=ohi[:], scalar1=0.0,
                                               scalar2=None, op0=ALU.is_equal),
           reads=[ohi], writes=[ohs])
    NW = 3
    w1c = [p.sb(f"m_w1c{i}", 128, 8 * 128, BF16) for i in range(NW)]
    w1b = [p.sb(f"m_w1b{i}", 128, 8 * 128, BF16) for i in range(NW)]
    w2c = [p.sb(f"m_w2c{i}", 128, 32 * 128, BF16) for i in range(NW)]
    xsl = [p.sb(f"m_xsl{i}", 128, 8 * TCW, BF16) for i in range(2)]
    h_sb = [p.sb(f"m_h{i}", 128, 32 * TCW, BF16) for i in range(2)]
    wbc = [p.sb(f"m_wbc{i}", 128, TCW, F32) for i in range(2)]
    b1r = p.sb("m_b1r", 32, 128, F32)
    b2r = p.sb("m_b2r", 8, 128, F32)
    b1a = p.sb("m_b1a", 128, 8 * 32, F32)
    b2a = p.sb("m_b2a", 128, 8 * 8, F32)
    gacc = p.sb("m_gacc", 128, 8 * TCW, F32)
    t2 = [p.sb(f"m_t2{i}", 128, TCW, F32) for i in range(2)]
    hps = [p.pf(0), p.pf(1)]
    yps = [p.pf(2), p.pf(3)]
    bps = p.pf(4)
    wps = p.pf(5)
    for e in range(N_EXP):
        p.dma("sync", b1r[:], b1_hbm[32 * e:32 * (e + 1), :])
        p.dma("sync", b2r[:], b2_hbm[8 * e:8 * (e + 1), :])
        p.emit("tensor", lambda t: t.transpose(out=bps[0:128, 0:32], in_=b1r[:],
                                               identity=ident_f32[0:32, 0:32]),
               reads=[b1r, ident_f32], writes=[bps])
        p.emit("scalar", lambda s, e=e: s.copy(out=b1a[:, 32 * e:32 * (e + 1)],
                                               in_=bps[0:128, 0:32]),
               reads=[bps], writes=[b1a])
        p.emit("tensor", lambda t: t.transpose(out=bps[0:128, 32:40], in_=b2r[:],
                                               identity=ident_f32[0:8, 0:8]),
               reads=[b2r, ident_f32], writes=[bps])
        p.emit("scalar", lambda s, e=e: s.copy(out=b2a[:, 8 * e:8 * (e + 1)],
                                               in_=bps[0:128, 32:40]),
               reads=[bps], writes=[b2a])
    for tc in range(NTC):
        it = tc % 2
        for k in range(8):
            p.dma("sync", xsl[it][:, TCW * k:TCW * (k + 1)],
                  xnT_hbm[128 * k:128 * (k + 1), TCW * tc:TCW * (tc + 1)])
        p.emit("vector", lambda v: v.memset(gacc[:], 0.0),
               reads=[gacc], writes=[gacc])
        for e in range(N_EXP):
            ie = e % 2
            p.emit("tensor", lambda t, e=e, tc=tc: t.matmul(
                out=wps[:], lhsT=ohs[:, 128 * e:128 * (e + 1)],
                rhs=wgtT[:, TCW * tc:TCW * (tc + 1)], start=True, stop=True),
                reads=[ohs, wgtT], writes=[wps])
            p.emit("scalar", lambda s, ie=ie: s.copy(out=wbc[ie][:], in_=wps[:]),
                   reads=[wps], writes=[wbc[ie]])
            for hc2 in range(16):
                iw = hc2 % NW
                p.dma("gpsimd",
                      w1c[iw][:].rearrange("p (k m) -> p k m", k=8),
                      w1_hbm[D * e:D * (e + 1),
                             256 * hc2:256 * hc2 + 128].rearrange(
                                 "(k p) m -> p k m", p=128))
                p.dma("gpsimd",
                      w1b[iw][:].rearrange("p (k m) -> p k m", k=8),
                      w1_hbm[D * e:D * (e + 1),
                             256 * hc2 + 128:256 * (hc2 + 1)].rearrange(
                                 "(k p) m -> p k m", p=128))

                def mm_h(t, it=it, iw=iw):
                    for k in range(8):
                        t.matmul(
                            out=hps[0][:], lhsT=w1c[iw][:, 128 * k:128 * (k + 1)],
                            rhs=xsl[it][:, TCW * k:TCW * (k + 1)],
                            start=(k == 0), stop=(k == 7))
                    for k in range(8):
                        inst = t.matmul(
                            out=hps[1][:], lhsT=w1b[iw][:, 128 * k:128 * (k + 1)],
                            rhs=xsl[it][:, TCW * k:TCW * (k + 1)],
                            start=(k == 0), stop=(k == 7))
                    return inst
                p.emit("tensor", mm_h, reads=[w1c[iw], w1b[iw], xsl[it]],
                       writes=[hps[0], hps[1]])
                p.emit("scalar", lambda s, hc2=hc2, ie=ie, e=e: s.activation(
                    out=h_sb[ie][:, TCW * 2 * hc2:TCW * (2 * hc2 + 1)],
                    in_=hps[0][:], func=AF.Gelu,
                    bias=b1a[:, 32 * e + 2 * hc2:32 * e + 2 * hc2 + 1]),
                    reads=[hps[0], b1a], writes=[h_sb[ie]])
                p.emit("scalar", lambda s, hc2=hc2, ie=ie, e=e: s.activation(
                    out=h_sb[ie][:, TCW * (2 * hc2 + 1):TCW * (2 * hc2 + 2)],
                    in_=hps[1][:], func=AF.Gelu,
                    bias=b1a[:, 32 * e + 2 * hc2 + 1:32 * e + 2 * hc2 + 2]),
                    reads=[hps[1], b1a], writes=[h_sb[ie]])
            for dc in range(8):
                iw = dc % NW
                i2 = dc % 2
                yp = yps[i2]
                p.dma("gpsimd",
                      w2c[iw][:].rearrange("p (k m) -> p k m", k=32),
                      w2_hbm[D_FF * e:D_FF * (e + 1),
                             128 * dc:128 * (dc + 1)].rearrange(
                                 "(k p) m -> p k m", p=128))

                def mm_y(t, ie=ie, yp=yp, iw=iw):
                    for k in range(32):
                        inst = t.matmul(
                            out=yp[:], lhsT=w2c[iw][:, 128 * k:128 * (k + 1)],
                            rhs=h_sb[ie][:, TCW * k:TCW * (k + 1)],
                            start=(k == 0), stop=(k == 31))
                    return inst
                p.emit("tensor", mm_y, reads=[w2c[iw], h_sb[ie]], writes=[yp])
                p.emit("vector", lambda v, i2=i2, ie=ie, e=e, dc=dc, yp=yp:
                       v.scalar_tensor_tensor(
                           out=t2[i2][:], in0=yp[:],
                           scalar=b2a[:, 8 * e + dc:8 * e + dc + 1],
                           in1=wbc[ie][:], op0=ALU.add, op1=ALU.mult),
                       reads=[yp, b2a, wbc[ie]], writes=[t2[i2]])
                p.emit("vector", lambda v, i2=i2, dc=dc: v.tensor_add(
                    out=gacc[:, TCW * dc:TCW * (dc + 1)],
                    in0=gacc[:, TCW * dc:TCW * (dc + 1)], in1=t2[i2][:]),
                    reads=[t2[i2], gacc], writes=[gacc])
        for dc in range(8):
            p.dma("gpsimd",
                  gT_hbm[128 * dc:128 * (dc + 1), TCW * tc:TCW * (tc + 1)],
                  gacc[:, TCW * dc:TCW * (dc + 1)])


NSLOT = 15
NPAD = NSLOT * 512


def phase_moe_routed(p, xn_tm_hbm, glog_hbm, w1_hbm, b1_hbm, w2_hbm, b2r8_hbm,
                     g_hbm, gpos_hbm, spos_hbm, wgtg_hbm, xpk_hbm, ident_f32,
                     ident_bf):
    nc = p.nc
    from concourse.bass_core_rust import IndirectOffsetOnAxis as IOA
    gl = [p.sb(f"r_gl{i}", 128, N_EXP, F32) for i in range(2)]
    nmx = [p.sb(f"r_nm{i}", 128, 1, F32) for i in range(2)]
    ssm = [p.sb(f"r_ss{i}", 128, 1, F32) for i in range(2)]
    rec = [p.sb(f"r_rc{i}", 128, 1, F32) for i in range(2)]
    pb = [p.sb(f"r_pb{i}", 128, N_EXP, F32) for i in range(2)]
    mx2 = [p.sb(f"r_mx{i}", 128, 1, F32) for i in range(2)]
    msk = [p.sb(f"r_msk{i}", 128, N_EXP, F32) for i in range(2)]
    wgt = [p.sb(f"r_wgt{i}", 128, N_EXP, F32) for i in range(2)]
    wcol_all = p.sb("r_wca", 128, NTT, F32)
    msk8 = p.sb("r_msk8", 8, NTOK, F32)
    ptA = p.pf(4)
    for tt in range(NTT):
        i = tt % 2
        p.dma("sync", gl[i][:], glog_hbm[128 * tt:128 * (tt + 1), :])
        p.emit("vector", lambda v, i=i: v.reduce_max(
            out=nmx[i][:], in_=gl[i][:], axis=AX.X, negate=True),
            reads=[gl[i]], writes=[nmx[i]])
        p.emit("scalar", lambda s, i=i: s.activation(
            out=pb[i][:], in_=gl[i][:], func=AF.Exp, bias=nmx[i][:, 0:1],
            accum_out=ssm[i][:, 0:1]), reads=[gl[i], nmx[i]],
            writes=[pb[i], ssm[i]])
        p.emit("vector", lambda v, i=i: v.reciprocal(out=rec[i][:], in_=ssm[i][:]),
               reads=[ssm[i]], writes=[rec[i]])
        p.emit("vector", lambda v, i=i: v.tensor_scalar(
            out=pb[i][:], in0=pb[i][:], scalar1=rec[i][:, 0:1], scalar2=None,
            op0=ALU.mult), reads=[pb[i], rec[i]], writes=[pb[i]])
        p.emit("vector", lambda v, i=i: v.reduce_max(
            out=mx2[i][:], in_=pb[i][:], axis=AX.X),
            reads=[pb[i]], writes=[mx2[i]])
        p.emit("vector", lambda v, i=i: v.tensor_scalar(
            out=msk[i][:], in0=pb[i][:], scalar1=mx2[i][:, 0:1], scalar2=None,
            op0=ALU.is_equal), reads=[pb[i], mx2[i]], writes=[msk[i]])
        p.emit("vector", lambda v, i=i: v.tensor_mul(
            out=wgt[i][:], in0=pb[i][:], in1=msk[i][:]),
            reads=[pb[i], msk[i]], writes=[wgt[i]])
        p.emit("vector", lambda v, i=i, tt=tt: v.reduce_sum(
            out=wcol_all[:, tt:tt + 1], in_=wgt[i][:], axis=AX.X),
            reads=[wgt[i]], writes=[wcol_all])
        p.emit("tensor", lambda t, i=i: t.transpose(
            out=ptA[0:8, 0:128], in_=msk[i][:], identity=ident_f32[:]),
            reads=[msk[i], ident_f32], writes=[ptA])
        p.emit("scalar", lambda s, tt=tt: s.copy(
            out=msk8[:, 128 * tt:128 * (tt + 1)], in_=ptA[0:8, 0:128]),
            reads=[ptA], writes=[msk8])
    R = p.sb("r_R", 8, NTOK, F32)
    counts = p.sb("r_cnt", 8, 1, F32)
    t1c = p.sb("r_t1", 8, 1, F32)
    frc = p.sb("r_fr", 8, 1, F32)
    ceilc = p.sb("r_ceil", 8, 1, F32)
    excl = p.sb("r_excl", 8, 1, F32)
    rowst = p.sb("r_rwst", 8, 1, F32)
    Lstr = p.sb("r_Lstr", 8, 8, F32)
    Lio = p.sb("r_Lio", 8, 8, F32)
    ones8 = p.sb("r_on8", 8, 1, F32)
    iotaS = p.sb("r_ioS", 8, NSLOT, F32)
    cmp = p.sb("r_cmp", 8, NSLOT, F32)
    T2E = p.sb("r_T2E", NSLOT, 1, F32)
    tpad = p.sb("r_tpad", 8, NSLOT, F32)
    T2Ebc = p.sb("r_T2Eb", 128, NSLOT, F32)
    e1024 = p.sb("r_e1k", 128, NSLOT, F32)
    e4096 = p.sb("r_e4k", 128, NSLOT, F32)
    e32 = p.sb("r_e32", 128, NSLOT, F32)
    oh0 = build_onehot0(p, "r_oh0")
    oh0b = p.sb("r_oh0b", 8, 128, BF16)
    p.emit("vector", lambda v: v.tensor_copy(out=oh0b[:], in_=oh0[:]),
           reads=[oh0], writes=[oh0b])
    p.emit("vector", lambda v: v.tensor_tensor_scan(
        out=R[:], data0=msk8[:], data1=msk8[:], initial=0.0, op0=ALU.add,
        op1=ALU.bypass), reads=[msk8], writes=[R])
    p.emit("vector", lambda v: v.reduce_sum(out=counts[:], in_=msk8[:], axis=AX.X),
           reads=[msk8], writes=[counts])
    thr = p.sb("r_thr", 8, 8, F32)
    p.emit("gpsimd", lambda g: g.iota(out=thr[:], pattern=[[512, 8]], base=0,
                                      channel_multiplier=0,
                                      allow_small_or_imprecise_dtypes=True),
           writes=[thr])
    p.emit("vector", lambda v: v.tensor_scalar(
        out=t1c[:, 0:8] if False else thr[:], in0=thr[:],
        scalar1=counts[:, 0:1], scalar2=None, op0=ALU.is_lt),
        reads=[thr, counts], writes=[thr])
    p.emit("vector", lambda v: v.reduce_sum(out=ceilc[:], in_=thr[:], axis=AX.X),
           reads=[thr], writes=[ceilc])
    p.emit("gpsimd", lambda g: g.iota(out=Lio[:], pattern=[[1, 8]], base=-1,
                                      channel_multiplier=-1,
                                      allow_small_or_imprecise_dtypes=True),
           writes=[Lio])
    p.emit("vector", lambda v: v.tensor_scalar(out=Lstr[:], in0=Lio[:],
                                               scalar1=0.0, scalar2=None,
                                               op0=ALU.is_ge),
           reads=[Lio], writes=[Lstr])
    p.emit("vector", lambda v: v.memset(ones8[:], 1.0), writes=[ones8])
    p.emit("tensor", lambda t: t.matmul(out=ptA[0:8, 0:1], lhsT=Lstr[:],
                                        rhs=ceilc[:], start=True, stop=True),
           reads=[Lstr, ceilc], writes=[ptA])
    p.emit("scalar", lambda s: s.copy(out=excl[:], in_=ptA[0:8, 0:1]),
           reads=[ptA], writes=[excl])
    p.emit("vector", lambda v: v.tensor_scalar(
        out=rowst[:], in0=excl[:], scalar1=512.0, scalar2=None, op0=ALU.mult),
        reads=[excl], writes=[rowst])
    p.emit("gpsimd", lambda g: g.iota(out=iotaS[:], pattern=[[1, NSLOT]], base=0,
                                      channel_multiplier=0,
                                      allow_small_or_imprecise_dtypes=True),
           writes=[iotaS])
    p.emit("vector", lambda v: v.tensor_scalar(
        out=cmp[:], in0=iotaS[:], scalar1=excl[:, 0:1], scalar2=None,
        op0=ALU.is_ge), reads=[iotaS, excl], writes=[cmp])
    p.emit("tensor", lambda t: t.matmul(out=ptA[0:1, 0:NSLOT], lhsT=ones8[:],
                                        rhs=cmp[:], start=True, stop=True),
           reads=[cmp, ones8], writes=[ptA])
    p.emit("vector", lambda v: v.memset(tpad[:], 0.0), writes=[tpad])
    p.emit("scalar", lambda s: s.copy(out=tpad[0:1, :], in_=ptA[0:1, 0:NSLOT]),
           reads=[ptA], writes=[tpad])
    p.emit("tensor", lambda t: t.matmul(out=ptA[0:128, 0:NSLOT], lhsT=oh0[:],
                                        rhs=tpad[:], start=True, stop=True),
           reads=[oh0, tpad], writes=[ptA])
    p.emit("scalar", lambda s: s.copy(out=T2Ebc[:], in_=ptA[0:128, 0:NSLOT]),
           reads=[ptA], writes=[T2Ebc])
    p.emit("vector", lambda v: v.tensor_scalar(
        out=T2Ebc[:], in0=T2Ebc[:], scalar1=1.0, scalar2=None,
        op0=ALU.subtract), reads=[T2Ebc], writes=[T2Ebc])
    for dst, f in ((e1024, 1024.0), (e4096, 4096.0), (e32, 128.0)):
        p.emit("vector", lambda v, dst=dst, f=f: v.tensor_scalar(
            out=dst[:], in0=T2Ebc[:], scalar1=f, scalar2=None, op0=ALU.mult),
            reads=[T2Ebc], writes=[dst])
    p.emit("vector", lambda v: v.tensor_scalar(
        out=R[:], in0=R[:], scalar1=rowst[:, 0:1], scalar2=1.0, op0=ALU.add,
        op1=ALU.subtract), reads=[R, rowst], writes=[R])
    p.emit("vector", lambda v: v.tensor_mul(out=R[:], in0=R[:], in1=msk8[:]),
           reads=[msk8, R], writes=[R])
    zrow = p.sb("r_zrow", 128, 2 * NPAD // 128, I32)
    p.emit("vector", lambda v: v.memset(zrow[:], 0), writes=[zrow])
    p.dma("gpsimd", gpos_hbm.rearrange("(p a) c -> p (a c)", p=128), zrow[:])
    p.emit("gpsimd", lambda g: g.iota(out=zrow[:], pattern=[[0, 2 * NPAD // 128]],
                                      base=NTOK, channel_multiplier=1),
           reads=[zrow], writes=[zrow])
    p.dma("gpsimd", spos_hbm.rearrange("(p a) c -> p (a c)", p=128), zrow[:])
    zf = p.sb("r_zf", 128, 2 * NPAD // 128, F32)
    p.emit("vector", lambda v: v.memset(zf[:], 0.0), writes=[zf])
    p.dma("gpsimd", wgtg_hbm.rearrange("(p a) c -> p (a c)", p=128), zf[:])
    Rt = [p.sb(f"r_Rt{i}", 128, 8, F32) for i in range(2)]
    destf = [p.sb(f"r_df{i}", 128, 1, F32) for i in range(2)]
    desti = [p.sb(f"r_di{i}", 128, 1, I32) for i in range(2)]
    iotok = [p.sb(f"r_it{i}", 128, 2, I32) for i in range(2)]
    wcol2 = [p.sb(f"r_w2c{i}", 128, 2, F32) for i in range(2)]
    for tt in range(NTT):
        i = tt % 2
        p.emit("tensor", lambda t, tt=tt: t.transpose(
            out=ptA[0:128, 0:8], in_=R[:, 128 * tt:128 * (tt + 1)],
            identity=ident_f32[0:8, 0:8]), reads=[R, ident_f32], writes=[ptA])
        p.emit("scalar", lambda s, i=i: s.copy(out=Rt[i][:], in_=ptA[0:128, 0:8]),
               reads=[ptA], writes=[Rt[i]])
        p.emit("vector", lambda v, i=i: v.reduce_sum(
            out=destf[i][:], in_=Rt[i][:], axis=AX.X),
            reads=[Rt[i]], writes=[destf[i]])
        p.emit("vector", lambda v, i=i: v.tensor_copy(
            out=desti[i][:], in_=destf[i][:]), reads=[destf[i]], writes=[desti[i]])
        p.emit("gpsimd", lambda g, i=i, tt=tt: g.iota(
            out=iotok[i][:], pattern=[[0, 2]], base=128 * tt,
            channel_multiplier=1), writes=[iotok[i]])
        p.emit("vector", lambda v, i=i, tt=tt: v.tensor_copy(
            out=wcol2[i][:, 0:1], in_=wcol_all[:, tt:tt + 1]),
            reads=[wcol_all], writes=[wcol2[i]])
        p.emit("vector", lambda v, i=i, tt=tt: v.tensor_copy(
            out=wcol2[i][:, 1:2], in_=wcol_all[:, tt:tt + 1]),
            reads=[wcol_all], writes=[wcol2[i]])
        p.idma("scat", lambda g, i=i: g.indirect_dma_start(
            out=gpos_hbm, out_offset=IOA(desti[i][:], 0), in_=iotok[i][:],
            in_offset=None), reads=[desti[i], iotok[i]],
            writes=[gpos_hbm.tensor])
        p.idma("scat", lambda g, i=i: g.indirect_dma_start(
            out=spos_hbm, out_offset=IOA(desti[i][:], 0), in_=iotok[i][:],
            in_offset=None), reads=[desti[i], iotok[i]],
            writes=[spos_hbm.tensor])
        p.idma("scat", lambda g, i=i: g.indirect_dma_start(
            out=wgtg_hbm, out_offset=IOA(desti[i][:], 0),
            in_=wcol2[i][:], in_offset=None),
            reads=[desti[i], wcol2[i]],
            writes=[wgtg_hbm.tensor])
    io8 = p.sb("r_io8", 128, 8, F32)
    io32 = p.sb("r_io32", 128, 32, F32)
    io1 = p.sb("r_io1", 128, 1, F32)
    p.emit("gpsimd", lambda g: g.iota(out=io8[:], pattern=[[128, 8]], base=0,
                                      channel_multiplier=1,
                                      allow_small_or_imprecise_dtypes=True),
           writes=[io8])
    p.emit("gpsimd", lambda g: g.iota(out=io32[:], pattern=[[128, 32]], base=0,
                                      channel_multiplier=1,
                                      allow_small_or_imprecise_dtypes=True),
           writes=[io32])
    p.emit("gpsimd", lambda g: g.iota(out=io1[:], pattern=[[1, 1]], base=0,
                                      channel_multiplier=1,
                                      allow_small_or_imprecise_dtypes=True),
           writes=[io1])
    xpk_stage = [p.sb(f"r_xps{i}", 128, D, BF16) for i in range(2)]
    wif = [p.sb(f"r_wif{i}", 128, 42, F32) for i in range(2)]
    wii = [p.sb(f"r_wii{i}", 128, 42, I32) for i in range(2)]
    soidx = [p.sb(f"r_so{u}", 128, 1, I32) for u in range(8)]
    wcu = [p.sb(f"r_wc{u}", 128, 1, F32) for u in range(8)]
    xgT = [p.sb(f"r_xgT{i}", 128, 8 * 512, BF16) for i in range(2)]
    w1g = [p.sb(f"r_w1g{j}", 128, D_FF, BF16) for j in range(8)]
    w1cp = None
    NW2 = 24
    w2g = [p.sb(f"r_w2g{i}", 128, 512, BF16) for i in range(NW2)]

    b1c = [p.sb(f"r_b1c{i}", 128, 32, F32) for i in range(2)]
    b2g8 = [p.sb(f"r_b2g8{i}", 8, D, BF16) for i in range(2)]
    h_sb = p.sb("r_h", 128, 32 * 512, BF16)
    ytm = [p.sb(f"r_ytm{i}", 128, 512, F32) for i in range(4)]
    hps = [p.pf(0), p.pf(1)]
    yps = [p.pf(2), p.pf(3), p.pf(4), p.pf(5)]
    ptb = [p.pbf(0), p.pbf(1)]
    p.emit("vector", lambda v: v.memset(b2g8[0][:], 0.0), writes=[b2g8[0]])
    p.emit("vector", lambda v: v.memset(b2g8[1][:], 0.0), writes=[b2g8[1]])
    pk_idx = [p.sb(f"r_pki{i}", 128, 1, I32) for i in range(2)]
    for c in range(NSLOT * 4):
        ic = c % 2
        p.dma("sync", pk_idx[ic][:], gpos_hbm[128 * c:128 * (c + 1), 0:1],
              allow_slow=True)
        p.idma("xg", lambda g, ic=ic: g.indirect_dma_start(
            out=xpk_stage[ic][:], out_offset=None, in_=xn_tm_hbm,
            in_offset=IOA(pk_idx[ic][:], 0)),
            reads=[pk_idx[ic], xn_tm_hbm.tensor], writes=[xpk_stage[ic]])
        p.dma("gpsimd", xpk_hbm[128 * c:128 * (c + 1), :], xpk_stage[ic][:])
    import os as _os
    _nslot = int(_os.environ.get("K_SLOTS", str(NSLOT)))
    _part = int(_os.environ.get("K_PART", "5"))
    def prefetch(s):
        sl = s % 2
        wv, wi = wif[sl], wii[sl]
        p.emit("vector", lambda v, s=s, wv=wv: v.tensor_scalar(
            out=wv[:, 0:8], in0=io8[:], scalar1=e1024[:, s:s + 1],
            scalar2=None, op0=ALU.add), reads=[io8, e1024], writes=[wv])
        p.emit("vector", lambda v, s=s, wv=wv: v.tensor_scalar(
            out=wv[:, 8:40], in0=io32[:], scalar1=e4096[:, s:s + 1],
            scalar2=None, op0=ALU.add), reads=[io32, e4096], writes=[wv])
        p.emit("vector", lambda v, s=s, wv=wv: v.tensor_scalar(
            out=wv[:, 40:41], in0=io1[:], scalar1=e32[:, s:s + 1],
            scalar2=None, op0=ALU.add), reads=[io1, e32], writes=[wv])
        p.emit("vector", lambda v, s=s, wv=wv: v.tensor_copy(
            out=wv[:, 41:42], in_=T2Ebc[:, s:s + 1]), reads=[T2Ebc],
            writes=[wv])
        p.emit("vector", lambda v, wv=wv, wi=wi: v.tensor_copy(
            out=wi[:], in_=wv[:]), reads=[wv], writes=[wi])
        for u in range(4):
            r0 = 512 * s + 128 * u
            p.dma("sync", soidx[4 * sl + u][:], spos_hbm[r0:r0 + 128, 0:1],
                  allow_slow=True)
            p.dma("sync", wcu[4 * sl + u][:], wgtg_hbm[r0:r0 + 128, 0:1],
                  allow_slow=True)
        for j in range(8):
            p.dma("sync", xgT[sl][:, 512 * j:512 * (j + 1)],
                  xpk_hbm[512 * s:512 * (s + 1), 128 * j:128 * (j + 1)],
                  transpose=True)
        if _os.environ.get("K_W1PLAIN", "0") == "1":
            for j in range(8):
                p.dma("gpsimd", w1g[j][:], w1_hbm[128 * j:128 * (j + 1), :])
        else:
            for j in range(8):
                p.idma("w1", lambda g, j=j, wi=wi: g.indirect_dma_start(
                    out=w1g[j][:], out_offset=None, in_=w1_hbm,
                    in_offset=IOA(wi[:, j:j + 1], 0)),
                    reads=[wi, w1_hbm.tensor], writes=[w1g[j]])
        p.idma("b1", lambda g, wi=wi, sl=sl: g.indirect_dma_start(
            out=b1c[sl][:], out_offset=None, in_=b1_hbm,
            in_offset=IOA(wi[:, 40:41], 0)),
            reads=[wi, b1_hbm.tensor], writes=[b1c[sl]])
        p.idma("b2", lambda g, wi=wi, sl=sl: g.indirect_dma_start(
            out=b2g8[sl][:], out_offset=None, in_=b2r8_hbm,
            in_offset=IOA(wi[0:8, 41:42], 0)),
            reads=[wi, b2r8_hbm.tensor], writes=[b2g8[sl]])

    if _part >= 2:
        prefetch(0)
    for s in range(_nslot):
        sl = s % 2
        wi = wii[sl]
        if _part < 3:
            continue
        w1u = w1g
        _nhc = int(_os.environ.get("K_HC", "16"))
        for hc2 in range(_nhc):

            def mm_h(t, hc2=hc2, sl=sl):
                _rhs = lambda k: xgT[sl][:, 512 * k:512 * (k + 1)]
                for k in range(8):
                    t.matmul(out=hps[0][:],
                             lhsT=w1u[k][:, 256 * hc2:256 * hc2 + 128],
                             rhs=_rhs(k),
                             start=(k == 0), stop=(k == 7))
                for k in range(8):
                    inst = t.matmul(out=hps[1][:],
                                    lhsT=w1u[k][:, 256 * hc2 + 128:256 * (hc2 + 1)],
                                    rhs=_rhs(k),
                                    start=(k == 0), stop=(k == 7))
                return inst
            p.emit("tensor", mm_h, reads=list(w1u) + [xgT[sl]],
                   writes=[hps[0], hps[1]])
            p.emit("scalar", lambda s_, hc2=hc2, sl=sl: s_.activation(
                out=h_sb[:, 512 * 2 * hc2:512 * (2 * hc2 + 1)], in_=hps[0][:],
                func=AF.Gelu, bias=b1c[sl][:, 2 * hc2:2 * hc2 + 1]),
                reads=[hps[0], b1c[sl]], writes=[h_sb])
            p.emit("scalar", lambda s_, hc2=hc2, sl=sl: s_.activation(
                out=h_sb[:, 512 * (2 * hc2 + 1):512 * (2 * hc2 + 2)],
                in_=hps[1][:], func=AF.Gelu,
                bias=b1c[sl][:, 2 * hc2 + 1:2 * hc2 + 2]),
                reads=[hps[1], b1c[sl]], writes=[h_sb])
        if _part < 4:
            continue

        def y_mms(dh):
            for k in range(32):
                i2 = (32 * dh + k) % NW2
                p.idma("w2", lambda g, k=k, dh=dh, i2=i2, wi=wi:
                       g.indirect_dma_start(
                           out=w2g[i2][:], out_offset=None, in_=w2_hbm,
                           in_offset=IOA(wi[:, 8 + k:9 + k], 0),
                           element_offset=512 * dh),
                       reads=[wi, w2_hbm.tensor], writes=[w2g[i2]])

                def mm_y(t, k=k, i2=i2):
                    for u in range(4):
                        inst = t.matmul(
                            out=yps[u][:], lhsT=h_sb[:, 512 * k + 128 * u:
                                                     512 * k + 128 * (u + 1)],
                            rhs=w2g[i2][:], start=(k == 0), stop=False,
                            skip_group_check=True)
                    return inst
                p.emit("tensor", mm_y, reads=[h_sb, w2g[i2]],
                       writes=[yps[0], yps[1], yps[2], yps[3]])

        def y_close(dh):
            def mm_b2(t, dh=dh, sl=sl):
                for u in range(4):
                    inst = t.matmul(out=yps[u][:], lhsT=oh0b[:],
                                    rhs=b2g8[sl][:, 512 * dh:512 * (dh + 1)],
                                    start=False, stop=True,
                                    skip_group_check=True)
                return inst
            p.emit("tensor", mm_b2, reads=[oh0b, b2g8[sl]],
                   writes=[yps[0], yps[1], yps[2], yps[3]])
            if _part < 5:
                return
            for u in range(4):
                p.emit("vector", lambda v, u=u, sl=sl: v.tensor_scalar(
                    out=ytm[u][:], in0=yps[u][:],
                    scalar1=wcu[4 * sl + u][:, 0:1],
                    scalar2=None, op0=ALU.mult),
                    reads=[yps[u], wcu[4 * sl + u]], writes=[ytm[u]])

        def y_scat(dh):
            if _part < 5:
                return
            for u in range(4):
                p.idma("yscat", lambda g, u=u, dh=dh, sl=sl:
                       g.indirect_dma_start(
                           out=g_hbm, out_offset=IOA(soidx[4 * sl + u][:], 0),
                           in_=ytm[u][:], in_offset=None,
                           element_offset=512 * dh),
                       reads=[soidx[4 * sl + u], ytm[u]],
                       writes=[g_hbm.tensor])

        y_mms(0)
        if s + 1 < _nslot:
            prefetch(s + 1)
        y_close(0)
        y_mms(1)
        y_scat(0)
        y_close(1)
        y_scat(1)

def phase_combine_ln(p, x_hbm, s_hbm, gT_hbm, x1_hbm, xnT_hbm, rstd2_hbm,
                     ident_f32, g_token_major=False):
    nc = p.nc
    tag = "c6"
    xb = [p.sb(tag + f"x{i}", 128, D, F32) for i in range(2)]
    sbuf_s = [p.sb(tag + f"s{i}", 128, D, F32) for i in range(2)]
    gtile = [p.sb(tag + f"g{i}", 128, D, F32) for i in range(2)]
    gblk = [p.sb(tag + f"gb{i}", 128, 128, F32) for i in range(2)]
    sg = [p.sb(tag + f"sg{i}", 128, D, F32) for i in range(2)]
    x1 = [p.sb(tag + f"x1{i}", 128, D, F32) for i in range(2)]
    xn = [p.sb(tag + f"xn{i}", 128, D, F32) for i in range(2)]
    sq = [p.sb(tag + f"sq{i}", 128, D, F32) for i in range(2)]
    ssum = [p.sb(tag + f"ssum{i}", 128, 1, F32) for i in range(2)]
    sqs = [p.sb(tag + f"sqs{i}", 128, 1, F32) for i in range(2)]
    mean = [p.sb(tag + f"mean{i}", 128, 1, F32) for i in range(2)]
    sqd = [p.sb(tag + f"sqd{i}", 128, 1, F32) for i in range(2)]
    nvar = [p.sb(tag + f"nvar{i}", 128, 1, F32) for i in range(2)]
    sd = [p.sb(tag + f"sd{i}", 128, 1, F32) for i in range(2)]
    rstd = [p.sb(tag + f"rstd{i}", 128, 1, F32) for i in range(2)]
    sq2s = [p.sb(tag + f"sq2s{i}", 128, 1, F32) for i in range(2)]
    sd2 = [p.sb(tag + f"sd2{i}", 128, 1, F32) for i in range(2)]
    rstd2 = [p.sb(tag + f"rstd2{i}", 128, 1, F32) for i in range(2)]
    r2row = p.sb(tag + "r2row", 1, NTOK, F32)
    stage = [p.sb(tag + f"stg{j}", 128, NTOK, BF16) for j in range(8)]
    pstr = [p.pf(0), p.pf(1)]
    ptr2 = [p.pf(2), p.pf(3)]
    r2ps = p.pf(4)
    epscol = p.ccol(EPS)
    for tt in range(NTT):
        i = tt % 2
        xt, st, gt, x1t, xnt = xb[i], sbuf_s[i], gtile[i], x1[i], xn[i]
        p.dma("sync", xt[:], x_hbm[128 * tt:128 * (tt + 1), :])
        p.dma("sync", st[:], s_hbm[128 * tt:128 * (tt + 1), :])
        if g_token_major:
            p.dma("sync", gt[:], gT_hbm[128 * tt:128 * (tt + 1), :])
        else:
            for j in range(8):
                gb = gblk[j % 2]
                p.dma("sync", gb[:], gT_hbm[128 * j:128 * (j + 1),
                                            128 * tt:128 * (tt + 1)])
                pt = ptr2[j % 2]
                p.emit("tensor", lambda t, gb=gb, pt=pt: t.transpose(
                    out=pt[:, 0:128], in_=gb[:], identity=ident_f32[:]),
                    reads=[gb, ident_f32], writes=[pt])
                p.emit("scalar", lambda s, pt=pt, j=j, gt=gt: s.copy(
                    out=gt[:, 128 * j:128 * (j + 1)], in_=pt[:, 0:128]),
                    reads=[pt], writes=[gt])
        p.emit("scalar", lambda s, gt=gt, i=i: s.activation(
            out=sg[i][:], in_=gt[:], func=AF.Silu), reads=[gt], writes=[sg[i]])
        p.emit("vector", lambda v, st=st, i=i: v.tensor_mul(
            out=sg[i][:], in0=sg[i][:], in1=st[:]),
            reads=[st, sg[i]], writes=[sg[i]])
        p.emit("vector", lambda v, xt=xt, i=i, x1t=x1t: v.tensor_add(
            out=x1t[:], in0=xt[:], in1=sg[i][:]),
            reads=[xt, sg[i]], writes=[x1t])
        p.dma("gpsimd", x1_hbm[128 * tt:128 * (tt + 1), :], x1t[:])
        p.emit("vector", lambda v, x1t=x1t, i=i: v.reduce_sum(
            out=ssum[i][:], in_=x1t[:], axis=AX.X), reads=[x1t], writes=[ssum[i]])
        p.emit("vector", lambda v, x1t=x1t, i=i: v.scalar_tensor_tensor(
            out=sq[i][:], in0=x1t[:], scalar=1.0, in1=x1t[:], op0=ALU.bypass,
            op1=ALU.mult, accum_out=sqs[i][:, 0:1]), reads=[x1t],
            writes=[sq[i], sqs[i]])
        p.emit("vector", lambda v, i=i: v.tensor_scalar(
            out=mean[i][:], in0=ssum[i][:], scalar1=1.0 / D, scalar2=None,
            op0=ALU.mult), reads=[ssum[i]], writes=[mean[i]])
        p.emit("vector", lambda v, i=i: v.tensor_scalar(
            out=sqd[i][:], in0=sqs[i][:], scalar1=1.0 / D, scalar2=None,
            op0=ALU.mult), reads=[sqs[i]], writes=[sqd[i]])
        p.emit("vector", lambda v, i=i: v.scalar_tensor_tensor(
            out=nvar[i][:], in0=mean[i][:], scalar=mean[i][:, 0:1], in1=sqd[i][:],
            op0=ALU.mult, op1=ALU.subtract), reads=[mean[i], sqd[i]],
            writes=[nvar[i]])
        p.emit("scalar", lambda s, i=i: s.activation(
            out=sd[i][:], in_=nvar[i][:], func=AF.Sqrt, scale=-1.0,
            bias=epscol[:, 0:1]), reads=[nvar[i], epscol], writes=[sd[i]])
        p.emit("vector", lambda v, i=i: v.reciprocal(out=rstd[i][:], in_=sd[i][:]),
               reads=[sd[i]], writes=[rstd[i]])
        p.emit("vector", lambda v, x1t=x1t, xnt=xnt, i=i: v.tensor_scalar(
            out=xnt[:], in0=x1t[:], scalar1=mean[i][:, 0:1],
            scalar2=rstd[i][:, 0:1], op0=ALU.subtract, op1=ALU.mult),
            reads=[x1t, mean[i], rstd[i]], writes=[xnt])
        p.emit("vector", lambda v, xnt=xnt, i=i: v.scalar_tensor_tensor(
            out=sq[i][:], in0=xnt[:], scalar=1.0, in1=xnt[:], op0=ALU.bypass,
            op1=ALU.mult, accum_out=sq2s[i][:, 0:1]), reads=[xnt],
            writes=[sq[i], sq2s[i]])
        p.emit("scalar", lambda s, i=i: s.activation(
            out=sd2[i][:], in_=sq2s[i][:], func=AF.Sqrt, scale=1.0 / D,
            bias=epscol[:, 0:1]), reads=[sq2s[i], epscol], writes=[sd2[i]])
        p.emit("vector", lambda v, i=i: v.reciprocal(
            out=rstd2[i][:], in_=sd2[i][:]), reads=[sd2[i]], writes=[rstd2[i]])
        p.emit("tensor", lambda t, i=i: t.transpose(
            out=r2ps[0:1, 0:128], in_=rstd2[i][:], identity=ident_f32[:]),
            reads=[rstd2[i], ident_f32], writes=[r2ps])
        p.emit("scalar", lambda s, tt=tt: s.copy(
            out=r2row[0:1, 128 * tt:128 * (tt + 1)], in_=r2ps[0:1, 0:128]),
            reads=[r2ps], writes=[r2row])
        for j in range(8):
            pt = pstr[j % 2]
            p.emit("tensor", lambda t, xnt=xnt, pt=pt, j=j: t.transpose(
                out=pt[:, 0:128], in_=xnt[:, 128 * j:128 * (j + 1)],
                identity=ident_f32[:]), reads=[xnt, ident_f32], writes=[pt])
            p.emit("vector", lambda v, pt=pt, j=j, tt=tt: v.tensor_copy(
                out=stage[j][:, 128 * tt:128 * (tt + 1)], in_=pt[:, 0:128]),
                reads=[pt], writes=[stage[j]])
    for j in range(8):
        p.dma("gpsimd", xnT_hbm[128 * j:128 * (j + 1), :], stage[j][:])
    p.dma("gpsimd", rstd2_hbm[:], r2row[:])


def phase_mamba_in(p, xnT_hbm, rstd2_hbm, normw_hbm, win_hbm, convw_hbm,
                   convb_hbm, xinT_hbm, resT_hbm, xsiluT_hbm, ident_f32):
    nc = p.nc
    tag = "c7"
    oh0 = build_onehot0(p, tag + "oh0")
    winm = [p.sb(tag + f"winm{i}", 128, 8 * 128, BF16) for i in range(2)]
    r2pad = p.sb(tag + "r2p", 8, NTOK, F32)
    r2bc = p.sb(tag + "r2bc", 128, NTOK, BF16)
    rps = p.pf(4)
    p.emit("vector", lambda v: v.memset(r2pad[:], 0.0), writes=[r2pad])
    p.dma("sync", r2pad[0:1, :], rstd2_hbm[:])
    for c in range(NTOK // 512):
        p.emit("tensor", lambda t, c=c: t.matmul(
            out=rps[:], lhsT=oh0[:], rhs=r2pad[:, 512 * c:512 * (c + 1)],
            start=True, stop=True), reads=[oh0, r2pad], writes=[rps])
        p.emit("scalar", lambda s, c=c: s.copy(
            out=r2bc[:, 512 * c:512 * (c + 1)], in_=rps[:]),
            reads=[rps], writes=[r2bc])
    nwr = p.sb(tag + "nwr", 8, 128, F32)
    nwc = p.sb(tag + "nwc", 128, 8, F32)
    p.dma("sync", nwr[:], normw_hbm[:])
    p.emit("tensor", lambda t: t.transpose(out=rps[0:128, 0:8], in_=nwr[:],
                                           identity=ident_f32[0:8, 0:8]),
           reads=[nwr, ident_f32], writes=[rps])
    p.emit("scalar", lambda s: s.copy(out=nwc[:], in_=rps[0:128, 0:8]),
           reads=[rps], writes=[nwc])
    hT = p.sb(tag + "hT", 128, 8 * NTOK, BF16)
    xc = [p.sb(tag + f"xc{i}", 128, NTOK, BF16) for i in range(2)]
    t1 = [p.sb(tag + f"t1{i}", 128, NTOK, BF16) for i in range(2)]
    for k in range(8):
        i = k % 2
        p.dma("sync", xc[i][:], xnT_hbm[128 * k:128 * (k + 1), :])
        p.emit("vector", lambda v, i=i, k=k: v.tensor_scalar(
            out=t1[i][:], in0=xc[i][:], scalar1=nwc[:, k:k + 1], scalar2=None,
            op0=ALU.mult), reads=[xc[i], nwc], writes=[t1[i]])
        p.emit("vector", lambda v, i=i, k=k: v.tensor_mul(
            out=hT[:, NTOK * k:NTOK * (k + 1)], in0=t1[i][:], in1=r2bc[:]),
            reads=[t1[i], r2bc], writes=[hT])
    pps = [p.pf(0), p.pf(1)]
    stg = [p.sb(tag + f"stg{i}", 128, NTOK, BF16) for i in range(2)]
    for m in range(32):
        st = stg[m % 2]
        wm = winm[m % 2]
        p.dma("gpsimd", wm[:].rearrange("p (k f) -> p k f", k=8),
              win_hbm[:, 128 * m:128 * (m + 1)].rearrange("(k p) f -> p k f",
                                                          p=128))
        for tc in range(8):
            pt = pps[tc % 2]

            def mms(t, m=m, tc=tc, pt=pt, wm=wm):
                for k in range(8):
                    inst = t.matmul(
                        out=pt[:], lhsT=wm[:, 128 * k:128 * (k + 1)],
                        rhs=hT[:, NTOK * k + 512 * tc:NTOK * k + 512 * (tc + 1)],
                        start=(k == 0), stop=(k == 7))
                return inst
            p.emit("tensor", mms, reads=[wm, hT], writes=[pt])
            p.emit("vector", lambda v, st=st, tc=tc, pt=pt: v.tensor_copy(
                out=st[:, 512 * tc:512 * (tc + 1)], in_=pt[:]),
                reads=[pt], writes=[st])
        if m < 16:
            p.dma("gpsimd", xinT_hbm[128 * m:128 * (m + 1), :], st[:])
        else:
            p.dma("gpsimd", resT_hbm[128 * (m - 16):128 * (m - 15), :], st[:])
    cw = p.sb(tag + "cw", 128, 4 * 16, F32)
    cbr = p.sb(tag + "cbr", 16, 128, F32)
    cbc = p.sb(tag + "cbc", 128, 16, F32)
    for dc in range(16):
        p.dma("sync", cw[:, 4 * dc:4 * (dc + 1)],
              convw_hbm[128 * dc:128 * (dc + 1), :])
    p.dma("sync", cbr[:], convb_hbm[:])
    p.emit("tensor", lambda t: t.transpose(out=rps[0:128, 8:24], in_=cbr[:],
                                           identity=ident_f32[0:16, 0:16]),
           reads=[cbr, ident_f32], writes=[rps])
    p.emit("scalar", lambda s: s.copy(out=cbc[:], in_=rps[0:128, 8:24]),
           reads=[rps], writes=[cbc])
    xpad = [p.sb(tag + f"xp{i}", 128, 3 + S, BF16) for i in range(2)]
    acc = [p.sb(tag + f"ac{i}", 128, S, F32) for i in range(2)]
    xsl = [p.sb(tag + f"xs{i}", 128, S, BF16) for i in range(2)]
    for b in range(B_PER_CORE):
        for dc in range(16):
            i = (b * 16 + dc) % 2
            xp, at, xo = xpad[i], acc[i], xsl[i]
            p.emit("vector", lambda v, xp=xp: v.memset(xp[:, 0:3], 0.0),
                   reads=[xp], writes=[xp])
            p.dma("sync", xp[:, 3:3 + S],
                  xinT_hbm[128 * dc:128 * (dc + 1), S * b:S * (b + 1)])
            p.emit("vector", lambda v, xp=xp, at=at, dc=dc: v.tensor_scalar(
                out=at[:], in0=xp[:, 0:S], scalar1=cw[:, 4 * dc:4 * dc + 1],
                scalar2=None, op0=ALU.mult), reads=[xp, cw], writes=[at])
            for j in range(1, 4):
                p.emit("vector", lambda v, xp=xp, at=at, dc=dc, j=j:
                       v.scalar_tensor_tensor(
                           out=at[:], in0=xp[:, j:j + S],
                           scalar=cw[:, 4 * dc + j:4 * dc + j + 1], in1=at[:],
                           op0=ALU.mult, op1=ALU.add),
                       reads=[xp, cw, at], writes=[at])
            p.emit("scalar", lambda s, at=at, xo=xo, dc=dc: s.activation(
                out=xo[:], in_=at[:], func=AF.Silu, bias=cbc[:, dc:dc + 1]),
                reads=[at, cbc], writes=[xo])
            p.dma("gpsimd", xsiluT_hbm[128 * dc:128 * (dc + 1), S * b:S * (b + 1)],
                  xo[:])


def phase_ssm_proj(p, xsiluT_hbm, wx_hbm, wdt_hbm, bdt_hbm, deltaT_hbm, duT_hbm,
                   Bm_hbm, Cm_hbm, ident_f32):
    nc = p.nc
    tag = "c8"
    wx = p.sb(tag + "wx", 128, 16 * 96, BF16)
    for k in range(16):
        p.dma("gpsimd", wx[:, 96 * k:96 * (k + 1)],
              wx_hbm[128 * k:128 * (k + 1), :])
    xs = [p.sb(tag + f"xs{i}", 128, 512, BF16) for i in range(16)]
    pdl = [p.pf(0), p.pf(1)]
    pB = p.pf(2)
    pC = p.pf(3)
    dltb = p.sb(tag + "dltb", 64, NTOK, BF16)
    Bms = p.sb(tag + "Bms", 16, NTOK, F32)
    Cms = p.sb(tag + "Cms", 16, NTOK, F32)
    xst = [p.sb(tag + f"xst{i}", 128, 512, BF16) for i in range(2)]
    for tc in range(8):
        for k in range(16):
            p.dma("sync", xs[k][:],
                  xsiluT_hbm[128 * k:128 * (k + 1), 512 * tc:512 * (tc + 1)])
        pd = pdl[tc % 2]

        def mm_d(t, tc=tc, pd=pd):
            for k in range(16):
                inst = t.matmul(out=pd[0:64, :], lhsT=wx[:, 96 * k:96 * k + 64],
                                rhs=xs[k][:], start=(k == 0), stop=(k == 15))
            return inst
        p.emit("tensor", mm_d, reads=[wx] + xs, writes=[pd])
        p.emit("vector", lambda v, tc=tc, pd=pd: v.tensor_copy(
            out=dltb[:, 512 * tc:512 * (tc + 1)], in_=pd[0:64, :]),
            reads=[pd], writes=[dltb])

        def mm_B(t, tc=tc):
            for k in range(16):
                inst = t.matmul(out=pB[0:16, :], lhsT=wx[:, 96 * k + 64:96 * k + 80],
                                rhs=xs[k][:], start=(k == 0), stop=(k == 15))
            return inst
        p.emit("tensor", mm_B, reads=[wx] + xs, writes=[pB])
        p.emit("scalar", lambda s, tc=tc: s.copy(
            out=Bms[:, 512 * tc:512 * (tc + 1)], in_=pB[0:16, :]),
            reads=[pB], writes=[Bms])

        def mm_C(t, tc=tc):
            for k in range(16):
                inst = t.matmul(out=pC[0:16, :], lhsT=wx[:, 96 * k + 80:96 * k + 96],
                                rhs=xs[k][:], start=(k == 0), stop=(k == 15))
            return inst
        p.emit("tensor", mm_C, reads=[wx] + xs, writes=[pC])
        p.emit("scalar", lambda s, tc=tc: s.copy(
            out=Cms[:, 512 * tc:512 * (tc + 1)], in_=pC[0:16, :]),
            reads=[pC], writes=[Cms])
    p.dma("gpsimd", Bm_hbm[:], Bms[:])
    p.dma("gpsimd", Cm_hbm[:], Cms[:])
    wdt = p.sb(tag + "wdt", 64, D_INNER, BF16)
    p.dma("gpsimd", wdt[:], wdt_hbm[:])
    bdr = p.sb(tag + "bdr", 16, 128, F32)
    bdc = p.sb(tag + "bdc", 128, 16, F32)
    rps = p.pf(4)
    p.dma("sync", bdr[:], bdt_hbm[:])
    p.emit("tensor", lambda t: t.transpose(out=rps[0:128, 0:16], in_=bdr[:],
                                           identity=ident_f32[0:16, 0:16]),
           reads=[bdr, ident_f32], writes=[rps])
    p.emit("scalar", lambda s: s.copy(out=bdc[:], in_=rps[0:128, 0:16]),
           reads=[rps], writes=[bdc])
    onec = p.ccol(1.0)
    pz = [p.pf(0), p.pf(1)]
    negz = [p.sb(tag + f"ng{i}", 128, 512, F32) for i in range(2)]
    nab = [p.sb(tag + f"na{i}", 128, 512, F32) for i in range(2)]
    e1 = [p.sb(tag + f"e1{i}", 128, 512, F32) for i in range(2)]
    l1 = [p.sb(tag + f"l1{i}", 128, 512, F32) for i in range(2)]
    rl = [p.sb(tag + f"rl{i}", 128, 512, F32) for i in range(2)]
    dl = [p.sb(tag + f"dl{i}", 128, 512, F32) for i in range(2)]
    dlb = [p.sb(tag + f"dlb{i}", 128, 512, BF16) for i in range(2)]
    dub = [p.sb(tag + f"du{i}", 128, 512, BF16) for i in range(2)]
    for m in range(16):
        for tc in range(8):
            i = (m * 8 + tc) % 2
            zp = pz[i]
            p.emit("tensor", lambda t, m=m, tc=tc, zp=zp: t.matmul(
                out=zp[:], lhsT=wdt[:, 128 * m:128 * (m + 1)],
                rhs=dltb[:, 512 * tc:512 * (tc + 1)], start=True, stop=True),
                reads=[wdt, dltb], writes=[zp])
            p.emit("vector", lambda v, i=i, m=m, zp=zp: v.tensor_scalar(
                out=rl[i][:], in0=zp[:], scalar1=bdc[:, m:m + 1], scalar2=None,
                op0=ALU.add), reads=[zp, bdc], writes=[rl[i]])
            p.emit("vector", lambda v, i=i: v.tensor_scalar(
                out=negz[i][:], in0=rl[i][:], scalar1=-1.0, scalar2=None,
                op0=ALU.mult), reads=[rl[i]], writes=[negz[i]])
            p.emit("vector", lambda v, i=i: v.tensor_tensor(
                out=nab[i][:], in0=negz[i][:], in1=rl[i][:], op=ALU.min),
                reads=[negz[i], rl[i]], writes=[nab[i]])
            p.emit("scalar", lambda s, i=i: s.activation(
                out=e1[i][:], in_=nab[i][:], func=AF.Exp),
                reads=[nab[i]], writes=[e1[i]])
            p.emit("scalar", lambda s, i=i: s.activation(
                out=l1[i][:], in_=e1[i][:], func=AF.Ln, bias=onec[:, 0:1]),
                reads=[e1[i], onec], writes=[l1[i]])
            p.emit("vector", lambda v, i=i: v.tensor_scalar_max(
                out=rl[i][:], in0=rl[i][:], scalar1=0.0),
                reads=[rl[i]], writes=[rl[i]])
            p.emit("vector", lambda v, i=i: v.tensor_add(
                out=dl[i][:], in0=rl[i][:], in1=l1[i][:]),
                reads=[rl[i], l1[i]], writes=[dl[i]])
            p.emit("vector", lambda v, i=i: v.tensor_copy(
                out=dlb[i][:], in_=dl[i][:]), reads=[dl[i]], writes=[dlb[i]])
            p.dma("gpsimd", deltaT_hbm[128 * m:128 * (m + 1),
                                       512 * tc:512 * (tc + 1)], dlb[i][:])
            p.dma("sync", xst[i][:],
                  xsiluT_hbm[128 * m:128 * (m + 1), 512 * tc:512 * (tc + 1)])
            p.emit("vector", lambda v, i=i: v.tensor_mul(
                out=dub[i][:], in0=dl[i][:], in1=xst[i][:]),
                reads=[dl[i], xst[i]], writes=[dub[i]])
            p.dma("gpsimd", duT_hbm[128 * m:128 * (m + 1),
                                    512 * tc:512 * (tc + 1)], dub[i][:])


def phase_scan_setup(p, deltaT_hbm, duT_hbm, xsiluT_hbm, resT_hbm, Bm_hbm,
                     Cm_hbm, A_hbm, Dv_hbm, yT_hbm, ident_f32):
    nc = p.nc
    tag = "c9"
    TH = 1024
    A_sb = p.sb(tag + "A", 128, 16 * D_STATE, F32)
    for dc in range(16):
        p.dma("sync", A_sb[:, 16 * dc:16 * (dc + 1)],
              A_hbm[128 * dc:128 * (dc + 1), :])
    Dr = p.sb(tag + "Dr", 16, 128, F32)
    Dc = p.sb(tag + "Dc", 128, 16, F32)
    rps = p.pf(4)
    p.dma("sync", Dr[:], Dv_hbm[:])
    p.emit("tensor", lambda t: t.transpose(out=rps[0:128, 0:16], in_=Dr[:],
                                           identity=ident_f32[0:16, 0:16]),
           reads=[Dr, ident_f32], writes=[rps])
    p.emit("scalar", lambda s: s.copy(out=Dc[:], in_=rps[0:128, 0:16]),
           reads=[rps], writes=[Dc])
    oh16 = p.sb(tag + "oh16", 16, 16 * 128, F32)
    ohi = p.sb(tag + "ohi", 16, 16 * 128, F32)
    p.emit("gpsimd", lambda g: g.iota(out=ohi[:], pattern=[[-1, 16], [0, 128]],
                                      base=0, channel_multiplier=1,
                                      allow_small_or_imprecise_dtypes=True),
           writes=[ohi])
    p.emit("vector", lambda v: v.tensor_scalar(out=oh16[:], in0=ohi[:],
                                               scalar1=0.0, scalar2=None,
                                               op0=ALU.is_equal),
           reads=[ohi], writes=[oh16])
    Brow = p.sb(tag + "Brow", 16, TH, F32)
    Crow = p.sb(tag + "Crow", 16, TH, F32)
    Bbc = p.sb(tag + "Bbc", 128, 16 * TH, BF16)
    Cbc = p.sb(tag + "Cbc", 128, 16 * TH, BF16)
    pbc = [p.pf(0), p.pf(1)]
    dts = [p.sb(tag + f"dt{i}", 128, TH, BF16) for i in range(3)]
    dus = [p.sb(tag + f"du{i}", 128, TH, BF16) for i in range(3)]
    xss = [p.sb(tag + f"xs{i}", 128, TH, BF16) for i in range(3)]
    rss = [p.sb(tag + f"rs{i}", 128, TH, BF16) for i in range(3)]
    dA = [p.sb(tag + f"dA{i}", 128, TH, BF16) for i in range(2)]
    db = [p.sb(tag + f"db{i}", 128, TH, BF16) for i in range(2)]
    cyall = p.sb(tag + "cy", 128, 16 * TH, BF16)
    treeA = p.sb(tag + "trA", 128, 8 * TH, BF16)
    yf = [p.sb(tag + "yf0", 128, TH, F32)] * 2
    y2 = [p.sb(tag + "y20", 128, TH, F32)] * 2
    sres = [p.sb(tag + "sr0", 128, TH, F32)] * 2
    yo = [p.sb(tag + f"yo{i}", 128, TH, BF16) for i in range(2)]
    carry_sv = p.sb(tag + "csv", 128, 16 * D_STATE, F32)

    def run_block(b, th):
            c0 = S * b + TH * th
            p.dma("sync", Brow[:], Bm_hbm[:, c0:c0 + TH])
            p.dma("sync", Crow[:], Cm_hbm[:, c0:c0 + TH])
            for n in range(16):
                for cc in range(TH // 512):
                    pt = pbc[cc % 2]
                    p.emit("tensor", lambda t, n=n, cc=cc, pt=pt: t.matmul(
                        out=pt[:], lhsT=oh16[:, 128 * n:128 * (n + 1)],
                        rhs=Brow[:, 512 * cc:512 * (cc + 1)],
                        start=True, stop=True), reads=[oh16, Brow], writes=[pt])
                    p.emit("scalar", lambda s, n=n, cc=cc, pt=pt: s.copy(
                        out=Bbc[:, TH * n + 512 * cc:TH * n + 512 * (cc + 1)],
                        in_=pt[:]), reads=[pt], writes=[Bbc])
                    p.emit("tensor", lambda t, n=n, cc=cc, pt=pt: t.matmul(
                        out=pt[:], lhsT=oh16[:, 128 * n:128 * (n + 1)],
                        rhs=Crow[:, 512 * cc:512 * (cc + 1)],
                        start=True, stop=True), reads=[oh16, Crow], writes=[pt])
                    p.emit("scalar", lambda s, n=n, cc=cc, pt=pt: s.copy(
                        out=Cbc[:, TH * n + 512 * cc:TH * n + 512 * (cc + 1)],
                        in_=pt[:]), reads=[pt], writes=[Cbc])
            for dc in range(16):
                i = dc % 2
                il = dc % 3
                p.dma("sync", dts[il][:],
                      deltaT_hbm[128 * dc:128 * (dc + 1), c0:c0 + TH])
                p.dma("sync", dus[il][:],
                      duT_hbm[128 * dc:128 * (dc + 1), c0:c0 + TH])
                p.dma("sync", xss[il][:],
                      xsiluT_hbm[128 * dc:128 * (dc + 1), c0:c0 + TH])
                p.dma("sync", rss[il][:],
                      resT_hbm[128 * dc:128 * (dc + 1), c0:c0 + TH])
                for n in range(16):
                    jn = n % 2
                    p.emit("scalar", lambda s, il=il, dc=dc, n=n, jn=jn: s.activation(
                        out=dA[jn][:], in_=dts[il][:], func=AF.Exp,
                        scale=A_sb[:, 16 * dc + n:16 * dc + n + 1]),
                        reads=[dts[il], A_sb], writes=[dA[jn]])
                    p.emit("vector", lambda v, il=il, n=n, jn=jn: v.tensor_mul(
                        out=db[jn][:], in0=dus[il][:],
                        in1=Bbc[:, TH * n:TH * (n + 1)]),
                        reads=[dus[il], Bbc], writes=[db[jn]])
                    if th == 0:
                        init = 0.0
                        rd_init = []
                    else:
                        init = carry_sv[:, 16 * dc + n:16 * dc + n + 1]
                        rd_init = [carry_sv]

                    def scan_op(v, init=init, n=n, jn=jn):
                        return v.tensor_tensor_scan(
                            out=cyall[:, TH * n:TH * (n + 1)], data0=dA[jn][:],
                            data1=db[jn][:], initial=init, op0=ALU.mult,
                            op1=ALU.add)
                    p.emit("vector", scan_op, reads=[dA[jn], db[jn]] + rd_init,
                           writes=[cyall])
                    if th < S // TH - 1:
                        p.emit("vector", lambda v, dc=dc, n=n: v.tensor_copy(
                            out=carry_sv[:, 16 * dc + n:16 * dc + n + 1],
                            in_=cyall[:, TH * (n + 1) - 1:TH * (n + 1)]),
                            reads=[cyall], writes=[carry_sv])
                    p.emit("vector", lambda v, n=n: v.tensor_mul(
                        out=cyall[:, TH * n:TH * (n + 1)],
                        in0=cyall[:, TH * n:TH * (n + 1)],
                        in1=Cbc[:, TH * n:TH * (n + 1)]),
                        reads=[cyall, Cbc], writes=[cyall])
                p.emit("vector", lambda v: v.tensor_add(
                    out=treeA[:], in0=cyall[:, 0:8 * TH],
                    in1=cyall[:, 8 * TH:16 * TH]), reads=[cyall], writes=[treeA])
                p.emit("vector", lambda v: v.tensor_add(
                    out=treeA[:, 0:4 * TH], in0=treeA[:, 0:4 * TH],
                    in1=treeA[:, 4 * TH:8 * TH]), reads=[treeA], writes=[treeA])
                p.emit("vector", lambda v: v.tensor_add(
                    out=treeA[:, 0:2 * TH], in0=treeA[:, 0:2 * TH],
                    in1=treeA[:, 2 * TH:4 * TH]), reads=[treeA], writes=[treeA])
                p.emit("vector", lambda v, i=i: v.tensor_add(
                    out=yf[i][:], in0=treeA[:, 0:TH], in1=treeA[:, TH:2 * TH]),
                    reads=[treeA], writes=[yf[i]])
                p.emit("vector", lambda v, i=i, il=il, dc=dc: v.scalar_tensor_tensor(
                    out=y2[i][:], in0=xss[il][:], scalar=Dc[:, dc:dc + 1],
                    in1=yf[i][:], op0=ALU.mult, op1=ALU.add),
                    reads=[xss[il], Dc, yf[i]], writes=[y2[i]])
                p.emit("scalar", lambda s, i=i, il=il: s.activation(
                    out=sres[i][:], in_=rss[il][:], func=AF.Silu),
                    reads=[rss[il]], writes=[sres[i]])
                p.emit("vector", lambda v, i=i: v.tensor_mul(
                    out=yo[i][:], in0=y2[i][:], in1=sres[i][:]),
                    reads=[y2[i], sres[i]], writes=[yo[i]])
                p.dma("gpsimd", yT_hbm[128 * dc:128 * (dc + 1), c0:c0 + TH],
                      yo[i][:])
    return run_block, S // TH


def phase_l1out(p, yT_hbm, wout_hbm, xnT_hbm, s1_hbm, ident_f32):
    nc = p.nc
    tag = "cA"
    wout = p.sb(tag + "wo", 128, 16 * D, BF16)
    for k in range(16):
        p.dma("gpsimd", wout[:, D * k:D * (k + 1)],
              wout_hbm[128 * k:128 * (k + 1), :])
    yc = [p.sb(tag + f"y{i}", 128, 16 * 512, BF16) for i in range(2)]
    xn1 = [p.sb(tag + f"xn{i}", 128, 512, BF16) for i in range(2)]
    s1T = [p.sb(tag + f"sT{i}", 128, 512, F32) for i in range(2)]
    stg = [p.sb(tag + f"st{i}", 128, D, F32) for i in range(4)]
    pps = [p.pf(0), p.pf(1)]
    ptr = [p.pf(2), p.pf(3)]
    for tc in range(8):
        i = tc % 2
        for k in range(16):
            p.dma("sync", yc[i][:, 512 * k:512 * (k + 1)],
                  yT_hbm[128 * k:128 * (k + 1), 512 * tc:512 * (tc + 1)])
        for m in range(8):
            im = (tc * 8 + m) % 2
            p.dma("sync", xn1[im][:],
                  xnT_hbm[128 * m:128 * (m + 1), 512 * tc:512 * (tc + 1)])

            def mms(t, m=m, i=i, im=im):
                for k in range(16):
                    inst = t.matmul(
                        out=pps[im][:], lhsT=wout[:, D * k + 128 * m:D * k + 128 * (m + 1)],
                        rhs=yc[i][:, 512 * k:512 * (k + 1)],
                        start=(k == 0), stop=(k == 15))
                return inst
            p.emit("tensor", mms, reads=[wout, yc[i]], writes=[pps[im]])
            p.emit("vector", lambda v, im=im: v.scalar_tensor_tensor(
                out=s1T[im][:], in0=pps[im][:], scalar=1.0, in1=xn1[im][:],
                op0=ALU.bypass, op1=ALU.add),
                reads=[pps[im], xn1[im]], writes=[s1T[im]])
            for u in range(4):
                pt = ptr[u % 2]
                p.emit("tensor", lambda t, im=im, u=u, pt=pt: t.transpose(
                    out=pt[:, 0:128], in_=s1T[im][:, 128 * u:128 * (u + 1)],
                    identity=ident_f32[:]), reads=[s1T[im], ident_f32],
                    writes=[pt])
                p.emit("scalar", lambda s, u=u, m=m, pt=pt: s.copy(
                    out=stg[u][:, 128 * m:128 * (m + 1)], in_=pt[:, 0:128]),
                    reads=[pt], writes=[stg[u]])
        for u in range(4):
            tt = tc * 4 + u
            p.dma("gpsimd", s1_hbm[128 * tt:128 * (tt + 1), :], stg[u][:])


def phase_ffn8(p, xnT_hbm, wg_hbm, wu_hbm, wd_hbm, gsilT_hbm, hT_hbm, g1_hbm,
               ident_bf, ident_f32):
    nc = p.nc
    tag = "cB"
    SC = 1.0 / 127.0
    HCH = HID_I8 // 128
    raw = None
    xsl = [p.sb(tag + f"x{i}", 128, 8 * 512, BF16) for i in range(2)]
    pps = [p.pf(0), p.pf(1)]
    out1 = [p.sb(tag + f"o{i}", 128, 512, BF16) for i in range(2)]
    wT_sh = p.sb(tag + "wTsh", 128, 8 * HID_I8, BF16)
    wgT, raw = load_weight_T(p, wg_hbm, HID_I8, D, ident_bf, tag + "wg", raw,
                             wT=wT_sh)
    for tc in range(8):
        i = tc % 2
        for k in range(8):
            p.dma("sync", xsl[i][:, 512 * k:512 * (k + 1)],
                  xnT_hbm[128 * k:128 * (k + 1), 512 * tc:512 * (tc + 1)])
        for hc in range(HCH):
            ih = (tc * HCH + hc) % 2

            def mms(t, hc=hc, i=i, ih=ih, wT=wgT):
                for k in range(8):
                    inst = t.matmul(
                        out=pps[ih][:],
                        lhsT=wT[:, HID_I8 * k + 128 * hc:HID_I8 * k + 128 * (hc + 1)],
                        rhs=xsl[i][:, 512 * k:512 * (k + 1)],
                        start=(k == 0), stop=(k == 7))
                return inst
            p.emit("tensor", mms, reads=[wgT, xsl[i]], writes=[pps[ih]])
            p.emit("scalar", lambda s, ih=ih: s.activation(
                out=out1[ih][:], in_=pps[ih][:], func=AF.Silu, scale=SC),
                reads=[pps[ih]], writes=[out1[ih]])
            p.dma("gpsimd", gsilT_hbm[128 * hc:128 * (hc + 1),
                                      512 * tc:512 * (tc + 1)], out1[ih][:])
    wuT, raw = load_weight_T(p, wu_hbm, HID_I8, D, ident_bf, tag + "wu", raw,
                             wT=wT_sh)
    gsl = [p.sb(tag + f"g{i}", 128, 512, BF16) for i in range(2)]
    for tc in range(8):
        i = tc % 2
        for k in range(8):
            p.dma("sync", xsl[i][:, 512 * k:512 * (k + 1)],
                  xnT_hbm[128 * k:128 * (k + 1), 512 * tc:512 * (tc + 1)])
        for hc in range(HCH):
            ih = (tc * HCH + hc) % 2
            p.dma("sync", gsl[ih][:],
                  gsilT_hbm[128 * hc:128 * (hc + 1), 512 * tc:512 * (tc + 1)])

            def mms(t, hc=hc, i=i, ih=ih, wT=wuT):
                for k in range(8):
                    inst = t.matmul(
                        out=pps[ih][:],
                        lhsT=wT[:, HID_I8 * k + 128 * hc:HID_I8 * k + 128 * (hc + 1)],
                        rhs=xsl[i][:, 512 * k:512 * (k + 1)],
                        start=(k == 0), stop=(k == 7))
                return inst
            p.emit("tensor", mms, reads=[wuT, xsl[i]], writes=[pps[ih]])
            p.emit("vector", lambda v, ih=ih: v.scalar_tensor_tensor(
                out=out1[ih][:], in0=pps[ih][:], scalar=SC, in1=gsl[ih][:],
                op0=ALU.mult, op1=ALU.mult),
                reads=[pps[ih], gsl[ih]], writes=[out1[ih]])
            p.dma("gpsimd", hT_hbm[128 * hc:128 * (hc + 1),
                                   512 * tc:512 * (tc + 1)], out1[ih][:])
    wdT, raw = load_weight_T(p, wd_hbm, D, HID_I8, ident_bf, tag + "wd", raw,
                             wT=wT_sh)
    hsl = [p.sb(tag + "h0", 128, HCH * 512, BF16)] * 2
    g1T = [p.sb(tag + f"gT{i}", 128, 512, F32) for i in range(2)]
    stg = [p.sb(tag + f"st{i}", 128, D, F32) for i in range(4)]
    ptr = [p.pf(2), p.pf(3)]
    for tc in range(8):
        i = tc % 2
        for k in range(HCH):
            p.dma("sync", hsl[i][:, 512 * k:512 * (k + 1)],
                  hT_hbm[128 * k:128 * (k + 1), 512 * tc:512 * (tc + 1)])
        for m in range(8):
            im = (tc * 8 + m) % 2

            def mms(t, m=m, i=i, im=im):
                for k in range(HCH):
                    inst = t.matmul(
                        out=pps[im][:], lhsT=wdT[:, D * k + 128 * m:D * k + 128 * (m + 1)],
                        rhs=hsl[i][:, 512 * k:512 * (k + 1)],
                        start=(k == 0), stop=(k == HCH - 1))
                return inst
            p.emit("tensor", mms, reads=[wdT, hsl[i]], writes=[pps[im]])
            p.emit("scalar", lambda s, im=im: s.activation(
                out=g1T[im][:], in_=pps[im][:], func=AF.Copy, scale=SC),
                reads=[pps[im]], writes=[g1T[im]])
            for u in range(4):
                pt = ptr[u % 2]
                p.emit("tensor", lambda t, im=im, u=u, pt=pt: t.transpose(
                    out=pt[:, 0:128], in_=g1T[im][:, 128 * u:128 * (u + 1)],
                    identity=ident_f32[:]), reads=[g1T[im], ident_f32],
                    writes=[pt])
                p.emit("scalar", lambda s, u=u, m=m, pt=pt: s.copy(
                    out=stg[u][:, 128 * m:128 * (m + 1)], in_=pt[:, 0:128]),
                    reads=[pt], writes=[stg[u]])
        for u in range(4):
            tt = tc * 4 + u
            p.dma("gpsimd", g1_hbm[128 * tt:128 * (tt + 1), :], stg[u][:])


def phase_final(p, x1_hbm, s1_hbm, g1_hbm, out_hbm):
    nc = p.nc
    tag = "cF"
    x1 = [p.sb(tag + f"x{i}", 128, D, F32) for i in range(2)]
    s1 = [p.sb(tag + f"s{i}", 128, D, F32) for i in range(2)]
    g1 = [p.sb(tag + f"g{i}", 128, D, F32) for i in range(2)]
    sg = [p.sb(tag + f"sg{i}", 128, D, F32) for i in range(2)]
    ot = [p.sb(tag + f"o{i}", 128, D, F32) for i in range(2)]
    for tt in range(NTT):
        i = tt % 2
        p.dma("sync", x1[i][:], x1_hbm[128 * tt:128 * (tt + 1), :])
        p.dma("sync", s1[i][:], s1_hbm[128 * tt:128 * (tt + 1), :])
        p.dma("sync", g1[i][:], g1_hbm[128 * tt:128 * (tt + 1), :])
        p.emit("scalar", lambda s, i=i: s.activation(
            out=sg[i][:], in_=g1[i][:], func=AF.Silu), reads=[g1[i]],
            writes=[sg[i]])
        p.emit("vector", lambda v, i=i: v.tensor_mul(
            out=sg[i][:], in0=sg[i][:], in1=s1[i][:]),
            reads=[s1[i], sg[i]], writes=[sg[i]])
        p.emit("vector", lambda v, i=i: v.tensor_add(
            out=ot[i][:], in0=x1[i][:], in1=sg[i][:]),
            reads=[x1[i], sg[i]], writes=[ot[i]])
        p.dma("gpsimd", out_hbm[128 * tt:128 * (tt + 1), :], ot[i][:])


import contextlib


class Launch:
    def __init__(self, name):
        self.es = contextlib.ExitStack()
        self.nc = bass.Bass("TRN2", target_bir_lowering=False, name=name)
        self.p = P(self.nc, self.es)
        self.aps = {}

    def inp(self, name, shape, dtype):
        ap = self.nc.dram_tensor(name, list(shape), dtype, kind="ExternalInput").ap()
        self.aps[name] = ap
        return ap

    def out(self, name, shape, dtype):
        ap = self.nc.dram_tensor(name, list(shape), dtype, kind="ExternalOutput").ap()
        self.aps[name] = ap
        return ap

    def run(self, in_maps, out_names):
        with self.es:
            self.p.build()
            res = run_bass_kernel_spmd(self.nc, in_maps,
                                       core_ids=list(range(len(in_maps))))
        return {n: [res.results[c][n] for c in range(len(in_maps))]
                for n in out_names}


def _rep(arr, n=N_CORE):
    return [arr for _ in range(n)]


def kernel_unfused(**inputs):
    f32 = np.float32
    x_full = np.ascontiguousarray(inputs["L0_attn_x"], f32)
    xs = [np.ascontiguousarray(x_full[2 * c:2 * c + 2].reshape(NTOK, D))
          for c in range(N_CORE)]
    wq = np.ascontiguousarray(inputs["L0_attn_Wq"], f32)
    wk = np.ascontiguousarray(inputs["L0_attn_Wk"], f32)
    wv = np.ascontiguousarray(inputs["L0_attn_Wv"], f32)
    wo = np.ascontiguousarray(inputs["L0_attn_Wo"], f32)
    slopes = np.ascontiguousarray(inputs["L0_attn_slopes"], f32).reshape(1, H)
    gate_w = np.ascontiguousarray(inputs["L0_ffn_gate_w"], f32)
    gate_b = np.ascontiguousarray(inputs["L0_ffn_gate_b"], f32).reshape(1, N_EXP)

    L1 = Launch("ln0")
    a = L1.aps
    L1.inp("x", (NTOK, D), F32)
    L1.inp("gw", (D, N_EXP), F32)
    L1.inp("gb", (1, N_EXP), F32)
    L1.out("xnT", (D, NTOK), BF16)
    L1.out("glog", (NTOK, N_EXP), F32)
    L1.p.warmup((AF.Sqrt,))
    idf = build_identity(L1.p, F32, "idf")
    phase_ln(L1.p, "p1", a["x"], a["xnT"], idf, gate=(a["gw"], a["gb"], a["glog"]))
    r1 = L1.run([{"x": xs[c], "gw": gate_w, "gb": gate_b} for c in range(N_CORE)],
                ["xnT", "glog"])

    L2 = Launch("qkv")
    a = L2.aps
    L2.inp("xnT", (D, NTOK), BF16)
    L2.inp("wq", (D, D), F32)
    L2.inp("wk", (D, D), F32)
    L2.inp("wv", (D, D), F32)
    L2.out("qT", (D, NTOK), BF16)
    L2.out("kT", (D, NTOK), BF16)
    L2.out("v", (NTOK, D), BF16)
    L2.p.warmup(())
    idb = build_identity(L2.p, BF16, "idb")
    phase_qkv(L2.p, a["xnT"], a["wq"], a["wk"], a["wv"], a["qT"], a["kT"], a["v"], idb)
    r2 = L2.run([{"xnT": r1["xnT"][c], "wq": wq, "wk": wk, "wv": wv}
                 for c in range(N_CORE)], ["qT", "kT", "v"])

    L3 = Launch("att")
    a = L3.aps
    L3.inp("qT", (D, NTOK), BF16)
    L3.inp("kT", (D, NTOK), BF16)
    L3.inp("v", (NTOK, D), BF16)
    L3.inp("slopes", (1, H), F32)
    L3.out("aoT", (D, NTOK), BF16)
    L3.p.warmup((AF.Exp,))
    idb = build_identity(L3.p, BF16, "idb")
    phase_att(L3.p, a["qT"], a["kT"], a["v"], a["slopes"], a["aoT"], idb)
    r3 = L3.run([{"qT": r2["qT"][c], "kT": r2["kT"][c], "v": r2["v"][c],
                  "slopes": slopes} for c in range(N_CORE)], ["aoT"])

    L4 = Launch("wo")
    a = L4.aps
    L4.inp("aoT", (D, NTOK), BF16)
    L4.inp("wo", (D, D), F32)
    L4.out("s", (NTOK, D), F32)
    L4.p.warmup(())
    idb = build_identity(L4.p, BF16, "idb")
    idf = build_identity(L4.p, F32, "idf")
    phase_wo(L4.p, a["aoT"], a["wo"], a["s"], idb, idf)
    r4 = L4.run([{"aoT": r3["aoT"][c], "wo": wo} for c in range(N_CORE)], ["s"])

    if inputs.get("_partial") == "s":
        return r4["s"], r1["glog"], r1["xnT"]

    w1 = np.ascontiguousarray(inputs["L0_ffn_w1"], f32).reshape(N_EXP * D, D_FF)
    b1 = np.ascontiguousarray(np.ascontiguousarray(inputs["L0_ffn_b1"], f32)
                              .reshape(N_EXP, 32, 128).transpose(0, 2, 1)
                              .reshape(N_EXP * 128, 32))
    w2 = np.ascontiguousarray(inputs["L0_ffn_w2"], f32).reshape(N_EXP * D_FF, D)
    b2 = np.ascontiguousarray(inputs["L0_ffn_b2"], f32).reshape(N_EXP * 8, 128)

    L5 = Launch("moe")
    a = L5.aps
    L5.inp("xnT", (D, NTOK), BF16)
    L5.inp("glog", (NTOK, N_EXP), F32)
    L5.inp("w1", (N_EXP * D, D_FF), F32)
    L5.inp("b1", (N_EXP * 32, 128), F32)
    L5.inp("w2", (N_EXP * D_FF, D), F32)
    L5.inp("b2", (N_EXP * 8, 128), F32)
    L5.out("gT", (D, NTOK), F32)
    L5.p.warmup((AF.Exp, AF.Gelu))
    idf = build_identity(L5.p, F32, "idf")
    phase_moe(L5.p, a["xnT"], a["glog"], a["w1"], a["b1"], a["w2"], a["b2"],
              a["gT"], idf)
    r5 = L5.run([{"xnT": r1["xnT"][c], "glog": r1["glog"][c], "w1": w1, "b1": b1,
                  "w2": w2, "b2": b2} for c in range(N_CORE)], ["gT"])

    L6 = Launch("comb")
    a = L6.aps
    L6.inp("x", (NTOK, D), F32)
    L6.inp("s", (NTOK, D), F32)
    L6.inp("gT", (D, NTOK), F32)
    L6.out("x1", (NTOK, D), F32)
    L6.out("xnT1", (D, NTOK), BF16)
    L6.out("rstd2", (1, NTOK), F32)
    L6.p.warmup((AF.Silu, AF.Sqrt))
    idf = build_identity(L6.p, F32, "idf")
    phase_combine_ln(L6.p, a["x"], a["s"], a["gT"], a["x1"], a["xnT1"],
                     a["rstd2"], idf)
    r6 = L6.run([{"x": xs[c], "s": r4["s"][c], "gT": r5["gT"][c]}
                 for c in range(N_CORE)], ["x1", "xnT1", "rstd2"])

    if inputs.get("_partial") == "x1":
        return r6["x1"], r6["xnT1"], r6["rstd2"]

    norm_w = np.ascontiguousarray(inputs["L1_attn_norm_w"], f32).reshape(8, 128)
    w_in = np.ascontiguousarray(inputs["L1_attn_W_in"], f32)
    conv_w = np.ascontiguousarray(inputs["L1_attn_conv_w"], f32)
    conv_b = np.ascontiguousarray(inputs["L1_attn_conv_b"], f32).reshape(16, 128)
    w_x = np.ascontiguousarray(inputs["L1_attn_W_x"], f32)
    w_dt = np.ascontiguousarray(inputs["L1_attn_W_dt"], f32)
    b_dt = np.ascontiguousarray(inputs["L1_attn_b_dt"], f32).reshape(16, 128)
    A_w = np.ascontiguousarray(inputs["L1_attn_A"], f32)
    D_w = np.ascontiguousarray(inputs["L1_attn_D"], f32).reshape(16, 128)
    w_out = np.ascontiguousarray(inputs["L1_attn_W_out"], f32)
    wg8 = np.ascontiguousarray(inputs["L1_ffn_w_gate"], np.int32)
    wu8 = np.ascontiguousarray(inputs["L1_ffn_w_up"], np.int32)
    wd8 = np.ascontiguousarray(inputs["L1_ffn_w_down"], np.int32)

    L7 = Launch("min")
    a = L7.aps
    L7.inp("xnT1", (D, NTOK), BF16)
    L7.inp("rstd2", (1, NTOK), F32)
    L7.inp("norm_w", (8, 128), F32)
    L7.inp("w_in", (D, 2 * D_INNER), F32)
    L7.inp("conv_w", (D_INNER, CONV_K), F32)
    L7.inp("conv_b", (16, 128), F32)
    L7.out("xinT", (D_INNER, NTOK), BF16)
    L7.out("resT", (D_INNER, NTOK), BF16)
    L7.out("xsiluT", (D_INNER, NTOK), BF16)
    L7.p.warmup((AF.Silu,))
    idf = build_identity(L7.p, F32, "idf")
    phase_mamba_in(L7.p, a["xnT1"], a["rstd2"], a["norm_w"], a["w_in"],
                   a["conv_w"], a["conv_b"], a["xinT"], a["resT"], a["xsiluT"],
                   idf)
    r7 = L7.run([{"xnT1": r6["xnT1"][c], "rstd2": r6["rstd2"][c],
                  "norm_w": norm_w, "w_in": w_in, "conv_w": conv_w,
                  "conv_b": conv_b} for c in range(N_CORE)],
                ["xinT", "resT", "xsiluT"])

    L8 = Launch("sproj")
    a = L8.aps
    L8.inp("xsiluT", (D_INNER, NTOK), BF16)
    L8.inp("w_x", (D_INNER, 96), F32)
    L8.inp("w_dt", (DT_RANK, D_INNER), F32)
    L8.inp("b_dt", (16, 128), F32)
    L8.out("deltaT", (D_INNER, NTOK), BF16)
    L8.out("duT", (D_INNER, NTOK), BF16)
    L8.out("Bm", (D_STATE, NTOK), F32)
    L8.out("Cm", (D_STATE, NTOK), F32)
    L8.p.warmup((AF.Exp, AF.Ln))
    idf = build_identity(L8.p, F32, "idf")
    phase_ssm_proj(L8.p, a["xsiluT"], a["w_x"], a["w_dt"], a["b_dt"],
                   a["deltaT"], a["duT"], a["Bm"], a["Cm"], idf)
    r8 = L8.run([{"xsiluT": r7["xsiluT"][c], "w_x": w_x, "w_dt": w_dt,
                  "b_dt": b_dt} for c in range(N_CORE)],
                ["deltaT", "duT", "Bm", "Cm"])

    L9 = Launch("scan")
    a = L9.aps
    L9.inp("deltaT", (D_INNER, NTOK), BF16)
    L9.inp("duT", (D_INNER, NTOK), BF16)
    L9.inp("xsiluT", (D_INNER, NTOK), BF16)
    L9.inp("resT", (D_INNER, NTOK), BF16)
    L9.inp("Bm", (D_STATE, NTOK), F32)
    L9.inp("Cm", (D_STATE, NTOK), F32)
    L9.inp("A", (D_INNER, D_STATE), F32)
    L9.inp("D", (16, 128), F32)
    L9.out("yT", (D_INNER, NTOK), BF16)
    L9.p.warmup((AF.Exp, AF.Silu))
    idf = build_identity(L9.p, F32, "idf")
    phase_scan(L9.p, a["deltaT"], a["duT"], a["xsiluT"], a["resT"], a["Bm"],
               a["Cm"], a["A"], a["D"], a["yT"], idf)
    r9 = L9.run([{"deltaT": r8["deltaT"][c], "duT": r8["duT"][c],
                  "xsiluT": r7["xsiluT"][c], "resT": r7["resT"][c],
                  "Bm": r8["Bm"][c], "Cm": r8["Cm"][c], "A": A_w, "D": D_w}
                 for c in range(N_CORE)], ["yT"])

    L10 = Launch("l1out")
    a = L10.aps
    L10.inp("yT", (D_INNER, NTOK), BF16)
    L10.inp("w_out", (D_INNER, D), F32)
    L10.inp("xnT1", (D, NTOK), BF16)
    L10.out("s1", (NTOK, D), F32)
    L10.p.warmup(())
    idf = build_identity(L10.p, F32, "idf")
    phase_l1out(L10.p, a["yT"], a["w_out"], a["xnT1"], a["s1"], idf)
    r10 = L10.run([{"yT": r9["yT"][c], "w_out": w_out, "xnT1": r6["xnT1"][c]}
                   for c in range(N_CORE)], ["s1"])

    L11 = Launch("ffn8")
    a = L11.aps
    L11.inp("xnT1", (D, NTOK), BF16)
    L11.inp("wg", (HID_I8, D), I32)
    L11.inp("wu", (HID_I8, D), I32)
    L11.inp("wd", (D, HID_I8), I32)
    gsilT = L11.nc.dram_tensor("gsilT", [HID_I8, NTOK], BF16, kind="Internal").ap()
    hTd = L11.nc.dram_tensor("hTd", [HID_I8, NTOK], BF16, kind="Internal").ap()
    L11.out("g1", (NTOK, D), F32)
    L11.p.warmup((AF.Silu,))
    idb = build_identity(L11.p, BF16, "idb")
    idf = build_identity(L11.p, F32, "idf")
    phase_ffn8(L11.p, a["xnT1"], a["wg"], a["wu"], a["wd"], gsilT, hTd,
               a["g1"], idb, idf)
    r11 = L11.run([{"xnT1": r6["xnT1"][c], "wg": wg8, "wu": wu8, "wd": wd8}
                   for c in range(N_CORE)], ["g1"])

    L12 = Launch("fin")
    a = L12.aps
    L12.inp("x1", (NTOK, D), F32)
    L12.inp("s1", (NTOK, D), F32)
    L12.inp("g1", (NTOK, D), F32)
    L12.out("out", (NTOK, D), F32)
    L12.p.warmup((AF.Silu,))
    phase_final(L12.p, a["x1"], a["s1"], a["g1"], a["out"])
    r12 = L12.run([{"x1": r6["x1"][c], "s1": r10["s1"][c], "g1": r11["g1"][c]}
                   for c in range(N_CORE)], ["out"])

    out = np.stack([np.asarray(r12["out"][c], f32).reshape(2, S, D)
                    for c in range(N_CORE)])
    return out.reshape(16, S, D)


def kernel(**inputs):
    f32 = np.float32
    x_full = np.ascontiguousarray(inputs["L0_attn_x"], f32)
    xs = [np.ascontiguousarray(x_full[2 * c:2 * c + 2].reshape(NTOK, D))
          for c in range(N_CORE)]
    wq = np.ascontiguousarray(inputs["L0_attn_Wq"], f32)
    wk = np.ascontiguousarray(inputs["L0_attn_Wk"], f32)
    wv = np.ascontiguousarray(inputs["L0_attn_Wv"], f32)
    wo = np.ascontiguousarray(inputs["L0_attn_Wo"], f32)
    slopes = np.ascontiguousarray(inputs["L0_attn_slopes"], f32).reshape(1, H)
    gate_w = np.ascontiguousarray(inputs["L0_ffn_gate_w"], f32)
    gate_b = np.ascontiguousarray(inputs["L0_ffn_gate_b"], f32).reshape(1, N_EXP)
    w1 = np.ascontiguousarray(inputs["L0_ffn_w1"], f32).reshape(N_EXP * D, D_FF)
    b1 = np.ascontiguousarray(np.ascontiguousarray(inputs["L0_ffn_b1"], f32)
                              .reshape(N_EXP, 32, 128).transpose(0, 2, 1)
                              .reshape(N_EXP * 128, 32))
    w2 = np.ascontiguousarray(inputs["L0_ffn_w2"], f32).reshape(N_EXP * D_FF, D)
    b2 = np.ascontiguousarray(inputs["L0_ffn_b2"], f32).reshape(N_EXP * 8, 128)
    b2r8 = np.ascontiguousarray(inputs["L0_ffn_b2"], f32)
    norm_w = np.ascontiguousarray(inputs["L1_attn_norm_w"], f32).reshape(8, 128)
    w_in = np.ascontiguousarray(inputs["L1_attn_W_in"], f32)
    conv_w = np.ascontiguousarray(inputs["L1_attn_conv_w"], f32)
    conv_b = np.ascontiguousarray(inputs["L1_attn_conv_b"], f32).reshape(16, 128)
    w_x = np.ascontiguousarray(inputs["L1_attn_W_x"], f32)
    w_dt = np.ascontiguousarray(inputs["L1_attn_W_dt"], f32)
    b_dt = np.ascontiguousarray(inputs["L1_attn_b_dt"], f32).reshape(16, 128)
    A_w = np.ascontiguousarray(inputs["L1_attn_A"], f32)
    D_w = np.ascontiguousarray(inputs["L1_attn_D"], f32).reshape(16, 128)
    w_out = np.ascontiguousarray(inputs["L1_attn_W_out"], f32)
    wg8 = np.ascontiguousarray(inputs["L1_ffn_w_gate"], np.int32)
    wu8 = np.ascontiguousarray(inputs["L1_ffn_w_up"], np.int32)
    wd8 = np.ascontiguousarray(inputs["L1_ffn_w_down"], np.int32)

    L = Launch("mega")
    nc, p = L.nc, L.p
    a = L.aps
    L.inp("x", (NTOK, D), F32)
    L.inp("wq", (D, D), F32)
    L.inp("wk", (D, D), F32)
    L.inp("wv", (D, D), F32)
    L.inp("wo", (D, D), F32)
    L.inp("slopes", (1, H), F32)
    L.inp("gw", (D, N_EXP), F32)
    L.inp("gb", (1, N_EXP), F32)
    L.inp("w1", (N_EXP * D, D_FF), F32)
    L.inp("b1", (N_EXP * 128, 32), F32)
    L.inp("w2", (N_EXP * D_FF, D), F32)
    L.inp("b2", (N_EXP * 8, 128), F32)
    L.inp("b2r8", (N_EXP, D), F32)
    L.inp("norm_w", (8, 128), F32)
    L.inp("w_in", (D, 2 * D_INNER), F32)
    L.inp("conv_w", (D_INNER, CONV_K), F32)
    L.inp("conv_b", (16, 128), F32)
    L.inp("w_x", (D_INNER, 96), F32)
    L.inp("w_dt", (DT_RANK, D_INNER), F32)
    L.inp("b_dt", (16, 128), F32)
    L.inp("A", (D_INNER, D_STATE), F32)
    L.inp("D", (16, 128), F32)
    L.inp("w_out", (D_INNER, D), F32)
    L.inp("wg", (HID_I8, D), I32)
    L.inp("wu", (HID_I8, D), I32)
    L.inp("wd", (D, HID_I8), I32)
    L.out("out", (NTOK, D), F32)

    def scr(name, shape, dtype):
        return nc.dram_tensor(name, list(shape), dtype, kind="Internal").ap()

    xnT0 = scr("xnT0", (D, NTOK), BF16)
    xn_tm = scr("xn_tm", (NTOK, D), BF16)
    gpos = scr("gpos", (NPAD, 2), I32)
    spos = scr("spos", (NPAD, 2), I32)
    wgtg = scr("wgtg", (NPAD, 2), F32)
    xpk = scr("xpk", (NPAD, D), BF16)
    glog = scr("glog", (NTOK, N_EXP), F32)
    qT = scr("qT", (D, NTOK), BF16)
    kT = scr("kT", (D, NTOK), BF16)
    vtm = scr("vtm", (NTOK, D), BF16)
    aoT = scr("aoT", (D, NTOK), BF16)
    s0 = scr("s0", (NTOK, D), F32)
    gT = scr("gT", (NTOK + 128, D), F32)
    x1 = scr("x1", (NTOK, D), F32)
    xnT1 = scr("xnT1", (D, NTOK), BF16)
    rstd2 = scr("rstd2", (1, NTOK), F32)
    xinT = scr("xinT", (D_INNER, NTOK), BF16)
    resT = scr("resT", (D_INNER, NTOK), BF16)
    xsiluT = scr("xsiluT", (D_INNER, NTOK), BF16)
    deltaT = scr("deltaT", (D_INNER, NTOK), BF16)
    duT = scr("duT", (D_INNER, NTOK), BF16)
    Bm = scr("Bm", (D_STATE, NTOK), F32)
    Cm = scr("Cm", (D_STATE, NTOK), F32)
    yT = scr("yT", (D_INNER, NTOK), BF16)
    s1 = scr("s1", (NTOK, D), F32)
    gsilT = scr("gsilT", (HID_I8, NTOK), BF16)
    hTd = scr("hTd", (HID_I8, NTOK), BF16)
    g1 = scr("g1", (NTOK, D), F32)
    wgT_s = scr("wgT_s", (D, HID_I8), BF16)
    wuT_s = scr("wuT_s", (D, HID_I8), BF16)
    wdT_s = scr("wdT_s", (HID_I8, D), BF16)

    p.warmup((AF.Sqrt, AF.Exp, AF.Gelu, AF.Silu, AF.Ln))
    idf = build_identity(p, F32, "idf")
    idb = build_identity(p, BF16, "idb")
    p.ccol(EPS)
    p.ccol(1.0)
    import os as _os
    _np_max = int(_os.environ.get("K_PHASES", "99"))
    _ph = [0]

    def _go():
        _ph[0] += 1
        return _ph[0] <= _np_max

    if _go():
        with p.scope():
            phase_ln(p, "p1", a["x"], xnT0, idf, gate=(a["gw"], a["gb"], glog),
                     xn_tm_hbm=xn_tm)
        p.barrier()
    if _go():
        with p.scope():
            phase_qkv(p, xnT0, a["wq"], a["wk"], a["wv"], qT, kT, vtm, idb)
        p.barrier()
    if _go():
        with p.scope():
            phase_att(p, qT, kT, vtm, a["slopes"], aoT, idb)
        p.barrier()
    if _go():
        with p.scope():
            phase_wo(p, aoT, a["wo"], s0, idb, idf)
        p.barrier()
    if _go():
        with p.scope():
            phase_moe_routed(p, xn_tm, glog, a["w1"], a["b1"], a["w2"],
                             a["b2r8"], gT, gpos, spos, wgtg, xpk, idf, idb)
        p.barrier()
    if _go():
        with p.scope():
            phase_combine_ln(p, a["x"], s0, gT, x1, xnT1, rstd2, idf,
                             g_token_major=True)
        p.barrier()
    if _go():
        with p.scope():
            phase_mamba_in(p, xnT1, rstd2, a["norm_w"], a["w_in"], a["conv_w"],
                           a["conv_b"], xinT, resT, xsiluT, idf)
        p.barrier()
    if _go():
        with p.scope():
            phase_ssm_proj(p, xsiluT, a["w_x"], a["w_dt"], a["b_dt"], deltaT, duT,
                           Bm, Cm, idf)
        p.barrier()
    if _go():
        with p.scope():
            run_scan, nth = phase_scan_setup(p, deltaT, duT, xsiluT, resT, Bm, Cm,
                                             a["A"], a["D"], yT, idf)
            for b in range(B_PER_CORE):
                for th in range(nth):
                    run_scan(b, th)
        p.barrier()
    if _go():
        with p.scope():
            phase_ffn8(p, xnT1, a["wg"], a["wu"], a["wd"], gsilT, hTd, g1, idb, idf)
        p.barrier()
    if _go():
        with p.scope():
            phase_l1out(p, yT, a["w_out"], xnT1, s1, idf)
        p.barrier()
    if _go():
        with p.scope():
            phase_final(p, x1, s1, g1, a["out"])

    in_maps = [{"x": xs[c], "wq": wq, "wk": wk, "wv": wv, "wo": wo,
                "slopes": slopes, "gw": gate_w, "gb": gate_b, "w1": w1,
                "b1": b1, "w2": w2, "b2": b2, "b2r8": b2r8, "norm_w": norm_w,
                "w_in": w_in,
                "conv_w": conv_w, "conv_b": conv_b, "w_x": w_x, "w_dt": w_dt,
                "b_dt": b_dt, "A": A_w, "D": D_w, "w_out": w_out, "wg": wg8,
                "wu": wu8, "wd": wd8} for c in range(N_CORE)]
    res = L.run(in_maps, ["out"])
    out = np.stack([np.asarray(res["out"][c], f32).reshape(2, S, D)
                    for c in range(N_CORE)])
    return out.reshape(16, S, D)


# version 22
# speedup vs baseline: 1.1910x; 1.0209x over previous
import numpy as np
import ml_dtypes

import concourse.bass as bass
from concourse.bass import PSumTensorHandle
import concourse.mybir as mybir
from concourse.bass_utils import run_bass_kernel_spmd

F32 = mybir.dt.float32
BF16 = mybir.dt.bfloat16
I32 = mybir.dt.int32
AF = mybir.ActivationFunctionType
ALU = mybir.AluOpType
AX = mybir.AxisListType
BF = ml_dtypes.bfloat16

N_CORE = 8
B_PER_CORE = 2
S = 2048
D = 1024
NTOK = B_PER_CORE * S
NTT = NTOK // 128
H = 16
DK = 64
N_EXP = 8
D_FF = 4096
D_INNER = 2048
D_STATE = 16
DT_RANK = 64
CONV_K = 4
HID_I8 = 3072
EPS = 1e-5
NEG = -1.0e30

ENGINES = ("sync", "scalar", "vector", "tensor", "gpsimd")


class P:
    def __init__(self, nc, es):
        self.nc = nc
        self.es = es
        self.thunks = {e: [] for e in ENGINES}
        self.sem = {e: es.enter_context(nc.semaphore("sem_" + e)) for e in ENGINES}
        self.NP = 2
        self.streams = {}
        self.gq_hist = []
        self.GQ_DEPTH = 4
        self.cnt = {e: 0 for e in ENGINES}
        self.waited = {e: {} for e in ENGINES}
        self.tiles = {}
        self.consts = {}

    def sb(self, name, pdim, fdim, dtype):
        es = getattr(self, "cur_es", None)
        if es is None:
            return self.nc.alloc_sbuf_tensor(name, [pdim, fdim], dtype)
        return es.enter_context(self.nc.sbuf_tensor(name, [pdim, fdim], dtype))

    import contextlib as _ctx

    @_ctx.contextmanager
    def scope(self):
        import contextlib
        prev = getattr(self, "cur_es", None)
        self.cur_es = contextlib.ExitStack()
        try:
            yield
        finally:
            self.cur_es.close()
            self.cur_es = prev

    def barrier(self):
        tgt = []
        for e in ENGINES:
            if self.cnt[e]:
                tgt.append((("e", e), self.sem[e], self.cnt[e]))
        for skey, st in self.streams.items():
            for i in range(self.NP):
                if st["per"][i]:
                    tgt.append((("d", (skey, i)), st["sems"][i],
                                16 * st["per"][i]))
        for e in ENGINES:
            wm = self.waited[e]
            waits = []
            for key, s, v in tgt:
                if key == ("e", e):
                    continue
                if wm.get(key, -1) < v:
                    wm[key] = v
                    waits.append((s, v))
            if waits:
                def thunk(engine, waits=waits):
                    for s, v in waits:
                        engine.wait_ge(s, v)
                self.thunks[e].append(thunk)

    def pf(self, i):
        key = ("f", i)
        if key not in self.consts:
            self.consts[key] = psum(self.nc, f"pbf32_{i}", 128, 512, F32)
        return self.consts[key]

    def pbf(self, i):
        key = ("b", i)
        if key not in self.consts:
            self.consts[key] = psum(self.nc, f"pbbf_{i}", 128, 1024, BF16)
        return self.consts[key]

    def ccol(self, val):
        if val not in self.consts:
            t = sb(self.nc, f"cc_{len(self.consts)}", 128, 1, F32)
            self.emit("vector", lambda v, t=t, val=val: v.memset(t[:], val),
                      writes=[t])
            self.consts[val] = t
        return self.consts[val]

    def warmup(self, funcs=(AF.Sqrt,)):
        nc = self.nc
        wsc = sb(nc, "warm_sc", 128, 1024, F32)
        wsb = sb(nc, "warm_sb", 128, 512, BF16)
        wcol = sb(nc, "warm_col", 128, 1, F32)
        wps = self.pf(5)
        wpsb = self.pbf(1)
        self.emit("vector", lambda v: v.memset(wsc[:], 1.0), writes=[wsc])
        self.emit("vector", lambda v: v.memset(wcol[:], 1.0), writes=[wcol])
        self.emit("vector", lambda v: v.tensor_copy(out=wsb[:], in_=wsc[:, 0:512]),
                  reads=[wsc], writes=[wsb])
        self.emit("tensor", lambda t: t.matmul(out=wps[:, 0:128], lhsT=wsb[:, 0:128],
                                               rhs=wsb[:, 0:128], start=True,
                                               stop=True), reads=[wsb], writes=[wps])
        self.emit("tensor", lambda t: t.matmul(out=wps[:, 0:128], lhsT=wsc[:, 0:128],
                                               rhs=wsc[:, 0:128], start=True,
                                               stop=True), reads=[wsc], writes=[wps])
        self.emit("gpsimd", lambda g: g.iota(out=wsc[:, 0:128], pattern=[[-1, 128]],
                                             base=0, channel_multiplier=1,
                                             allow_small_or_imprecise_dtypes=True),
                  writes=[wsc])
        self.emit("vector", lambda v: v.tensor_scalar(
            out=wsb[:, 0:128], in0=wsc[:, 0:128], scalar1=0.0, scalar2=None,
            op0=ALU.is_equal), reads=[wsc], writes=[wsb])
        self.emit("tensor", lambda t: t.transpose(out=wpsb[:, 0:128], in_=wsb[:, 0:128],
                                                  identity=wsb[:, 0:128]),
                  reads=[wsb], writes=[wpsb])
        self.emit("scalar", lambda s: s.copy(out=wsc[:, 0:128], in_=wps[:, 0:128]),
                  reads=[wps], writes=[wsc])
        self.emit("vector", lambda v: v.tensor_copy(out=wsb[:, 0:128],
                                                    in_=wpsb[:, 0:128]),
                  reads=[wpsb], writes=[wsb])
        self.emit("vector", lambda v: v.tensor_scalar(
            out=wsc[:], in0=wsc[:], scalar1=wcol[:, 0:1], scalar2=wcol[:, 0:1],
            op0=ALU.subtract, op1=ALU.mult), reads=[wsc, wcol], writes=[wsc])
        self.emit("vector", lambda v: v.scalar_tensor_tensor(
            out=wsc[:], in0=wsc[:], scalar=1.0, in1=wsc[:], op0=ALU.bypass,
            op1=ALU.mult, accum_out=wcol[:, 0:1]), reads=[wsc], writes=[wsc, wcol])
        self.emit("vector", lambda v: v.reduce_max(out=wcol[:], in_=wsc[:], axis=AX.X,
                                                   negate=True),
                  reads=[wsc], writes=[wcol])
        self.emit("vector", lambda v: v.reciprocal(out=wcol[:], in_=wcol[:]),
                  reads=[wcol], writes=[wcol])
        self.emit("vector", lambda v: v.tensor_tensor_scan(
            out=wsb[:], data0=wsb[:], data1=wsb[:], initial=0.0, op0=ALU.mult,
            op1=ALU.add), reads=[wsb], writes=[wsb])
        self.emit("gpsimd", lambda g: g.iota(out=wsb[:], pattern=[[1, 512]], base=0,
                                             channel_multiplier=1,
                                             allow_small_or_imprecise_dtypes=True),
                  writes=[wsb])
        self.emit("vector", lambda v: v.memset(wsc[:], 1.0), writes=[wsc])
        for f in funcs:
            self.emit("scalar", lambda s, f=f: s.activation(out=wsc[:], in_=wsc[:],
                                                            func=f),
                      reads=[wsc], writes=[wsc])
        for k in range(12):
            self.emit("scalar", lambda s: s.activation(out=wsc[:], in_=wsc[:],
                                                       func=AF.Copy),
                      reads=[wsc], writes=[wsc])
        self.emit("vector", lambda v: v.tensor_copy(out=wsb[:], in_=wsc[:, 0:512]),
                  reads=[wsc], writes=[wsb])
        self.emit("gpsimd", lambda g: g.tensor_copy(out=wsc[:, 0:512], in_=wsc[:, 512:1024]),
                  reads=[wsc], writes=[wsc])
        self.emit("tensor", lambda t: t.matmul(out=wps[:, 0:128], lhsT=wsb[:, 0:128],
                                               rhs=wsb[:, 0:128], start=True,
                                               stop=True), reads=[wsb], writes=[wps])
        self.emit("scalar", lambda s: s.copy(out=wsc[:, 0:128], in_=wps[:, 0:128]),
                  reads=[wps], writes=[wsc])

    def _sem_val(self, tok):
        kind, key, c = tok
        if kind == "e":
            return (self.sem[key], c, ("e", key))
        skey, idx = key
        return (self.streams[skey]["sems"][idx], 16 * c, ("d", key))

    def _entry(self, h):
        k = id(h)
        if k not in self.tiles:
            self.tiles[k] = {"w": None, "r": []}
        return self.tiles[k]

    def _resolve(self, eng, reads, writes):
        need = []
        for h in reads:
            e = self._entry(h)
            if e["w"] is not None:
                need.append(e["w"])
        for h in writes:
            e = self._entry(h)
            if e["w"] is not None:
                need.append(e["w"])
            need.extend(e["r"])
        waits = []
        wm = self.waited[eng]
        best = {}
        for tok in need:
            if tok[0] == "e" and tok[1] == eng and tok[2] == self.cnt[eng]:
                pass
            sem, val, key = self._sem_val(tok)
            if val > best.get(key, (None, -1))[1]:
                best[key] = (sem, val)
        for key, (sem, val) in best.items():
            if wm.get(key, -1) < val:
                wm[key] = val
                waits.append((sem, val))
        return waits

    def _update(self, tok, reads, writes):
        for h in writes:
            self.tiles[id(h)] = {"w": tok, "r": []}
        for h in reads:
            self._entry(h)["r"].append(tok)

    def emit(self, eng, build, reads=(), writes=()):
        assert not getattr(self, "_building", False), (
            "emit() during build replay (ccol called inside a lambda?)")
        rd, wr = [], list(writes)
        for h in reads:
            if isinstance(h, PSumTensorHandle):
                wr.append(h)
            else:
                rd.append(h)
        reads, writes = rd, wr
        waits = self._resolve(eng, reads, writes)
        sem = self.sem[eng]
        self.cnt[eng] += 1
        tok = ("e", eng, self.cnt[eng])

        def thunk(engine, waits=waits, build=build, sem=sem):
            for s, v in waits:
                engine.wait_ge(s, v)
            inst = build(engine)
            inst.then_inc(sem, 1)

        self.thunks[eng].append(thunk)
        self._update(tok, reads, writes)
        return tok

    def dma(self, eng, out, in_, accum=False, allow_slow=False, transpose=False):
        self._dma_transpose = transpose
        reads = [in_.tensor]
        writes = [out.tensor]
        if accum:
            reads.append(out.tensor)
        waits = self._resolve(eng, reads, writes)
        nbytes = 1
        for d in out.shape:
            nbytes *= int(d)
        nbytes *= np.dtype(mybir.dt.np(out.dtype)).itemsize
        skey = (eng, nbytes)
        if skey not in self.streams:
            self.streams[skey] = {
                "sems": [self.es.enter_context(
                    self.nc.semaphore(f"dma_{eng}_{len(self.streams)}_{i}"))
                    for i in range(self.NP)],
                "cnt": 0, "per": [0] * self.NP}
        st = self.streams[skey]
        idx = st["cnt"] % self.NP
        st["cnt"] += 1
        st["per"][idx] += 1
        sem = st["sems"][idx]
        tok = ("d", (skey, idx), st["per"][idx])
        kw = {}
        if accum:
            kw["accum_op"] = ALU.add
        if allow_slow:
            kw["allow_slow_non_contiguous"] = True
        if getattr(self, "_dma_transpose", False):
            kw["transpose"] = True

        gq = None
        if eng == "gpsimd":
            if len(self.gq_hist) >= self.GQ_DEPTH:
                gq = self.gq_hist[-self.GQ_DEPTH]
            self.gq_hist.append((sem, 16 * st["per"][idx]))

        def thunk(engine, waits=waits, out=out, in_=in_, sem=sem, kw=kw, gq=gq):
            for s, v in waits:
                engine.wait_ge(s, v)
            if gq is not None:
                engine.wait_ge(gq[0], gq[1])
            engine.dma_start(out=out, in_=in_, **kw).then_inc(sem, 16)

        self.thunks[eng].append(thunk)
        self._update(tok, reads, writes)
        return tok

    def idma(self, tag, build, reads=(), writes=()):
        eng = "gpsimd"
        waits = self._resolve(eng, list(reads), list(writes))
        skey = (eng, tag)
        if skey not in self.streams:
            self.streams[skey] = {
                "sems": [self.es.enter_context(
                    self.nc.semaphore(f"idma_{tag}_{i}"))
                    for i in range(self.NP)],
                "cnt": 0, "per": [0] * self.NP}
        st = self.streams[skey]
        idx = st["cnt"] % self.NP
        st["cnt"] += 1
        st["per"][idx] += 1
        sem = st["sems"][idx]
        tok = ("d", (skey, idx), st["per"][idx])

        if len(self.gq_hist) >= self.GQ_DEPTH:
            gq = self.gq_hist[-self.GQ_DEPTH]
        else:
            gq = None
        self.gq_hist.append((sem, 16 * st["per"][idx]))

        def thunk(engine, waits=waits, build=build, sem=sem, gq=gq):
            for s, v in waits:
                engine.wait_ge(s, v)
            if gq is not None:
                engine.wait_ge(gq[0], gq[1])
            build(engine).then_inc(sem, 16)

        self.thunks[eng].append(thunk)
        self._update(tok, list(reads), list(writes))
        return tok

    def finish(self):
        fins = []
        for skey, st in self.streams.items():
            for i in range(self.NP):
                if st["per"][i]:
                    fins.append((st["sems"][i], 16 * st["per"][i]))
        for e in ENGINES:
            if self.cnt[e] and e != "gpsimd":
                fins.append((self.sem[e], self.cnt[e]))

        def thunk(engine, fins=fins):
            for s, v in fins:
                engine.wait_ge(s, v)

        self.thunks["gpsimd"].append(thunk)

    def build(self):
        self.finish()
        self._building = True
        with self.nc.Block() as block:
            for eng in ENGINES:
                lst = self.thunks[eng]
                if not lst:
                    continue
                dec = getattr(block, eng)

                def body(engine, lst=lst):
                    for t in lst:
                        t(engine)

                dec(body)


def sb(nc, name, pdim, fdim, dtype):
    return nc.alloc_sbuf_tensor(name, [pdim, fdim], dtype)


def psum(nc, name, pdim=128, fdim=512, dtype=F32):
    return nc.alloc_psum_tensor(name, [pdim, fdim], dtype)


def build_identity(p, dtype, name):
    nc = p.nc
    idn = p.sb(name, 128, 128, dtype)
    iot = p.sb(name + "_i", 128, 128, F32)
    p.emit("gpsimd", lambda g: g.iota(out=iot[:], pattern=[[-1, 128]], base=0,
                                      channel_multiplier=1,
                                      allow_small_or_imprecise_dtypes=True),
           writes=[iot])
    p.emit("vector", lambda v: v.tensor_scalar(out=idn[:], in0=iot[:], scalar1=0.0,
                                               scalar2=None, op0=ALU.is_equal),
           reads=[iot], writes=[idn])
    return idn


def build_ones_row(p, name="ones_row", n=128):
    ones = p.sb(name, 1, n, F32)
    p.emit("vector", lambda v: v.memset(ones[:], 1.0), writes=[ones])
    return ones


def build_onehot0(p, name="onehot0"):
    oh = p.sb(name, 8, 128, F32)
    p.emit("vector", lambda v: v.memset(oh[:], 0.0), writes=[oh])
    p.emit("vector", lambda v: v.memset(oh[0:1, :], 1.0), reads=[oh], writes=[oh])
    return oh


def phase_ln(p, tag, x_hbm, xnT_hbm, ident_f32, gate=None, rstd2_hbm=None,
             xn_tm_hbm=None):
    nc = p.nc
    xb = [p.sb(tag + f"xb{i}", 128, D, F32) for i in range(2)]
    xn = [p.sb(tag + f"xn{i}", 128, D, F32) for i in range(2)]
    sq = [p.sb(tag + f"sq{i}", 128, D, F32) for i in range(2)]
    ssum = [p.sb(tag + f"ssum{i}", 128, 1, F32) for i in range(2)]
    sqs = [p.sb(tag + f"sqs{i}", 128, 1, F32) for i in range(2)]
    mean = [p.sb(tag + f"mean{i}", 128, 1, F32) for i in range(2)]
    sqd = [p.sb(tag + f"sqd{i}", 128, 1, F32) for i in range(2)]
    nvar = [p.sb(tag + f"nvar{i}", 128, 1, F32) for i in range(2)]
    sd = [p.sb(tag + f"sd{i}", 128, 1, F32) for i in range(2)]
    rstd = [p.sb(tag + f"rstd{i}", 128, 1, F32) for i in range(2)]
    pstr = [p.pf(0), p.pf(1)]
    epscol = p.ccol(EPS)
    if xn_tm_hbm is not None:
        xnb = [p.sb(tag + f"xnb{i}", 128, D, BF16) for i in range(2)]
    stage = [p.sb(tag + f"stg{j}", 128, NTOK, BF16) for j in range(8)]
    if gate is not None:
        gate_w_hbm, gate_b_hbm, glog_hbm = gate
        gw = p.sb(tag + "gw", 128, 8 * 8, F32)
        gb = p.sb(tag + "gb", 8, 8, F32)
        oh0 = build_onehot0(p, tag + "oh0")
        p.emit("vector", lambda v: v.memset(gb[:], 0.0), writes=[gb])
        xnTf = [p.sb(tag + f"xnTf{i}", 128, D, F32) for i in range(2)]
        glog_ps = [p.pf(2), p.pf(3)]
        glog_sb = [p.sb(tag + f"glsb{i}", 128, 8, F32) for i in range(2)]
        for k in range(8):
            p.dma("sync", gw[:, 8 * k:8 * (k + 1)], gate_w_hbm[128 * k:128 * (k + 1), :])
        p.dma("sync", gb[0:1, :], gate_b_hbm[:])
    if rstd2_hbm is not None:
        sq2s = [p.sb(tag + f"sq2s{i}", 128, 1, F32) for i in range(2)]
        sd2 = [p.sb(tag + f"sd2{i}", 128, 1, F32) for i in range(2)]
        rstd2 = [p.sb(tag + f"rstd2{i}", 128, 1, F32) for i in range(2)]
        r2ps = p.pf(4)
        r2row = p.sb(tag + "r2row", 1, NTOK, F32)

    for tt in range(NTT):
        i = tt % 2
        xt, xnt = xb[i], xn[i]
        p.dma("sync", xt[:], x_hbm[128 * tt:128 * (tt + 1), :])
        p.emit("vector", lambda v, xt=xt, i=i: v.reduce_sum(
            out=ssum[i][:], in_=xt[:], axis=AX.X), reads=[xt], writes=[ssum[i]])
        p.emit("vector", lambda v, xt=xt, i=i: v.scalar_tensor_tensor(
            out=sq[i][:], in0=xt[:], scalar=1.0, in1=xt[:], op0=ALU.bypass,
            op1=ALU.mult, accum_out=sqs[i][:, 0:1]), reads=[xt],
            writes=[sq[i], sqs[i]])
        p.emit("vector", lambda v, i=i: v.tensor_scalar(
            out=mean[i][:], in0=ssum[i][:], scalar1=1.0 / D, scalar2=None,
            op0=ALU.mult), reads=[ssum[i]], writes=[mean[i]])
        p.emit("vector", lambda v, i=i: v.tensor_scalar(
            out=sqd[i][:], in0=sqs[i][:], scalar1=1.0 / D, scalar2=None,
            op0=ALU.mult), reads=[sqs[i]], writes=[sqd[i]])
        p.emit("vector", lambda v, i=i: v.scalar_tensor_tensor(
            out=nvar[i][:], in0=mean[i][:], scalar=mean[i][:, 0:1], in1=sqd[i][:],
            op0=ALU.mult, op1=ALU.subtract), reads=[mean[i], sqd[i]],
            writes=[nvar[i]])
        p.emit("scalar", lambda s, i=i: s.activation(
            out=sd[i][:], in_=nvar[i][:], func=AF.Sqrt, scale=-1.0,
            bias=epscol[:, 0:1]), reads=[nvar[i], epscol], writes=[sd[i]])
        p.emit("vector", lambda v, i=i: v.reciprocal(out=rstd[i][:], in_=sd[i][:]),
               reads=[sd[i]], writes=[rstd[i]])
        p.emit("vector", lambda v, xt=xt, xnt=xnt, i=i: v.tensor_scalar(
            out=xnt[:], in0=xt[:], scalar1=mean[i][:, 0:1], scalar2=rstd[i][:, 0:1],
            op0=ALU.subtract, op1=ALU.mult), reads=[xt, mean[i], rstd[i]],
            writes=[xnt])
        if xn_tm_hbm is not None:
            p.emit("vector", lambda v, xnt=xnt, i=i: v.tensor_copy(
                out=xnb[i][:], in_=xnt[:]), reads=[xnt], writes=[xnb[i]])
            p.dma("gpsimd", xn_tm_hbm[128 * tt:128 * (tt + 1), :], xnb[i][:])
        if rstd2_hbm is not None:
            p.emit("vector", lambda v, xnt=xnt, i=i: v.scalar_tensor_tensor(
                out=sq[i][:], in0=xnt[:], scalar=1.0, in1=xnt[:], op0=ALU.bypass,
                op1=ALU.mult, accum_out=sq2s[i][:, 0:1]), reads=[xnt],
                writes=[sq[i], sq2s[i]])
            p.emit("scalar", lambda s, i=i: s.activation(
                out=sd2[i][:], in_=sq2s[i][:], func=AF.Sqrt, scale=1.0 / D,
                bias=epscol[:, 0:1]), reads=[sq2s[i], epscol], writes=[sd2[i]])
            p.emit("vector", lambda v, i=i: v.reciprocal(
                out=rstd2[i][:], in_=sd2[i][:]), reads=[sd2[i]], writes=[rstd2[i]])
            p.emit("tensor", lambda t, i=i: t.transpose(
                out=r2ps[0:1, 0:128], in_=rstd2[i][:], identity=ident_f32[:]),
                reads=[rstd2[i], ident_f32], writes=[r2ps])
            p.emit("scalar", lambda s, tt=tt: s.copy(
                out=r2row[0:1, 128 * tt:128 * (tt + 1)], in_=r2ps[0:1, 0:128]),
                reads=[r2ps], writes=[r2row])
        for j in range(8):
            pt = pstr[j % 2]
            p.emit("tensor", lambda t, xnt=xnt, pt=pt, j=j: t.transpose(
                out=pt[:, 0:128], in_=xnt[:, 128 * j:128 * (j + 1)],
                identity=ident_f32[:]), reads=[xnt, ident_f32], writes=[pt])
            p.emit("vector", lambda v, pt=pt, j=j, tt=tt: v.tensor_copy(
                out=stage[j][:, 128 * tt:128 * (tt + 1)], in_=pt[:, 0:128]),
                reads=[pt], writes=[stage[j]])
            if gate is not None:
                p.emit("scalar", lambda s, pt=pt, j=j, i=i: s.copy(
                    out=xnTf[i][:, 128 * j:128 * (j + 1)], in_=pt[:, 0:128]),
                    reads=[pt], writes=[xnTf[i]])
        if gate is not None:
            gps = glog_ps[i]

            def gate_mms(t, i=i, gps=gps):
                for j in range(8):
                    t.matmul(out=gps[:, 0:8], lhsT=xnTf[i][:, 128 * j:128 * (j + 1)],
                             rhs=gw[:, 8 * j:8 * (j + 1)], start=(j == 0), stop=False)
                return t.matmul(out=gps[:, 0:8], lhsT=oh0[:], rhs=gb[:],
                                start=False, stop=True)
            p.emit("tensor", gate_mms, reads=[xnTf[i], gw, gb, oh0], writes=[gps])
            p.emit("scalar", lambda s, i=i, gps=gps: s.copy(
                out=glog_sb[i][:], in_=gps[:, 0:8]), reads=[gps],
                writes=[glog_sb[i]])
            p.dma("gpsimd", glog_hbm[128 * tt:128 * (tt + 1), :], glog_sb[i][:])
    for j in range(8):
        p.dma("gpsimd", xnT_hbm[128 * j:128 * (j + 1), :], stage[j][:])
    if rstd2_hbm is not None:
        p.dma("gpsimd", rstd2_hbm[:], r2row[:])



def load_weight_T(p, w_hbm, rows, cols, ident_bf, tag, raw=None, wT=None):
    nc = p.nc
    rch = rows // 128
    cch = cols // 128
    if raw is None:
        raw = p.sb(tag + "_raw", 128, rch * cols, BF16)
    if wT is None:
        wT = p.sb(tag + "_T", 128, cch * rows, BF16)
    ps = [p.pbf(0), p.pbf(1)]
    for i in range(rch):
        p.dma("gpsimd", raw[:, i * cols:(i + 1) * cols],
              w_hbm[i * 128:(i + 1) * 128, :])
    for i in range(rch):
        for j in range(cch):
            pt = ps[(i * cch + j) % 2]
            p.emit("tensor", lambda t, i=i, j=j, pt=pt: t.transpose(
                out=pt[:, 0:128], in_=raw[:, i * cols + j * 128:i * cols + (j + 1) * 128],
                identity=ident_bf[:]), reads=[raw, ident_bf], writes=[pt])
            p.emit("scalar", lambda s, i=i, j=j, pt=pt: s.copy(
                out=wT[:, j * rows + i * 128:j * rows + (i + 1) * 128],
                in_=pt[:, 0:128]), reads=[pt], writes=[wT])
    return wT, raw


def load_weight_direct(p, w_hbm, rows, cols, tag):
    nc = p.nc
    rch = rows // 128
    w = p.sb(tag + "_W", 128, rch * cols, BF16)
    for j in range(rch):
        p.dma("gpsimd", w[:, j * cols:(j + 1) * cols],
              w_hbm[j * 128:(j + 1) * 128, :])
    return w


def load_xnT(p, xnT_hbm, tag):
    xs = [p.sb(tag + f"_x{j}", 128, NTOK, BF16) for j in range(8)]
    for j in range(8):
        p.dma("sync", xs[j][:], xnT_hbm[128 * j:128 * (j + 1), :])
    return xs


def phase_qkv(p, xnT_hbm, wq_hbm, wk_hbm, wv_hbm, qT_hbm, kT_hbm, v_hbm, ident_bf):
    nc = p.nc
    raw = None
    wqT, raw = load_weight_T(p, wq_hbm, D, D, ident_bf, "wq", raw)
    wkT, raw = load_weight_T(p, wk_hbm, D, D, ident_bf, "wk", raw)
    wvT, raw = load_weight_T(p, wv_hbm, D, D, ident_bf, "wv", raw)
    xn = load_xnT(p, xnT_hbm, "p2")
    pps = [p.pf(0), p.pf(1)]
    stg = [p.sb(f"p2stg{i}", 128, NTOK, BF16) for i in range(2)]
    vstg = [p.sb(f"p2vs{i}", 128, D, BF16) for i in range(2)]
    for wi, (wT, oT) in enumerate(((wqT, qT_hbm), (wkT, kT_hbm))):
        for m in range(8):
            st = stg[m % 2]
            for tc in range(8):
                pt = pps[tc % 2]

                def mms(t, wT=wT, m=m, tc=tc, pt=pt):
                    for k in range(8):
                        inst = t.matmul(
                            out=pt[:], lhsT=wT[:, k * D + 128 * m:k * D + 128 * (m + 1)],
                            rhs=xn[k][:, 512 * tc:512 * (tc + 1)],
                            start=(k == 0), stop=(k == 7))
                    return inst
                p.emit("tensor", mms, reads=[wT] + xn, writes=[pt])
                p.emit("vector", lambda v, st=st, tc=tc, pt=pt: v.tensor_copy(
                    out=st[:, 512 * tc:512 * (tc + 1)], in_=pt[:]),
                    reads=[pt], writes=[st])
            p.dma("gpsimd", oT[128 * m:128 * (m + 1), :], st[:])
    for tt in range(NTT):
        vst = vstg[tt % 2]
        for n in range(2):
            pt = pps[n % 2]

            def mms(t, tt=tt, n=n, pt=pt):
                for k in range(8):
                    inst = t.matmul(
                        out=pt[:], lhsT=xn[k][:, 128 * tt:128 * (tt + 1)],
                        rhs=wvT[:, k * D + 512 * n:k * D + 512 * (n + 1)],
                        start=(k == 0), stop=(k == 7))
                return inst
            p.emit("tensor", mms, reads=[wvT] + xn, writes=[pt])
            p.emit("vector", lambda v, vst=vst, n=n, pt=pt: v.tensor_copy(
                out=vst[:, 512 * n:512 * (n + 1)], in_=pt[:]),
                reads=[pt], writes=[vst])
        p.dma("gpsimd", v_hbm[128 * tt:128 * (tt + 1), :], vst[:])


def phase_att(p, qT_hbm, kT_hbm, v_hbm, slopes_hbm, aoT_hbm, ident_bf):
    nc = p.nc
    KW = 384
    rel = p.sb("rel", 128, KW, F32)
    tmpm = p.sb("tmpm", 128, KW, F32)
    maskadd = p.sb("maskadd", 128, KW, F32)
    bias = p.sb("bias", 128, H * KW, F32)
    oh0 = build_onehot0(p, "att_oh0")
    sl8 = p.sb("sl8", 8, 16, F32)
    slbc = p.sb("slbc", 128, 16, F32)
    slps = p.pf(2)
    p.emit("gpsimd", lambda g: g.iota(out=rel[:], pattern=[[-1, KW]], base=128,
                                      channel_multiplier=1,
                                      allow_small_or_imprecise_dtypes=True),
           writes=[rel])
    p.emit("vector", lambda v: v.memset(sl8[:], 0.0), writes=[sl8])
    p.dma("sync", sl8[0:1, :], slopes_hbm[:])
    p.emit("tensor", lambda t: t.matmul(out=slps[:, 0:16], lhsT=oh0[:], rhs=sl8[:],
                                        start=True, stop=True),
           reads=[oh0, sl8], writes=[slps])
    p.emit("scalar", lambda s: s.copy(out=slbc[:], in_=slps[:, 0:16]),
           reads=[slps], writes=[slbc])
    p.emit("vector", lambda v: v.tensor_scalar(out=tmpm[:], in0=rel[:], scalar1=128.0,
                                               scalar2=None, op0=ALU.is_gt),
           reads=[rel], writes=[tmpm])
    p.emit("vector", lambda v: v.tensor_scalar(out=maskadd[:], in0=rel[:],
                                               scalar1=-128.0, scalar2=None,
                                               op0=ALU.is_lt),
           reads=[rel], writes=[maskadd])
    p.emit("vector", lambda v: v.tensor_add(out=maskadd[:], in0=maskadd[:],
                                            in1=tmpm[:]),
           reads=[tmpm, maskadd], writes=[maskadd])
    p.emit("vector", lambda v: v.tensor_scalar(out=maskadd[:], in0=maskadd[:],
                                               scalar1=NEG, scalar2=None,
                                               op0=ALU.mult),
           reads=[maskadd], writes=[maskadd])
    for h in range(H):
        p.emit("scalar", lambda s, h=h: s.mul(out=bias[:, KW * h:KW * (h + 1)],
                                              in_=rel[:], mul=slbc[:, h:h + 1]),
               reads=[rel, slbc], writes=[bias])
        p.emit("vector", lambda v, h=h: v.tensor_add(
            out=bias[:, KW * h:KW * (h + 1)], in0=bias[:, KW * h:KW * (h + 1)],
            in1=maskadd[:]), reads=[maskadd, bias], writes=[bias])
    qb = [p.sb(f"att_q{j}", 128, S, BF16) for j in range(8)]
    kb = [p.sb(f"att_k{j}", 128, S, BF16) for j in range(8)]
    vb = [p.sb(f"att_v{j}", 128, D, BF16) for j in range(16)]
    ao = [p.sb(f"att_o{j}", 128, S, BF16) for j in range(8)]
    ND = 4
    scs = [p.sb(f"att_sc{i}", 128, KW, F32) for i in range(ND)]
    esc = [p.sb(f"att_esc{i}", 128, KW, F32) for i in range(ND)]
    attb = [p.sb(f"att_ab{i}", 128, KW, BF16) for i in range(ND)]
    attT = [p.sb(f"att_aT{i}", 128, KW, BF16) for i in range(ND)]
    nmx = [p.sb(f"att_nm{i}", 128, 1, F32) for i in range(ND)]
    ssm = [p.sb(f"att_ss{i}", 128, 1, F32) for i in range(ND)]
    rec = [p.sb(f"att_rc{i}", 128, 1, F32) for i in range(ND)]
    psc = [p.pf(0), p.pf(1), p.pf(3), p.pf(5)]
    ptr = [p.pbf(0), p.pbf(1)]
    pav = [p.pf(2), p.pf(4)]
    NQT = S // 128
    for b in range(B_PER_CORE):
        c0 = b * S
        for j in range(8):
            p.dma("sync", qb[j][:], qT_hbm[128 * j:128 * (j + 1), c0:c0 + S])
            p.dma("sync", kb[j][:], kT_hbm[128 * j:128 * (j + 1), c0:c0 + S])
        for j in range(16):
            p.dma("sync", vb[j][:], v_hbm[c0 + 128 * j:c0 + 128 * (j + 1), :])
        iters = []
        for h in range(H):
            for qt in range(NQT):
                qs = 128 * qt
                k0 = qs - 128
                cb0 = 0
                if qt == 0:
                    k0, kw, cb0 = 0, 256, 128
                elif qt == NQT - 1:
                    kw = 256
                else:
                    kw = KW
                iters.append((h // 2, 64 * (h % 2), h, qs, k0, kw, cb0))
        NIT = len(iters)
        NST = 9

        def stage(s, it):
            ch, r0, h, qs, k0, kw, cb0 = iters[it]
            i = it % ND
            i2 = it % 2
            nch = kw // 128
            if s == 0:
                p.emit("tensor", lambda t: t.matmul(
                    out=psc[i][:, 0:kw], lhsT=qb[ch][r0:r0 + 64, qs:qs + 128],
                    rhs=kb[ch][r0:r0 + 64, k0:k0 + kw], start=True, stop=True),
                    reads=[qb[ch], kb[ch]], writes=[psc[i]])
            elif s == 1:
                p.emit("vector", lambda v: v.scalar_tensor_tensor(
                    out=scs[i][:, 0:kw], in0=psc[i][:, 0:kw], scalar=0.125,
                    in1=bias[:, KW * h + cb0:KW * h + cb0 + kw],
                    op0=ALU.mult, op1=ALU.add),
                    reads=[psc[i], bias], writes=[scs[i]])
                p.emit("vector", lambda v: v.reduce_max(
                    out=nmx[i][:], in_=scs[i][:, 0:kw], axis=AX.X, negate=True),
                    reads=[scs[i]], writes=[nmx[i]])
            elif s == 2:
                p.emit("scalar", lambda s_: s_.activation(
                    out=esc[i][:, 0:kw], in_=scs[i][:, 0:kw], func=AF.Exp,
                    bias=nmx[i][:, 0:1], accum_out=ssm[i][:, 0:1]),
                    reads=[scs[i], nmx[i]], writes=[esc[i], ssm[i]])
            elif s == 3:
                p.emit("vector", lambda v: v.reciprocal(
                    out=rec[i][:], in_=ssm[i][:]), reads=[ssm[i]], writes=[rec[i]])
            elif s == 4:
                p.emit("scalar", lambda s_: s_.activation(
                    out=attb[i][:, 0:kw], in_=esc[i][:, 0:kw], func=AF.Copy,
                    scale=rec[i][:, 0:1]), reads=[esc[i], rec[i]],
                    writes=[attb[i]])
            elif s == 5:
                for c in range(nch):
                    p.emit("tensor", lambda t, c=c: t.transpose(
                        out=ptr[i2][:, 128 * c:128 * (c + 1)],
                        in_=attb[i][:, 128 * c:128 * (c + 1)],
                        identity=ident_bf[:]),
                        reads=[attb[i], ident_bf], writes=[ptr[i2]])
            elif s == 6:
                p.emit("scalar", lambda s_: s_.copy(
                    out=attT[i][:, 0:kw], in_=ptr[i2][:, 0:kw]),
                    reads=[ptr[i2]], writes=[attT[i]])
            elif s == 7:
                def avmms(t):
                    for c in range(nch):
                        kt = (k0 + 128 * c) // 128
                        inst = t.matmul(
                            out=pav[i2][0:64, 0:128],
                            lhsT=vb[kt][:, 64 * h:64 * (h + 1)],
                            rhs=attT[i][:, 128 * c:128 * (c + 1)],
                            start=(c == 0), stop=(c == nch - 1))
                    return inst
                p.emit("tensor", avmms, reads=[attT[i]] + vb, writes=[pav[i2]])
            elif s == 8:
                p.emit("vector", lambda v: v.tensor_copy(
                    out=ao[ch][r0:r0 + 64, qs:qs + 128],
                    in_=pav[i2][0:64, 0:128]),
                    reads=[pav[i2]], writes=[ao[ch]])

        for slot in range(NIT + NST - 1):
            for s in range(NST - 1, -1, -1):
                it = slot - s
                if 0 <= it < NIT:
                    stage(s, it)
        for j in range(8):
            p.dma("gpsimd", aoT_hbm[128 * j:128 * (j + 1), c0:c0 + S], ao[j][:])

def phase_wo(p, aoT_hbm, wo_hbm, s_hbm, ident_bf, ident_f32):
    nc = p.nc
    woT, _ = load_weight_T(p, wo_hbm, D, D, ident_bf, "wo")
    ao = load_xnT(p, aoT_hbm, "p4")
    pps = [p.pf(0), p.pf(1)]
    ptr = [p.pf(2), p.pf(3)]
    sT = [p.sb(f"p4sT{i}", 128, 512, F32) for i in range(2)]
    stg = [p.sb(f"p4stg{i}", 128, D, F32) for i in range(4)]
    for tc in range(8):
        for m in range(8):
            i = (tc * 8 + m) % 2

            def mms(t, m=m, tc=tc, i=i):
                for k in range(8):
                    inst = t.matmul(
                        out=pps[i][:], lhsT=woT[:, k * D + 128 * m:k * D + 128 * (m + 1)],
                        rhs=ao[k][:, 512 * tc:512 * (tc + 1)],
                        start=(k == 0), stop=(k == 7))
                return inst
            p.emit("tensor", mms, reads=[woT] + ao, writes=[pps[i]])
            p.emit("vector", lambda v, i=i: v.tensor_copy(out=sT[i][:], in_=pps[i][:]),
                   reads=[pps[i]], writes=[sT[i]])
            for u in range(4):
                p.emit("tensor", lambda t, i=i, u=u: t.transpose(
                    out=ptr[i][:, 0:128], in_=sT[i][:, 128 * u:128 * (u + 1)],
                    identity=ident_f32[:]), reads=[sT[i], ident_f32], writes=[ptr[i]])
                p.emit("scalar", lambda s, i=i, u=u, m=m: s.copy(
                    out=stg[u][:, 128 * m:128 * (m + 1)], in_=ptr[i][:, 0:128]),
                    reads=[ptr[i]], writes=[stg[u]])
        for u in range(4):
            tt = tc * 4 + u
            p.dma("gpsimd", s_hbm[128 * tt:128 * (tt + 1), :], stg[u][:])


def _run(build_fn, in_map_fn, out_names, n_cores=N_CORE):
    import contextlib
    es = contextlib.ExitStack()
    nc = bass.Bass("TRN2", target_bir_lowering=False)
    with es:
        p = P(nc, es)
        build_fn(p)
        p.build()
        in_maps = [in_map_fn(c) for c in range(n_cores)]
        res = run_bass_kernel_spmd(nc, in_maps, core_ids=list(range(n_cores)))
    return [[res.results[c][n] for c in range(n_cores)] for n in out_names]


def phase_prep_ffnw(p, wg_hbm, wu_hbm, wd_hbm, wgT_hbm, wuT_hbm, wdT_hbm,
                    ident_bf):
    raw = p.sb("pf_raw", 128, 24 * 1024, BF16)
    wT = p.sb("pf_wT", 128, 8 * HID_I8, BF16)
    for w_hbm, wT_hbm, rows, cols in ((wg_hbm, wgT_hbm, HID_I8, D),
                                      (wu_hbm, wuT_hbm, HID_I8, D),
                                      (wd_hbm, wdT_hbm, D, HID_I8)):
        load_weight_T(p, w_hbm, rows, cols, ident_bf, "pf", raw=raw, wT=wT)
        cch = cols // 128
        for j in range(cch):
            p.dma("gpsimd", wT_hbm[128 * j:128 * (j + 1), :],
                  wT[:, j * rows:(j + 1) * rows])


def phase_moe(p, xnT_hbm, glog_hbm, w1_hbm, b1_hbm, w2_hbm, b2_hbm, gT_hbm,
              ident_f32):
    nc = p.nc
    TCW = 512
    NTC = NTOK // TCW
    gl = [p.sb(f"m_gl{i}", 128, N_EXP, F32) for i in range(2)]
    nmx = [p.sb(f"m_nm{i}", 128, 1, F32) for i in range(2)]
    ssm = [p.sb(f"m_ss{i}", 128, 1, F32) for i in range(2)]
    rec = [p.sb(f"m_rc{i}", 128, 1, F32) for i in range(2)]
    pb = [p.sb(f"m_pb{i}", 128, N_EXP, F32) for i in range(2)]
    mx2 = [p.sb(f"m_mx{i}", 128, 1, F32) for i in range(2)]
    msk = [p.sb(f"m_msk{i}", 128, N_EXP, F32) for i in range(2)]
    wgt = [p.sb(f"m_wgt{i}", 128, N_EXP, F32) for i in range(2)]
    wgtT = p.sb("m_wgtT", 8, NTOK, F32)
    ptr = p.pf(4)
    for tt in range(NTT):
        i = tt % 2
        p.dma("sync", gl[i][:], glog_hbm[128 * tt:128 * (tt + 1), :])
        p.emit("vector", lambda v, i=i: v.reduce_max(
            out=nmx[i][:], in_=gl[i][:], axis=AX.X, negate=True),
            reads=[gl[i]], writes=[nmx[i]])
        p.emit("scalar", lambda s, i=i: s.activation(
            out=pb[i][:], in_=gl[i][:], func=AF.Exp, bias=nmx[i][:, 0:1],
            accum_out=ssm[i][:, 0:1]), reads=[gl[i], nmx[i]],
            writes=[pb[i], ssm[i]])
        p.emit("vector", lambda v, i=i: v.reciprocal(out=rec[i][:], in_=ssm[i][:]),
               reads=[ssm[i]], writes=[rec[i]])
        p.emit("vector", lambda v, i=i: v.tensor_scalar(
            out=pb[i][:], in0=pb[i][:], scalar1=rec[i][:, 0:1], scalar2=None,
            op0=ALU.mult), reads=[pb[i], rec[i]], writes=[pb[i]])
        p.emit("vector", lambda v, i=i: v.reduce_max(
            out=mx2[i][:], in_=pb[i][:], axis=AX.X),
            reads=[pb[i]], writes=[mx2[i]])
        p.emit("vector", lambda v, i=i: v.tensor_scalar(
            out=msk[i][:], in0=pb[i][:], scalar1=mx2[i][:, 0:1], scalar2=None,
            op0=ALU.is_equal), reads=[pb[i], mx2[i]], writes=[msk[i]])
        p.emit("vector", lambda v, i=i: v.tensor_mul(
            out=wgt[i][:], in0=pb[i][:], in1=msk[i][:]),
            reads=[pb[i], msk[i]], writes=[wgt[i]])
        p.emit("tensor", lambda t, i=i: t.transpose(
            out=ptr[0:8, 0:128], in_=wgt[i][:], identity=ident_f32[:]),
            reads=[wgt[i], ident_f32], writes=[ptr])
        p.emit("scalar", lambda s, tt=tt: s.copy(
            out=wgtT[:, 128 * tt:128 * (tt + 1)], in_=ptr[0:8, 0:128]),
            reads=[ptr], writes=[wgtT])
    ohs = p.sb("m_ohs", 8, 8 * 128, F32)
    ohi = p.sb("m_ohi", 8, 8 * 128, F32)
    p.emit("gpsimd", lambda g: g.iota(out=ohi[:], pattern=[[-1, 8], [0, 128]],
                                      base=0, channel_multiplier=1,
                                      allow_small_or_imprecise_dtypes=True),
           writes=[ohi])
    p.emit("vector", lambda v: v.tensor_scalar(out=ohs[:], in0=ohi[:], scalar1=0.0,
                                               scalar2=None, op0=ALU.is_equal),
           reads=[ohi], writes=[ohs])
    NW = 3
    w1c = [p.sb(f"m_w1c{i}", 128, 8 * 128, BF16) for i in range(NW)]
    w1b = [p.sb(f"m_w1b{i}", 128, 8 * 128, BF16) for i in range(NW)]
    w2c = [p.sb(f"m_w2c{i}", 128, 32 * 128, BF16) for i in range(NW)]
    xsl = [p.sb(f"m_xsl{i}", 128, 8 * TCW, BF16) for i in range(2)]
    h_sb = [p.sb(f"m_h{i}", 128, 32 * TCW, BF16) for i in range(2)]
    wbc = [p.sb(f"m_wbc{i}", 128, TCW, F32) for i in range(2)]
    b1r = p.sb("m_b1r", 32, 128, F32)
    b2r = p.sb("m_b2r", 8, 128, F32)
    b1a = p.sb("m_b1a", 128, 8 * 32, F32)
    b2a = p.sb("m_b2a", 128, 8 * 8, F32)
    gacc = p.sb("m_gacc", 128, 8 * TCW, F32)
    t2 = [p.sb(f"m_t2{i}", 128, TCW, F32) for i in range(2)]
    hps = [p.pf(0), p.pf(1)]
    yps = [p.pf(2), p.pf(3)]
    bps = p.pf(4)
    wps = p.pf(5)
    for e in range(N_EXP):
        p.dma("sync", b1r[:], b1_hbm[32 * e:32 * (e + 1), :])
        p.dma("sync", b2r[:], b2_hbm[8 * e:8 * (e + 1), :])
        p.emit("tensor", lambda t: t.transpose(out=bps[0:128, 0:32], in_=b1r[:],
                                               identity=ident_f32[0:32, 0:32]),
               reads=[b1r, ident_f32], writes=[bps])
        p.emit("scalar", lambda s, e=e: s.copy(out=b1a[:, 32 * e:32 * (e + 1)],
                                               in_=bps[0:128, 0:32]),
               reads=[bps], writes=[b1a])
        p.emit("tensor", lambda t: t.transpose(out=bps[0:128, 32:40], in_=b2r[:],
                                               identity=ident_f32[0:8, 0:8]),
               reads=[b2r, ident_f32], writes=[bps])
        p.emit("scalar", lambda s, e=e: s.copy(out=b2a[:, 8 * e:8 * (e + 1)],
                                               in_=bps[0:128, 32:40]),
               reads=[bps], writes=[b2a])
    for tc in range(NTC):
        it = tc % 2
        for k in range(8):
            p.dma("sync", xsl[it][:, TCW * k:TCW * (k + 1)],
                  xnT_hbm[128 * k:128 * (k + 1), TCW * tc:TCW * (tc + 1)])
        p.emit("vector", lambda v: v.memset(gacc[:], 0.0),
               reads=[gacc], writes=[gacc])
        for e in range(N_EXP):
            ie = e % 2
            p.emit("tensor", lambda t, e=e, tc=tc: t.matmul(
                out=wps[:], lhsT=ohs[:, 128 * e:128 * (e + 1)],
                rhs=wgtT[:, TCW * tc:TCW * (tc + 1)], start=True, stop=True),
                reads=[ohs, wgtT], writes=[wps])
            p.emit("scalar", lambda s, ie=ie: s.copy(out=wbc[ie][:], in_=wps[:]),
                   reads=[wps], writes=[wbc[ie]])
            for hc2 in range(16):
                iw = hc2 % NW
                p.dma("gpsimd",
                      w1c[iw][:].rearrange("p (k m) -> p k m", k=8),
                      w1_hbm[D * e:D * (e + 1),
                             256 * hc2:256 * hc2 + 128].rearrange(
                                 "(k p) m -> p k m", p=128))
                p.dma("gpsimd",
                      w1b[iw][:].rearrange("p (k m) -> p k m", k=8),
                      w1_hbm[D * e:D * (e + 1),
                             256 * hc2 + 128:256 * (hc2 + 1)].rearrange(
                                 "(k p) m -> p k m", p=128))

                def mm_h(t, it=it, iw=iw):
                    for k in range(8):
                        t.matmul(
                            out=hps[0][:], lhsT=w1c[iw][:, 128 * k:128 * (k + 1)],
                            rhs=xsl[it][:, TCW * k:TCW * (k + 1)],
                            start=(k == 0), stop=(k == 7))
                    for k in range(8):
                        inst = t.matmul(
                            out=hps[1][:], lhsT=w1b[iw][:, 128 * k:128 * (k + 1)],
                            rhs=xsl[it][:, TCW * k:TCW * (k + 1)],
                            start=(k == 0), stop=(k == 7))
                    return inst
                p.emit("tensor", mm_h, reads=[w1c[iw], w1b[iw], xsl[it]],
                       writes=[hps[0], hps[1]])
                p.emit("scalar", lambda s, hc2=hc2, ie=ie, e=e: s.activation(
                    out=h_sb[ie][:, TCW * 2 * hc2:TCW * (2 * hc2 + 1)],
                    in_=hps[0][:], func=AF.Gelu,
                    bias=b1a[:, 32 * e + 2 * hc2:32 * e + 2 * hc2 + 1]),
                    reads=[hps[0], b1a], writes=[h_sb[ie]])
                p.emit("scalar", lambda s, hc2=hc2, ie=ie, e=e: s.activation(
                    out=h_sb[ie][:, TCW * (2 * hc2 + 1):TCW * (2 * hc2 + 2)],
                    in_=hps[1][:], func=AF.Gelu,
                    bias=b1a[:, 32 * e + 2 * hc2 + 1:32 * e + 2 * hc2 + 2]),
                    reads=[hps[1], b1a], writes=[h_sb[ie]])
            for dc in range(8):
                iw = dc % NW
                i2 = dc % 2
                yp = yps[i2]
                p.dma("gpsimd",
                      w2c[iw][:].rearrange("p (k m) -> p k m", k=32),
                      w2_hbm[D_FF * e:D_FF * (e + 1),
                             128 * dc:128 * (dc + 1)].rearrange(
                                 "(k p) m -> p k m", p=128))

                def mm_y(t, ie=ie, yp=yp, iw=iw):
                    for k in range(32):
                        inst = t.matmul(
                            out=yp[:], lhsT=w2c[iw][:, 128 * k:128 * (k + 1)],
                            rhs=h_sb[ie][:, TCW * k:TCW * (k + 1)],
                            start=(k == 0), stop=(k == 31))
                    return inst
                p.emit("tensor", mm_y, reads=[w2c[iw], h_sb[ie]], writes=[yp])
                p.emit("vector", lambda v, i2=i2, ie=ie, e=e, dc=dc, yp=yp:
                       v.scalar_tensor_tensor(
                           out=t2[i2][:], in0=yp[:],
                           scalar=b2a[:, 8 * e + dc:8 * e + dc + 1],
                           in1=wbc[ie][:], op0=ALU.add, op1=ALU.mult),
                       reads=[yp, b2a, wbc[ie]], writes=[t2[i2]])
                p.emit("vector", lambda v, i2=i2, dc=dc: v.tensor_add(
                    out=gacc[:, TCW * dc:TCW * (dc + 1)],
                    in0=gacc[:, TCW * dc:TCW * (dc + 1)], in1=t2[i2][:]),
                    reads=[t2[i2], gacc], writes=[gacc])
        for dc in range(8):
            p.dma("gpsimd",
                  gT_hbm[128 * dc:128 * (dc + 1), TCW * tc:TCW * (tc + 1)],
                  gacc[:, TCW * dc:TCW * (dc + 1)])


NSLOT = 15
NPAD = NSLOT * 512


def phase_moe_routed(p, xn_tm_hbm, glog_hbm, w1_hbm, b1_hbm, w2_hbm, b2r8_hbm,
                     g_hbm, gpos_hbm, spos_hbm, wgtg_hbm, xpk_hbm, ident_f32,
                     ident_bf):
    nc = p.nc
    from concourse.bass_core_rust import IndirectOffsetOnAxis as IOA
    gl = [p.sb(f"r_gl{i}", 128, N_EXP, F32) for i in range(2)]
    nmx = [p.sb(f"r_nm{i}", 128, 1, F32) for i in range(2)]
    ssm = [p.sb(f"r_ss{i}", 128, 1, F32) for i in range(2)]
    rec = [p.sb(f"r_rc{i}", 128, 1, F32) for i in range(2)]
    pb = [p.sb(f"r_pb{i}", 128, N_EXP, F32) for i in range(2)]
    mx2 = [p.sb(f"r_mx{i}", 128, 1, F32) for i in range(2)]
    msk = [p.sb(f"r_msk{i}", 128, N_EXP, F32) for i in range(2)]
    wgt = [p.sb(f"r_wgt{i}", 128, N_EXP, F32) for i in range(2)]
    wcol_all = p.sb("r_wca", 128, NTT, F32)
    msk8 = p.sb("r_msk8", 8, NTOK, F32)
    ptA = p.pf(4)
    for tt in range(NTT):
        i = tt % 2
        p.dma("sync", gl[i][:], glog_hbm[128 * tt:128 * (tt + 1), :])
        p.emit("vector", lambda v, i=i: v.reduce_max(
            out=nmx[i][:], in_=gl[i][:], axis=AX.X, negate=True),
            reads=[gl[i]], writes=[nmx[i]])
        p.emit("scalar", lambda s, i=i: s.activation(
            out=pb[i][:], in_=gl[i][:], func=AF.Exp, bias=nmx[i][:, 0:1],
            accum_out=ssm[i][:, 0:1]), reads=[gl[i], nmx[i]],
            writes=[pb[i], ssm[i]])
        p.emit("vector", lambda v, i=i: v.reciprocal(out=rec[i][:], in_=ssm[i][:]),
               reads=[ssm[i]], writes=[rec[i]])
        p.emit("vector", lambda v, i=i: v.tensor_scalar(
            out=pb[i][:], in0=pb[i][:], scalar1=rec[i][:, 0:1], scalar2=None,
            op0=ALU.mult), reads=[pb[i], rec[i]], writes=[pb[i]])
        p.emit("vector", lambda v, i=i: v.reduce_max(
            out=mx2[i][:], in_=pb[i][:], axis=AX.X),
            reads=[pb[i]], writes=[mx2[i]])
        p.emit("vector", lambda v, i=i: v.tensor_scalar(
            out=msk[i][:], in0=pb[i][:], scalar1=mx2[i][:, 0:1], scalar2=None,
            op0=ALU.is_equal), reads=[pb[i], mx2[i]], writes=[msk[i]])
        p.emit("vector", lambda v, i=i: v.tensor_mul(
            out=wgt[i][:], in0=pb[i][:], in1=msk[i][:]),
            reads=[pb[i], msk[i]], writes=[wgt[i]])
        p.emit("vector", lambda v, i=i, tt=tt: v.reduce_sum(
            out=wcol_all[:, tt:tt + 1], in_=wgt[i][:], axis=AX.X),
            reads=[wgt[i]], writes=[wcol_all])
        p.emit("tensor", lambda t, i=i: t.transpose(
            out=ptA[0:8, 0:128], in_=msk[i][:], identity=ident_f32[:]),
            reads=[msk[i], ident_f32], writes=[ptA])
        p.emit("scalar", lambda s, tt=tt: s.copy(
            out=msk8[:, 128 * tt:128 * (tt + 1)], in_=ptA[0:8, 0:128]),
            reads=[ptA], writes=[msk8])
    R = p.sb("r_R", 8, NTOK, F32)
    counts = p.sb("r_cnt", 8, 1, F32)
    t1c = p.sb("r_t1", 8, 1, F32)
    frc = p.sb("r_fr", 8, 1, F32)
    ceilc = p.sb("r_ceil", 8, 1, F32)
    excl = p.sb("r_excl", 8, 1, F32)
    rowst = p.sb("r_rwst", 8, 1, F32)
    Lstr = p.sb("r_Lstr", 8, 8, F32)
    Lio = p.sb("r_Lio", 8, 8, F32)
    ones8 = p.sb("r_on8", 8, 1, F32)
    iotaS = p.sb("r_ioS", 8, NSLOT, F32)
    cmp = p.sb("r_cmp", 8, NSLOT, F32)
    T2E = p.sb("r_T2E", NSLOT, 1, F32)
    tpad = p.sb("r_tpad", 8, NSLOT, F32)
    T2Ebc = p.sb("r_T2Eb", 128, NSLOT, F32)
    e1024 = p.sb("r_e1k", 128, NSLOT, F32)
    e4096 = p.sb("r_e4k", 128, NSLOT, F32)
    e32 = p.sb("r_e32", 128, NSLOT, F32)
    oh0 = build_onehot0(p, "r_oh0")
    oh0b = p.sb("r_oh0b", 8, 128, BF16)
    p.emit("vector", lambda v: v.tensor_copy(out=oh0b[:], in_=oh0[:]),
           reads=[oh0], writes=[oh0b])
    p.emit("vector", lambda v: v.tensor_tensor_scan(
        out=R[:], data0=msk8[:], data1=msk8[:], initial=0.0, op0=ALU.add,
        op1=ALU.bypass), reads=[msk8], writes=[R])
    p.emit("vector", lambda v: v.reduce_sum(out=counts[:], in_=msk8[:], axis=AX.X),
           reads=[msk8], writes=[counts])
    thr = p.sb("r_thr", 8, 8, F32)
    p.emit("gpsimd", lambda g: g.iota(out=thr[:], pattern=[[512, 8]], base=0,
                                      channel_multiplier=0,
                                      allow_small_or_imprecise_dtypes=True),
           writes=[thr])
    p.emit("vector", lambda v: v.tensor_scalar(
        out=t1c[:, 0:8] if False else thr[:], in0=thr[:],
        scalar1=counts[:, 0:1], scalar2=None, op0=ALU.is_lt),
        reads=[thr, counts], writes=[thr])
    p.emit("vector", lambda v: v.reduce_sum(out=ceilc[:], in_=thr[:], axis=AX.X),
           reads=[thr], writes=[ceilc])
    p.emit("gpsimd", lambda g: g.iota(out=Lio[:], pattern=[[1, 8]], base=-1,
                                      channel_multiplier=-1,
                                      allow_small_or_imprecise_dtypes=True),
           writes=[Lio])
    p.emit("vector", lambda v: v.tensor_scalar(out=Lstr[:], in0=Lio[:],
                                               scalar1=0.0, scalar2=None,
                                               op0=ALU.is_ge),
           reads=[Lio], writes=[Lstr])
    p.emit("vector", lambda v: v.memset(ones8[:], 1.0), writes=[ones8])
    p.emit("tensor", lambda t: t.matmul(out=ptA[0:8, 0:1], lhsT=Lstr[:],
                                        rhs=ceilc[:], start=True, stop=True),
           reads=[Lstr, ceilc], writes=[ptA])
    p.emit("scalar", lambda s: s.copy(out=excl[:], in_=ptA[0:8, 0:1]),
           reads=[ptA], writes=[excl])
    p.emit("vector", lambda v: v.tensor_scalar(
        out=rowst[:], in0=excl[:], scalar1=512.0, scalar2=None, op0=ALU.mult),
        reads=[excl], writes=[rowst])
    p.emit("gpsimd", lambda g: g.iota(out=iotaS[:], pattern=[[1, NSLOT]], base=0,
                                      channel_multiplier=0,
                                      allow_small_or_imprecise_dtypes=True),
           writes=[iotaS])
    p.emit("vector", lambda v: v.tensor_scalar(
        out=cmp[:], in0=iotaS[:], scalar1=excl[:, 0:1], scalar2=None,
        op0=ALU.is_ge), reads=[iotaS, excl], writes=[cmp])
    p.emit("tensor", lambda t: t.matmul(out=ptA[0:1, 0:NSLOT], lhsT=ones8[:],
                                        rhs=cmp[:], start=True, stop=True),
           reads=[cmp, ones8], writes=[ptA])
    p.emit("vector", lambda v: v.memset(tpad[:], 0.0), writes=[tpad])
    p.emit("scalar", lambda s: s.copy(out=tpad[0:1, :], in_=ptA[0:1, 0:NSLOT]),
           reads=[ptA], writes=[tpad])
    p.emit("tensor", lambda t: t.matmul(out=ptA[0:128, 0:NSLOT], lhsT=oh0[:],
                                        rhs=tpad[:], start=True, stop=True),
           reads=[oh0, tpad], writes=[ptA])
    p.emit("scalar", lambda s: s.copy(out=T2Ebc[:], in_=ptA[0:128, 0:NSLOT]),
           reads=[ptA], writes=[T2Ebc])
    p.emit("vector", lambda v: v.tensor_scalar(
        out=T2Ebc[:], in0=T2Ebc[:], scalar1=1.0, scalar2=None,
        op0=ALU.subtract), reads=[T2Ebc], writes=[T2Ebc])
    for dst, f in ((e1024, 1024.0), (e4096, 4096.0), (e32, 128.0)):
        p.emit("vector", lambda v, dst=dst, f=f: v.tensor_scalar(
            out=dst[:], in0=T2Ebc[:], scalar1=f, scalar2=None, op0=ALU.mult),
            reads=[T2Ebc], writes=[dst])
    p.emit("vector", lambda v: v.tensor_scalar(
        out=R[:], in0=R[:], scalar1=rowst[:, 0:1], scalar2=1.0, op0=ALU.add,
        op1=ALU.subtract), reads=[R, rowst], writes=[R])
    p.emit("vector", lambda v: v.tensor_mul(out=R[:], in0=R[:], in1=msk8[:]),
           reads=[msk8, R], writes=[R])
    zrow = p.sb("r_zrow", 128, 2 * NPAD // 128, I32)
    p.emit("vector", lambda v: v.memset(zrow[:], 0), writes=[zrow])
    p.dma("gpsimd", gpos_hbm.rearrange("(p a) c -> p (a c)", p=128), zrow[:])
    p.emit("gpsimd", lambda g: g.iota(out=zrow[:], pattern=[[0, 2 * NPAD // 128]],
                                      base=NTOK, channel_multiplier=1),
           reads=[zrow], writes=[zrow])
    p.dma("gpsimd", spos_hbm.rearrange("(p a) c -> p (a c)", p=128), zrow[:])
    zf = p.sb("r_zf", 128, 2 * NPAD // 128, F32)
    p.emit("vector", lambda v: v.memset(zf[:], 0.0), writes=[zf])
    p.dma("gpsimd", wgtg_hbm.rearrange("(p a) c -> p (a c)", p=128), zf[:])
    Rt = [p.sb(f"r_Rt{i}", 128, 8, F32) for i in range(2)]
    destf = [p.sb(f"r_df{i}", 128, 1, F32) for i in range(2)]
    desti = [p.sb(f"r_di{i}", 128, 1, I32) for i in range(2)]
    iotok = [p.sb(f"r_it{i}", 128, 2, I32) for i in range(2)]
    wcol2 = [p.sb(f"r_w2c{i}", 128, 2, F32) for i in range(2)]
    for tt in range(NTT):
        i = tt % 2
        p.emit("tensor", lambda t, tt=tt: t.transpose(
            out=ptA[0:128, 0:8], in_=R[:, 128 * tt:128 * (tt + 1)],
            identity=ident_f32[0:8, 0:8]), reads=[R, ident_f32], writes=[ptA])
        p.emit("scalar", lambda s, i=i: s.copy(out=Rt[i][:], in_=ptA[0:128, 0:8]),
               reads=[ptA], writes=[Rt[i]])
        p.emit("vector", lambda v, i=i: v.reduce_sum(
            out=destf[i][:], in_=Rt[i][:], axis=AX.X),
            reads=[Rt[i]], writes=[destf[i]])
        p.emit("vector", lambda v, i=i: v.tensor_copy(
            out=desti[i][:], in_=destf[i][:]), reads=[destf[i]], writes=[desti[i]])
        p.emit("gpsimd", lambda g, i=i, tt=tt: g.iota(
            out=iotok[i][:], pattern=[[0, 2]], base=128 * tt,
            channel_multiplier=1), writes=[iotok[i]])
        p.emit("vector", lambda v, i=i, tt=tt: v.tensor_copy(
            out=wcol2[i][:, 0:1], in_=wcol_all[:, tt:tt + 1]),
            reads=[wcol_all], writes=[wcol2[i]])
        p.emit("vector", lambda v, i=i, tt=tt: v.tensor_copy(
            out=wcol2[i][:, 1:2], in_=wcol_all[:, tt:tt + 1]),
            reads=[wcol_all], writes=[wcol2[i]])
        p.idma("scat", lambda g, i=i: g.indirect_dma_start(
            out=gpos_hbm, out_offset=IOA(desti[i][:], 0), in_=iotok[i][:],
            in_offset=None), reads=[desti[i], iotok[i]],
            writes=[gpos_hbm.tensor])
        p.idma("scat", lambda g, i=i: g.indirect_dma_start(
            out=spos_hbm, out_offset=IOA(desti[i][:], 0), in_=iotok[i][:],
            in_offset=None), reads=[desti[i], iotok[i]],
            writes=[spos_hbm.tensor])
        p.idma("scat", lambda g, i=i: g.indirect_dma_start(
            out=wgtg_hbm, out_offset=IOA(desti[i][:], 0),
            in_=wcol2[i][:], in_offset=None),
            reads=[desti[i], wcol2[i]],
            writes=[wgtg_hbm.tensor])
    io8 = p.sb("r_io8", 128, 8, F32)
    io32 = p.sb("r_io32", 128, 32, F32)
    io1 = p.sb("r_io1", 128, 1, F32)
    p.emit("gpsimd", lambda g: g.iota(out=io8[:], pattern=[[128, 8]], base=0,
                                      channel_multiplier=1,
                                      allow_small_or_imprecise_dtypes=True),
           writes=[io8])
    p.emit("gpsimd", lambda g: g.iota(out=io32[:], pattern=[[128, 32]], base=0,
                                      channel_multiplier=1,
                                      allow_small_or_imprecise_dtypes=True),
           writes=[io32])
    p.emit("gpsimd", lambda g: g.iota(out=io1[:], pattern=[[1, 1]], base=0,
                                      channel_multiplier=1,
                                      allow_small_or_imprecise_dtypes=True),
           writes=[io1])
    xpk_stage = [p.sb(f"r_xps{i}", 128, D, BF16) for i in range(2)]
    wif = [p.sb(f"r_wif{i}", 128, 42, F32) for i in range(2)]
    wii = [p.sb(f"r_wii{i}", 128, 42, I32) for i in range(2)]
    soidx = [p.sb(f"r_so{u}", 128, 1, I32) for u in range(8)]
    wcu = [p.sb(f"r_wc{u}", 128, 1, F32) for u in range(8)]
    xgT = [p.sb(f"r_xgT{i}", 128, 8 * 512, BF16) for i in range(2)]
    w1g = [p.sb(f"r_w1g{j}", 128, D_FF, BF16) for j in range(8)]
    w1cp = None
    NW2 = 24
    w2g = [p.sb(f"r_w2g{i}", 128, 512, BF16) for i in range(NW2)]

    b1c = [p.sb(f"r_b1c{i}", 128, 32, F32) for i in range(2)]
    b2g8 = [p.sb(f"r_b2g8{i}", 8, D, BF16) for i in range(2)]
    h_sb = p.sb("r_h", 128, 32 * 512, BF16)
    ytm = [p.sb(f"r_ytm{i}", 128, 512, F32) for i in range(4)]
    hps = [p.pf(0), p.pf(1)]
    yps = [p.pf(2), p.pf(3), p.pf(4), p.pf(5)]
    ptb = [p.pbf(0), p.pbf(1)]
    p.emit("vector", lambda v: v.memset(b2g8[0][:], 0.0), writes=[b2g8[0]])
    p.emit("vector", lambda v: v.memset(b2g8[1][:], 0.0), writes=[b2g8[1]])
    pk_idx = [p.sb(f"r_pki{i}", 128, 1, I32) for i in range(2)]
    for c in range(NSLOT * 4):
        ic = c % 2
        p.dma("sync", pk_idx[ic][:], gpos_hbm[128 * c:128 * (c + 1), 0:1],
              allow_slow=True)
        p.idma("xg", lambda g, ic=ic: g.indirect_dma_start(
            out=xpk_stage[ic][:], out_offset=None, in_=xn_tm_hbm,
            in_offset=IOA(pk_idx[ic][:], 0)),
            reads=[pk_idx[ic], xn_tm_hbm.tensor], writes=[xpk_stage[ic]])
        p.dma("gpsimd", xpk_hbm[128 * c:128 * (c + 1), :], xpk_stage[ic][:])
    import os as _os
    _nslot = int(_os.environ.get("K_SLOTS", str(NSLOT)))
    _part = int(_os.environ.get("K_PART", "5"))
    def prefetch(s):
        sl = s % 2
        wv, wi = wif[sl], wii[sl]
        p.emit("vector", lambda v, s=s, wv=wv: v.tensor_scalar(
            out=wv[:, 0:8], in0=io8[:], scalar1=e1024[:, s:s + 1],
            scalar2=None, op0=ALU.add), reads=[io8, e1024], writes=[wv])
        p.emit("vector", lambda v, s=s, wv=wv: v.tensor_scalar(
            out=wv[:, 8:40], in0=io32[:], scalar1=e4096[:, s:s + 1],
            scalar2=None, op0=ALU.add), reads=[io32, e4096], writes=[wv])
        p.emit("vector", lambda v, s=s, wv=wv: v.tensor_scalar(
            out=wv[:, 40:41], in0=io1[:], scalar1=e32[:, s:s + 1],
            scalar2=None, op0=ALU.add), reads=[io1, e32], writes=[wv])
        p.emit("vector", lambda v, s=s, wv=wv: v.tensor_copy(
            out=wv[:, 41:42], in_=T2Ebc[:, s:s + 1]), reads=[T2Ebc],
            writes=[wv])
        p.emit("vector", lambda v, wv=wv, wi=wi: v.tensor_copy(
            out=wi[:], in_=wv[:]), reads=[wv], writes=[wi])
        for u in range(4):
            r0 = 512 * s + 128 * u
            p.dma("sync", soidx[4 * sl + u][:], spos_hbm[r0:r0 + 128, 0:1],
                  allow_slow=True)
            p.dma("sync", wcu[4 * sl + u][:], wgtg_hbm[r0:r0 + 128, 0:1],
                  allow_slow=True)
        for j in range(8):
            p.dma("sync", xgT[sl][:, 512 * j:512 * (j + 1)],
                  xpk_hbm[512 * s:512 * (s + 1), 128 * j:128 * (j + 1)],
                  transpose=True)
        if _os.environ.get("K_W1PLAIN", "0") == "1":
            for j in range(8):
                p.dma("gpsimd", w1g[j][:], w1_hbm[128 * j:128 * (j + 1), :])
        else:
            for j in range(8):
                p.idma("w1", lambda g, j=j, wi=wi: g.indirect_dma_start(
                    out=w1g[j][:], out_offset=None, in_=w1_hbm,
                    in_offset=IOA(wi[:, j:j + 1], 0)),
                    reads=[wi, w1_hbm.tensor], writes=[w1g[j]])
        p.idma("b1", lambda g, wi=wi, sl=sl: g.indirect_dma_start(
            out=b1c[sl][:], out_offset=None, in_=b1_hbm,
            in_offset=IOA(wi[:, 40:41], 0)),
            reads=[wi, b1_hbm.tensor], writes=[b1c[sl]])
        p.idma("b2", lambda g, wi=wi, sl=sl: g.indirect_dma_start(
            out=b2g8[sl][:], out_offset=None, in_=b2r8_hbm,
            in_offset=IOA(wi[0:8, 41:42], 0)),
            reads=[wi, b2r8_hbm.tensor], writes=[b2g8[sl]])

    if _part >= 2:
        prefetch(0)
    for s in range(_nslot):
        sl = s % 2
        wi = wii[sl]
        if _part < 3:
            continue
        w1u = w1g
        _nhc = int(_os.environ.get("K_HC", "16"))
        for hc2 in range(_nhc):

            def mm_h(t, hc2=hc2, sl=sl):
                _rhs = lambda k: xgT[sl][:, 512 * k:512 * (k + 1)]
                for k in range(8):
                    t.matmul(out=hps[0][:],
                             lhsT=w1u[k][:, 256 * hc2:256 * hc2 + 128],
                             rhs=_rhs(k),
                             start=(k == 0), stop=(k == 7))
                for k in range(8):
                    inst = t.matmul(out=hps[1][:],
                                    lhsT=w1u[k][:, 256 * hc2 + 128:256 * (hc2 + 1)],
                                    rhs=_rhs(k),
                                    start=(k == 0), stop=(k == 7))
                return inst
            p.emit("tensor", mm_h, reads=list(w1u) + [xgT[sl]],
                   writes=[hps[0], hps[1]])
            p.emit("scalar", lambda s_, hc2=hc2, sl=sl: s_.activation(
                out=h_sb[:, 512 * 2 * hc2:512 * (2 * hc2 + 1)], in_=hps[0][:],
                func=AF.Gelu, bias=b1c[sl][:, 2 * hc2:2 * hc2 + 1]),
                reads=[hps[0], b1c[sl]], writes=[h_sb])
            p.emit("scalar", lambda s_, hc2=hc2, sl=sl: s_.activation(
                out=h_sb[:, 512 * (2 * hc2 + 1):512 * (2 * hc2 + 2)],
                in_=hps[1][:], func=AF.Gelu,
                bias=b1c[sl][:, 2 * hc2 + 1:2 * hc2 + 2]),
                reads=[hps[1], b1c[sl]], writes=[h_sb])
        if _part < 4:
            continue

        def y_mms(dh):
            for k in range(32):
                i2 = (32 * dh + k) % NW2
                p.idma("w2", lambda g, k=k, dh=dh, i2=i2, wi=wi:
                       g.indirect_dma_start(
                           out=w2g[i2][:], out_offset=None, in_=w2_hbm,
                           in_offset=IOA(wi[:, 8 + k:9 + k], 0),
                           element_offset=512 * dh),
                       reads=[wi, w2_hbm.tensor], writes=[w2g[i2]])

                def mm_y(t, k=k, i2=i2):
                    for u in range(4):
                        inst = t.matmul(
                            out=yps[u][:], lhsT=h_sb[:, 512 * k + 128 * u:
                                                     512 * k + 128 * (u + 1)],
                            rhs=w2g[i2][:], start=(k == 0), stop=False,
                            skip_group_check=True)
                    return inst
                p.emit("tensor", mm_y, reads=[h_sb, w2g[i2]],
                       writes=[yps[0], yps[1], yps[2], yps[3]])

        def y_close(dh):
            def mm_b2(t, dh=dh, sl=sl):
                for u in range(4):
                    inst = t.matmul(out=yps[u][:], lhsT=oh0b[:],
                                    rhs=b2g8[sl][:, 512 * dh:512 * (dh + 1)],
                                    start=False, stop=True,
                                    skip_group_check=True)
                return inst
            p.emit("tensor", mm_b2, reads=[oh0b, b2g8[sl]],
                   writes=[yps[0], yps[1], yps[2], yps[3]])
            if _part < 5:
                return
            for u in range(4):
                p.emit("vector", lambda v, u=u, sl=sl: v.tensor_scalar(
                    out=ytm[u][:], in0=yps[u][:],
                    scalar1=wcu[4 * sl + u][:, 0:1],
                    scalar2=None, op0=ALU.mult),
                    reads=[yps[u], wcu[4 * sl + u]], writes=[ytm[u]])

        def y_scat(dh):
            if _part < 5:
                return
            for u in range(4):
                p.idma("yscat", lambda g, u=u, dh=dh, sl=sl:
                       g.indirect_dma_start(
                           out=g_hbm, out_offset=IOA(soidx[4 * sl + u][:], 0),
                           in_=ytm[u][:], in_offset=None,
                           element_offset=512 * dh),
                       reads=[soidx[4 * sl + u], ytm[u]],
                       writes=[g_hbm.tensor])

        y_mms(0)
        if s + 1 < _nslot:
            prefetch(s + 1)
        y_close(0)
        y_mms(1)
        y_scat(0)
        y_close(1)
        y_scat(1)

def phase_combine_ln(p, x_hbm, s_hbm, gT_hbm, x1_hbm, xnT_hbm, rstd2_hbm,
                     ident_f32, g_token_major=False):
    nc = p.nc
    tag = "c6"
    xb = [p.sb(tag + f"x{i}", 128, D, F32) for i in range(2)]
    sbuf_s = [p.sb(tag + f"s{i}", 128, D, F32) for i in range(2)]
    gtile = [p.sb(tag + f"g{i}", 128, D, F32) for i in range(2)]
    gblk = [p.sb(tag + f"gb{i}", 128, 128, F32) for i in range(2)]
    sg = [p.sb(tag + f"sg{i}", 128, D, F32) for i in range(2)]
    x1 = [p.sb(tag + f"x1{i}", 128, D, F32) for i in range(2)]
    xn = [p.sb(tag + f"xn{i}", 128, D, F32) for i in range(2)]
    sq = [p.sb(tag + f"sq{i}", 128, D, F32) for i in range(2)]
    ssum = [p.sb(tag + f"ssum{i}", 128, 1, F32) for i in range(2)]
    sqs = [p.sb(tag + f"sqs{i}", 128, 1, F32) for i in range(2)]
    mean = [p.sb(tag + f"mean{i}", 128, 1, F32) for i in range(2)]
    sqd = [p.sb(tag + f"sqd{i}", 128, 1, F32) for i in range(2)]
    nvar = [p.sb(tag + f"nvar{i}", 128, 1, F32) for i in range(2)]
    sd = [p.sb(tag + f"sd{i}", 128, 1, F32) for i in range(2)]
    rstd = [p.sb(tag + f"rstd{i}", 128, 1, F32) for i in range(2)]
    sq2s = [p.sb(tag + f"sq2s{i}", 128, 1, F32) for i in range(2)]
    sd2 = [p.sb(tag + f"sd2{i}", 128, 1, F32) for i in range(2)]
    rstd2 = [p.sb(tag + f"rstd2{i}", 128, 1, F32) for i in range(2)]
    r2row = p.sb(tag + "r2row", 1, NTOK, F32)
    stage = [p.sb(tag + f"stg{j}", 128, NTOK, BF16) for j in range(8)]
    pstr = [p.pf(0), p.pf(1)]
    ptr2 = [p.pf(2), p.pf(3)]
    r2ps = p.pf(4)
    epscol = p.ccol(EPS)
    for tt in range(NTT):
        i = tt % 2
        xt, st, gt, x1t, xnt = xb[i], sbuf_s[i], gtile[i], x1[i], xn[i]
        p.dma("sync", xt[:], x_hbm[128 * tt:128 * (tt + 1), :])
        p.dma("sync", st[:], s_hbm[128 * tt:128 * (tt + 1), :])
        if g_token_major:
            p.dma("sync", gt[:], gT_hbm[128 * tt:128 * (tt + 1), :])
        else:
            for j in range(8):
                gb = gblk[j % 2]
                p.dma("sync", gb[:], gT_hbm[128 * j:128 * (j + 1),
                                            128 * tt:128 * (tt + 1)])
                pt = ptr2[j % 2]
                p.emit("tensor", lambda t, gb=gb, pt=pt: t.transpose(
                    out=pt[:, 0:128], in_=gb[:], identity=ident_f32[:]),
                    reads=[gb, ident_f32], writes=[pt])
                p.emit("scalar", lambda s, pt=pt, j=j, gt=gt: s.copy(
                    out=gt[:, 128 * j:128 * (j + 1)], in_=pt[:, 0:128]),
                    reads=[pt], writes=[gt])
        p.emit("scalar", lambda s, gt=gt, i=i: s.activation(
            out=sg[i][:], in_=gt[:], func=AF.Silu), reads=[gt], writes=[sg[i]])
        p.emit("vector", lambda v, st=st, i=i: v.tensor_mul(
            out=sg[i][:], in0=sg[i][:], in1=st[:]),
            reads=[st, sg[i]], writes=[sg[i]])
        p.emit("vector", lambda v, xt=xt, i=i, x1t=x1t: v.tensor_add(
            out=x1t[:], in0=xt[:], in1=sg[i][:]),
            reads=[xt, sg[i]], writes=[x1t])
        p.dma("gpsimd", x1_hbm[128 * tt:128 * (tt + 1), :], x1t[:])
        p.emit("vector", lambda v, x1t=x1t, i=i: v.reduce_sum(
            out=ssum[i][:], in_=x1t[:], axis=AX.X), reads=[x1t], writes=[ssum[i]])
        p.emit("vector", lambda v, x1t=x1t, i=i: v.scalar_tensor_tensor(
            out=sq[i][:], in0=x1t[:], scalar=1.0, in1=x1t[:], op0=ALU.bypass,
            op1=ALU.mult, accum_out=sqs[i][:, 0:1]), reads=[x1t],
            writes=[sq[i], sqs[i]])
        p.emit("vector", lambda v, i=i: v.tensor_scalar(
            out=mean[i][:], in0=ssum[i][:], scalar1=1.0 / D, scalar2=None,
            op0=ALU.mult), reads=[ssum[i]], writes=[mean[i]])
        p.emit("vector", lambda v, i=i: v.tensor_scalar(
            out=sqd[i][:], in0=sqs[i][:], scalar1=1.0 / D, scalar2=None,
            op0=ALU.mult), reads=[sqs[i]], writes=[sqd[i]])
        p.emit("vector", lambda v, i=i: v.scalar_tensor_tensor(
            out=nvar[i][:], in0=mean[i][:], scalar=mean[i][:, 0:1], in1=sqd[i][:],
            op0=ALU.mult, op1=ALU.subtract), reads=[mean[i], sqd[i]],
            writes=[nvar[i]])
        p.emit("scalar", lambda s, i=i: s.activation(
            out=sd[i][:], in_=nvar[i][:], func=AF.Sqrt, scale=-1.0,
            bias=epscol[:, 0:1]), reads=[nvar[i], epscol], writes=[sd[i]])
        p.emit("vector", lambda v, i=i: v.reciprocal(out=rstd[i][:], in_=sd[i][:]),
               reads=[sd[i]], writes=[rstd[i]])
        p.emit("vector", lambda v, x1t=x1t, xnt=xnt, i=i: v.tensor_scalar(
            out=xnt[:], in0=x1t[:], scalar1=mean[i][:, 0:1],
            scalar2=rstd[i][:, 0:1], op0=ALU.subtract, op1=ALU.mult),
            reads=[x1t, mean[i], rstd[i]], writes=[xnt])
        p.emit("vector", lambda v, xnt=xnt, i=i: v.scalar_tensor_tensor(
            out=sq[i][:], in0=xnt[:], scalar=1.0, in1=xnt[:], op0=ALU.bypass,
            op1=ALU.mult, accum_out=sq2s[i][:, 0:1]), reads=[xnt],
            writes=[sq[i], sq2s[i]])
        p.emit("scalar", lambda s, i=i: s.activation(
            out=sd2[i][:], in_=sq2s[i][:], func=AF.Sqrt, scale=1.0 / D,
            bias=epscol[:, 0:1]), reads=[sq2s[i], epscol], writes=[sd2[i]])
        p.emit("vector", lambda v, i=i: v.reciprocal(
            out=rstd2[i][:], in_=sd2[i][:]), reads=[sd2[i]], writes=[rstd2[i]])
        p.emit("tensor", lambda t, i=i: t.transpose(
            out=r2ps[0:1, 0:128], in_=rstd2[i][:], identity=ident_f32[:]),
            reads=[rstd2[i], ident_f32], writes=[r2ps])
        p.emit("scalar", lambda s, tt=tt: s.copy(
            out=r2row[0:1, 128 * tt:128 * (tt + 1)], in_=r2ps[0:1, 0:128]),
            reads=[r2ps], writes=[r2row])
        for j in range(8):
            pt = pstr[j % 2]
            p.emit("tensor", lambda t, xnt=xnt, pt=pt, j=j: t.transpose(
                out=pt[:, 0:128], in_=xnt[:, 128 * j:128 * (j + 1)],
                identity=ident_f32[:]), reads=[xnt, ident_f32], writes=[pt])
            p.emit("vector", lambda v, pt=pt, j=j, tt=tt: v.tensor_copy(
                out=stage[j][:, 128 * tt:128 * (tt + 1)], in_=pt[:, 0:128]),
                reads=[pt], writes=[stage[j]])
    for j in range(8):
        p.dma("gpsimd", xnT_hbm[128 * j:128 * (j + 1), :], stage[j][:])
    p.dma("gpsimd", rstd2_hbm[:], r2row[:])


def phase_mamba_in(p, xnT_hbm, rstd2_hbm, normw_hbm, win_hbm, convw_hbm,
                   convb_hbm, xinT_hbm, resT_hbm, xsiluT_hbm, ident_f32):
    nc = p.nc
    tag = "c7"
    oh0 = build_onehot0(p, tag + "oh0")
    winm = [p.sb(tag + f"winm{i}", 128, 8 * 128, BF16) for i in range(2)]
    r2pad = p.sb(tag + "r2p", 8, NTOK, F32)
    r2bc = p.sb(tag + "r2bc", 128, NTOK, BF16)
    rps = p.pf(4)
    p.emit("vector", lambda v: v.memset(r2pad[:], 0.0), writes=[r2pad])
    p.dma("sync", r2pad[0:1, :], rstd2_hbm[:])
    for c in range(NTOK // 512):
        p.emit("tensor", lambda t, c=c: t.matmul(
            out=rps[:], lhsT=oh0[:], rhs=r2pad[:, 512 * c:512 * (c + 1)],
            start=True, stop=True), reads=[oh0, r2pad], writes=[rps])
        p.emit("scalar", lambda s, c=c: s.copy(
            out=r2bc[:, 512 * c:512 * (c + 1)], in_=rps[:]),
            reads=[rps], writes=[r2bc])
    nwr = p.sb(tag + "nwr", 8, 128, F32)
    nwc = p.sb(tag + "nwc", 128, 8, F32)
    p.dma("sync", nwr[:], normw_hbm[:])
    p.emit("tensor", lambda t: t.transpose(out=rps[0:128, 0:8], in_=nwr[:],
                                           identity=ident_f32[0:8, 0:8]),
           reads=[nwr, ident_f32], writes=[rps])
    p.emit("scalar", lambda s: s.copy(out=nwc[:], in_=rps[0:128, 0:8]),
           reads=[rps], writes=[nwc])
    hT = p.sb(tag + "hT", 128, 8 * NTOK, BF16)
    xc = [p.sb(tag + f"xc{i}", 128, NTOK, BF16) for i in range(2)]
    t1 = [p.sb(tag + f"t1{i}", 128, NTOK, BF16) for i in range(2)]
    for k in range(8):
        i = k % 2
        p.dma("sync", xc[i][:], xnT_hbm[128 * k:128 * (k + 1), :])
        p.emit("vector", lambda v, i=i, k=k: v.tensor_scalar(
            out=t1[i][:], in0=xc[i][:], scalar1=nwc[:, k:k + 1], scalar2=None,
            op0=ALU.mult), reads=[xc[i], nwc], writes=[t1[i]])
        p.emit("vector", lambda v, i=i, k=k: v.tensor_mul(
            out=hT[:, NTOK * k:NTOK * (k + 1)], in0=t1[i][:], in1=r2bc[:]),
            reads=[t1[i], r2bc], writes=[hT])
    pps = [p.pf(0), p.pf(1)]
    stg = [p.sb(tag + f"stg{i}", 128, NTOK, BF16) for i in range(2)]
    for m in range(32):
        st = stg[m % 2]
        wm = winm[m % 2]
        p.dma("gpsimd", wm[:].rearrange("p (k f) -> p k f", k=8),
              win_hbm[:, 128 * m:128 * (m + 1)].rearrange("(k p) f -> p k f",
                                                          p=128))
        for tc in range(8):
            pt = pps[tc % 2]

            def mms(t, m=m, tc=tc, pt=pt, wm=wm):
                for k in range(8):
                    inst = t.matmul(
                        out=pt[:], lhsT=wm[:, 128 * k:128 * (k + 1)],
                        rhs=hT[:, NTOK * k + 512 * tc:NTOK * k + 512 * (tc + 1)],
                        start=(k == 0), stop=(k == 7))
                return inst
            p.emit("tensor", mms, reads=[wm, hT], writes=[pt])
            p.emit("vector", lambda v, st=st, tc=tc, pt=pt: v.tensor_copy(
                out=st[:, 512 * tc:512 * (tc + 1)], in_=pt[:]),
                reads=[pt], writes=[st])
        if m < 16:
            p.dma("gpsimd", xinT_hbm[128 * m:128 * (m + 1), :], st[:])
        else:
            p.dma("gpsimd", resT_hbm[128 * (m - 16):128 * (m - 15), :], st[:])
    cw = p.sb(tag + "cw", 128, 4 * 16, F32)
    cbr = p.sb(tag + "cbr", 16, 128, F32)
    cbc = p.sb(tag + "cbc", 128, 16, F32)
    for dc in range(16):
        p.dma("sync", cw[:, 4 * dc:4 * (dc + 1)],
              convw_hbm[128 * dc:128 * (dc + 1), :])
    p.dma("sync", cbr[:], convb_hbm[:])
    p.emit("tensor", lambda t: t.transpose(out=rps[0:128, 8:24], in_=cbr[:],
                                           identity=ident_f32[0:16, 0:16]),
           reads=[cbr, ident_f32], writes=[rps])
    p.emit("scalar", lambda s: s.copy(out=cbc[:], in_=rps[0:128, 8:24]),
           reads=[rps], writes=[cbc])
    xpad = [p.sb(tag + f"xp{i}", 128, 3 + S, BF16) for i in range(2)]
    acc = [p.sb(tag + f"ac{i}", 128, S, F32) for i in range(2)]
    xsl = [p.sb(tag + f"xs{i}", 128, S, BF16) for i in range(2)]
    for b in range(B_PER_CORE):
        for dc in range(16):
            i = (b * 16 + dc) % 2
            xp, at, xo = xpad[i], acc[i], xsl[i]
            p.emit("vector", lambda v, xp=xp: v.memset(xp[:, 0:3], 0.0),
                   reads=[xp], writes=[xp])
            p.dma("sync", xp[:, 3:3 + S],
                  xinT_hbm[128 * dc:128 * (dc + 1), S * b:S * (b + 1)])
            p.emit("vector", lambda v, xp=xp, at=at, dc=dc: v.tensor_scalar(
                out=at[:], in0=xp[:, 0:S], scalar1=cw[:, 4 * dc:4 * dc + 1],
                scalar2=None, op0=ALU.mult), reads=[xp, cw], writes=[at])
            for j in range(1, 4):
                p.emit("vector", lambda v, xp=xp, at=at, dc=dc, j=j:
                       v.scalar_tensor_tensor(
                           out=at[:], in0=xp[:, j:j + S],
                           scalar=cw[:, 4 * dc + j:4 * dc + j + 1], in1=at[:],
                           op0=ALU.mult, op1=ALU.add),
                       reads=[xp, cw, at], writes=[at])
            p.emit("scalar", lambda s, at=at, xo=xo, dc=dc: s.activation(
                out=xo[:], in_=at[:], func=AF.Silu, bias=cbc[:, dc:dc + 1]),
                reads=[at, cbc], writes=[xo])
            p.dma("gpsimd", xsiluT_hbm[128 * dc:128 * (dc + 1), S * b:S * (b + 1)],
                  xo[:])


def phase_ssm_proj(p, xsiluT_hbm, wx_hbm, wdt_hbm, bdt_hbm, deltaT_hbm, duT_hbm,
                   Bm_hbm, Cm_hbm, ident_f32):
    nc = p.nc
    tag = "c8"
    wx = p.sb(tag + "wx", 128, 16 * 96, BF16)
    for k in range(16):
        p.dma("gpsimd", wx[:, 96 * k:96 * (k + 1)],
              wx_hbm[128 * k:128 * (k + 1), :])
    xs = [p.sb(tag + f"xs{i}", 128, 512, BF16) for i in range(16)]
    pdl = [p.pf(0), p.pf(1)]
    pB = p.pf(2)
    pC = p.pf(3)
    dltb = p.sb(tag + "dltb", 64, NTOK, BF16)
    Bms = p.sb(tag + "Bms", 16, NTOK, F32)
    Cms = p.sb(tag + "Cms", 16, NTOK, F32)
    xst = [p.sb(tag + f"xst{i}", 128, 512, BF16) for i in range(2)]
    for tc in range(8):
        for k in range(16):
            p.dma("sync", xs[k][:],
                  xsiluT_hbm[128 * k:128 * (k + 1), 512 * tc:512 * (tc + 1)])
        pd = pdl[tc % 2]

        def mm_d(t, tc=tc, pd=pd):
            for k in range(16):
                inst = t.matmul(out=pd[0:64, :], lhsT=wx[:, 96 * k:96 * k + 64],
                                rhs=xs[k][:], start=(k == 0), stop=(k == 15))
            return inst
        p.emit("tensor", mm_d, reads=[wx] + xs, writes=[pd])
        p.emit("vector", lambda v, tc=tc, pd=pd: v.tensor_copy(
            out=dltb[:, 512 * tc:512 * (tc + 1)], in_=pd[0:64, :]),
            reads=[pd], writes=[dltb])

        def mm_B(t, tc=tc):
            for k in range(16):
                inst = t.matmul(out=pB[0:16, :], lhsT=wx[:, 96 * k + 64:96 * k + 80],
                                rhs=xs[k][:], start=(k == 0), stop=(k == 15))
            return inst
        p.emit("tensor", mm_B, reads=[wx] + xs, writes=[pB])
        p.emit("scalar", lambda s, tc=tc: s.copy(
            out=Bms[:, 512 * tc:512 * (tc + 1)], in_=pB[0:16, :]),
            reads=[pB], writes=[Bms])

        def mm_C(t, tc=tc):
            for k in range(16):
                inst = t.matmul(out=pC[0:16, :], lhsT=wx[:, 96 * k + 80:96 * k + 96],
                                rhs=xs[k][:], start=(k == 0), stop=(k == 15))
            return inst
        p.emit("tensor", mm_C, reads=[wx] + xs, writes=[pC])
        p.emit("scalar", lambda s, tc=tc: s.copy(
            out=Cms[:, 512 * tc:512 * (tc + 1)], in_=pC[0:16, :]),
            reads=[pC], writes=[Cms])
    p.dma("gpsimd", Bm_hbm[:], Bms[:])
    p.dma("gpsimd", Cm_hbm[:], Cms[:])
    wdt = p.sb(tag + "wdt", 64, D_INNER, BF16)
    p.dma("gpsimd", wdt[:], wdt_hbm[:])
    bdr = p.sb(tag + "bdr", 16, 128, F32)
    bdc = p.sb(tag + "bdc", 128, 16, F32)
    rps = p.pf(4)
    p.dma("sync", bdr[:], bdt_hbm[:])
    p.emit("tensor", lambda t: t.transpose(out=rps[0:128, 0:16], in_=bdr[:],
                                           identity=ident_f32[0:16, 0:16]),
           reads=[bdr, ident_f32], writes=[rps])
    p.emit("scalar", lambda s: s.copy(out=bdc[:], in_=rps[0:128, 0:16]),
           reads=[rps], writes=[bdc])
    onec = p.ccol(1.0)
    pz = [p.pf(0), p.pf(1)]
    negz = [p.sb(tag + f"ng{i}", 128, 512, F32) for i in range(2)]
    nab = [p.sb(tag + f"na{i}", 128, 512, F32) for i in range(2)]
    e1 = [p.sb(tag + f"e1{i}", 128, 512, F32) for i in range(2)]
    l1 = [p.sb(tag + f"l1{i}", 128, 512, F32) for i in range(2)]
    rl = [p.sb(tag + f"rl{i}", 128, 512, F32) for i in range(2)]
    dl = [p.sb(tag + f"dl{i}", 128, 512, F32) for i in range(2)]
    dlb = [p.sb(tag + f"dlb{i}", 128, 512, BF16) for i in range(2)]
    dub = [p.sb(tag + f"du{i}", 128, 512, BF16) for i in range(2)]
    for m in range(16):
        for tc in range(8):
            i = (m * 8 + tc) % 2
            zp = pz[i]
            p.emit("tensor", lambda t, m=m, tc=tc, zp=zp: t.matmul(
                out=zp[:], lhsT=wdt[:, 128 * m:128 * (m + 1)],
                rhs=dltb[:, 512 * tc:512 * (tc + 1)], start=True, stop=True),
                reads=[wdt, dltb], writes=[zp])
            p.emit("vector", lambda v, i=i, m=m, zp=zp: v.tensor_scalar(
                out=rl[i][:], in0=zp[:], scalar1=bdc[:, m:m + 1], scalar2=None,
                op0=ALU.add), reads=[zp, bdc], writes=[rl[i]])
            p.emit("vector", lambda v, i=i: v.tensor_scalar(
                out=negz[i][:], in0=rl[i][:], scalar1=-1.0, scalar2=None,
                op0=ALU.mult), reads=[rl[i]], writes=[negz[i]])
            p.emit("vector", lambda v, i=i: v.tensor_tensor(
                out=nab[i][:], in0=negz[i][:], in1=rl[i][:], op=ALU.min),
                reads=[negz[i], rl[i]], writes=[nab[i]])
            p.emit("scalar", lambda s, i=i: s.activation(
                out=e1[i][:], in_=nab[i][:], func=AF.Exp),
                reads=[nab[i]], writes=[e1[i]])
            p.emit("scalar", lambda s, i=i: s.activation(
                out=l1[i][:], in_=e1[i][:], func=AF.Ln, bias=onec[:, 0:1]),
                reads=[e1[i], onec], writes=[l1[i]])
            p.emit("vector", lambda v, i=i: v.tensor_scalar_max(
                out=rl[i][:], in0=rl[i][:], scalar1=0.0),
                reads=[rl[i]], writes=[rl[i]])
            p.emit("vector", lambda v, i=i: v.tensor_add(
                out=dl[i][:], in0=rl[i][:], in1=l1[i][:]),
                reads=[rl[i], l1[i]], writes=[dl[i]])
            p.emit("vector", lambda v, i=i: v.tensor_copy(
                out=dlb[i][:], in_=dl[i][:]), reads=[dl[i]], writes=[dlb[i]])
            p.dma("gpsimd", deltaT_hbm[128 * m:128 * (m + 1),
                                       512 * tc:512 * (tc + 1)], dlb[i][:])
            p.dma("sync", xst[i][:],
                  xsiluT_hbm[128 * m:128 * (m + 1), 512 * tc:512 * (tc + 1)])
            p.emit("vector", lambda v, i=i: v.tensor_mul(
                out=dub[i][:], in0=dl[i][:], in1=xst[i][:]),
                reads=[dl[i], xst[i]], writes=[dub[i]])
            p.dma("gpsimd", duT_hbm[128 * m:128 * (m + 1),
                                    512 * tc:512 * (tc + 1)], dub[i][:])


def phase_scan_setup(p, deltaT_hbm, duT_hbm, xsiluT_hbm, resT_hbm, Bm_hbm,
                     Cm_hbm, A_hbm, Dv_hbm, yT_hbm, ident_f32):
    nc = p.nc
    tag = "c9"
    TH = 1024
    A_sb = p.sb(tag + "A", 128, 16 * D_STATE, F32)
    for dc in range(16):
        p.dma("sync", A_sb[:, 16 * dc:16 * (dc + 1)],
              A_hbm[128 * dc:128 * (dc + 1), :])
    Dr = p.sb(tag + "Dr", 16, 128, F32)
    Dc = p.sb(tag + "Dc", 128, 16, F32)
    rps = p.pf(4)
    p.dma("sync", Dr[:], Dv_hbm[:])
    p.emit("tensor", lambda t: t.transpose(out=rps[0:128, 0:16], in_=Dr[:],
                                           identity=ident_f32[0:16, 0:16]),
           reads=[Dr, ident_f32], writes=[rps])
    p.emit("scalar", lambda s: s.copy(out=Dc[:], in_=rps[0:128, 0:16]),
           reads=[rps], writes=[Dc])
    oh16 = p.sb(tag + "oh16", 16, 16 * 128, F32)
    ohi = p.sb(tag + "ohi", 16, 16 * 128, F32)
    p.emit("gpsimd", lambda g: g.iota(out=ohi[:], pattern=[[-1, 16], [0, 128]],
                                      base=0, channel_multiplier=1,
                                      allow_small_or_imprecise_dtypes=True),
           writes=[ohi])
    p.emit("vector", lambda v: v.tensor_scalar(out=oh16[:], in0=ohi[:],
                                               scalar1=0.0, scalar2=None,
                                               op0=ALU.is_equal),
           reads=[ohi], writes=[oh16])
    Brow = p.sb(tag + "Brow", 16, TH, F32)
    Crow = p.sb(tag + "Crow", 16, TH, F32)
    Bbc = p.sb(tag + "Bbc", 128, 16 * TH, BF16)
    Cbc = p.sb(tag + "Cbc", 128, 16 * TH, BF16)
    pbc = [p.pf(0), p.pf(1)]
    dts = [p.sb(tag + f"dt{i}", 128, TH, BF16) for i in range(3)]
    dus = [p.sb(tag + f"du{i}", 128, TH, BF16) for i in range(3)]
    xss = [p.sb(tag + f"xs{i}", 128, TH, BF16) for i in range(3)]
    rss = [p.sb(tag + f"rs{i}", 128, TH, BF16) for i in range(3)]
    dA = [p.sb(tag + f"dA{i}", 128, TH, BF16) for i in range(2)]
    db = [p.sb(tag + f"db{i}", 128, TH, BF16) for i in range(2)]
    cyall = p.sb(tag + "cy", 128, 16 * TH, BF16)
    treeA = p.sb(tag + "trA", 128, 8 * TH, BF16)
    yf = [p.sb(tag + "yf0", 128, TH, F32)] * 2
    y2 = [p.sb(tag + "y20", 128, TH, F32)] * 2
    sres = [p.sb(tag + "sr0", 128, TH, F32)] * 2
    yo = [p.sb(tag + f"yo{i}", 128, TH, BF16) for i in range(2)]
    carry_sv = p.sb(tag + "csv", 128, 16 * D_STATE, F32)

    def run_block(b, th):
            c0 = S * b + TH * th
            p.dma("sync", Brow[:], Bm_hbm[:, c0:c0 + TH])
            p.dma("sync", Crow[:], Cm_hbm[:, c0:c0 + TH])
            for n in range(16):
                for cc in range(TH // 512):
                    pt = pbc[cc % 2]
                    p.emit("tensor", lambda t, n=n, cc=cc, pt=pt: t.matmul(
                        out=pt[:], lhsT=oh16[:, 128 * n:128 * (n + 1)],
                        rhs=Brow[:, 512 * cc:512 * (cc + 1)],
                        start=True, stop=True), reads=[oh16, Brow], writes=[pt])
                    p.emit("scalar", lambda s, n=n, cc=cc, pt=pt: s.copy(
                        out=Bbc[:, TH * n + 512 * cc:TH * n + 512 * (cc + 1)],
                        in_=pt[:]), reads=[pt], writes=[Bbc])
                    p.emit("tensor", lambda t, n=n, cc=cc, pt=pt: t.matmul(
                        out=pt[:], lhsT=oh16[:, 128 * n:128 * (n + 1)],
                        rhs=Crow[:, 512 * cc:512 * (cc + 1)],
                        start=True, stop=True), reads=[oh16, Crow], writes=[pt])
                    p.emit("scalar", lambda s, n=n, cc=cc, pt=pt: s.copy(
                        out=Cbc[:, TH * n + 512 * cc:TH * n + 512 * (cc + 1)],
                        in_=pt[:]), reads=[pt], writes=[Cbc])
            for dc in range(16):
                i = dc % 2
                il = dc % 3
                p.dma("sync", dts[il][:],
                      deltaT_hbm[128 * dc:128 * (dc + 1), c0:c0 + TH])
                p.dma("sync", dus[il][:],
                      duT_hbm[128 * dc:128 * (dc + 1), c0:c0 + TH])
                p.dma("sync", xss[il][:],
                      xsiluT_hbm[128 * dc:128 * (dc + 1), c0:c0 + TH])
                p.dma("sync", rss[il][:],
                      resT_hbm[128 * dc:128 * (dc + 1), c0:c0 + TH])
                for n in range(16):
                    jn = n % 2
                    p.emit("scalar", lambda s, il=il, dc=dc, n=n, jn=jn: s.activation(
                        out=dA[jn][:], in_=dts[il][:], func=AF.Exp,
                        scale=A_sb[:, 16 * dc + n:16 * dc + n + 1]),
                        reads=[dts[il], A_sb], writes=[dA[jn]])
                    p.emit("vector", lambda v, il=il, n=n, jn=jn: v.tensor_mul(
                        out=db[jn][:], in0=dus[il][:],
                        in1=Bbc[:, TH * n:TH * (n + 1)]),
                        reads=[dus[il], Bbc], writes=[db[jn]])
                    if th == 0:
                        init = 0.0
                        rd_init = []
                    else:
                        init = carry_sv[:, 16 * dc + n:16 * dc + n + 1]
                        rd_init = [carry_sv]

                    def scan_op(v, init=init, n=n, jn=jn):
                        return v.tensor_tensor_scan(
                            out=cyall[:, TH * n:TH * (n + 1)], data0=dA[jn][:],
                            data1=db[jn][:], initial=init, op0=ALU.mult,
                            op1=ALU.add)
                    p.emit("vector", scan_op, reads=[dA[jn], db[jn]] + rd_init,
                           writes=[cyall])
                    if th < S // TH - 1:
                        p.emit("vector", lambda v, dc=dc, n=n: v.tensor_copy(
                            out=carry_sv[:, 16 * dc + n:16 * dc + n + 1],
                            in_=cyall[:, TH * (n + 1) - 1:TH * (n + 1)]),
                            reads=[cyall], writes=[carry_sv])
                    p.emit("vector", lambda v, n=n: v.tensor_mul(
                        out=cyall[:, TH * n:TH * (n + 1)],
                        in0=cyall[:, TH * n:TH * (n + 1)],
                        in1=Cbc[:, TH * n:TH * (n + 1)]),
                        reads=[cyall, Cbc], writes=[cyall])
                p.emit("vector", lambda v: v.tensor_add(
                    out=treeA[:], in0=cyall[:, 0:8 * TH],
                    in1=cyall[:, 8 * TH:16 * TH]), reads=[cyall], writes=[treeA])
                p.emit("vector", lambda v: v.tensor_add(
                    out=treeA[:, 0:4 * TH], in0=treeA[:, 0:4 * TH],
                    in1=treeA[:, 4 * TH:8 * TH]), reads=[treeA], writes=[treeA])
                p.emit("vector", lambda v: v.tensor_add(
                    out=treeA[:, 0:2 * TH], in0=treeA[:, 0:2 * TH],
                    in1=treeA[:, 2 * TH:4 * TH]), reads=[treeA], writes=[treeA])
                p.emit("vector", lambda v, i=i: v.tensor_add(
                    out=yf[i][:], in0=treeA[:, 0:TH], in1=treeA[:, TH:2 * TH]),
                    reads=[treeA], writes=[yf[i]])
                p.emit("vector", lambda v, i=i, il=il, dc=dc: v.scalar_tensor_tensor(
                    out=y2[i][:], in0=xss[il][:], scalar=Dc[:, dc:dc + 1],
                    in1=yf[i][:], op0=ALU.mult, op1=ALU.add),
                    reads=[xss[il], Dc, yf[i]], writes=[y2[i]])
                p.emit("scalar", lambda s, i=i, il=il: s.activation(
                    out=sres[i][:], in_=rss[il][:], func=AF.Silu),
                    reads=[rss[il]], writes=[sres[i]])
                p.emit("vector", lambda v, i=i: v.tensor_mul(
                    out=yo[i][:], in0=y2[i][:], in1=sres[i][:]),
                    reads=[y2[i], sres[i]], writes=[yo[i]])
                p.dma("gpsimd", yT_hbm[128 * dc:128 * (dc + 1), c0:c0 + TH],
                      yo[i][:])
    return run_block, S // TH


def phase_l1out(p, yT_hbm, wout_hbm, xnT_hbm, s1_hbm, ident_f32):
    nc = p.nc
    tag = "cA"
    wout = p.sb(tag + "wo", 128, 16 * D, BF16)
    for k in range(16):
        p.dma("gpsimd", wout[:, D * k:D * (k + 1)],
              wout_hbm[128 * k:128 * (k + 1), :])
    yc = [p.sb(tag + f"y{i}", 128, 16 * 512, BF16) for i in range(2)]
    xn1 = [p.sb(tag + f"xn{i}", 128, 512, BF16) for i in range(2)]
    s1T = [p.sb(tag + f"sT{i}", 128, 512, F32) for i in range(2)]
    stg = [p.sb(tag + f"st{i}", 128, D, F32) for i in range(4)]
    pps = [p.pf(0), p.pf(1)]
    ptr = [p.pf(2), p.pf(3), p.pf(4), p.pf(5)]
    for tc in range(8):
        i = tc % 2
        for k in range(16):
            p.dma("sync", yc[i][:, 512 * k:512 * (k + 1)],
                  yT_hbm[128 * k:128 * (k + 1), 512 * tc:512 * (tc + 1)])
        for m in range(8):
            im = (tc * 8 + m) % 2
            p.dma("sync", xn1[im][:],
                  xnT_hbm[128 * m:128 * (m + 1), 512 * tc:512 * (tc + 1)])

            def mms(t, m=m, i=i, im=im):
                for k in range(16):
                    inst = t.matmul(
                        out=pps[im][:], lhsT=wout[:, D * k + 128 * m:D * k + 128 * (m + 1)],
                        rhs=yc[i][:, 512 * k:512 * (k + 1)],
                        start=(k == 0), stop=(k == 15))
                return inst
            p.emit("tensor", mms, reads=[wout, yc[i]], writes=[pps[im]])
            p.emit("vector", lambda v, im=im: v.scalar_tensor_tensor(
                out=s1T[im][:], in0=pps[im][:], scalar=1.0, in1=xn1[im][:],
                op0=ALU.bypass, op1=ALU.add),
                reads=[pps[im], xn1[im]], writes=[s1T[im]])
            for u in range(4):
                pt = ptr[u % 4]
                p.emit("tensor", lambda t, im=im, u=u, pt=pt: t.transpose(
                    out=pt[:, 0:128], in_=s1T[im][:, 128 * u:128 * (u + 1)],
                    identity=ident_f32[:]), reads=[s1T[im], ident_f32],
                    writes=[pt])
                p.emit("scalar", lambda s, u=u, m=m, pt=pt: s.copy(
                    out=stg[u][:, 128 * m:128 * (m + 1)], in_=pt[:, 0:128]),
                    reads=[pt], writes=[stg[u]])
        for u in range(4):
            tt = tc * 4 + u
            p.dma("gpsimd", s1_hbm[128 * tt:128 * (tt + 1), :], stg[u][:])


def phase_ffn8(p, xnT_hbm, wg_hbm, wu_hbm, wd_hbm, gsilT_hbm, hT_hbm, g1_hbm,
               ident_bf, ident_f32):
    nc = p.nc
    tag = "cB"
    SC = 1.0 / 127.0
    HCH = HID_I8 // 128
    raw = None
    xsl = [p.sb(tag + f"x{i}", 128, 8 * 512, BF16) for i in range(2)]
    pps = [p.pf(0), p.pf(1)]
    out1 = [p.sb(tag + f"o{i}", 128, 512, BF16) for i in range(2)]
    wT_sh = p.sb(tag + "wTsh", 128, 8 * HID_I8, BF16)
    wgT, raw = load_weight_T(p, wg_hbm, HID_I8, D, ident_bf, tag + "wg", raw,
                             wT=wT_sh)
    for tc in range(8):
        i = tc % 2
        for k in range(8):
            p.dma("sync", xsl[i][:, 512 * k:512 * (k + 1)],
                  xnT_hbm[128 * k:128 * (k + 1), 512 * tc:512 * (tc + 1)])
        for hc in range(HCH):
            ih = (tc * HCH + hc) % 2

            def mms(t, hc=hc, i=i, ih=ih, wT=wgT):
                for k in range(8):
                    inst = t.matmul(
                        out=pps[ih][:],
                        lhsT=wT[:, HID_I8 * k + 128 * hc:HID_I8 * k + 128 * (hc + 1)],
                        rhs=xsl[i][:, 512 * k:512 * (k + 1)],
                        start=(k == 0), stop=(k == 7))
                return inst
            p.emit("tensor", mms, reads=[wgT, xsl[i]], writes=[pps[ih]])
            p.emit("scalar", lambda s, ih=ih: s.activation(
                out=out1[ih][:], in_=pps[ih][:], func=AF.Silu, scale=SC),
                reads=[pps[ih]], writes=[out1[ih]])
            p.dma("gpsimd", gsilT_hbm[128 * hc:128 * (hc + 1),
                                      512 * tc:512 * (tc + 1)], out1[ih][:])
    wuT, raw = load_weight_T(p, wu_hbm, HID_I8, D, ident_bf, tag + "wu", raw,
                             wT=wT_sh)
    gsl = [p.sb(tag + f"g{i}", 128, 512, BF16) for i in range(2)]
    for tc in range(8):
        i = tc % 2
        for k in range(8):
            p.dma("sync", xsl[i][:, 512 * k:512 * (k + 1)],
                  xnT_hbm[128 * k:128 * (k + 1), 512 * tc:512 * (tc + 1)])
        for hc in range(HCH):
            ih = (tc * HCH + hc) % 2
            p.dma("sync", gsl[ih][:],
                  gsilT_hbm[128 * hc:128 * (hc + 1), 512 * tc:512 * (tc + 1)])

            def mms(t, hc=hc, i=i, ih=ih, wT=wuT):
                for k in range(8):
                    inst = t.matmul(
                        out=pps[ih][:],
                        lhsT=wT[:, HID_I8 * k + 128 * hc:HID_I8 * k + 128 * (hc + 1)],
                        rhs=xsl[i][:, 512 * k:512 * (k + 1)],
                        start=(k == 0), stop=(k == 7))
                return inst
            p.emit("tensor", mms, reads=[wuT, xsl[i]], writes=[pps[ih]])
            p.emit("vector", lambda v, ih=ih: v.scalar_tensor_tensor(
                out=out1[ih][:], in0=pps[ih][:], scalar=SC, in1=gsl[ih][:],
                op0=ALU.mult, op1=ALU.mult),
                reads=[pps[ih], gsl[ih]], writes=[out1[ih]])
            p.dma("gpsimd", hT_hbm[128 * hc:128 * (hc + 1),
                                   512 * tc:512 * (tc + 1)], out1[ih][:])
    wdT, raw = load_weight_T(p, wd_hbm, D, HID_I8, ident_bf, tag + "wd", raw,
                             wT=wT_sh)
    hsl = [p.sb(tag + "h0", 128, HCH * 512, BF16)] * 2
    g1T = [p.sb(tag + f"gT{i}", 128, 512, F32) for i in range(2)]
    stg = [p.sb(tag + f"st{i}", 128, D, F32) for i in range(4)]
    ptr = [p.pf(2), p.pf(3), p.pf(4), p.pf(5)]
    for tc in range(8):
        i = tc % 2
        for k in range(HCH):
            p.dma("sync", hsl[i][:, 512 * k:512 * (k + 1)],
                  hT_hbm[128 * k:128 * (k + 1), 512 * tc:512 * (tc + 1)])
        for m in range(8):
            im = (tc * 8 + m) % 2

            def mms(t, m=m, i=i, im=im):
                for k in range(HCH):
                    inst = t.matmul(
                        out=pps[im][:], lhsT=wdT[:, D * k + 128 * m:D * k + 128 * (m + 1)],
                        rhs=hsl[i][:, 512 * k:512 * (k + 1)],
                        start=(k == 0), stop=(k == HCH - 1))
                return inst
            p.emit("tensor", mms, reads=[wdT, hsl[i]], writes=[pps[im]])
            p.emit("scalar", lambda s, im=im: s.activation(
                out=g1T[im][:], in_=pps[im][:], func=AF.Copy, scale=SC),
                reads=[pps[im]], writes=[g1T[im]])
            for u in range(4):
                pt = ptr[u % 4]
                p.emit("tensor", lambda t, im=im, u=u, pt=pt: t.transpose(
                    out=pt[:, 0:128], in_=g1T[im][:, 128 * u:128 * (u + 1)],
                    identity=ident_f32[:]), reads=[g1T[im], ident_f32],
                    writes=[pt])
                p.emit("scalar", lambda s, u=u, m=m, pt=pt: s.copy(
                    out=stg[u][:, 128 * m:128 * (m + 1)], in_=pt[:, 0:128]),
                    reads=[pt], writes=[stg[u]])
        for u in range(4):
            tt = tc * 4 + u
            p.dma("gpsimd", g1_hbm[128 * tt:128 * (tt + 1), :], stg[u][:])


def phase_final(p, x1_hbm, s1_hbm, g1_hbm, out_hbm):
    nc = p.nc
    tag = "cF"
    x1 = [p.sb(tag + f"x{i}", 128, D, F32) for i in range(2)]
    s1 = [p.sb(tag + f"s{i}", 128, D, F32) for i in range(2)]
    g1 = [p.sb(tag + f"g{i}", 128, D, F32) for i in range(2)]
    sg = [p.sb(tag + f"sg{i}", 128, D, F32) for i in range(2)]
    ot = [p.sb(tag + f"o{i}", 128, D, F32) for i in range(2)]
    for tt in range(NTT):
        i = tt % 2
        p.dma("sync", x1[i][:], x1_hbm[128 * tt:128 * (tt + 1), :])
        p.dma("sync", s1[i][:], s1_hbm[128 * tt:128 * (tt + 1), :])
        p.dma("sync", g1[i][:], g1_hbm[128 * tt:128 * (tt + 1), :])
        p.emit("scalar", lambda s, i=i: s.activation(
            out=sg[i][:], in_=g1[i][:], func=AF.Silu), reads=[g1[i]],
            writes=[sg[i]])
        p.emit("vector", lambda v, i=i: v.tensor_mul(
            out=sg[i][:], in0=sg[i][:], in1=s1[i][:]),
            reads=[s1[i], sg[i]], writes=[sg[i]])
        p.emit("vector", lambda v, i=i: v.tensor_add(
            out=ot[i][:], in0=x1[i][:], in1=sg[i][:]),
            reads=[x1[i], sg[i]], writes=[ot[i]])
        p.dma("gpsimd", out_hbm[128 * tt:128 * (tt + 1), :], ot[i][:])


import contextlib


class Launch:
    def __init__(self, name):
        self.es = contextlib.ExitStack()
        self.nc = bass.Bass("TRN2", target_bir_lowering=False, name=name)
        self.p = P(self.nc, self.es)
        self.aps = {}

    def inp(self, name, shape, dtype):
        ap = self.nc.dram_tensor(name, list(shape), dtype, kind="ExternalInput").ap()
        self.aps[name] = ap
        return ap

    def out(self, name, shape, dtype):
        ap = self.nc.dram_tensor(name, list(shape), dtype, kind="ExternalOutput").ap()
        self.aps[name] = ap
        return ap

    def run(self, in_maps, out_names):
        with self.es:
            self.p.build()
            res = run_bass_kernel_spmd(self.nc, in_maps,
                                       core_ids=list(range(len(in_maps))))
        return {n: [res.results[c][n] for c in range(len(in_maps))]
                for n in out_names}


def _rep(arr, n=N_CORE):
    return [arr for _ in range(n)]


def kernel_unfused(**inputs):
    f32 = np.float32
    x_full = np.ascontiguousarray(inputs["L0_attn_x"], f32)
    xs = [np.ascontiguousarray(x_full[2 * c:2 * c + 2].reshape(NTOK, D))
          for c in range(N_CORE)]
    wq = np.ascontiguousarray(inputs["L0_attn_Wq"], f32)
    wk = np.ascontiguousarray(inputs["L0_attn_Wk"], f32)
    wv = np.ascontiguousarray(inputs["L0_attn_Wv"], f32)
    wo = np.ascontiguousarray(inputs["L0_attn_Wo"], f32)
    slopes = np.ascontiguousarray(inputs["L0_attn_slopes"], f32).reshape(1, H)
    gate_w = np.ascontiguousarray(inputs["L0_ffn_gate_w"], f32)
    gate_b = np.ascontiguousarray(inputs["L0_ffn_gate_b"], f32).reshape(1, N_EXP)

    L1 = Launch("ln0")
    a = L1.aps
    L1.inp("x", (NTOK, D), F32)
    L1.inp("gw", (D, N_EXP), F32)
    L1.inp("gb", (1, N_EXP), F32)
    L1.out("xnT", (D, NTOK), BF16)
    L1.out("glog", (NTOK, N_EXP), F32)
    L1.p.warmup((AF.Sqrt,))
    idf = build_identity(L1.p, F32, "idf")
    phase_ln(L1.p, "p1", a["x"], a["xnT"], idf, gate=(a["gw"], a["gb"], a["glog"]))
    r1 = L1.run([{"x": xs[c], "gw": gate_w, "gb": gate_b} for c in range(N_CORE)],
                ["xnT", "glog"])

    L2 = Launch("qkv")
    a = L2.aps
    L2.inp("xnT", (D, NTOK), BF16)
    L2.inp("wq", (D, D), F32)
    L2.inp("wk", (D, D), F32)
    L2.inp("wv", (D, D), F32)
    L2.out("qT", (D, NTOK), BF16)
    L2.out("kT", (D, NTOK), BF16)
    L2.out("v", (NTOK, D), BF16)
    L2.p.warmup(())
    idb = build_identity(L2.p, BF16, "idb")
    phase_qkv(L2.p, a["xnT"], a["wq"], a["wk"], a["wv"], a["qT"], a["kT"], a["v"], idb)
    r2 = L2.run([{"xnT": r1["xnT"][c], "wq": wq, "wk": wk, "wv": wv}
                 for c in range(N_CORE)], ["qT", "kT", "v"])

    L3 = Launch("att")
    a = L3.aps
    L3.inp("qT", (D, NTOK), BF16)
    L3.inp("kT", (D, NTOK), BF16)
    L3.inp("v", (NTOK, D), BF16)
    L3.inp("slopes", (1, H), F32)
    L3.out("aoT", (D, NTOK), BF16)
    L3.p.warmup((AF.Exp,))
    idb = build_identity(L3.p, BF16, "idb")
    phase_att(L3.p, a["qT"], a["kT"], a["v"], a["slopes"], a["aoT"], idb)
    r3 = L3.run([{"qT": r2["qT"][c], "kT": r2["kT"][c], "v": r2["v"][c],
                  "slopes": slopes} for c in range(N_CORE)], ["aoT"])

    L4 = Launch("wo")
    a = L4.aps
    L4.inp("aoT", (D, NTOK), BF16)
    L4.inp("wo", (D, D), F32)
    L4.out("s", (NTOK, D), F32)
    L4.p.warmup(())
    idb = build_identity(L4.p, BF16, "idb")
    idf = build_identity(L4.p, F32, "idf")
    phase_wo(L4.p, a["aoT"], a["wo"], a["s"], idb, idf)
    r4 = L4.run([{"aoT": r3["aoT"][c], "wo": wo} for c in range(N_CORE)], ["s"])

    if inputs.get("_partial") == "s":
        return r4["s"], r1["glog"], r1["xnT"]

    w1 = np.ascontiguousarray(inputs["L0_ffn_w1"], f32).reshape(N_EXP * D, D_FF)
    b1 = np.ascontiguousarray(np.ascontiguousarray(inputs["L0_ffn_b1"], f32)
                              .reshape(N_EXP, 32, 128).transpose(0, 2, 1)
                              .reshape(N_EXP * 128, 32))
    w2 = np.ascontiguousarray(inputs["L0_ffn_w2"], f32).reshape(N_EXP * D_FF, D)
    b2 = np.ascontiguousarray(inputs["L0_ffn_b2"], f32).reshape(N_EXP * 8, 128)

    L5 = Launch("moe")
    a = L5.aps
    L5.inp("xnT", (D, NTOK), BF16)
    L5.inp("glog", (NTOK, N_EXP), F32)
    L5.inp("w1", (N_EXP * D, D_FF), F32)
    L5.inp("b1", (N_EXP * 32, 128), F32)
    L5.inp("w2", (N_EXP * D_FF, D), F32)
    L5.inp("b2", (N_EXP * 8, 128), F32)
    L5.out("gT", (D, NTOK), F32)
    L5.p.warmup((AF.Exp, AF.Gelu))
    idf = build_identity(L5.p, F32, "idf")
    phase_moe(L5.p, a["xnT"], a["glog"], a["w1"], a["b1"], a["w2"], a["b2"],
              a["gT"], idf)
    r5 = L5.run([{"xnT": r1["xnT"][c], "glog": r1["glog"][c], "w1": w1, "b1": b1,
                  "w2": w2, "b2": b2} for c in range(N_CORE)], ["gT"])

    L6 = Launch("comb")
    a = L6.aps
    L6.inp("x", (NTOK, D), F32)
    L6.inp("s", (NTOK, D), F32)
    L6.inp("gT", (D, NTOK), F32)
    L6.out("x1", (NTOK, D), F32)
    L6.out("xnT1", (D, NTOK), BF16)
    L6.out("rstd2", (1, NTOK), F32)
    L6.p.warmup((AF.Silu, AF.Sqrt))
    idf = build_identity(L6.p, F32, "idf")
    phase_combine_ln(L6.p, a["x"], a["s"], a["gT"], a["x1"], a["xnT1"],
                     a["rstd2"], idf)
    r6 = L6.run([{"x": xs[c], "s": r4["s"][c], "gT": r5["gT"][c]}
                 for c in range(N_CORE)], ["x1", "xnT1", "rstd2"])

    if inputs.get("_partial") == "x1":
        return r6["x1"], r6["xnT1"], r6["rstd2"]

    norm_w = np.ascontiguousarray(inputs["L1_attn_norm_w"], f32).reshape(8, 128)
    w_in = np.ascontiguousarray(inputs["L1_attn_W_in"], f32)
    conv_w = np.ascontiguousarray(inputs["L1_attn_conv_w"], f32)
    conv_b = np.ascontiguousarray(inputs["L1_attn_conv_b"], f32).reshape(16, 128)
    w_x = np.ascontiguousarray(inputs["L1_attn_W_x"], f32)
    w_dt = np.ascontiguousarray(inputs["L1_attn_W_dt"], f32)
    b_dt = np.ascontiguousarray(inputs["L1_attn_b_dt"], f32).reshape(16, 128)
    A_w = np.ascontiguousarray(inputs["L1_attn_A"], f32)
    D_w = np.ascontiguousarray(inputs["L1_attn_D"], f32).reshape(16, 128)
    w_out = np.ascontiguousarray(inputs["L1_attn_W_out"], f32)
    wg8 = np.ascontiguousarray(inputs["L1_ffn_w_gate"], np.int32)
    wu8 = np.ascontiguousarray(inputs["L1_ffn_w_up"], np.int32)
    wd8 = np.ascontiguousarray(inputs["L1_ffn_w_down"], np.int32)

    L7 = Launch("min")
    a = L7.aps
    L7.inp("xnT1", (D, NTOK), BF16)
    L7.inp("rstd2", (1, NTOK), F32)
    L7.inp("norm_w", (8, 128), F32)
    L7.inp("w_in", (D, 2 * D_INNER), F32)
    L7.inp("conv_w", (D_INNER, CONV_K), F32)
    L7.inp("conv_b", (16, 128), F32)
    L7.out("xinT", (D_INNER, NTOK), BF16)
    L7.out("resT", (D_INNER, NTOK), BF16)
    L7.out("xsiluT", (D_INNER, NTOK), BF16)
    L7.p.warmup((AF.Silu,))
    idf = build_identity(L7.p, F32, "idf")
    phase_mamba_in(L7.p, a["xnT1"], a["rstd2"], a["norm_w"], a["w_in"],
                   a["conv_w"], a["conv_b"], a["xinT"], a["resT"], a["xsiluT"],
                   idf)
    r7 = L7.run([{"xnT1": r6["xnT1"][c], "rstd2": r6["rstd2"][c],
                  "norm_w": norm_w, "w_in": w_in, "conv_w": conv_w,
                  "conv_b": conv_b} for c in range(N_CORE)],
                ["xinT", "resT", "xsiluT"])

    L8 = Launch("sproj")
    a = L8.aps
    L8.inp("xsiluT", (D_INNER, NTOK), BF16)
    L8.inp("w_x", (D_INNER, 96), F32)
    L8.inp("w_dt", (DT_RANK, D_INNER), F32)
    L8.inp("b_dt", (16, 128), F32)
    L8.out("deltaT", (D_INNER, NTOK), BF16)
    L8.out("duT", (D_INNER, NTOK), BF16)
    L8.out("Bm", (D_STATE, NTOK), F32)
    L8.out("Cm", (D_STATE, NTOK), F32)
    L8.p.warmup((AF.Exp, AF.Ln))
    idf = build_identity(L8.p, F32, "idf")
    phase_ssm_proj(L8.p, a["xsiluT"], a["w_x"], a["w_dt"], a["b_dt"],
                   a["deltaT"], a["duT"], a["Bm"], a["Cm"], idf)
    r8 = L8.run([{"xsiluT": r7["xsiluT"][c], "w_x": w_x, "w_dt": w_dt,
                  "b_dt": b_dt} for c in range(N_CORE)],
                ["deltaT", "duT", "Bm", "Cm"])

    L9 = Launch("scan")
    a = L9.aps
    L9.inp("deltaT", (D_INNER, NTOK), BF16)
    L9.inp("duT", (D_INNER, NTOK), BF16)
    L9.inp("xsiluT", (D_INNER, NTOK), BF16)
    L9.inp("resT", (D_INNER, NTOK), BF16)
    L9.inp("Bm", (D_STATE, NTOK), F32)
    L9.inp("Cm", (D_STATE, NTOK), F32)
    L9.inp("A", (D_INNER, D_STATE), F32)
    L9.inp("D", (16, 128), F32)
    L9.out("yT", (D_INNER, NTOK), BF16)
    L9.p.warmup((AF.Exp, AF.Silu))
    idf = build_identity(L9.p, F32, "idf")
    phase_scan(L9.p, a["deltaT"], a["duT"], a["xsiluT"], a["resT"], a["Bm"],
               a["Cm"], a["A"], a["D"], a["yT"], idf)
    r9 = L9.run([{"deltaT": r8["deltaT"][c], "duT": r8["duT"][c],
                  "xsiluT": r7["xsiluT"][c], "resT": r7["resT"][c],
                  "Bm": r8["Bm"][c], "Cm": r8["Cm"][c], "A": A_w, "D": D_w}
                 for c in range(N_CORE)], ["yT"])

    L10 = Launch("l1out")
    a = L10.aps
    L10.inp("yT", (D_INNER, NTOK), BF16)
    L10.inp("w_out", (D_INNER, D), F32)
    L10.inp("xnT1", (D, NTOK), BF16)
    L10.out("s1", (NTOK, D), F32)
    L10.p.warmup(())
    idf = build_identity(L10.p, F32, "idf")
    phase_l1out(L10.p, a["yT"], a["w_out"], a["xnT1"], a["s1"], idf)
    r10 = L10.run([{"yT": r9["yT"][c], "w_out": w_out, "xnT1": r6["xnT1"][c]}
                   for c in range(N_CORE)], ["s1"])

    L11 = Launch("ffn8")
    a = L11.aps
    L11.inp("xnT1", (D, NTOK), BF16)
    L11.inp("wg", (HID_I8, D), I32)
    L11.inp("wu", (HID_I8, D), I32)
    L11.inp("wd", (D, HID_I8), I32)
    gsilT = L11.nc.dram_tensor("gsilT", [HID_I8, NTOK], BF16, kind="Internal").ap()
    hTd = L11.nc.dram_tensor("hTd", [HID_I8, NTOK], BF16, kind="Internal").ap()
    L11.out("g1", (NTOK, D), F32)
    L11.p.warmup((AF.Silu,))
    idb = build_identity(L11.p, BF16, "idb")
    idf = build_identity(L11.p, F32, "idf")
    phase_ffn8(L11.p, a["xnT1"], a["wg"], a["wu"], a["wd"], gsilT, hTd,
               a["g1"], idb, idf)
    r11 = L11.run([{"xnT1": r6["xnT1"][c], "wg": wg8, "wu": wu8, "wd": wd8}
                   for c in range(N_CORE)], ["g1"])

    L12 = Launch("fin")
    a = L12.aps
    L12.inp("x1", (NTOK, D), F32)
    L12.inp("s1", (NTOK, D), F32)
    L12.inp("g1", (NTOK, D), F32)
    L12.out("out", (NTOK, D), F32)
    L12.p.warmup((AF.Silu,))
    phase_final(L12.p, a["x1"], a["s1"], a["g1"], a["out"])
    r12 = L12.run([{"x1": r6["x1"][c], "s1": r10["s1"][c], "g1": r11["g1"][c]}
                   for c in range(N_CORE)], ["out"])

    out = np.stack([np.asarray(r12["out"][c], f32).reshape(2, S, D)
                    for c in range(N_CORE)])
    return out.reshape(16, S, D)


def kernel(**inputs):
    f32 = np.float32
    x_full = np.ascontiguousarray(inputs["L0_attn_x"], f32)
    xs = [np.ascontiguousarray(x_full[2 * c:2 * c + 2].reshape(NTOK, D))
          for c in range(N_CORE)]
    wq = np.ascontiguousarray(inputs["L0_attn_Wq"], f32)
    wk = np.ascontiguousarray(inputs["L0_attn_Wk"], f32)
    wv = np.ascontiguousarray(inputs["L0_attn_Wv"], f32)
    wo = np.ascontiguousarray(inputs["L0_attn_Wo"], f32)
    slopes = np.ascontiguousarray(inputs["L0_attn_slopes"], f32).reshape(1, H)
    gate_w = np.ascontiguousarray(inputs["L0_ffn_gate_w"], f32)
    gate_b = np.ascontiguousarray(inputs["L0_ffn_gate_b"], f32).reshape(1, N_EXP)
    w1 = np.ascontiguousarray(inputs["L0_ffn_w1"], f32).reshape(N_EXP * D, D_FF)
    b1 = np.ascontiguousarray(np.ascontiguousarray(inputs["L0_ffn_b1"], f32)
                              .reshape(N_EXP, 32, 128).transpose(0, 2, 1)
                              .reshape(N_EXP * 128, 32))
    w2 = np.ascontiguousarray(inputs["L0_ffn_w2"], f32).reshape(N_EXP * D_FF, D)
    b2 = np.ascontiguousarray(inputs["L0_ffn_b2"], f32).reshape(N_EXP * 8, 128)
    b2r8 = np.ascontiguousarray(inputs["L0_ffn_b2"], f32)
    norm_w = np.ascontiguousarray(inputs["L1_attn_norm_w"], f32).reshape(8, 128)
    w_in = np.ascontiguousarray(inputs["L1_attn_W_in"], f32)
    conv_w = np.ascontiguousarray(inputs["L1_attn_conv_w"], f32)
    conv_b = np.ascontiguousarray(inputs["L1_attn_conv_b"], f32).reshape(16, 128)
    w_x = np.ascontiguousarray(inputs["L1_attn_W_x"], f32)
    w_dt = np.ascontiguousarray(inputs["L1_attn_W_dt"], f32)
    b_dt = np.ascontiguousarray(inputs["L1_attn_b_dt"], f32).reshape(16, 128)
    A_w = np.ascontiguousarray(inputs["L1_attn_A"], f32)
    D_w = np.ascontiguousarray(inputs["L1_attn_D"], f32).reshape(16, 128)
    w_out = np.ascontiguousarray(inputs["L1_attn_W_out"], f32)
    wg8 = np.ascontiguousarray(inputs["L1_ffn_w_gate"], np.int32)
    wu8 = np.ascontiguousarray(inputs["L1_ffn_w_up"], np.int32)
    wd8 = np.ascontiguousarray(inputs["L1_ffn_w_down"], np.int32)

    L = Launch("mega")
    nc, p = L.nc, L.p
    a = L.aps
    L.inp("x", (NTOK, D), F32)
    L.inp("wq", (D, D), F32)
    L.inp("wk", (D, D), F32)
    L.inp("wv", (D, D), F32)
    L.inp("wo", (D, D), F32)
    L.inp("slopes", (1, H), F32)
    L.inp("gw", (D, N_EXP), F32)
    L.inp("gb", (1, N_EXP), F32)
    L.inp("w1", (N_EXP * D, D_FF), F32)
    L.inp("b1", (N_EXP * 128, 32), F32)
    L.inp("w2", (N_EXP * D_FF, D), F32)
    L.inp("b2", (N_EXP * 8, 128), F32)
    L.inp("b2r8", (N_EXP, D), F32)
    L.inp("norm_w", (8, 128), F32)
    L.inp("w_in", (D, 2 * D_INNER), F32)
    L.inp("conv_w", (D_INNER, CONV_K), F32)
    L.inp("conv_b", (16, 128), F32)
    L.inp("w_x", (D_INNER, 96), F32)
    L.inp("w_dt", (DT_RANK, D_INNER), F32)
    L.inp("b_dt", (16, 128), F32)
    L.inp("A", (D_INNER, D_STATE), F32)
    L.inp("D", (16, 128), F32)
    L.inp("w_out", (D_INNER, D), F32)
    L.inp("wg", (HID_I8, D), I32)
    L.inp("wu", (HID_I8, D), I32)
    L.inp("wd", (D, HID_I8), I32)
    L.out("out", (NTOK, D), F32)

    def scr(name, shape, dtype):
        return nc.dram_tensor(name, list(shape), dtype, kind="Internal").ap()

    xnT0 = scr("xnT0", (D, NTOK), BF16)
    xn_tm = scr("xn_tm", (NTOK, D), BF16)
    gpos = scr("gpos", (NPAD, 2), I32)
    spos = scr("spos", (NPAD, 2), I32)
    wgtg = scr("wgtg", (NPAD, 2), F32)
    xpk = scr("xpk", (NPAD, D), BF16)
    glog = scr("glog", (NTOK, N_EXP), F32)
    qT = scr("qT", (D, NTOK), BF16)
    kT = scr("kT", (D, NTOK), BF16)
    vtm = scr("vtm", (NTOK, D), BF16)
    aoT = scr("aoT", (D, NTOK), BF16)
    s0 = scr("s0", (NTOK, D), F32)
    gT = scr("gT", (NTOK + 128, D), F32)
    x1 = scr("x1", (NTOK, D), F32)
    xnT1 = scr("xnT1", (D, NTOK), BF16)
    rstd2 = scr("rstd2", (1, NTOK), F32)
    xinT = scr("xinT", (D_INNER, NTOK), BF16)
    resT = scr("resT", (D_INNER, NTOK), BF16)
    xsiluT = scr("xsiluT", (D_INNER, NTOK), BF16)
    deltaT = scr("deltaT", (D_INNER, NTOK), BF16)
    duT = scr("duT", (D_INNER, NTOK), BF16)
    Bm = scr("Bm", (D_STATE, NTOK), F32)
    Cm = scr("Cm", (D_STATE, NTOK), F32)
    yT = scr("yT", (D_INNER, NTOK), BF16)
    s1 = scr("s1", (NTOK, D), F32)
    gsilT = scr("gsilT", (HID_I8, NTOK), BF16)
    hTd = scr("hTd", (HID_I8, NTOK), BF16)
    g1 = scr("g1", (NTOK, D), F32)
    wgT_s = scr("wgT_s", (D, HID_I8), BF16)
    wuT_s = scr("wuT_s", (D, HID_I8), BF16)
    wdT_s = scr("wdT_s", (HID_I8, D), BF16)

    p.warmup((AF.Sqrt, AF.Exp, AF.Gelu, AF.Silu, AF.Ln))
    idf = build_identity(p, F32, "idf")
    idb = build_identity(p, BF16, "idb")
    p.ccol(EPS)
    p.ccol(1.0)
    import os as _os
    _np_max = int(_os.environ.get("K_PHASES", "99"))
    _ph = [0]

    def _go():
        _ph[0] += 1
        return _ph[0] <= _np_max

    if _go():
        with p.scope():
            phase_ln(p, "p1", a["x"], xnT0, idf, gate=(a["gw"], a["gb"], glog),
                     xn_tm_hbm=xn_tm)
        p.barrier()
    if _go():
        with p.scope():
            phase_qkv(p, xnT0, a["wq"], a["wk"], a["wv"], qT, kT, vtm, idb)
        p.barrier()
    if _go():
        with p.scope():
            phase_att(p, qT, kT, vtm, a["slopes"], aoT, idb)
        p.barrier()
    if _go():
        with p.scope():
            phase_wo(p, aoT, a["wo"], s0, idb, idf)
        p.barrier()
    if _go():
        with p.scope():
            phase_moe_routed(p, xn_tm, glog, a["w1"], a["b1"], a["w2"],
                             a["b2r8"], gT, gpos, spos, wgtg, xpk, idf, idb)
        p.barrier()
    if _go():
        with p.scope():
            phase_combine_ln(p, a["x"], s0, gT, x1, xnT1, rstd2, idf,
                             g_token_major=True)
        p.barrier()
    if _go():
        with p.scope():
            phase_mamba_in(p, xnT1, rstd2, a["norm_w"], a["w_in"], a["conv_w"],
                           a["conv_b"], xinT, resT, xsiluT, idf)
        p.barrier()
    if _go():
        with p.scope():
            phase_ssm_proj(p, xsiluT, a["w_x"], a["w_dt"], a["b_dt"], deltaT, duT,
                           Bm, Cm, idf)
        p.barrier()
    if _go():
        with p.scope():
            run_scan, nth = phase_scan_setup(p, deltaT, duT, xsiluT, resT, Bm, Cm,
                                             a["A"], a["D"], yT, idf)
            for b in range(B_PER_CORE):
                for th in range(nth):
                    run_scan(b, th)
        p.barrier()
    if _go():
        with p.scope():
            phase_ffn8(p, xnT1, a["wg"], a["wu"], a["wd"], gsilT, hTd, g1, idb, idf)
        p.barrier()
    if _go():
        with p.scope():
            phase_l1out(p, yT, a["w_out"], xnT1, s1, idf)
        p.barrier()
    if _go():
        with p.scope():
            phase_final(p, x1, s1, g1, a["out"])

    in_maps = [{"x": xs[c], "wq": wq, "wk": wk, "wv": wv, "wo": wo,
                "slopes": slopes, "gw": gate_w, "gb": gate_b, "w1": w1,
                "b1": b1, "w2": w2, "b2": b2, "b2r8": b2r8, "norm_w": norm_w,
                "w_in": w_in,
                "conv_w": conv_w, "conv_b": conv_b, "w_x": w_x, "w_dt": w_dt,
                "b_dt": b_dt, "A": A_w, "D": D_w, "w_out": w_out, "wg": wg8,
                "wu": wu8, "wd": wd8} for c in range(N_CORE)]
    res = L.run(in_maps, ["out"])
    out = np.stack([np.asarray(res["out"][c], f32).reshape(2, S, D)
                    for c in range(N_CORE)])
    return out.reshape(16, S, D)
